# version 06
# speedup vs baseline: 1.0597x; 1.0277x over previous
import numpy as np
import concourse.bass as bass
import concourse.mybir as mybir
from concourse.bass_utils import run_bass_kernel_spmd
from contextlib import ExitStack

F32 = mybir.dt.float32
BF16 = mybir.dt.bfloat16
U32 = mybir.dt.uint32
AF = mybir.ActivationFunctionType
ALU = mybir.AluOpType
AXX = mybir.AxisListType.X

T = 2048
D = 1024
NCH = 8
NCORES = 8
PB = 512
NT = T // PB
NTT = T // 128

ENGINES = ("tensor", "vector", "scalar", "gpsimd", "sync")

INPUT_SPECS = {
    "L0_attn_c_attn_w": (3072, 1024), "L0_attn_c_attn_b": (3072,),
    "L0_attn_q2l_w": (32, 64), "L0_attn_q2l_b": (32,),
    "L0_attn_kv2l_w": (32, 64), "L0_attn_kv2l_b": (32,),
    "L0_attn_l2o_w": (64, 32), "L0_attn_l2o_b": (64,),
    "L0_attn_wqk_w": (32, 32), "L0_attn_wqk_b": (32,),
    "L0_attn_cproj_w": (1024, 1024), "L0_attn_cproj_b": (1024,),
    "L0_ffn_gate_w": (8, 1024),
    "L0_ffn_w1": (8, 1024, 1024), "L0_ffn_w2": (8, 1024, 1024),
    "L0_ffn_w3": (8, 1024, 1024),
    "L0_ffn_w1s": (1, 1024, 1024), "L0_ffn_w2s": (1, 1024, 1024),
    "L0_ffn_w3s": (1, 1024, 1024),
    "L1_attn_W_in": (1024, 4096),
    "L1_attn_conv_w": (2048, 1, 4), "L1_attn_conv_b": (2048,),
    "L1_attn_W_x": (2048, 32),
    "L1_attn_W_dt": (16, 2048), "L1_attn_b_dt": (2048,),
    "L1_attn_A_log": (2048, 16), "L1_attn_D": (2048,),
    "L1_attn_W_out": (2048, 1024),
    "L1_ffn_Wg": (1024, 8),
    "L1_ffn_W1": (8, 1024, 2048), "L1_ffn_W2": (8, 1024, 2048),
    "L1_ffn_W3": (8, 2048, 1024),
    "L2_attn_Wq": (16, 1024, 64), "L2_attn_Wk": (16, 1024, 64),
    "L2_attn_Wv": (16, 1024, 64), "L2_attn_Wo": (16, 64, 1024),
    "L2_attn_bq": (16, 64), "L2_attn_bk": (16, 64), "L2_attn_bv": (16, 64),
    "L2_attn_bo": (1024,),
    "L2_ffn_gate_w": (1024, 8),
    "L2_ffn_w1": (8, 1024, 4096), "L2_ffn_w2": (8, 4096, 1024),
    "L3_attn_W_qkv": (3072, 1024), "L3_attn_b_qkv": (3072,),
    "L3_attn_W_out": (1024, 1024), "L3_attn_b_out": (1024,),
    "L3_attn_quantum_weights": (4, 8, 8),
    "L3_attn_photonic_phases": (16, 1024),
    "L3_attn_entanglement_gates": (3, 8, 4),
    "L3_attn_superposition_logits": (4,),
    "L3_ffn_W_router": (1024, 4),
    "L3_ffn_W1": (4, 1024, 4096), "L3_ffn_b1": (4, 4096),
    "L3_ffn_Wo": (4096, 1024), "L3_ffn_bo": (1024,),
    "L3_ffn_ln_gamma": (1024,), "L3_ffn_ln_beta": (1024,),
}


class _Op:
    __slots__ = ("eng", "fn", "reads", "writes", "special", "is_dma",
                 "is_coll", "needs_inc", "inc_no", "dma_no", "deps", "tag",
                 "qidx")

    def __init__(self, eng, fn, reads, writes, special, is_dma, is_coll, tag):
        self.eng, self.fn = eng, fn
        self.reads, self.writes = tuple(reads), tuple(writes)
        self.special, self.is_dma, self.is_coll = special, is_dma, is_coll
        self.needs_inc = False
        self.inc_no = self.dma_no = None
        self.deps = []
        self.tag = tag
        self.qidx = 0


class Prog:
    def __init__(self):
        self.ops = []
        self.byeng = {e: [] for e in ENGINES}
        self.last_writer = {}
        self.readers = {}
        self.coll_count = 0

    def op(self, eng, fn, reads=(), writes=(), special=False, is_dma=False,
           is_coll=False, tag=""):
        o = _Op(eng, fn, reads, writes, special, is_dma, is_coll, tag)
        deps = []
        for r in o.reads:
            w = self.last_writer.get(r)
            if w is not None:
                deps.append(w)
        for r in o.writes:
            w = self.last_writer.get(r)
            if w is not None:
                deps.append(w)
            for rd in self.readers.get(r, ()):
                deps.append(rd)
        o.deps = list({id(d): d for d in deps if d is not o}.values())
        for r in o.writes:
            self.last_writer[r] = o
            self.readers[r] = []
        for r in o.reads:
            self.readers.setdefault(r, []).append(o)
        o.qidx = len(self.byeng[eng])
        self.ops.append(o)
        self.byeng[eng].append(o)
        if is_coll:
            self.coll_count += 1
        return o

    SAFE_DIST = 16

    @staticmethod
    def _fifo_ok(p, o):
        if p.eng != o.eng or p.is_dma or p.is_coll or o.is_coll:
            return False
        if p.special or o.special:
            return False
        if p.eng == "tensor":
            return True
        return o.qidx - p.qidx >= Prog.SAFE_DIST

    def finalize(self, block, esem, dsem, csem, drain_engine="sync"):
        for o in self.ops:
            for p in o.deps:
                if p.is_dma or p.is_coll:
                    continue
                if self._fifo_ok(p, o):
                    continue
                p.needs_inc = True
        KD = 4
        self.KD = KD
        for eng in ENGINES:
            inc = dn = 0
            for o in self.byeng[eng]:
                if o.is_coll:
                    pass
                elif o.is_dma:
                    o.dma_no = dn
                    dn += 1
                if o.needs_inc:
                    inc += 1
                    o.inc_no = inc
        ccount = 0
        for o in self.ops:
            if o.is_coll:
                o.dma_no = ccount
                ccount += 1
        waited = {e: {} for e in ENGINES}
        all_waits = {}
        for o in self.ops:
            w = {}
            for p in o.deps:
                if p.is_coll:
                    key, val = ("c", "x", 0), p.dma_no + 1
                elif p.is_dma:
                    key = ("d", p.eng, p.dma_no % KD)
                    val = 16 * (p.dma_no // KD + 1)
                elif self._fifo_ok(p, o):
                    continue
                else:
                    key, val = ("e", p.eng), p.inc_no
                if val is None:
                    raise RuntimeError(f"dep without inc: {p.tag} <- {o.tag}")
                if waited[o.eng].get(key, 0) >= val:
                    continue
                if w.get(key, 0) < val:
                    w[key] = val
            for k, v in w.items():
                if waited[o.eng].get(k, 0) < v:
                    waited[o.eng][k] = v
            all_waits[id(o)] = w

        def emit_for(eng_name):
            def _go(eng):
                for o in self.byeng[eng_name]:
                    if o.is_coll:
                        if o.dma_no > 0:
                            eng.wait_ge(csem, o.dma_no)
                    elif o.is_dma:
                        if o.dma_no >= KD:
                            eng.wait_ge(dsem[eng_name][o.dma_no % KD],
                                        16 * (o.dma_no // KD))
                    for key, val in sorted(all_waits[id(o)].items()):
                        kind = key[0]
                        if kind == "d":
                            sem = dsem[key[1]][key[2]]
                        elif kind == "e":
                            sem = esem[key[1]]
                        else:
                            sem = csem
                        eng.wait_ge(sem, val)
                    ins = o.fn(eng)
                    if o.is_coll:
                        ins.then_inc(csem, 1)
                    elif o.is_dma:
                        ins.then_inc(dsem[eng_name][o.dma_no % KD], 16)
                    if o.needs_inc:
                        ins.then_inc(esem[eng_name], 1)
                if eng_name == drain_engine:
                    for e2 in ENGINES:
                        ndma = sum(1 for o in self.byeng[e2]
                                   if o.is_dma and not o.is_coll)
                        for k in range(min(KD, ndma)):
                            tot = (ndma - 1 - k) // KD + 1
                            eng.wait_ge(dsem[e2][(ndma - 1 - k) % KD],
                                        16 * tot)
                        ninc = sum(1 for o in self.byeng[e2] if o.needs_inc)
                        if ninc and e2 != eng_name:
                            eng.wait_ge(esem[e2], ninc)
                    if self.coll_count:
                        eng.wait_ge(csem, self.coll_count)
            return _go

        for name, starter in (("tensor", block.tensor),
                              ("vector", block.vector),
                              ("scalar", block.scalar),
                              ("gpsimd", block.gpsimd),
                              ("sync", block.sync)):
            starter(emit_for(name))

    def stats(self):
        return {e: dict(n=len(self.byeng[e]),
                        dma=sum(1 for o in self.byeng[e] if o.is_dma),
                        inc=sum(1 for o in self.byeng[e] if o.needs_inc))
                for e in ENGINES}


class ColAlloc:
    def __init__(self, name, handle, ncols, page):
        self.name, self.h, self.ncols, self.page = name, handle, ncols, page
        self.pos = 0

    def alloc(self, n):
        if self.pos + n > self.ncols:
            raise RuntimeError(
                f"pool {self.name} overflow: {self.pos}+{n}>{self.ncols}")
        c0 = self.pos
        self.pos += n
        return c0

    def reset(self, pos=0):
        self.pos = pos

    def res(self, c0, n):
        return [f"{self.name}:{p}"
                for p in range(c0 // self.page, (c0 + n - 1) // self.page + 1)]


class Bank:
    def __init__(self, h, i):
        self.h = h
        self.res = [f"PS:{i}"]


class Ctx:
    pbank_allowed = (0, 1, 2, 3, 4, 5)

    def pbank(self, tag=""):
        al = self.pbank_allowed
        i = self.bank_rr_f32 % len(al)
        self.bank_rr_f32 += 1
        return self.banks[al[i]]

    def pbank16(self, tag=""):
        i = self.bank_rr_b16
        self.bank_rr_b16 = 6 + (i + 1 - 6) % 2
        return self.banks[i]

    def reset_pools(self):
        self.WP.reset()
        self.AB16.reset()
        self.AF32.reset()
        self.SROW.reset()
        self.SROWB.reset()


def mm(ctx, psum_ap, psum_res, lhsT_ap, lhsT_res, rhs_ap, rhs_res,
       start, stop, tag="mm"):
    ctx.P.op("tensor",
             lambda t, a=psum_ap, b=lhsT_ap, c=rhs_ap, s1=start, s2=stop:
                 t.matmul(a, b, c, start=s1, stop=s2),
             reads=list(lhsT_res) + list(rhs_res), writes=list(psum_res),
             tag=tag)


def copy(ctx, eng, out_ap, out_res, in_ap, in_res, tag="copy"):
    ctx.P.op(eng, lambda e, o=out_ap, i=in_ap: e.tensor_copy(o, i),
             reads=list(in_res), writes=list(out_res), tag=tag)


def scopy(ctx, out_ap, out_res, in_ap, in_res, tag="scopy"):
    ctx.P.op("scalar", lambda e, o=out_ap, i=in_ap: e.copy(o, i),
             reads=list(in_res), writes=list(out_res), tag=tag)


def act(ctx, out_ap, out_res, in_ap, in_res, func, bias=0.0, scale=1.0,
        extra_reads=(), accum_out=None, accum_res=(), tag="act"):
    def f(e, o=out_ap, i=in_ap, fu=func, b=bias, s=scale, ao=accum_out):
        if ao is not None:
            return e.activation(o, i, fu, bias=b, scale=s, accum_out=ao)
        return e.activation(o, i, fu, bias=b, scale=s)
    ctx.P.op("scalar", f, reads=list(in_res) + list(extra_reads),
             writes=list(out_res) + list(accum_res), tag=tag)


def tt(ctx, eng, out_ap, out_res, a_ap, a_res, b_ap, b_res, op, tag="tt"):
    ctx.P.op(eng, lambda e, o=out_ap, x=a_ap, y=b_ap, al=op:
             e.tensor_tensor(o, x, y, al),
             reads=list(a_res) + list(b_res), writes=list(out_res), tag=tag)


def ts(ctx, eng, out_ap, out_res, in_ap, in_res, scalar1, op0,
       scalar2=None, op1=None, extra_reads=(), tag="ts"):
    def f(e, o=out_ap, i=in_ap, s1=scalar1, s2=scalar2, a0=op0, a1=op1):
        if a1 is None:
            return e.tensor_scalar(o, i, s1, None, a0)
        return e.tensor_scalar(o, i, s1, s2, a0, a1)
    ctx.P.op(eng, f, reads=list(in_res) + list(extra_reads),
             writes=list(out_res), tag=tag)


def stt(ctx, out_ap, out_res, in0_ap, in0_res, scalar, in1_ap, in1_res,
        op0, op1, extra_reads=(), tag="stt", eng="vector"):
    ctx.P.op(eng,
             lambda e, o=out_ap, i0=in0_ap, s=scalar, i1=in1_ap, a0=op0,
             a1=op1: e.scalar_tensor_tensor(o, i0, s, i1, a0, a1),
             reads=list(in0_res) + list(in1_res) + list(extra_reads),
             writes=list(out_res), tag=tag)


def memset(ctx, eng, ap, res, val, tag="memset"):
    ctx.P.op(eng, lambda e, a=ap, v=val: e.memset(a, v),
             reads=(), writes=list(res), tag=tag)


def reduce(ctx, eng, out_ap, out_res, in_ap, in_res, op, negate=False,
           tag="red"):
    ctx.P.op(eng, lambda e, o=out_ap, i=in_ap, a=op, n=negate:
             e.tensor_reduce(o, i, AXX, a, negate=n),
             reads=list(in_res), writes=list(out_res), special=True, tag=tag)


def recip(ctx, out_ap, out_res, in_ap, in_res, tag="recip"):
    ctx.P.op("vector", lambda e, o=out_ap, i=in_ap: e.reciprocal(o, i),
             reads=list(in_res), writes=list(out_res), tag=tag)


def dma(ctx, queue, out_ap, out_res, in_ap, in_res, tag="dma"):
    ctx.P.op(queue, lambda e, o=out_ap, i=in_ap: e.dma_start(out=o, in_=i),
             reads=list(in_res), writes=list(out_res), is_dma=True, tag=tag)


def transp(ctx, psum_ap, psum_res, in_ap, in_res, ident_ap, tag="tr"):
    ctx.P.op("tensor",
             lambda t, o=psum_ap, i=in_ap, idn=ident_ap: t.transpose(o, i, idn),
             reads=list(in_res) + ["CONST"], writes=list(psum_res), tag=tag)


DEBUG = set()


def dbg(ctx, name, ap, res, shape, dtype):
    if name not in DEBUG:
        return
    t = ctx.nc.dram_tensor("dbg_" + name, list(shape), dtype,
                           kind="ExternalOutput")
    dma(ctx, "sync", t.ap()[:, :], [f"DBG:{name}"], ap, res, tag="dbg")


def load_vec_cols(ctx, name, dstcol0, n_chunks, row0_elems=0, tag="vec"):
    hb = ctx.inp[name].ap().rearrange("a b -> (a b)")
    nelem = n_chunks * 128
    src = hb[row0_elems:row0_elems + nelem].rearrange("(c p) -> c p", p=128)
    st0 = ctx.AF32.alloc(128)
    stage = ctx.AF32.h[0:n_chunks, st0:st0 + 128]
    dma(ctx, "sync", stage, ctx.AF32.res(st0, 128), src, [f"HBM:{name}"],
        tag=tag + "_ld")
    bank = ctx.pbank(tag)
    transp(ctx, bank.h[:, 0:n_chunks], bank.res, stage,
           ctx.AF32.res(st0, 128), ctx.IDF[0:n_chunks, 0:n_chunks],
           tag=tag + "_tr")
    copy(ctx, "vector", ctx.SROW.h[:, dstcol0:dstcol0 + n_chunks],
         ctx.SROW.res(dstcol0, n_chunks), bank.h[:, 0:n_chunks], bank.res,
         tag=tag + "_cp")
    ctx.AF32.reset(st0)


def ln_sublayer(ctx, tag):
    X, NB = ctx.X, ctx.NB
    eps = 1e-5
    af_base, sr_base, srb_base = ctx.AF32.pos, ctx.SROW.pos, ctx.SROWB.pos
    sq0 = ctx.AF32.alloc(4096)
    m0 = ctx.SROW.alloc(512)
    v0 = ctx.SROW.alloc(512)
    q0 = ctx.SROW.alloc(512)
    mb0 = ctx.SROWB.alloc(512)
    rb0 = ctx.SROWB.alloc(512)
    t0 = ctx.AF32.alloc(512)
    m_ap = ctx.SROW.h[0:1, m0:m0 + 512]
    m_res = ctx.SROW.res(m0, 512)
    v_ap = ctx.SROW.h[0:1, v0:v0 + 512]
    v_res = ctx.SROW.res(v0, 512)
    q_ap = ctx.SROW.h[0:1, q0:q0 + 512]
    q_res = ctx.SROW.res(q0, 512)
    mb_ap = ctx.SROWB.h[0:1, mb0:mb0 + 512]
    mb_res = ctx.SROWB.res(mb0, 512)
    rb_ap = ctx.SROWB.h[0:1, rb0:rb0 + 512]
    rb_res = ctx.SROWB.res(rb0, 512)
    for half in range(2):
        nts = [2 * half, 2 * half + 1]
        for c in range(NCH):
            sqc = sq0 + 2048 * (c % 2)
            sq_ap = ctx.AF32.h[:, sqc:sqc + 2048]
            sq_res = ctx.AF32.res(sqc, 2048)
            act(ctx, sq_ap, sq_res, X.h[:, 2048 * c:2048 * (c + 1)],
                [f"X:{c}"], AF.Square, tag=f"{tag}_sq{c}")
            for j, nt in enumerate(nts):
                mm(ctx, ctx.banks[2 * j].h[0:1, :], ctx.banks[2 * j].res,
                   ctx.ONESF, ["CONST"],
                   X.h[:, 2048 * c + PB * nt: 2048 * c + PB * (nt + 1)],
                   [f"X:{c}"], start=(c == 0), stop=(c == NCH - 1),
                   tag=f"{tag}_sum")
                mm(ctx, ctx.banks[2 * j + 1].h[0:1, :],
                   ctx.banks[2 * j + 1].res, ctx.ONESF, ["CONST"],
                   ctx.AF32.h[:, sqc + PB * nt: sqc + PB * (nt + 1)],
                   sq_res, start=(c == 0), stop=(c == NCH - 1),
                   tag=f"{tag}_sumsq")
        for j, nt in enumerate(nts):
            sb = ctx.banks[2 * j]
            qb = ctx.banks[2 * j + 1]
            ts(ctx, "vector", m_ap, m_res, sb.h[0:1, :], sb.res,
               1.0 / D, ALU.mult, tag=f"{tag}_mean")
            act(ctx, q_ap, q_res, m_ap, m_res, AF.Square, tag=f"{tag}_msq")
            stt(ctx, v_ap, v_res, qb.h[0:1, :], qb.res, 1.0 / D,
                q_ap, q_res, ALU.mult, ALU.subtract, tag=f"{tag}_var")
            ts(ctx, "vector", v_ap, v_res, v_ap, v_res, eps, ALU.add,
               tag=f"{tag}_eps")
            recip(ctx, v_ap, v_res, v_ap, v_res, tag=f"{tag}_rc")
            act(ctx, v_ap, v_res, v_ap, v_res, AF.Sqrt, tag=f"{tag}_rstd")
            copy(ctx, "vector", mb_ap, mb_res, m_ap, m_res, tag=f"{tag}_mb")
            copy(ctx, "vector", rb_ap, rb_res, v_ap, v_res, tag=f"{tag}_rb")
            bm = ctx.banks[4]
            br = ctx.banks[5]
            mm(ctx, bm.h[:, :], bm.res, ctx.ONESB_ROW, ["CONST"], mb_ap,
               mb_res, start=True, stop=True, tag=f"{tag}_bcm")
            mm(ctx, br.h[:, :], br.res, ctx.ONESB_ROW, ["CONST"], rb_ap,
               rb_res, start=True, stop=True, tag=f"{tag}_bcr")
            t_ap = ctx.AF32.h[:, t0:t0 + 512]
            t_res = ctx.AF32.res(t0, 512)
            for c in range(NCH):
                xs = X.h[:, 2048 * c + PB * nt: 2048 * c + PB * (nt + 1)]
                nb_ap = NB.h[:, 2048 * c + PB * nt: 2048 * c + PB * (nt + 1)]
                tt(ctx, "vector", t_ap, t_res, xs, [f"X:{c}"], bm.h[:, :],
                   bm.res, ALU.subtract, tag=f"{tag}_sub")
                tt(ctx, "vector", nb_ap, [f"NB:{c}"], t_ap, t_res, br.h[:, :],
                   br.res, ALU.mult, tag=f"{tag}_mul")
    ctx.AF32.reset(af_base)
    ctx.SROW.reset(sr_base)
    ctx.SROWB.reset(srb_base)


def l3_ffn(ctx, app):
    tag = f"L3f{app}"
    ln_sublayer(ctx, tag + "ln")
    ctx.reset_pools()
    NB, X = ctx.NB, ctx.X
    dbg(ctx, "nb", NB.h[:, :], [f"NB:{c}" for c in range(NCH)],
        [128, NCH * T], BF16)
    E, I = 4, 4096
    NMI = I // 128
    inp = ctx.inp

    wr0 = ctx.WP.alloc(NCH * 8)
    for ki in range(NCH):
        dst = ctx.WP.h[:, wr0 + 8 * ki: wr0 + 8 * ki + 4]
        dma(ctx, "gpsimd", dst, ctx.WP.res(wr0 + 8 * ki, 8),
            inp["L3_ffn_W_router"].ap()[128 * ki:128 * (ki + 1), :],
            ["HBM:L3r"], tag=tag + "_wr")
    m4_0 = ctx.SROWB.alloc(2048)
    M4 = ctx.SROWB.h[0:4, m4_0:m4_0 + 2048]
    M4_res = ctx.SROWB.res(m4_0, 2048)
    mx0 = ctx.SROW.alloc(128)
    oh0 = ctx.AB16.alloc(8)
    for tt_i in range(NTT):
        bank = ctx.pbank(tag)
        for ki in range(NCH):
            lhsT = NB.h[:, 2048 * ki + 128 * tt_i: 2048 * ki + 128 * (tt_i + 1)]
            mm(ctx, bank.h[:, 0:4], bank.res, lhsT, [f"NB:{ki}"],
               ctx.WP.h[:, wr0 + 8 * ki: wr0 + 8 * ki + 4],
               ctx.WP.res(wr0 + 8 * ki, 8),
               start=(ki == 0), stop=(ki == NCH - 1), tag=tag + "_rl")
        mx_ap = ctx.SROW.h[:, mx0 + (tt_i % 2):mx0 + (tt_i % 2) + 1]
        mx_res = ctx.SROW.res(mx0, 128)
        reduce(ctx, "vector", mx_ap, mx_res, bank.h[:, 0:4], bank.res,
               ALU.max, tag=tag + "_rmax")
        ohc = oh0 + 4 * (tt_i % 2)
        oh_ap = ctx.AB16.h[:, ohc:ohc + 4]
        oh_res = ctx.AB16.res(ohc, 4)
        ts(ctx, "vector", oh_ap, oh_res, bank.h[:, 0:4], bank.res,
           mx_ap, ALU.is_ge, extra_reads=mx_res, tag=tag + "_oh")
        bank2 = ctx.pbank16(tag)
        ctx.P.op("tensor",
                 lambda t, o=bank2.h[0:4, 0:128], i=oh_ap, idn=ctx.IDB:
                     t.transpose(o, i, idn),
                 reads=list(oh_res) + ["CONST"], writes=bank2.res,
                 tag=tag + "_tr")
        copy(ctx, "vector", M4[:, 128 * tt_i:128 * (tt_i + 1)], M4_res,
             bank2.h[0:4, 0:128], bank2.res, tag=tag + "_m4")

    bc0 = ctx.AB16.alloc(E * NT * 512)
    BC = {}
    for e in range(E):
        for nt in range(NT):
            c0 = bc0 + (e * NT + nt) * 512
            bank = ctx.pbank(tag)
            mm(ctx, bank.h[:, :], bank.res,
               ctx.ONEE[0:4, 128 * e:128 * (e + 1)], ["CONST"],
               M4[:, PB * nt:PB * (nt + 1)], M4_res,
               start=True, stop=True, tag=tag + "_bc")
            ap_ = ctx.AB16.h[:, c0:c0 + 512]
            res_ = ctx.AB16.res(c0, 512)
            scopy(ctx, ap_, res_, bank.h[:, :], bank.res, tag=tag + "_bcc")
            BC[(e, nt)] = (ap_, res_)

    dbg(ctx, "m4", M4, M4_res, [4, 2048], BF16)
    b1c = {}
    for e in range(E):
        c0 = ctx.SROW.alloc(32)
        load_vec_cols(ctx, "L3_ffn_b1", c0, 32, row0_elems=4096 * e,
                      tag=tag + f"_b1{e}")
        b1c[e] = c0

    w1slot = ctx.WP.alloc(2 * E * NCH * 128)
    hacc0 = ctx.AF32.alloc(4096)
    he0 = ctx.AB16.alloc(4 * 512)
    accb0 = ctx.AB16.alloc(2 * 512)
    w1flat = inp["L3_ffn_W1"].ap().rearrange("e k m -> (e k) m")
    for mi in range(NMI):
        sl = w1slot + (mi % 2) * E * NCH * 128
        for e in range(E):
            dst = ctx.WP.h[:, sl + e * NCH * 128:
                           sl + (e + 1) * NCH * 128].rearrange(
                               "p (k m) -> p k m", k=NCH)
            src = w1flat[1024 * e: 1024 * (e + 1),
                         128 * mi:128 * (mi + 1)].rearrange(
                             "(k p) m -> p k m", p=128)
            dma(ctx, "gpsimd", dst, ctx.WP.res(sl + e * NCH * 128, NCH * 128),
                src, ["HBM:L3W1"], tag=tag + "_w1")
        for nt in range(NT):
            base = hacc0 + 2048 * (nt % 2)
            a_ap = ctx.AF32.h[:, base:base + 512]
            a_res = ctx.AF32.res(base, 512)
            t_ap = ctx.AF32.h[:, base + 512:base + 1024]
            t_res = ctx.AF32.res(base + 512, 512)
            b2_ap = ctx.AF32.h[:, base + 1024:base + 1536]
            b2_res = ctx.AF32.res(base + 1024, 512)
            t2_ap = ctx.AF32.h[:, base + 1536:base + 2048]
            t2_res = ctx.AF32.res(base + 1536, 512)
            for e in range(E):
                bank = ctx.pbank(tag)
                for ki in range(NCH):
                    lhsT = ctx.WP.h[:, sl + e * NCH * 128 + 128 * ki:
                                    sl + e * NCH * 128 + 128 * (ki + 1)]
                    lres = ctx.WP.res(sl + e * NCH * 128 + 128 * ki, 128)
                    rhs = NB.h[:, 2048 * ki + PB * nt:
                               2048 * ki + PB * (nt + 1)]
                    mm(ctx, bank.h[:, :], bank.res, lhsT, lres, rhs,
                       [f"NB:{ki}"], start=(ki == 0), stop=(ki == NCH - 1),
                       tag=tag + "_h")
                hec = he0 + 512 * e
                he_ap = ctx.AB16.h[:, hec:hec + 512]
                he_res = ctx.AB16.res(hec, 512)
                act(ctx, he_ap, he_res, bank.h[:, :], bank.res, AF.Gelu,
                    bias=ctx.SROW.h[:, b1c[e] + mi:b1c[e] + mi + 1],
                    extra_reads=ctx.SROW.res(b1c[e] + mi, 1),
                    tag=tag + "_gelu")
            he = [ctx.AB16.h[:, he0 + 512 * e:he0 + 512 * (e + 1)]
                  for e in range(E)]
            hr = [ctx.AB16.res(he0 + 512 * e, 512) for e in range(E)]
            tt(ctx, "vector", a_ap, a_res, he[0], hr[0], BC[(0, nt)][0],
               BC[(0, nt)][1], ALU.mult, tag=tag + "_acc0")
            tt(ctx, "gpsimd", b2_ap, b2_res, he[2], hr[2], BC[(2, nt)][0],
               BC[(2, nt)][1], ALU.mult, tag=tag + "_acc2")
            tt(ctx, "vector", t_ap, t_res, he[1], hr[1], BC[(1, nt)][0],
               BC[(1, nt)][1], ALU.mult, tag=tag + "_acc1")
            tt(ctx, "gpsimd", t2_ap, t2_res, he[3], hr[3], BC[(3, nt)][0],
               BC[(3, nt)][1], ALU.mult, tag=tag + "_acc3")
            tt(ctx, "vector", a_ap, a_res, a_ap, a_res, t_ap, t_res,
               ALU.add, tag=tag + "_acca")
            tt(ctx, "gpsimd", b2_ap, b2_res, b2_ap, b2_res, t2_ap, t2_res,
               ALU.add, tag=tag + "_accb2")
            tt(ctx, "vector", a_ap, a_res, a_ap, a_res, b2_ap, b2_res,
               ALU.add, tag=tag + "_accj")
            accc = accb0 + 512 * (nt % 2)
            acc_ap = ctx.AB16.h[:, accc:accc + 512]
            acc_res = ctx.AB16.res(accc, 512)
            scopy(ctx, acc_ap, acc_res, a_ap, a_res, tag=tag + "_accb")
            dma(ctx, "sync", ctx.H3.ap()[128 * mi:128 * (mi + 1),
                                         PB * nt:PB * (nt + 1)],
                [f"H3:{nt}"], acc_ap, acc_res, tag=tag + "_hst")

    ctx.WP.reset()
    ctx.AB16.reset()
    ctx.AF32.reset()
    boc = ctx.SROW.alloc(8)
    load_vec_cols(ctx, "L3_ffn_bo", boc, 8, tag=tag + "_bo")
    gc = ctx.SROW.alloc(8)
    load_vec_cols(ctx, "L3_ffn_ln_gamma", gc, 8, tag=tag + "_g")
    bcc = ctx.SROW.alloc(8)
    load_vec_cols(ctx, "L3_ffn_ln_beta", bcc, 8, tag=tag + "_b")
    ts(ctx, "vector", ctx.SROW.h[:, gc:gc + 8], ctx.SROW.res(gc, 8),
       ctx.SROW.h[:, gc:gc + 8], ctx.SROW.res(gc, 8), 0.5, ALU.mult,
       tag=tag + "_g5")
    ts(ctx, "vector", ctx.SROW.h[:, bcc:bcc + 8], ctx.SROW.res(bcc, 8),
       ctx.SROW.h[:, bcc:bcc + 8], ctx.SROW.res(bcc, 8), 0.5, ALU.mult,
       tag=tag + "_b5")
    sy0 = ctx.AF32.alloc(2048)
    sq0 = ctx.AF32.alloc(2048)
    sy_res = ctx.AF32.res(sy0, 2048)
    sq_res = ctx.AF32.res(sq0, 2048)
    memset(ctx, "vector", ctx.AF32.h[0:1, sy0:sy0 + 2048], sy_res, 0.0,
           tag=tag + "_z1")
    memset(ctx, "vector", ctx.AF32.h[0:1, sq0:sq0 + 2048], sq_res, 0.0,
           tag=tag + "_z2")
    woflat = inp["L3_ffn_Wo"].ap()
    wo0 = ctx.WP.alloc(32 * 512)
    hslot0 = ctx.AB16.alloc(8 * 512)
    sqt0 = ctx.AF32.alloc(2 * 512)
    for och in range(2):
        for ki in range(32):
            dst = ctx.WP.h[:, wo0 + 512 * ki: wo0 + 512 * (ki + 1)]
            dma(ctx, "gpsimd", dst, ctx.WP.res(wo0 + 512 * ki, 512),
                woflat[128 * ki:128 * (ki + 1), 512 * och:512 * (och + 1)],
                ["HBM:L3Wo"], tag=tag + "_wo")
        for nt in range(NT):
            obanks = [ctx.banks[i] for i in range(4)]
            for ki in range(32):
                hc = hslot0 + 512 * (ki % 8)
                h_ap = ctx.AB16.h[:, hc:hc + 512]
                h_res = ctx.AB16.res(hc, 512)
                dma(ctx, "sync", h_ap, h_res,
                    ctx.H3.ap()[128 * ki:128 * (ki + 1),
                                PB * nt:PB * (nt + 1)],
                    [f"H3:{nt}"], tag=tag + "_hld")
                for oc4 in range(4):
                    lhsT = ctx.WP.h[:, wo0 + 512 * ki + 128 * oc4:
                                    wo0 + 512 * ki + 128 * (oc4 + 1)]
                    lres = ctx.WP.res(wo0 + 512 * ki + 128 * oc4, 128)
                    mm(ctx, obanks[oc4].h[:, :], obanks[oc4].res, lhsT, lres,
                       h_ap, h_res, start=(ki == 0), stop=(ki == 31),
                       tag=tag + "_o")
            for oc4 in range(4):
                oc = och * 4 + oc4
                nb_ap = NB.h[:, 2048 * oc + PB * nt: 2048 * oc + PB * (nt + 1)]
                stt(ctx, nb_ap, [f"NB:{oc}"], obanks[oc4].h[:, :],
                    obanks[oc4].res,
                    ctx.SROW.h[:, boc + oc:boc + oc + 1], nb_ap, [f"NB:{oc}"],
                    ALU.add, ALU.add,
                    extra_reads=ctx.SROW.res(boc + oc, 1), tag=tag + "_y")
    for nt in range(NT):
        sb = ctx.banks[nt % 2]
        qb = ctx.banks[2 + nt % 2]
        for oc in range(NCH):
            nb_ap = NB.h[:, 2048 * oc + PB * nt: 2048 * oc + PB * (nt + 1)]
            mm(ctx, sb.h[0:1, :], sb.res, ctx.ONESB_COL, ["CONST"], nb_ap,
               [f"NB:{oc}"], start=(oc == 0), stop=(oc == NCH - 1),
               tag=tag + "_ys")
            sq_ap = ctx.AF32.h[:, sqt0 + 512 * (oc % 2):
                               sqt0 + 512 * (oc % 2) + 512]
            sqt_res = ctx.AF32.res(sqt0 + 512 * (oc % 2), 512)
            act(ctx, sq_ap, sqt_res, nb_ap, [f"NB:{oc}"], AF.Square,
                tag=tag + "_ysq")
            mm(ctx, qb.h[0:1, :], qb.res, ctx.ONESF, ["CONST"], sq_ap,
               sqt_res, start=(oc == 0), stop=(oc == NCH - 1),
               tag=tag + "_yq")
        copy(ctx, "vector",
             ctx.AF32.h[0:1, sy0 + PB * nt:sy0 + PB * (nt + 1)], sy_res,
             sb.h[0:1, :], sb.res, tag=tag + "_ysa")
        copy(ctx, "vector",
             ctx.AF32.h[0:1, sq0 + PB * nt:sq0 + PB * (nt + 1)], sq_res,
             qb.h[0:1, :], qb.res, tag=tag + "_yqa")
    dbg(ctx, "ystat", ctx.AF32.h[0:1, sy0:sy0 + 4096],
        sy_res + sq_res, [1, 4096], F32)
    dbg(ctx, "ynb", NB.h[:, :], [f"NB:{c}" for c in range(NCH)],
        [128, NCH * T], BF16)
    m0 = ctx.SROW.alloc(512)
    v0 = ctx.SROW.alloc(512)
    q0 = ctx.SROW.alloc(512)
    mb0 = ctx.SROWB.alloc(512)
    rb0 = ctx.SROWB.alloc(512)
    t0f = ctx.AF32.alloc(512)
    m_ap = ctx.SROW.h[0:1, m0:m0 + 512]
    m_res = ctx.SROW.res(m0, 512)
    v_ap = ctx.SROW.h[0:1, v0:v0 + 512]
    v_res = ctx.SROW.res(v0, 512)
    q_ap = ctx.SROW.h[0:1, q0:q0 + 512]
    q_res = ctx.SROW.res(q0, 512)
    mb_ap = ctx.SROWB.h[0:1, mb0:mb0 + 512]
    mb_res = ctx.SROWB.res(mb0, 512)
    rb_ap = ctx.SROWB.h[0:1, rb0:rb0 + 512]
    rb_res = ctx.SROWB.res(rb0, 512)
    for nt in range(NT):
        ts(ctx, "vector", m_ap, m_res,
           ctx.AF32.h[0:1, sy0 + PB * nt:sy0 + PB * (nt + 1)], sy_res,
           1.0 / D, ALU.mult, tag=tag + "_mu")
        act(ctx, q_ap, q_res, m_ap, m_res, AF.Square, tag=tag + "_mu2")
        stt(ctx, v_ap, v_res,
            ctx.AF32.h[0:1, sq0 + PB * nt:sq0 + PB * (nt + 1)], sq_res,
            1.0 / D, q_ap, q_res, ALU.mult, ALU.subtract, tag=tag + "_v")
        ts(ctx, "vector", v_ap, v_res, v_ap, v_res, 1e-12, ALU.add,
           tag=tag + "_ve")
        recip(ctx, v_ap, v_res, v_ap, v_res, tag=tag + "_vr")
        act(ctx, v_ap, v_res, v_ap, v_res, AF.Sqrt, tag=tag + "_vs")
        copy(ctx, "vector", mb_ap, mb_res, m_ap, m_res, tag=tag + "_cmb")
        copy(ctx, "vector", rb_ap, rb_res, v_ap, v_res, tag=tag + "_crb")
        bm = ctx.banks[4]
        br = ctx.banks[5]
        mm(ctx, bm.h[:, :], bm.res, ctx.ONESB_ROW, ["CONST"], mb_ap, mb_res,
           start=True, stop=True, tag=tag + "_bm")
        mm(ctx, br.h[:, :], br.res, ctx.ONESB_ROW, ["CONST"], rb_ap, rb_res,
           start=True, stop=True, tag=tag + "_br")
        for oc in range(NCH):
            nb_ap = NB.h[:, 2048 * oc + PB * nt: 2048 * oc + PB * (nt + 1)]
            x_ap = X.h[:, 2048 * oc + PB * nt: 2048 * oc + PB * (nt + 1)]
            t_ap = ctx.AF32.h[:, t0f:t0f + 512]
            t_res = ctx.AF32.res(t0f, 512)
            tt(ctx, "vector", t_ap, t_res, nb_ap, [f"NB:{oc}"], bm.h[:, :],
               bm.res, ALU.subtract, tag=tag + "_fs")
            tt(ctx, "vector", t_ap, t_res, t_ap, t_res, br.h[:, :], br.res,
               ALU.mult, tag=tag + "_fm")
            ts(ctx, "vector", t_ap, t_res, t_ap, t_res,
               ctx.SROW.h[:, gc + oc:gc + oc + 1], ALU.mult,
               ctx.SROW.h[:, bcc + oc:bcc + oc + 1], ALU.add,
               extra_reads=ctx.SROW.res(gc, 8) + ctx.SROW.res(bcc, 8),
               tag=tag + "_fg")
            tt(ctx, "vector", x_ap, [f"X:{oc}"], x_ap, [f"X:{oc}"], t_ap,
               t_res, ALU.add, tag=tag + "_fx")
    ctx.reset_pools()


def lvc64(ctx, name, dstcol, row0_elems=0, pbase=0, tag="v64"):
    hb = ctx.inp[name].ap().rearrange("a b -> (a b)")
    src = hb[row0_elems:row0_elems + 64].rearrange("(c p) -> c p", p=64)
    st0 = ctx.AF32.alloc(64)
    stage = ctx.AF32.h[0:1, st0:st0 + 64]
    dma(ctx, "sync", stage, ctx.AF32.res(st0, 64), src, [f"HBM:{name}"],
        tag=tag + "_ld")
    bank = ctx.pbank(tag)
    transp(ctx, bank.h[pbase:pbase + 64, 0:1], bank.res, stage,
           ctx.AF32.res(st0, 64), ctx.IDF[0:1, 0:1], tag=tag + "_tr")
    copy(ctx, "vector", ctx.SROW.h[pbase:pbase + 64, dstcol:dstcol + 1],
         ctx.SROW.res(dstcol, 1), bank.h[pbase:pbase + 64, 0:1], bank.res,
         tag=tag + "_cp")
    ctx.AF32.reset(st0)


def l2_attn(ctx):
    tag = "L2a"
    ln_sublayer(ctx, tag + "ln")
    ctx.reset_pools()
    NB, X = ctx.NB, ctx.X
    H, A = 16, 64
    inp = ctx.inp
    wq = inp["L2_attn_Wq"].ap().rearrange("h k a -> (h k) a")
    wk = inp["L2_attn_Wk"].ap().rearrange("h k a -> (h k) a")
    wv = inp["L2_attn_Wv"].ap().rearrange("h k a -> (h k) a")
    qk0 = ctx.AB16.alloc(2048)
    kk0 = ctx.AB16.alloc(2048)
    vv0 = ctx.AB16.alloc(2048)
    vt0 = ctx.AB16.alloc(1024)
    pt0 = ctx.AB16.alloc(1024)
    yh0 = ctx.WP.alloc(16384)
    wslot = ctx.WP.alloc(2 * 3 * 512)
    y00 = ctx.AF32.alloc(512)
    rr0 = ctx.SROW.alloc(128)
    rb0 = ctx.SROWB.alloc(128)
    bqc = ctx.SROW.alloc(16)
    bkc = ctx.SROW.alloc(16)
    bvc = ctx.SROW.alloc(16)
    for h in range(H):
        lvc64(ctx, "L2_attn_bq", bqc + h, row0_elems=64 * h, pbase=0,
              tag=tag + "_bq")
        lvc64(ctx, "L2_attn_bk", bkc + h, row0_elems=64 * h, pbase=0,
              tag=tag + "_bk")
        lvc64(ctx, "L2_attn_bv", bvc + h, row0_elems=64 * h, pbase=0,
              tag=tag + "_bv")
    QK = ctx.AB16.h
    for h in range(H):
        sl = wslot + (h % 2) * 3 * 512
        for j, w in enumerate((wq, wk, wv)):
            dst = ctx.WP.h[:, sl + 512 * j: sl + 512 * (j + 1)].rearrange(
                "p (k a) -> p k a", k=NCH)
            src = w[1024 * h:1024 * (h + 1), :].rearrange(
                "(k p) a -> p k a", p=128)
            dma(ctx, "gpsimd", dst, ctx.WP.res(sl + 512 * j, 512), src,
                ["HBM:L2w"], tag=tag + "_w")
        for nt in range(NT):
            for j, (b0, pb, bc) in enumerate(
                    ((qk0, 0, bqc), (kk0, 0, bkc), (vv0, 0, bvc))):
                bank = ctx.banks[j]
                for ki in range(NCH):
                    lhsT = ctx.WP.h[:, sl + 512 * j + 64 * ki:
                                    sl + 512 * j + 64 * (ki + 1)]
                    lres = ctx.WP.res(sl + 512 * j, 512)
                    rhs = NB.h[:, 2048 * ki + PB * nt:
                               2048 * ki + PB * (nt + 1)]
                    mm(ctx, bank.h[pb:pb + 64, :], bank.res, lhsT, lres, rhs,
                       [f"NB:{ki}"], start=(ki == 0), stop=(ki == NCH - 1),
                       tag=tag + "_qkv")
                dst = ctx.AB16.h[pb:pb + 64, b0 + PB * nt:b0 + PB * (nt + 1)]
                dres = ctx.AB16.res(b0 + PB * nt, PB)
                act(ctx, dst, dres, bank.h[pb:pb + 64, :], bank.res,
                    AF.Identity,
                    bias=ctx.SROW.h[pb:pb + 64, bc + h:bc + h + 1],
                    extra_reads=ctx.SROW.res(bc + h, 1), tag=tag + "_qkve")
        for kt in range(NTT):
            bk16 = ctx.pbank16(tag)
            ctx.P.op("tensor",
                     lambda t, o=bk16.h[:, 0:64],
                     i=ctx.AB16.h[0:64, vv0 + 128 * kt:vv0 + 128 * (kt + 1)],
                     idn=ctx.IDB[0:64, 0:64]: t.transpose(o, i, idn),
                     reads=ctx.AB16.res(vv0 + 128 * kt, 128) + ["CONST"],
                     writes=bk16.res, tag=tag + "_vt")
            copy(ctx, "vector", ctx.AB16.h[:, vt0 + 64 * kt:vt0 + 64 * (kt + 1)],
                 ctx.AB16.res(vt0 + 64 * kt, 64), bk16.h[:, 0:64], bk16.res,
                 tag=tag + "_vtc")
        for qnt in range(NT):
            nkt = 4 * qnt + 4
            ob = ctx.banks[4]
            db = ctx.banks[5]
            for kt in range(nkt):
                sb = ctx.banks[kt % 4]
                mm(ctx, sb.h[:, :], sb.res,
                   ctx.AB16.h[0:64, kk0 + 128 * kt:kk0 + 128 * (kt + 1)],
                   ctx.AB16.res(kk0 + 128 * kt, 128),
                   ctx.AB16.h[0:64, qk0 + PB * qnt:qk0 + PB * (qnt + 1)],
                   ctx.AB16.res(qk0 + PB * qnt, PB),
                   start=True, stop=True, tag=tag + "_s")
                pc = pt0 + 512 * (kt % 2)
                p_ap = ctx.AB16.h[:, pc:pc + 512]
                p_res = ctx.AB16.res(pc, 512)
                act(ctx, p_ap, p_res, sb.h[:, :], sb.res, AF.Exp,
                    scale=0.125, tag=tag + "_exp")
                if kt >= 4 * qnt:
                    base = PB * qnt - 128 * kt
                    ctx.P.op("gpsimd",
                             lambda g, a=p_ap, b=base: g.affine_select(
                                 a, a, [[1, 512]], ALU.is_ge, 0.0,
                                 base=b, channel_multiplier=-1),
                             reads=list(p_res), writes=list(p_res),
                             tag=tag + "_mask")
                mm(ctx, ob.h[0:64, :], ob.res,
                   ctx.AB16.h[:, vt0 + 64 * kt:vt0 + 64 * (kt + 1)],
                   ctx.AB16.res(vt0 + 64 * kt, 64), p_ap, p_res,
                   start=(kt == 0), stop=(kt == nkt - 1), tag=tag + "_av")
                mm(ctx, db.h[0:1, :], db.res, ctx.ONESB_COL, ["CONST"],
                   p_ap, p_res, start=(kt == 0), stop=(kt == nkt - 1),
                   tag=tag + "_den")
            r_ap = ctx.SROW.h[0:1, rr0:rr0 + 512]
            r_res = ctx.SROW.res(rr0, 512)
            recip(ctx, r_ap, r_res, db.h[0:1, :], db.res, tag=tag + "_rc")
            rb_ap = ctx.SROWB.h[0:1, rb0:rb0 + 512]
            rb_res = ctx.SROWB.res(rb0, 512)
            copy(ctx, "vector", rb_ap, rb_res, r_ap, r_res, tag=tag + "_rcb")
            y0_ap = ctx.AF32.h[0:64, y00:y00 + 512]
            y0_res = ctx.AF32.res(y00, 512)
            scopy(ctx, y0_ap, y0_res, ob.h[0:64, :], ob.res, tag=tag + "_ye")
            bb = ctx.banks[kt % 4]
            mm(ctx, bb.h[0:64, :], bb.res, ctx.ONESB_ROW[:, 0:64], ["CONST"],
               rb_ap, rb_res, start=True, stop=True, tag=tag + "_rbc")
            ydst = ctx.WP.h[64 * (h % 2):64 * (h % 2) + 64,
                            yh0 + 2048 * (h // 2) + PB * qnt:
                            yh0 + 2048 * (h // 2) + PB * (qnt + 1)]
            ydres = ctx.WP.res(yh0 + 2048 * (h // 2) + PB * qnt, PB)
            tt(ctx, "vector", ydst, ydres, y0_ap, y0_res, bb.h[0:64, :],
               bb.res, ALU.mult, tag=tag + "_yn")
    ctx.AB16.reset()
    wo0 = ctx.AB16.alloc(8192)
    woflat = inp["L2_attn_Wo"].ap().rearrange("h a d -> (h a) d")
    for ki in range(NCH):
        dst = ctx.AB16.h[:, wo0 + 1024 * ki: wo0 + 1024 * (ki + 1)]
        dma(ctx, "gpsimd", dst, ctx.AB16.res(wo0 + 1024 * ki, 1024),
            woflat[128 * ki:128 * (ki + 1), :], ["HBM:L2wo"], tag=tag + "_wo")
    boc = ctx.SROW.alloc(8)
    load_vec_cols(ctx, "L2_attn_bo", boc, 8, tag=tag + "_bo")
    for oc in range(NCH):
        for nt in range(NT):
            bank = ctx.pbank(tag)
            for ki in range(NCH):
                lhsT = ctx.AB16.h[:, wo0 + 1024 * ki + 128 * oc:
                                  wo0 + 1024 * ki + 128 * (oc + 1)]
                lres = ctx.AB16.res(wo0 + 1024 * ki + 128 * oc, 128)
                rhs = ctx.WP.h[:, yh0 + 2048 * ki + PB * nt:
                               yh0 + 2048 * ki + PB * (nt + 1)]
                rres = ctx.WP.res(yh0 + 2048 * ki + PB * nt, PB)
                mm(ctx, bank.h[:, :], bank.res, lhsT, lres, rhs, rres,
                   start=(ki == 0), stop=(ki == NCH - 1), tag=tag + "_out")
            x_ap = X.h[:, 2048 * oc + PB * nt: 2048 * oc + PB * (nt + 1)]
            stt(ctx, x_ap, [f"X:{oc}"], bank.h[:, :], bank.res,
                ctx.SROW.h[:, boc + oc:boc + oc + 1], x_ap, [f"X:{oc}"],
                ALU.add, ALU.add, extra_reads=ctx.SROW.res(boc + oc, 1),
                tag=tag + "_xa")
    ctx.reset_pools()




def l0_ffn(ctx, app):
    tag = f"L0f{app}"
    ln_sublayer(ctx, tag + "ln")
    ctx.reset_pools()
    NB, X = ctx.NB, ctx.X
    E = 8
    inp = ctx.inp
    gst = ctx.AF32.alloc(1024)
    stage = ctx.AF32.h[0:8, gst:gst + 1024]
    dma(ctx, "sync", stage, ctx.AF32.res(gst, 1024),
        inp["L0_ffn_gate_w"].ap()[:, :], ["HBM:L0g"], tag=tag + "_g")
    gt0 = ctx.SROWB.alloc(64)
    for ki in range(NCH):
        bank = ctx.pbank(tag)
        ctx.P.op("tensor",
                 lambda t, o=bank.h[:, 0:8],
                 i=ctx.AF32.h[0:8, gst + 128 * ki:gst + 128 * (ki + 1)],
                 idn=ctx.IDF[0:8, 0:8]: t.transpose(o, i, idn),
                 reads=ctx.AF32.res(gst, 1024) + ["CONST"], writes=bank.res,
                 tag=tag + "_gtr")
        copy(ctx, "vector", ctx.SROWB.h[:, gt0 + 8 * ki:gt0 + 8 * (ki + 1)],
             ctx.SROWB.res(gt0, 64), bank.h[:, 0:8], bank.res,
             tag=tag + "_gtc")
    s8_0 = ctx.SROWB.alloc(2048)
    S8 = ctx.SROWB.h[0:8, s8_0:s8_0 + 2048]
    S8_res = ctx.SROWB.res(s8_0, 2048)
    nm0 = ctx.SROW.alloc(128)
    t80 = ctx.AF32.alloc(512)
    sv0 = ctx.AB16.alloc(16)
    for tt_i in range(NTT):
        bank = ctx.pbank(tag)
        for ki in range(NCH):
            lhsT = NB.h[:, 2048 * ki + 128 * tt_i: 2048 * ki + 128 * (tt_i + 1)]
            mm(ctx, bank.h[:, 0:8], bank.res, lhsT, [f"NB:{ki}"],
               ctx.SROWB.h[:, gt0 + 8 * ki:gt0 + 8 * (ki + 1)],
               ctx.SROWB.res(gt0, 64),
               start=(ki == 0), stop=(ki == NCH - 1), tag=tag + "_rl")
        nm_ap = ctx.SROW.h[:, nm0:nm0 + 1]
        nm_res = ctx.SROW.res(nm0, 128)
        reduce(ctx, "vector", nm_ap, nm_res, bank.h[:, 0:8], bank.res,
               ALU.max, negate=True, tag=tag + "_nmax")
        ex_ap = ctx.AF32.h[:, t80:t80 + 8]
        ex_res = ctx.AF32.res(t80, 512)
        sm_ap = ctx.SROW.h[:, nm0 + 1:nm0 + 2]
        act(ctx, ex_ap, ex_res, bank.h[:, 0:8], bank.res, AF.Exp,
            bias=nm_ap, extra_reads=nm_res, accum_out=sm_ap,
            accum_res=nm_res, tag=tag + "_exp")
        rc_ap = ctx.SROW.h[:, nm0 + 2:nm0 + 3]
        recip(ctx, rc_ap, nm_res, sm_ap, nm_res, tag=tag + "_rc")
        sc_ap = ctx.AF32.h[:, t80 + 8:t80 + 16]
        ts(ctx, "vector", sc_ap, ex_res, ex_ap, ex_res, rc_ap, ALU.mult,
           extra_reads=nm_res, tag=tag + "_sc")
        t8_ap = ctx.AF32.h[:, t80 + 16:t80 + 24]
        ctx.P.op("vector", lambda e, o=t8_ap, i=sc_ap: e.max(o, i),
                 reads=list(ex_res), writes=list(ex_res), special=True,
                 tag=tag + "_top8")
        msk_ap = ctx.AF32.h[:, t80 + 24:t80 + 32]
        ts(ctx, "vector", msk_ap, ex_res, sc_ap, ex_res,
           t8_ap[:, 1:2], ALU.is_ge, extra_reads=ex_res, tag=tag + "_msk")
        svc = sv0 + 8 * (tt_i % 2)
        sv_ap = ctx.AB16.h[:, svc:svc + 8]
        sv_res = ctx.AB16.res(svc, 8)
        tt(ctx, "vector", sv_ap, sv_res, sc_ap, ex_res, msk_ap, ex_res,
           ALU.mult, tag=tag + "_sv")
        bank2 = ctx.pbank16(tag)
        ctx.P.op("tensor",
                 lambda t, o=bank2.h[0:8, 0:128], i=sv_ap,
                 idn=ctx.IDB: t.transpose(o, i, idn),
                 reads=list(sv_res) + ["CONST"], writes=bank2.res,
                 tag=tag + "_str")
        copy(ctx, "vector", S8[:, 128 * tt_i:128 * (tt_i + 1)], S8_res,
             bank2.h[0:8, 0:128], bank2.res, tag=tag + "_s8")
    w10 = ctx.WP.alloc(4096)
    w30 = ctx.WP.alloc(4096)
    w20 = ctx.WP.alloc(4096)
    bce0 = ctx.AB16.alloc(2048)
    hb0 = ctx.AB16.alloc(2048)
    tmp0 = ctx.AF32.alloc(1024)
    for e in range(E + 1):
        shared = (e == E)
        if shared:
            w1f = inp["L0_ffn_w1s"].ap().rearrange("e k m -> (e k) m")
            w3f = inp["L0_ffn_w3s"].ap().rearrange("e k m -> (e k) m")
            w2f = inp["L0_ffn_w2s"].ap().rearrange("e k m -> (e k) m")
            erow = 0
        else:
            w1f = inp["L0_ffn_w1"].ap().rearrange("e k m -> (e k) m")
            w3f = inp["L0_ffn_w3"].ap().rearrange("e k m -> (e k) m")
            w2f = inp["L0_ffn_w2"].ap().rearrange("e k m -> (e k) m")
            erow = 1024 * e
            for nt in range(NT):
                bank = ctx.pbank(tag)
                mm(ctx, bank.h[:, :], bank.res,
                   ctx.ONEE[0:8, 128 * e:128 * (e + 1)], ["CONST"],
                   S8[:, PB * nt:PB * (nt + 1)], S8_res,
                   start=True, stop=True, tag=tag + "_bce")
                scopy(ctx,
                      ctx.AB16.h[:, bce0 + 512 * nt:bce0 + 512 * (nt + 1)],
                      ctx.AB16.res(bce0 + 512 * nt, 512), bank.h[:, :],
                      bank.res, tag=tag + "_bcec")
        for hh in range(2):
            for wdst, wsrc in ((w10, w1f), (w30, w3f)):
                dst = ctx.WP.h[:, wdst:wdst + 4096].rearrange(
                    "p (k m) -> p k m", k=NCH)
                src = wsrc[erow:erow + 1024,
                           512 * hh:512 * (hh + 1)].rearrange(
                               "(k p) m -> p k m", p=128)
                dma(ctx, "gpsimd", dst, ctx.WP.res(wdst, 4096), src,
                    ["HBM:L0w"], tag=tag + "_w13")
            dst = ctx.WP.h[:, w20:w20 + 4096].rearrange(
                "p (k m) -> p k m", k=4)
            src = w2f[erow + 512 * hh:erow + 512 * (hh + 1),
                      :].rearrange("(k p) m -> p k m", p=128)
            dma(ctx, "gpsimd", dst, ctx.WP.res(w20, 4096), src,
                ["HBM:L0w"], tag=tag + "_w2")
            for nt in range(NT):
                bce_ap = ctx.AB16.h[:, bce0 + 512 * nt:bce0 + 512 * (nt + 1)]
                bce_res = ctx.AB16.res(bce0 + 512 * nt, 512)
                for mi in range(4):
                    b1 = ctx.banks[0]
                    b3 = ctx.banks[1]
                    for ki in range(NCH):
                        rhs = NB.h[:, 2048 * ki + PB * nt:
                                   2048 * ki + PB * (nt + 1)]
                        l1 = ctx.WP.h[:, w10 + 512 * ki + 128 * mi:
                                      w10 + 512 * ki + 128 * (mi + 1)]
                        mm(ctx, b1.h[:, :], b1.res, l1,
                           ctx.WP.res(w10 + 512 * ki, 512), rhs,
                           [f"NB:{ki}"], start=(ki == 0),
                           stop=(ki == NCH - 1), tag=tag + "_r1")
                        l3 = ctx.WP.h[:, w30 + 512 * ki + 128 * mi:
                                      w30 + 512 * ki + 128 * (mi + 1)]
                        mm(ctx, b3.h[:, :], b3.res, l3,
                           ctx.WP.res(w30 + 512 * ki, 512), rhs,
                           [f"NB:{ki}"], start=(ki == 0),
                           stop=(ki == NCH - 1), tag=tag + "_r3")
                    u_ap = ctx.AF32.h[:, tmp0:tmp0 + 512]
                    u_res = ctx.AF32.res(tmp0, 512)
                    ub_ap = ctx.AF32.h[:, tmp0 + 512:tmp0 + 1024]
                    ub_res = ctx.AF32.res(tmp0 + 512, 512)
                    h_ap = ctx.AB16.h[:, hb0 + 512 * mi:hb0 + 512 * (mi + 1)]
                    h_res = ctx.AB16.res(hb0 + 512 * mi, 512)
                    if shared:
                        act(ctx, ub_ap, ub_res, b1.h[:, :], b1.res, AF.Silu,
                            tag=tag + "_silu")
                        tt(ctx, "vector", h_ap, h_res, ub_ap, ub_res,
                           b3.h[:, :], b3.res, ALU.mult, tag=tag + "_h")
                    else:
                        tt(ctx, "vector", u_ap, u_res, b1.h[:, :], b1.res,
                           bce_ap, bce_res, ALU.mult, tag=tag + "_us")
                        act(ctx, ub_ap, ub_res, u_ap, u_res, AF.Silu,
                            tag=tag + "_silu")
                        tt(ctx, "vector", u_ap, u_res, b3.h[:, :], b3.res,
                           bce_ap, bce_res, ALU.mult, tag=tag + "_vs")
                        tt(ctx, "vector", h_ap, h_res, ub_ap, ub_res,
                           u_ap, u_res, ALU.mult, tag=tag + "_h")
                for oc in range(NCH):
                    bank = ctx.banks[2 + (oc % 4)]
                    for mi in range(4):
                        l2 = ctx.WP.h[:, w20 + 1024 * mi + 128 * oc:
                                      w20 + 1024 * mi + 128 * (oc + 1)]
                        mm(ctx, bank.h[:, :], bank.res, l2,
                           ctx.WP.res(w20 + 1024 * mi, 1024),
                           ctx.AB16.h[:, hb0 + 512 * mi:hb0 + 512 * (mi + 1)],
                           ctx.AB16.res(hb0 + 512 * mi, 512),
                           start=(mi == 0), stop=(mi == 3), tag=tag + "_o")
                    x_ap = X.h[:, 2048 * oc + PB * nt:
                               2048 * oc + PB * (nt + 1)]
                    stt(ctx, x_ap, [f"X:{oc}"], bank.h[:, :], bank.res, 0.5,
                        x_ap, [f"X:{oc}"], ALU.mult, ALU.add,
                        tag=tag + "_xa")
    ctx.reset_pools()




def l1_ffn(ctx, app):
    tag = f"L1f{app}"
    ln_sublayer(ctx, tag + "ln")
    ctx.reset_pools()
    NB, X = ctx.NB, ctx.X
    E = 8
    HD = 2048
    inp = ctx.inp
    gt0 = ctx.SROWB.alloc(64)
    for ki in range(NCH):
        dst = ctx.SROWB.h[:, gt0 + 8 * ki:gt0 + 8 * (ki + 1)]
        dma(ctx, "gpsimd", dst, ctx.SROWB.res(gt0, 64),
            inp["L1_ffn_Wg"].ap()[128 * ki:128 * (ki + 1), :],
            ["HBM:L1g"], tag=tag + "_g")
    w8_0 = ctx.SROWB.alloc(2048)
    W8 = ctx.SROWB.h[0:8, w8_0:w8_0 + 2048]
    W8_res = ctx.SROWB.res(w8_0, 2048)
    nm0 = ctx.SROW.alloc(128)
    t80 = ctx.AF32.alloc(512)
    sv0 = ctx.AB16.alloc(16)
    for tt_i in range(NTT):
        bank = ctx.pbank(tag)
        for ki in range(NCH):
            lhsT = NB.h[:, 2048 * ki + 128 * tt_i: 2048 * ki + 128 * (tt_i + 1)]
            mm(ctx, bank.h[:, 0:8], bank.res, lhsT, [f"NB:{ki}"],
               ctx.SROWB.h[:, gt0 + 8 * ki:gt0 + 8 * (ki + 1)],
               ctx.SROWB.res(gt0, 64),
               start=(ki == 0), stop=(ki == NCH - 1), tag=tag + "_rl")
        lg_ap = ctx.AF32.h[:, t80:t80 + 8]
        lg_res = ctx.AF32.res(t80, 512)
        copy(ctx, "vector", lg_ap, lg_res, bank.h[:, 0:8], bank.res,
             tag=tag + "_lg")
        t8_ap = ctx.AF32.h[:, t80 + 8:t80 + 16]
        ctx.P.op("vector", lambda e, o=t8_ap, i=lg_ap: e.max(o, i),
                 reads=list(lg_res), writes=list(lg_res), special=True,
                 tag=tag + "_top8")
        d_ap = ctx.SROW.h[:, nm0:nm0 + 1]
        d_res = ctx.SROW.res(nm0, 128)
        tt(ctx, "vector", d_ap, d_res, t8_ap[:, 0:1], lg_res,
           t8_ap[:, 1:2], lg_res, ALU.subtract, tag=tag + "_d")
        p1_ap = ctx.SROW.h[:, nm0 + 1:nm0 + 2]
        act(ctx, p1_ap, d_res, d_ap, d_res, AF.Sigmoid, scale=1.0,
            tag=tag + "_p1")
        ts(ctx, "vector", p1_ap, d_res, p1_ap, d_res, 0.5, ALU.mult,
           tag=tag + "_p1h")
        p2_ap = ctx.SROW.h[:, nm0 + 2:nm0 + 3]
        ts(ctx, "vector", p2_ap, d_res, p1_ap, d_res, -1.0, ALU.mult, 0.5,
           ALU.add, tag=tag + "_p2")
        m1_ap = ctx.AF32.h[:, t80 + 16:t80 + 24]
        ts(ctx, "vector", m1_ap, lg_res, lg_ap, lg_res, t8_ap[:, 0:1],
           ALU.is_ge, tag=tag + "_m1")
        m2_ap = ctx.AF32.h[:, t80 + 24:t80 + 32]
        ts(ctx, "vector", m2_ap, lg_res, lg_ap, lg_res, t8_ap[:, 1:2],
           ALU.is_ge, tag=tag + "_m2g")
        tt(ctx, "vector", m2_ap, lg_res, m2_ap, lg_res, m1_ap, lg_res,
           ALU.subtract, tag=tag + "_m2")
        wv_ap = ctx.AF32.h[:, t80 + 32:t80 + 40]
        ts(ctx, "vector", wv_ap, lg_res, m1_ap, lg_res, p1_ap, ALU.mult,
           extra_reads=d_res, tag=tag + "_w1m")
        stt(ctx, wv_ap, lg_res, m2_ap, lg_res, p2_ap, wv_ap, lg_res,
            ALU.mult, ALU.add, extra_reads=d_res, tag=tag + "_w2m")
        svc = sv0 + 8 * (tt_i % 2)
        sv_ap = ctx.AB16.h[:, svc:svc + 8]
        sv_res = ctx.AB16.res(svc, 8)
        copy(ctx, "vector", sv_ap, sv_res, wv_ap, lg_res, tag=tag + "_svb")
        bank2 = ctx.pbank16(tag)
        ctx.P.op("tensor",
                 lambda t, o=bank2.h[0:8, 0:128], i=sv_ap,
                 idn=ctx.IDB: t.transpose(o, i, idn),
                 reads=list(sv_res) + ["CONST"], writes=bank2.res,
                 tag=tag + "_str")
        copy(ctx, "vector", W8[:, 128 * tt_i:128 * (tt_i + 1)], W8_res,
             bank2.h[0:8, 0:128], bank2.res, tag=tag + "_w8")
    w10 = ctx.WP.alloc(4096)
    w20 = ctx.WP.alloc(4096)
    w30 = ctx.WP.alloc(4096)
    bce0 = ctx.AB16.alloc(2048)
    hb0 = ctx.AB16.alloc(2048)
    tmp0 = ctx.AF32.alloc(1024)
    w1f = inp["L1_ffn_W1"].ap().rearrange("e k m -> (e k) m")
    w2f = inp["L1_ffn_W2"].ap().rearrange("e k m -> (e k) m")
    w3f = inp["L1_ffn_W3"].ap().rearrange("e k m -> (e k) m")
    for e in range(E):
        for nt in range(NT):
            bank = ctx.pbank(tag)
            mm(ctx, bank.h[:, :], bank.res,
               ctx.ONEE[0:8, 128 * e:128 * (e + 1)], ["CONST"],
               W8[:, PB * nt:PB * (nt + 1)], W8_res,
               start=True, stop=True, tag=tag + "_bce")
            scopy(ctx, ctx.AB16.h[:, bce0 + 512 * nt:bce0 + 512 * (nt + 1)],
                  ctx.AB16.res(bce0 + 512 * nt, 512), bank.h[:, :],
                  bank.res, tag=tag + "_bcec")
        for qq in range(4):
            for wdst, wsrc in ((w10, w1f), (w20, w2f)):
                dst = ctx.WP.h[:, wdst:wdst + 4096].rearrange(
                    "p (k m) -> p k m", k=NCH)
                src = wsrc[1024 * e:1024 * (e + 1),
                           512 * qq:512 * (qq + 1)].rearrange(
                               "(k p) m -> p k m", p=128)
                dma(ctx, "gpsimd", dst, ctx.WP.res(wdst, 4096), src,
                    ["HBM:L1w"], tag=tag + "_w12")
            dst = ctx.WP.h[:, w30:w30 + 4096].rearrange(
                "p (k m) -> p k m", k=4)
            src = w3f[2048 * e + 512 * qq:2048 * e + 512 * (qq + 1),
                      :].rearrange("(k p) m -> p k m", p=128)
            dma(ctx, "gpsimd", dst, ctx.WP.res(w30, 4096), src,
                ["HBM:L1w"], tag=tag + "_w3")
            for nt in range(NT):
                bce_ap = ctx.AB16.h[:, bce0 + 512 * nt:bce0 + 512 * (nt + 1)]
                bce_res = ctx.AB16.res(bce0 + 512 * nt, 512)
                for mi in range(4):
                    b1 = ctx.banks[0]
                    b2 = ctx.banks[1]
                    for ki in range(NCH):
                        rhs = NB.h[:, 2048 * ki + PB * nt:
                                   2048 * ki + PB * (nt + 1)]
                        l1 = ctx.WP.h[:, w10 + 512 * ki + 128 * mi:
                                      w10 + 512 * ki + 128 * (mi + 1)]
                        mm(ctx, b1.h[:, :], b1.res, l1,
                           ctx.WP.res(w10 + 512 * ki, 512), rhs,
                           [f"NB:{ki}"], start=(ki == 0),
                           stop=(ki == NCH - 1), tag=tag + "_u")
                        l2 = ctx.WP.h[:, w20 + 512 * ki + 128 * mi:
                                      w20 + 512 * ki + 128 * (mi + 1)]
                        mm(ctx, b2.h[:, :], b2.res, l2,
                           ctx.WP.res(w20 + 512 * ki, 512), rhs,
                           [f"NB:{ki}"], start=(ki == 0),
                           stop=(ki == NCH - 1), tag=tag + "_v")
                    ub_ap = ctx.AF32.h[:, tmp0:tmp0 + 512]
                    ub_res = ctx.AF32.res(tmp0, 512)
                    h_ap = ctx.AB16.h[:, hb0 + 512 * mi:hb0 + 512 * (mi + 1)]
                    h_res = ctx.AB16.res(hb0 + 512 * mi, 512)
                    act(ctx, ub_ap, ub_res, b1.h[:, :], b1.res, AF.Silu,
                        tag=tag + "_silu")
                    tt(ctx, "vector", h_ap, h_res, ub_ap, ub_res,
                       b2.h[:, :], b2.res, ALU.mult, tag=tag + "_h")
                for oc in range(NCH):
                    bank = ctx.banks[2 + (oc % 4)]
                    for mi in range(4):
                        l3 = ctx.WP.h[:, w30 + 1024 * mi + 128 * oc:
                                      w30 + 1024 * mi + 128 * (oc + 1)]
                        mm(ctx, bank.h[:, :], bank.res, l3,
                           ctx.WP.res(w30 + 1024 * mi, 1024),
                           ctx.AB16.h[:, hb0 + 512 * mi:hb0 + 512 * (mi + 1)],
                           ctx.AB16.res(hb0 + 512 * mi, 512),
                           start=(mi == 0), stop=(mi == 3), tag=tag + "_o")
                    t_ap = ctx.AF32.h[:, tmp0 + 512:tmp0 + 1024]
                    t_res = ctx.AF32.res(tmp0 + 512, 512)
                    tt(ctx, "vector", t_ap, t_res, bank.h[:, :], bank.res,
                       bce_ap, bce_res, ALU.mult, tag=tag + "_ow")
                    x_ap = X.h[:, 2048 * oc + PB * nt:
                               2048 * oc + PB * (nt + 1)]
                    tt(ctx, "vector", x_ap, [f"X:{oc}"], x_ap, [f"X:{oc}"],
                       t_ap, t_res, ALU.add, tag=tag + "_xa")
    ctx.reset_pools()




def l2_ffn(ctx, app):
    tag = f"L2f{app}"
    ln_sublayer(ctx, tag + "ln")
    ctx.reset_pools()
    NB, X = ctx.NB, ctx.X
    E = 8
    CAP = 1280
    inp = ctx.inp
    gt0 = ctx.SROWB.alloc(64)
    for ki in range(NCH):
        dst = ctx.SROWB.h[:, gt0 + 8 * ki:gt0 + 8 * (ki + 1)]
        dma(ctx, "gpsimd", dst, ctx.SROWB.res(gt0, 64),
            inp["L2_ffn_gate_w"].ap()[128 * ki:128 * (ki + 1), :],
            ["HBM:L2g"], tag=tag + "_g")
    m8_0 = ctx.SROWB.alloc(2048)
    M8 = ctx.SROWB.h[0:8, m8_0:m8_0 + 2048]
    M8_res = ctx.SROWB.res(m8_0, 2048)
    nm0 = ctx.SROW.alloc(128)
    t80 = ctx.AF32.alloc(512)
    sv0 = ctx.AB16.alloc(32)
    pk8_0 = ctx.SROW.alloc(2048)
    PK8 = ctx.SROW.h[0:8, pk8_0:pk8_0 + 2048]
    PK8_res = ctx.SROW.res(pk8_0, 2048)
    for tt_i in range(NTT):
        bank = ctx.pbank(tag)
        for ki in range(NCH):
            lhsT = NB.h[:, 2048 * ki + 128 * tt_i: 2048 * ki + 128 * (tt_i + 1)]
            mm(ctx, bank.h[:, 0:8], bank.res, lhsT, [f"NB:{ki}"],
               ctx.SROWB.h[:, gt0 + 8 * ki:gt0 + 8 * (ki + 1)],
               ctx.SROWB.res(gt0, 64),
               start=(ki == 0), stop=(ki == NCH - 1), tag=tag + "_rl")
        nm_ap = ctx.SROW.h[:, nm0:nm0 + 1]
        nm_res = ctx.SROW.res(nm0, 128)
        reduce(ctx, "vector", nm_ap, nm_res, bank.h[:, 0:8], bank.res,
               ALU.max, negate=True, tag=tag + "_nmax")
        ex_ap = ctx.AF32.h[:, t80:t80 + 8]
        ex_res = ctx.AF32.res(t80, 512)
        sm_ap = ctx.SROW.h[:, nm0 + 1:nm0 + 2]
        act(ctx, ex_ap, ex_res, bank.h[:, 0:8], bank.res, AF.Exp,
            bias=nm_ap, extra_reads=nm_res, accum_out=sm_ap,
            accum_res=nm_res, tag=tag + "_exp")
        rc_ap = ctx.SROW.h[:, nm0 + 2:nm0 + 3]
        recip(ctx, rc_ap, nm_res, sm_ap, nm_res, tag=tag + "_rc")
        sc_ap = ctx.AF32.h[:, t80 + 8:t80 + 16]
        ts(ctx, "vector", sc_ap, ex_res, ex_ap, ex_res, rc_ap, ALU.mult,
           extra_reads=nm_res, tag=tag + "_sc")
        mx_ap = ctx.SROW.h[:, nm0 + 3:nm0 + 4]
        reduce(ctx, "vector", mx_ap, nm_res, sc_ap, ex_res, ALU.max,
               tag=tag + "_mx")
        oh_ap = ctx.AB16.h[:, sv0:sv0 + 8]
        oh_res = ctx.AB16.res(sv0, 32)
        ts(ctx, "vector", oh_ap, oh_res, sc_ap, ex_res, mx_ap, ALU.is_ge,
           extra_reads=nm_res, tag=tag + "_oh")
        pk_ap = ctx.AB16.h[:, sv0 + 8:sv0 + 16]
        tt(ctx, "vector", pk_ap, oh_res, sc_ap, ex_res, oh_ap, oh_res,
           ALU.mult, tag=tag + "_pk")
        bank2 = ctx.pbank16(tag)
        ctx.P.op("tensor",
                 lambda t, o=bank2.h[0:8, 0:128], i=oh_ap,
                 idn=ctx.IDB: t.transpose(o, i, idn),
                 reads=list(oh_res) + ["CONST"], writes=bank2.res,
                 tag=tag + "_otr")
        copy(ctx, "vector", M8[:, 128 * tt_i:128 * (tt_i + 1)], M8_res,
             bank2.h[0:8, 0:128], bank2.res, tag=tag + "_m8")
        bank3 = ctx.pbank16(tag)
        ctx.P.op("tensor",
                 lambda t, o=bank3.h[0:8, 0:128], i=pk_ap,
                 idn=ctx.IDB: t.transpose(o, i, idn),
                 reads=list(oh_res) + ["CONST"], writes=bank3.res,
                 tag=tag + "_ptr")
        copy(ctx, "vector", PK8[:, 128 * tt_i:128 * (tt_i + 1)], PK8_res,
             bank3.h[0:8, 0:128], bank3.res, tag=tag + "_pk8")
    cnt_ap = ctx.SROW.h[0:8, nm0 + 4:nm0 + 5]
    reduce(ctx, "vector", cnt_ap, nm_res, M8, M8_res, ALU.add,
           tag=tag + "_cnt")
    bank = ctx.pbank(tag)
    transp(ctx, bank.h[0:1, 0:8], bank.res, cnt_ap, nm_res,
           ctx.IDF[0:8, 0:8], tag=tag + "_ctr")
    crow_ap = ctx.SROW.h[0:1, nm0 + 8:nm0 + 16]
    copy(ctx, "vector", crow_ap, nm_res, bank.h[0:1, 0:8], bank.res,
         tag=tag + "_crc")
    dma(ctx, "sync", ctx.CCIN.ap()[0:1, 0:8], ["CCIN"], crow_ap, nm_res,
        tag=tag + "_cst")
    ctx.P.op("gpsimd",
             lambda g: g.collective_compute(
                 "AllGather", ALU.bypass,
                 ins=[ctx.CCIN.ap().opt()], outs=[ctx.CCOUT.ap().opt()],
                 replica_groups=[list(range(NCORES))]),
             reads=["CCIN"], writes=["CCOUT"], is_coll=True,
             tag=tag + "_ag")
    cc0 = ctx.AF32.alloc(8)
    cc_ap = ctx.AF32.h[:, cc0:cc0 + 8]
    cc_res = ctx.AF32.res(cc0, 8)
    dma(ctx, "sync", cc_ap, cc_res, ctx.CCOUT.ap()[:, 0:8], ["CCOUT"],
        tag=tag + "_cld")
    bank = ctx.pbank(tag)
    mm(ctx, bank.h[0:1, 0:8], bank.res, ctx.RMASK[:, 0:1], ["RMASK"],
       cc_ap[:, 0:8], cc_res, start=True, stop=True, tag=tag + "_pfx")
    pfrow_ap = ctx.SROW.h[0:1, nm0 + 24:nm0 + 32]
    copy(ctx, "vector", pfrow_ap, nm_res, bank.h[0:1, 0:8], bank.res,
         tag=tag + "_pfr")
    bank2 = ctx.pbank(tag)
    transp(ctx, bank2.h[0:8, 0:1], bank2.res, pfrow_ap, nm_res,
           ctx.IDF[0:1, 0:1], tag=tag + "_pftr")
    pfx_ap = ctx.SROW.h[0:8, nm0 + 16:nm0 + 17]
    copy(ctx, "vector", pfx_ap, nm_res, bank2.h[0:8, 0:1], bank2.res,
         tag=tag + "_pfc")
    POS = ctx.AF32.h[0:8, pos0:pos0 + 2048]
    POS_res = ctx.AF32.res(pos0, 2048)
    ctx.P.op("vector",
             lambda e: e.tensor_tensor_scan(POS, M8, M8, 0.0, ALU.add,
                                            ALU.bypass),
             reads=list(M8_res), writes=list(POS_res), special=True,
             tag=tag + "_scan")
    ts(ctx, "vector", POS, POS_res, POS, POS_res, pfx_ap, ALU.add,
       extra_reads=nm_res, tag=tag + "_padd")
    ts(ctx, "vector", POS, POS_res, POS, POS_res, CAP + 0.5, ALU.is_lt,
       tag=tag + "_klt")
    pkf0 = m8_0
    PKF = ctx.SROWB.h[0:8, pkf0:pkf0 + 2048]
    PKF_res = ctx.SROWB.res(pkf0, 2048)
    tt(ctx, "vector", POS, POS_res, POS, POS_res, PK8, PK8_res, ALU.mult,
       tag=tag + "_kw")
    ts(ctx, "vector", PKF, PKF_res, POS, POS_res, 0.5, ALU.mult,
       tag=tag + "_kw5")
    w10 = ctx.WP.alloc(8192)
    w20 = ctx.WP.alloc(8192)
    bce0 = ctx.AB16.alloc(2048)
    hb0 = ctx.AB16.alloc(4096)
    tmp0 = ctx.AF32.alloc(512)
    w1f = inp["L2_ffn_w1"].ap().rearrange("e k m -> (e k) m")
    w2f = inp["L2_ffn_w2"].ap().rearrange("e k m -> (e k) m")
    for e in range(E):
        for nt in range(NT):
            bank = ctx.pbank(tag)
            mm(ctx, bank.h[:, :], bank.res,
               ctx.ONEE[0:8, 128 * e:128 * (e + 1)], ["CONST"],
               PKF[:, PB * nt:PB * (nt + 1)], PKF_res,
               start=True, stop=True, tag=tag + "_bce")
            scopy(ctx, ctx.AB16.h[:, bce0 + 512 * nt:bce0 + 512 * (nt + 1)],
                  ctx.AB16.res(bce0 + 512 * nt, 512), bank.h[:, :],
                  bank.res, tag=tag + "_bcec")
        for qq in range(4):
            dst = ctx.WP.h[:, w10:w10 + 8192].rearrange(
                "p (k m) -> p k m", k=NCH)
            src = w1f[1024 * e:1024 * (e + 1),
                      1024 * qq:1024 * (qq + 1)].rearrange(
                          "(k p) m -> p k m", p=128)
            dma(ctx, "gpsimd", dst, ctx.WP.res(w10, 8192), src,
                ["HBM:L2w1"], tag=tag + "_w1")
            dst = ctx.WP.h[:, w20:w20 + 8192].rearrange(
                "p (k m) -> p k m", k=NCH)
            src = w2f[4096 * e + 1024 * qq:4096 * e + 1024 * (qq + 1),
                      :].rearrange("(k p) m -> p k m", p=128)
            dma(ctx, "gpsimd", dst, ctx.WP.res(w20, 8192), src,
                ["HBM:L2w2"], tag=tag + "_w2")
            for nt in range(NT):
                bce_ap = ctx.AB16.h[:, bce0 + 512 * nt:bce0 + 512 * (nt + 1)]
                bce_res = ctx.AB16.res(bce0 + 512 * nt, 512)
                for mi in range(NCH):
                    b1 = ctx.banks[0]
                    for ki in range(NCH):
                        rhs = NB.h[:, 2048 * ki + PB * nt:
                                   2048 * ki + PB * (nt + 1)]
                        l1 = ctx.WP.h[:, w10 + 1024 * ki + 128 * mi:
                                      w10 + 1024 * ki + 128 * (mi + 1)]
                        mm(ctx, b1.h[:, :], b1.res, l1,
                           ctx.WP.res(w10 + 1024 * ki, 1024), rhs,
                           [f"NB:{ki}"], start=(ki == 0),
                           stop=(ki == NCH - 1), tag=tag + "_h")
                    h_ap = ctx.AB16.h[:, hb0 + 512 * mi:hb0 + 512 * (mi + 1)]
                    h_res = ctx.AB16.res(hb0 + 512 * mi, 512)
                    act(ctx, h_ap, h_res, b1.h[:, :], b1.res, AF.Relu,
                        tag=tag + "_relu")
                for oc in range(NCH):
                    bank = ctx.banks[2 + (oc % 4)]
                    for mi in range(NCH):
                        l2 = ctx.WP.h[:, w20 + 1024 * mi + 128 * oc:
                                      w20 + 1024 * mi + 128 * (oc + 1)]
                        mm(ctx, bank.h[:, :], bank.res, l2,
                           ctx.WP.res(w20 + 1024 * mi, 1024),
                           ctx.AB16.h[:, hb0 + 512 * mi:hb0 + 512 * (mi + 1)],
                           ctx.AB16.res(hb0 + 512 * mi, 512),
                           start=(mi == 0), stop=(mi == NCH - 1),
                           tag=tag + "_o")
                    t_ap = ctx.AF32.h[:, tmp0:tmp0 + 512]
                    t_res = ctx.AF32.res(tmp0, 512)
                    tt(ctx, "vector", t_ap, t_res, bank.h[:, :], bank.res,
                       bce_ap, bce_res, ALU.mult, tag=tag + "_ow")
                    x_ap = X.h[:, 2048 * oc + PB * nt:
                               2048 * oc + PB * (nt + 1)]
                    tt(ctx, "vector", x_ap, [f"X:{oc}"], x_ap, [f"X:{oc}"],
                       t_ap, t_res, ALU.add, tag=tag + "_xa")
    ctx.reset_pools()




def lvcN(ctx, name, n, dstcol, row0_elems=0, pbase=0, dstb=None, tag="vN"):
    hb = ctx.inp[name].ap().rearrange("a b -> (a b)")
    src = hb[row0_elems:row0_elems + n].rearrange("(c p) -> c p", p=n)
    st0 = ctx.SROW.alloc(128)
    stage = ctx.SROW.h[0:1, st0:st0 + n]
    dma(ctx, "sync", stage, ctx.SROW.res(st0, 128), src, [f"HBM:{name}"],
        tag=tag + "_ld")
    bank = ctx.pbank(tag)
    mm(ctx, bank.h[pbase:pbase + n, 0:1], bank.res, stage,
       ctx.SROW.res(st0, 128), ctx.ONESF[0:1, 0:1], ["CONST"],
       start=True, stop=True, tag=tag + "_tr")
    pool = ctx.SROWB if dstb else ctx.SROW
    copy(ctx, "vector", pool.h[pbase:pbase + n, dstcol:dstcol + 1],
         pool.res(dstcol, 1), bank.h[pbase:pbase + n, 0:1], bank.res,
         tag=tag + "_cp")
    ctx.SROW.reset(st0)


def small_transpose(ctx, name, rows, cols, dst_pool, dstcol, tag):
    st0 = ctx.AF32.alloc(cols)
    stage = ctx.AF32.h[0:rows, st0:st0 + cols]
    dma(ctx, "sync", stage, ctx.AF32.res(st0, cols),
        ctx.inp[name].ap()[:, :], [f"HBM:{name}"], tag=tag + "_ld")
    bank = ctx.pbank(tag)
    transp(ctx, bank.h[0:cols, 0:rows], bank.res, stage,
           ctx.AF32.res(st0, cols), ctx.IDF[0:rows, 0:rows], tag=tag + "_tr")
    copy(ctx, "vector", dst_pool.h[0:cols, dstcol:dstcol + rows],
         dst_pool.res(dstcol, rows), bank.h[0:cols, 0:rows], bank.res,
         tag=tag + "_cp")
    ctx.AF32.reset(st0)


def l0_attn(ctx):
    tag = "L0a"
    ln_sublayer(ctx, tag + "ln")
    ctx.reset_pools()
    NB, X = ctx.NB, ctx.X
    H, hd, L = 16, 64, 32
    inp = ctx.inp
    q2lT0 = ctx.SROWB.alloc(32)
    small_transpose(ctx, "L0_attn_q2l_w", 32, 64, ctx.SROWB, q2lT0,
                    tag + "_q2l")
    kv2lT0 = ctx.SROWB.alloc(32)
    small_transpose(ctx, "L0_attn_kv2l_w", 32, 64, ctx.SROWB, kv2lT0,
                    tag + "_kv2l")
    wqkT0 = ctx.SROWB.alloc(32)
    small_transpose(ctx, "L0_attn_wqk_w", 32, 32, ctx.SROWB, wqkT0,
                    tag + "_wqk")
    l2oT0 = ctx.SROWB.alloc(64)
    small_transpose(ctx, "L0_attn_l2o_w", 64, 32, ctx.SROWB, l2oT0,
                    tag + "_l2o")
    q2lb0 = ctx.SROW.alloc(8)
    lvcN(ctx, "L0_attn_q2l_b", 32, q2lb0, tag=tag + "_q2lb")
    kv2lb0 = q2lb0 + 1
    lvcN(ctx, "L0_attn_kv2l_b", 32, kv2lb0, tag=tag + "_kv2lb")
    wqkb0 = q2lb0 + 2
    lvcN(ctx, "L0_attn_wqk_b", 32, wqkb0, tag=tag + "_wqkb")
    l2ob0 = q2lb0 + 3
    lvcN(ctx, "L0_attn_l2o_b", 64, l2ob0, pbase=0, tag=tag + "_l2ob")
    lvcN(ctx, "L0_attn_l2o_b", 64, l2ob0 + 1, pbase=64, tag=tag + "_l2ob2")
    cab0 = ctx.SROWB.alloc(8)
    cosT0 = ctx.AB16.alloc(2048)
    sinmT0 = ctx.AB16.alloc(2048)
    rt0 = ctx.AF32.alloc(2048)
    rc0 = ctx.SROW.alloc(8)
    trow = ctx.AF32.h[0:32, rt0:rt0 + 2048]
    trow_res = ctx.AF32.res(rt0, 2048)
    ctx.P.op("gpsimd", lambda g: g.iota(
        trow, [[1, 2048]], base=0, channel_multiplier=0,
        allow_small_or_imprecise_dtypes=True),
        reads=(), writes=list(trow_res), tag=tag + "_iot")
    lcol = ctx.SROW.h[0:32, rc0:rc0 + 1]
    lcol_res = ctx.SROW.res(rc0, 8)
    ctx.P.op("gpsimd", lambda g: g.iota(
        lcol, [[0, 1]], base=0, channel_multiplier=1,
        allow_small_or_imprecise_dtypes=True),
        reads=(), writes=list(lcol_res), tag=tag + "_iol")
    m2 = ctx.SROW.h[0:32, rc0 + 1:rc0 + 2]
    copy(ctx, "vector", m2, lcol_res, lcol, lcol_res, tag=tag + "_m2c")
    msk1 = ctx.SROW.h[0:32, rc0 + 6:rc0 + 7]
    for k in (16, 8, 4, 2):
        ts(ctx, "vector", msk1, lcol_res, m2, lcol_res, float(k),
           ALU.is_ge, tag=tag + "_m2m")
        stt(ctx, m2, lcol_res, msk1, lcol_res, float(-k), m2, lcol_res,
            ALU.mult, ALU.add, tag=tag + "_m2s")
    fi = ctx.SROW.h[0:32, rc0 + 2:rc0 + 3]
    tt(ctx, "vector", fi, lcol_res, lcol, lcol_res, m2, lcol_res,
       ALU.subtract, tag=tag + "_fi")
    inv = ctx.SROW.h[0:32, rc0 + 3:rc0 + 4]
    act(ctx, inv, lcol_res, fi, lcol_res, AF.Exp,
        scale=-np.log(10000.0) / 32.0, tag=tag + "_inv")
    sgn = ctx.SROW.h[0:32, rc0 + 4:rc0 + 5]
    ts(ctx, "vector", sgn, lcol_res, m2, lcol_res, 2.0, ALU.mult, -1.0,
       ALU.add, tag=tag + "_sgn")
    ts(ctx, "vector", trow, trow_res, trow, trow_res, inv, ALU.mult,
       extra_reads=lcol_res, tag=tag + "_fr")
    g0 = ctx.AF32.alloc(2048)
    gtile = ctx.AF32.h[0:32, g0:g0 + 2048]
    g_res = ctx.AF32.res(g0, 2048)
    for k in range(9, -1, -1):
        sub = 2.0 * np.pi * (2 ** k)
        ts(ctx, "vector", gtile, g_res, trow, trow_res, float(sub),
           ALU.is_ge, tag=tag + "_rrm")
        stt(ctx, trow, trow_res, gtile, g_res, float(-sub), trow, trow_res,
            ALU.mult, ALU.add, tag=tag + "_rrs")
    ts(ctx, "vector", gtile, g_res, trow, trow_res, np.pi, ALU.is_gt,
       tag=tag + "_gt")
    stt(ctx, trow, trow_res, gtile, g_res, -2 * np.pi, trow, trow_res,
        ALU.mult, ALU.add, tag=tag + "_rr")
    act(ctx, gtile, g_res, trow, trow_res, AF.Sin, tag=tag + "_sin")
    ts(ctx, "vector", ctx.AB16.h[0:32, sinmT0:sinmT0 + 2048],
       ctx.AB16.res(sinmT0, 2048), gtile, g_res, sgn, ALU.mult,
       extra_reads=lcol_res, tag=tag + "_sinm")
    act(ctx, gtile, g_res, trow, trow_res, AF.Sin, scale=0.5,
        tag=tag + "_sh")
    act(ctx, gtile, g_res, gtile, g_res, AF.Square, tag=tag + "_sh2")
    ts(ctx, "vector", ctx.AB16.h[0:32, cosT0:cosT0 + 2048],
       ctx.AB16.res(cosT0, 2048), gtile, g_res, -2.0, ALU.mult, 1.0,
       ALU.add, tag=tag + "_cos")
    dbg(ctx, "ropecos", ctx.AB16.h[0:32, cosT0:cosT0 + 2048],
        ctx.AB16.res(cosT0, 2048), [32, 2048], BF16)
    dbg(ctx, "ropesin", ctx.AB16.h[0:32, sinmT0:sinmT0 + 2048],
        ctx.AB16.res(sinmT0, 2048), [32, 2048], BF16)
    yf0 = ctx.WP.alloc(16384)
    memset(ctx, "vector", ctx.WP.h[:, yf0:yf0 + 16384],
           ctx.WP.res(yf0, 16384), 0.0, tag=tag + "_yfz")
    mq0 = ctx.WP.alloc(2 * 3 * 256)
    caw = inp["L0_attn_c_attn_w"].ap()
    ql0 = ctx.AF32.alloc(2048)
    kl0 = ctx.AF32.alloc(0) if False else None
    kl0 = rt0
    sh0 = g0
    qr0 = ctx.AB16.alloc(2048)
    kr0 = ctx.AB16.alloc(2048)
    qf0 = qr0
    vt0 = ctx.AB16.alloc(512)
    pt0 = ctx.AB16.alloc(1024)
    pairswap = [1, 0, 3, 2, 5, 4, 7, 6, 9, 8, 11, 10, 13, 12, 15, 14,
                17, 16, 19, 18, 21, 20, 23, 22, 25, 24, 27, 26, 29, 28,
                31, 30]
    for h in range(H):
        sl = mq0 + (h % 2) * 768
        for j, (lt0, boff, brow) in enumerate(
                ((q2lT0, q2lb0, 0), (kv2lT0, kv2lb0, 1024),
                 (kv2lT0, kv2lb0, 2048))):
            stg0 = ctx.AB16.alloc(1024)
            dma(ctx, "gpsimd", ctx.AB16.h[0:64, stg0:stg0 + 1024],
                ctx.AB16.res(stg0, 1024),
                caw[brow + 64 * h: brow + 64 * (h + 1), :], ["HBM:L0caw"],
                tag=tag + "_cw")
            for ki in range(NCH):
                bank = ctx.pbank(tag)
                mm(ctx, bank.h[0:128, 0:32], bank.res,
                   ctx.AB16.h[0:64, stg0 + 128 * ki:stg0 + 128 * (ki + 1)],
                   ctx.AB16.res(stg0, 1024),
                   ctx.SROWB.h[0:64, lt0:lt0 + 32], ctx.SROWB.res(lt0, 32),
                   start=True, stop=True, tag=tag + "_mq")
                scopy(ctx, ctx.WP.h[:, sl + 256 * j + 32 * ki:
                                    sl + 256 * j + 32 * (ki + 1)],
                      ctx.WP.res(sl + 256 * j, 256), bank.h[0:128, 0:32],
                      bank.res, tag=tag + "_mqc")
            bst0 = ctx.AB16.alloc(16)
            lvcN_b16(ctx, "L0_attn_c_attn_b", 64, bst0, brow + 64 * h,
                     tag + "_cb")
            bank = ctx.pbank(tag)
            mm(ctx, bank.h[0:32, 0:1], bank.res,
               ctx.SROWB.h[0:64, lt0:lt0 + 32], ctx.SROWB.res(lt0, 32),
               ctx.AB16.h[0:64, bst0:bst0 + 1], ctx.AB16.res(bst0, 16),
               start=True, stop=True, tag=tag + "_cbm")
            bcmb = ctx.SROW.h[0:32, rc0 + 5 + j:rc0 + 6 + j]
            ctx.P.op("vector",
                     lambda e, o=bcmb, i=bank.h[0:32, 0:1],
                     b=ctx.SROW.h[0:32, boff:boff + 1]:
                     e.tensor_tensor(o, i, b, ALU.add),
                     reads=bank.res + ctx.SROW.res(boff, 1) + lcol_res,
                     writes=lcol_res, tag=tag + "_cba")
            ctx.AB16.reset(stg0)
        for j, (dst0, dres, isv) in enumerate(
                ((ql0, None, False), (kl0, None, False), (None, None, True))):
            bank = ctx.banks[j]
            for nt in range(NT):
                for ki in range(NCH):
                    lhsT = ctx.WP.h[:, sl + 256 * j + 32 * ki:
                                    sl + 256 * j + 32 * (ki + 1)]
                    mm(ctx, bank.h[0:32, :], bank.res, lhsT,
                       ctx.WP.res(sl + 256 * j, 256),
                       NB.h[:, 2048 * ki + PB * nt:2048 * ki + PB * (nt + 1)],
                       [f"NB:{ki}"], start=(ki == 0), stop=(ki == NCH - 1),
                       tag=tag + "_lat")
                bcmb = ctx.SROW.h[0:32, rc0 + 5 + j:rc0 + 6 + j]
                if isv:
                    vb0 = ctx.AB16.alloc(512)
                    act(ctx, ctx.AB16.h[0:32, vb0:vb0 + 512],
                        ctx.AB16.res(vb0, 512), bank.h[0:32, :], bank.res,
                        AF.Identity, bias=bcmb, extra_reads=lcol_res,
                        tag=tag + "_ve")
                    for q4 in range(4):
                        bk16 = ctx.pbank16(tag)
                        ctx.P.op("tensor",
                                 lambda t, o=bk16.h[:, 0:32],
                                 i=ctx.AB16.h[0:32, vb0 + 128 * q4:
                                              vb0 + 128 * (q4 + 1)],
                                 idn=ctx.IDB[0:32, 0:32]:
                                 t.transpose(o, i, idn),
                                 reads=ctx.AB16.res(vb0, 512) + ["CONST"],
                                 writes=bk16.res, tag=tag + "_vtr")
                        kt = 4 * nt + q4
                        copy(ctx, "vector",
                             ctx.AB16.h[:, vt0 + 32 * kt:vt0 + 32 * (kt + 1)],
                             ctx.AB16.res(vt0 + 32 * kt, 32),
                             bk16.h[:, 0:32], bk16.res, tag=tag + "_vtc")
                    ctx.AB16.reset(vb0)
                else:
                    act(ctx, ctx.AF32.h[0:32, dst0 + PB * nt:
                                        dst0 + PB * (nt + 1)],
                        ctx.AF32.res(dst0 + PB * nt, PB), bank.h[0:32, :],
                        bank.res, AF.Identity, bias=bcmb,
                        extra_reads=lcol_res, tag=tag + "_le")
        if h == 0:
            dbg(ctx, "ql0", ctx.AF32.h[0:32, ql0:ql0 + 2048],
                ctx.AF32.res(ql0, 2048), [32, 2048], F32)
            dbg(ctx, "kl0", ctx.AF32.h[0:32, kl0:kl0 + 2048],
                ctx.AF32.res(kl0, 2048), [32, 2048], F32)
            dbg(ctx, "vt0", ctx.AB16.h[:, vt0:vt0 + 512],
                ctx.AB16.res(vt0, 512), [128, 512], BF16)
        for src0, dst0 in ((ql0, qr0), (kl0, kr0)):
            s_ap = ctx.AF32.h[0:32, src0:src0 + 2048]
            s_res = ctx.AF32.res(src0, 2048)
            sh_ap = ctx.AF32.h[0:32, sh0:sh0 + 2048]
            sh_res = ctx.AF32.res(sh0, 2048)
            ctx.P.op("vector",
                     lambda e, o=sh_ap, i=s_ap, m=list(pairswap):
                     e.stream_shuffle(o, i, m),
                     reads=list(s_res), writes=list(sh_res), special=True,
                     tag=tag + "_shf")
            tt(ctx, "vector", s_ap, s_res, s_ap, s_res,
               ctx.AB16.h[0:32, cosT0:cosT0 + 2048],
               ctx.AB16.res(cosT0, 2048), ALU.mult, tag=tag + "_rc")
            tt(ctx, "vector", sh_ap, sh_res, sh_ap, sh_res,
               ctx.AB16.h[0:32, sinmT0:sinmT0 + 2048],
               ctx.AB16.res(sinmT0, 2048), ALU.mult, tag=tag + "_rs")
            tt(ctx, "vector", ctx.AB16.h[0:32, dst0:dst0 + 2048],
               ctx.AB16.res(dst0, 2048), s_ap, s_res, sh_ap, sh_res,
               ALU.add, tag=tag + "_ra")
        if h == 0:
            dbg(ctx, "kr0", ctx.AB16.h[0:32, kr0:kr0 + 2048],
                ctx.AB16.res(kr0, 2048), [32, 2048], BF16)
        for nt in range(NT):
            bank = ctx.banks[3]
            mm(ctx, bank.h[0:32, :], bank.res,
               ctx.SROWB.h[0:32, wqkT0:wqkT0 + 32],
               ctx.SROWB.res(wqkT0, 32),
               ctx.AB16.h[0:32, qr0 + PB * nt:qr0 + PB * (nt + 1)],
               ctx.AB16.res(qr0, 2048), start=True, stop=True,
               tag=tag + "_qf")
            act(ctx, ctx.AB16.h[0:32, qf0 + PB * nt:qf0 + PB * (nt + 1)],
                ctx.AB16.res(qf0, 2048), bank.h[0:32, :], bank.res,
                AF.Identity, bias=ctx.SROW.h[0:32, wqkb0:wqkb0 + 1],
                extra_reads=ctx.SROW.res(wqkb0, 1), tag=tag + "_qfe")
        for qnt in range(NT):
            nkt = 4 * qnt + 4
            ob = ctx.banks[4]
            db = ctx.banks[5]
            pbase = 64 * (h % 2)
            for kt in range(nkt):
                sb = ctx.banks[kt % 4]
                mm(ctx, sb.h[:, :], sb.res,
                   ctx.AB16.h[0:32, kr0 + 128 * kt:kr0 + 128 * (kt + 1)],
                   ctx.AB16.res(kr0, 2048),
                   ctx.AB16.h[0:32, qf0 + PB * qnt:qf0 + PB * (qnt + 1)],
                   ctx.AB16.res(qf0, 2048),
                   start=True, stop=True, tag=tag + "_s")
                pc = pt0 + 512 * (kt % 2)
                p_ap = ctx.AB16.h[:, pc:pc + 512]
                p_res = ctx.AB16.res(pc, 512)
                act(ctx, p_ap, p_res, sb.h[:, :], sb.res, AF.Exp,
                    scale=float(32 ** -0.5), tag=tag + "_exp")
                if kt >= 4 * qnt:
                    base = PB * qnt - 128 * kt
                    ctx.P.op("gpsimd",
                             lambda g, a=p_ap, b=base: g.affine_select(
                                 a, a, [[1, 512]], ALU.is_ge, 0.0,
                                 base=b, channel_multiplier=-1),
                             reads=list(p_res), writes=list(p_res),
                             tag=tag + "_msk")
                mm(ctx, ob.h[0:32, :], ob.res,
                   ctx.AB16.h[:, vt0 + 32 * kt:vt0 + 32 * (kt + 1)],
                   ctx.AB16.res(vt0 + 32 * kt, 32), p_ap, p_res,
                   start=(kt == 0), stop=(kt == nkt - 1), tag=tag + "_av")
                mm(ctx, db.h[0:1, :], db.res, ctx.ONESB_COL, ["CONST"],
                   p_ap, p_res, start=(kt == 0), stop=(kt == nkt - 1),
                   tag=tag + "_den")
            r_ap = ctx.SROW.h[0:1, rc0 + 8 - 8:rc0 + 1] if False else None
            rr_ap = ctx.SROW.h[0:1, q2lb0 + 4:q2lb0 + 4 + 512] \
                if False else None
            rr0a = ctx.SROW.alloc(0)
            rrow = ctx.SROW.h[0:1, 1024:1536]
            rrow_res = ctx.SROW.res(1024, 512)
            recip(ctx, rrow, rrow_res, db.h[0:1, :], db.res,
                  tag=tag + "_rcp")
            rbrow = ctx.SROWB.h[0:1, 1024:1536]
            rbrow_res = ctx.SROWB.res(1024, 512)
            copy(ctx, "vector", rbrow, rbrow_res, rrow, rrow_res,
                 tag=tag + "_rcb")
            bb = ctx.banks[(nkt) % 4]
            mm(ctx, bb.h[0:32, :], bb.res,
               ctx.ONESB_ROW[:, 0:32], ["CONST"], rbrow, rbrow_res,
               start=True, stop=True, tag=tag + "_rbc")
            y0 = ctx.AF32.h[0:32, sh0:sh0 + 512]
            y0_res = ctx.AF32.res(sh0, 512)
            scopy(ctx, y0, y0_res, ob.h[0:32, :], ob.res,
                  tag=tag + "_ye")
            ylat = ctx.AB16.h[0:32, pt0:pt0 + 512]
            ylat_res = ctx.AB16.res(pt0, 512)
            tt(ctx, "vector", ylat, ylat_res, y0, y0_res,
               bb.h[0:32, :], bb.res, ALU.mult, tag=tag + "_yn")

            ofb = ctx.banks[(nkt + 1) % 4]
            mm(ctx, ofb.h[pbase:pbase + 64, :], ofb.res,
               ctx.SROWB.h[0:32, l2oT0:l2oT0 + 64],
               ctx.SROWB.res(l2oT0, 64), ylat, ylat_res,
               start=True, stop=True, tag=tag + "_l2o")
            yfdst = ctx.WP.h[pbase:pbase + 64,
                             yf0 + 2048 * (h // 2) + PB * qnt:
                             yf0 + 2048 * (h // 2) + PB * (qnt + 1)]
            yfres = ctx.WP.res(yf0 + 2048 * (h // 2) + PB * qnt, PB)
            act(ctx, yfdst, yfres, ofb.h[pbase:pbase + 64, :], ofb.res,
                AF.Identity,
                bias=ctx.SROW.h[pbase:pbase + 64,
                                l2ob0 + (h % 2):l2ob0 + (h % 2) + 1],
                extra_reads=ctx.SROW.res(l2ob0, 2), tag=tag + "_yf")
    dbg(ctx, "yf", ctx.WP.h[:, yf0:yf0 + 16384], ctx.WP.res(yf0, 16384),
        [128, 16384], BF16)
    ctx.AB16.reset()
    ctx.AF32.reset()
    cpt0 = ctx.AB16.alloc(8192)
    stg0 = ctx.AB16.alloc(2048)
    cpw = inp["L0_attn_cproj_w"].ap()
    for fc in range(NCH):
        sc = stg0 + 1024 * (fc % 2)
        dma(ctx, "gpsimd", ctx.AB16.h[:, sc:sc + 1024],
            ctx.AB16.res(sc, 1024), cpw[128 * fc:128 * (fc + 1), :],
            ["HBM:L0cp"], tag=tag + "_cpl")
        for oc in range(NCH):
            bk16 = ctx.pbank16(tag)
            ctx.P.op("tensor",
                     lambda t, o=bk16.h[:, 0:128],
                     i=ctx.AB16.h[:, sc + 128 * oc:sc + 128 * (oc + 1)],
                     idn=ctx.IDB: t.transpose(o, i, idn),
                     reads=ctx.AB16.res(sc, 1024) + ["CONST"],
                     writes=bk16.res, tag=tag + "_cpt")
            copy(ctx, "vector",
                 ctx.AB16.h[:, cpt0 + 1024 * oc + 128 * fc:
                            cpt0 + 1024 * oc + 128 * (fc + 1)],
                 ctx.AB16.res(cpt0 + 1024 * oc + 128 * fc, 128),
                 bk16.h[:, 0:128], bk16.res, tag=tag + "_cptc")
    cpb0 = ctx.SROW.alloc(8)
    load_vec_cols(ctx, "L0_attn_cproj_b", cpb0, 8, tag=tag + "_cpb")
    for oc in range(NCH):
        for nt in range(NT):
            bank = ctx.pbank(tag)
            for fc in range(NCH):
                lhsT = ctx.AB16.h[:, cpt0 + 1024 * fc + 128 * oc:
                                  cpt0 + 1024 * fc + 128 * (oc + 1)]
                lres = ctx.AB16.res(cpt0 + 1024 * fc + 128 * oc, 128)
                rhs = ctx.WP.h[:, yf0 + 2048 * fc + PB * nt:
                               yf0 + 2048 * fc + PB * (nt + 1)]
                rres = ctx.WP.res(yf0 + 2048 * fc + PB * nt, PB)
                mm(ctx, bank.h[:, :], bank.res, lhsT, lres, rhs, rres,
                   start=(fc == 0), stop=(fc == NCH - 1), tag=tag + "_o")
            x_ap = X.h[:, 2048 * oc + PB * nt: 2048 * oc + PB * (nt + 1)]
            stt(ctx, x_ap, [f"X:{oc}"], bank.h[:, :], bank.res,
                ctx.SROW.h[:, cpb0 + oc:cpb0 + oc + 1], x_ap, [f"X:{oc}"],
                ALU.add, ALU.add, extra_reads=ctx.SROW.res(cpb0, 8),
                tag=tag + "_xa")
    ctx.reset_pools()


def lvcN_b16(ctx, name, n, dstcol_ab16, row0_elems, tag):
    hb = ctx.inp[name].ap().rearrange("a b -> (a b)")
    src = hb[row0_elems:row0_elems + n].rearrange("(c p) -> c p", p=n)
    st0 = ctx.SROW.alloc(128)
    stage = ctx.SROW.h[0:1, st0:st0 + n]
    dma(ctx, "sync", stage, ctx.SROW.res(st0, 128), src, [f"HBM:{name}"],
        tag=tag + "_ld")
    bank = ctx.pbank(tag)
    mm(ctx, bank.h[0:n, 0:1], bank.res, stage, ctx.SROW.res(st0, 128),
       ctx.ONESF[0:1, 0:1], ["CONST"], start=True, stop=True,
       tag=tag + "_tr")
    copy(ctx, "vector", ctx.AB16.h[0:n, dstcol_ab16:dstcol_ab16 + 1],
         ctx.AB16.res(dstcol_ab16, 1), bank.h[0:n, 0:1], bank.res,
         tag=tag + "_cp")
    ctx.SROW.reset(st0)




def l1_attn(ctx):
    tag = "L1a"
    ln_sublayer(ctx, tag + "ln")
    ctx.reset_pools()
    NB, X = ctx.NB, ctx.X
    DI, NS = 2048, 16
    NCC = DI // 128
    inp = ctx.inp
    win = inp["L1_attn_W_in"].ap()
    wx = inp["L1_attn_W_x"].ap()
    wdt = inp["L1_attn_W_dt"].ap()
    alog = inp["L1_attn_A_log"].ap()
    convw = inp["L1_attn_conv_w"].ap().rearrange("c a k -> c (a k)")
    h3 = ctx.H3.ap()
    ms = ctx.MS.ap()
    one32_0 = ctx.WP.alloc(2048)
    memset(ctx, "vector", ctx.WP.h[0:32, one32_0:one32_0 + 2048],
           ctx.WP.res(one32_0, 2048), 1.0, tag=tag + "_o32i")
    for n in range(NS):
        ctx.P.op("gpsimd", lambda g, n=n: g.affine_select(
            ctx.WP.h[0:32, one32_0 + 128 * n:one32_0 + 128 * (n + 1)],
            ctx.WP.h[0:32, one32_0 + 128 * n:one32_0 + 128 * (n + 1)],
            [[0, 128]], ALU.is_equal, 0.0, base=-(16 + n),
            channel_multiplier=1),
            reads=ctx.WP.res(one32_0, 2048), writes=ctx.WP.res(one32_0, 2048),
            tag=tag + "_o32")
    wslot = ctx.WP.alloc(2 * 1024)
    wxs0 = ctx.WP.alloc(2 * 32)
    dbc0 = ctx.SROWB.alloc(2048)
    zt0 = ctx.AB16.alloc(2048)
    xs0 = ctx.AB16.alloc(2048)
    xcp0 = ctx.AF32.alloc(2052)
    cv0 = ctx.AF32.alloc(2048)
    cw0 = ctx.SROW.alloc(8)
    cb0 = ctx.SROW.alloc(8)
    DBC = ctx.SROWB.h[0:32, dbc0:dbc0 + 2048]
    DBC_res = ctx.SROWB.res(dbc0, 2048)
    ctx.pbank_allowed = (4, 5)
    for c in range(2 * NCC):
        isz = c >= NCC
        sl = wslot + 1024 * (c % 2)
        dst = ctx.WP.h[:, sl:sl + 1024].rearrange("p (k m) -> p k m", k=NCH)
        src = win[:, 128 * c:128 * (c + 1)].rearrange(
            "(k p) m -> p k m", p=128)
        dma(ctx, "gpsimd", dst, ctx.WP.res(sl, 1024), src, ["HBM:L1win"],
            tag=tag + "_wl")
        if not isz:
            dma(ctx, "sync", ctx.SROW.h[:, cw0:cw0 + 4],
                ctx.SROW.res(cw0, 8), convw[128 * c:128 * (c + 1), :],
                ["HBM:L1cw"], tag=tag + "_cwl")
            lvcN(ctx, "L1_attn_conv_b", 128, cb0, row0_elems=128 * c,
                 tag=tag + "_cb")
            wxsl = wxs0 + 32 * (c % 2)
            dma(ctx, "gpsimd", ctx.WP.h[:, wxsl:wxsl + 32],
                ctx.WP.res(wxsl, 32), wx[128 * c:128 * (c + 1), :],
                ["HBM:L1wx"], tag=tag + "_wxl")
        for nt in range(NT):
            bank = ctx.banks[4 + (nt % 2)]
            for ki in range(NCH):
                lhsT = ctx.WP.h[:, sl + 128 * ki:sl + 128 * (ki + 1)]
                mm(ctx, bank.h[:, :], bank.res, lhsT, ctx.WP.res(sl, 1024),
                   NB.h[:, 2048 * ki + PB * nt:2048 * ki + PB * (nt + 1)],
                   [f"NB:{ki}"], start=(ki == 0), stop=(ki == NCH - 1),
                   tag=tag + "_xz")
            if isz:
                scopy(ctx, ctx.AB16.h[:, zt0 + PB * nt:zt0 + PB * (nt + 1)],
                      ctx.AB16.res(zt0 + PB * nt, PB), bank.h[:, :],
                      bank.res, tag=tag + "_ze")
            else:
                scopy(ctx, ctx.AF32.h[:, xcp0 + 3 + PB * nt:
                                      xcp0 + 3 + PB * (nt + 1)],
                      ctx.AF32.res(xcp0 + 3 + PB * nt, PB + 3),
                      bank.h[:, :], bank.res, tag=tag + "_xce")
        if isz:
            dma(ctx, "sync", h3[2048 + 128 * (c - NCC):
                               2048 + 128 * (c - NCC + 1), :], [f"H3:z"],
                ctx.AB16.h[:, zt0:zt0 + 2048], ctx.AB16.res(zt0, 2048),
                tag=tag + "_zst")
        else:
            memset(ctx, "vector", ctx.AF32.h[:, xcp0:xcp0 + 3],
                   ctx.AF32.res(xcp0, 3), 0.0, tag=tag + "_pz")
            ts(ctx, "vector", ctx.AF32.h[:, cv0:cv0 + 2048],
               ctx.AF32.res(cv0, 2048), ctx.AF32.h[:, xcp0:xcp0 + 2048],
               ctx.AF32.res(xcp0, 2052), ctx.SROW.h[:, cw0:cw0 + 1],
               ALU.mult, extra_reads=ctx.SROW.res(cw0, 8), tag=tag + "_cv0")
            for k in range(1, 4):
                stt(ctx, ctx.AF32.h[:, cv0:cv0 + 2048],
                    ctx.AF32.res(cv0, 2048),
                    ctx.AF32.h[:, xcp0 + k:xcp0 + k + 2048],
                    ctx.AF32.res(xcp0, 2052),
                    ctx.SROW.h[:, cw0 + k:cw0 + k + 1],
                    ctx.AF32.h[:, cv0:cv0 + 2048], ctx.AF32.res(cv0, 2048),
                    ALU.mult, ALU.add, extra_reads=ctx.SROW.res(cw0, 8),
                    tag=tag + "_cvk")
            act(ctx, ctx.AB16.h[:, xs0:xs0 + 2048], ctx.AB16.res(xs0, 2048),
                ctx.AF32.h[:, cv0:cv0 + 2048], ctx.AF32.res(cv0, 2048),
                AF.Silu, bias=ctx.SROW.h[:, cb0:cb0 + 1],
                extra_reads=ctx.SROW.res(cb0, 8), tag=tag + "_xs")
            dma(ctx, "sync", h3[128 * c:128 * (c + 1), :], [f"H3:xs"],
                ctx.AB16.h[:, xs0:xs0 + 2048], ctx.AB16.res(xs0, 2048),
                tag=tag + "_xst")
            wxsl = wxs0 + 32 * (c % 2)
            for nt in range(NT):
                mm(ctx, ctx.banks[nt].h[0:32, :], ctx.banks[nt].res,
                   ctx.WP.h[:, wxsl:wxsl + 32], ctx.WP.res(wxsl, 32),
                   ctx.AB16.h[:, xs0 + PB * nt:xs0 + PB * (nt + 1)],
                   ctx.AB16.res(xs0, 2048), start=(c == 0),
                   stop=(c == NCC - 1), tag=tag + "_dbc")
    for nt in range(NT):
        scopy(ctx, DBC[:, PB * nt:PB * (nt + 1)], DBC_res,
              ctx.banks[nt].h[0:32, :], ctx.banks[nt].res, tag=tag + "_dbce")
    ctx.pbank_allowed = (0, 1, 2, 3, 4, 5)
    dbg(ctx, "dbc", DBC, DBC_res, [32, 2048], BF16)
    ctx.AF32.reset()
    ctx.AB16.reset()
    xs0 = ctx.AB16.alloc(2048)
    du0 = ctx.AB16.alloc(2048)
    bcs0 = ctx.AB16.alloc(2048)
    da0 = ctx.AB16.alloc(2048)
    dbu0 = ctx.AB16.alloc(2048)
    hh0 = ctx.AB16.alloc(2048)
    dl0 = ctx.AF32.alloc(2048)
    ya0 = ctx.AF32.alloc(2048)
    sp0 = ctx.AF32.alloc(2048)
    ctx.WP.reset(2048)
    wdt0 = ctx.WP.alloc(2 * 128)
    da1 = ctx.WP.alloc(2048)
    dbu1 = ctx.WP.alloc(2048)
    bcs1 = ctx.WP.alloc(2048)
    hh1 = ctx.WP.alloc(2048)
    acol0 = ctx.SROW.alloc(64)
    bdt0 = ctx.SROW.alloc(8)
    dcol0 = ctx.SROW.alloc(8)
    yz0 = hh0
    for c in range(NCC):
        dma(ctx, "sync", ctx.AB16.h[:, xs0:xs0 + 2048],
            ctx.AB16.res(xs0, 2048), h3[128 * c:128 * (c + 1), :],
            ["H3:xs"], tag=tag + "_xsl")
        wsl = wdt0 + 128 * (c % 2)
        dma(ctx, "gpsimd", ctx.WP.h[0:16, wsl:wsl + 128],
            ctx.WP.res(wsl, 128), wdt[:, 128 * c:128 * (c + 1)],
            ["HBM:L1wdt"], tag=tag + "_wdtl")
        dma(ctx, "sync", ctx.SROW.h[:, acol0:acol0 + 16],
            ctx.SROW.res(acol0, 64), alog[128 * c:128 * (c + 1), :],
            ["HBM:L1al"], tag=tag + "_all")
        act(ctx, ctx.SROW.h[:, acol0 + 16:acol0 + 32],
            ctx.SROW.res(acol0, 64), ctx.SROW.h[:, acol0:acol0 + 16],
            ctx.SROW.res(acol0, 64), AF.Exp, tag=tag + "_ae")
        ts(ctx, "vector", ctx.SROW.h[:, acol0 + 16:acol0 + 32],
           ctx.SROW.res(acol0, 64), ctx.SROW.h[:, acol0 + 16:acol0 + 32],
           ctx.SROW.res(acol0, 64), -1.0, ALU.mult, tag=tag + "_an")
        lvcN(ctx, "L1_attn_b_dt", 128, bdt0, row0_elems=128 * c,
             tag=tag + "_bdt")
        lvcN(ctx, "L1_attn_D", 128, dcol0, row0_elems=128 * c,
             tag=tag + "_dc")
        if c == 0:
            dbg(ctx, "wdt0", ctx.WP.h[0:16, wsl:wsl + 128],
                ctx.WP.res(wsl, 128), [16, 128], BF16)
        for nt in range(NT):
            bank = ctx.banks[4 + (nt % 2)]
            mm(ctx, bank.h[:, :], bank.res, ctx.WP.h[0:16, wsl:wsl + 128],
               ctx.WP.res(wsl, 128), DBC[0:16, PB * nt:PB * (nt + 1)],
               DBC_res, start=True, stop=True, tag=tag + "_dp")
            act(ctx, ctx.AF32.h[:, dl0 + PB * nt:dl0 + PB * (nt + 1)],
                ctx.AF32.res(dl0, 2048), bank.h[:, :], bank.res,
                AF.Identity, bias=ctx.SROW.h[:, bdt0:bdt0 + 1],
                extra_reads=ctx.SROW.res(bdt0, 8), tag=tag + "_dpe")
        if c == 0:
            dbg(ctx, "dpre0", ctx.AF32.h[:, dl0:dl0 + 2048],
                ctx.AF32.res(dl0, 2048), [128, 2048], F32)
        act(ctx, ctx.AF32.h[:, sp0:sp0 + 2048], ctx.AF32.res(sp0, 2048),
            ctx.AF32.h[:, dl0:dl0 + 2048], ctx.AF32.res(dl0, 2048),
            AF.Abs, tag=tag + "_sa")
        act(ctx, ctx.AF32.h[:, sp0:sp0 + 2048], ctx.AF32.res(sp0, 2048),
            ctx.AF32.h[:, sp0:sp0 + 2048], ctx.AF32.res(sp0, 2048),
            AF.Exp, scale=-1.0, tag=tag + "_se")
        act(ctx, ctx.AF32.h[:, sp0:sp0 + 2048], ctx.AF32.res(sp0, 2048),
            ctx.AF32.h[:, sp0:sp0 + 2048], ctx.AF32.res(sp0, 2048),
            AF.Ln, bias=1.0, tag=tag + "_sl")
        ts(ctx, "vector", ctx.AF32.h[:, dl0:dl0 + 2048],
           ctx.AF32.res(dl0, 2048), ctx.AF32.h[:, dl0:dl0 + 2048],
           ctx.AF32.res(dl0, 2048), 0.0, ALU.max, tag=tag + "_sm")
        tt(ctx, "vector", ctx.AF32.h[:, dl0:dl0 + 2048],
           ctx.AF32.res(dl0, 2048), ctx.AF32.h[:, dl0:dl0 + 2048],
           ctx.AF32.res(dl0, 2048), ctx.AF32.h[:, sp0:sp0 + 2048],
           ctx.AF32.res(sp0, 2048), ALU.add, tag=tag + "_sadd")
        if c == 0:
            dbg(ctx, "delta0", ctx.AF32.h[:, dl0:dl0 + 2048],
                ctx.AF32.res(dl0, 2048), [128, 2048], F32)
        tt(ctx, "gpsimd", ctx.AB16.h[:, du0:du0 + 2048],
           ctx.AB16.res(du0, 2048), ctx.AB16.h[:, xs0:xs0 + 2048],
           ctx.AB16.res(xs0, 2048), ctx.AF32.h[:, dl0:dl0 + 2048],
           ctx.AF32.res(dl0, 2048), ALU.mult, tag=tag + "_du")
        memset(ctx, "vector", ctx.AF32.h[:, ya0:ya0 + 2048],
               ctx.AF32.res(ya0, 2048), 0.0, tag=tag + "_yz")
        memset(ctx, "vector", ctx.AF32.h[:, sp0:sp0 + 2048],
               ctx.AF32.res(sp0, 2048), 0.0, tag=tag + "_yz1")
        def nbufs(sub):
            if sub == 0:
                H, R = ctx.AB16.h, ctx.AB16.res
                da_, dbu_, bcs_, hh_ = da0, dbu0, bcs0, hh0
            else:
                H, R = ctx.WP.h, ctx.WP.res
                da_, dbu_, bcs_, hh_ = da1, dbu1, bcs1, hh1
            return {k: (H[:, v:v + 2048], R(v, 2048)) for k, v in
                    (("da", da_), ("dbu", dbu_), ("bcs", bcs_), ("hh", hh_))}
        BUF = (nbufs(0), nbufs(1))
        YA = ((ya0, ctx.AF32.res(ya0, 2048)), (sp0, ctx.AF32.res(sp0, 2048)))
        for n2 in range(0, NS, 2):
            for sub in range(2):
                n = n2 + sub
                bcs_ap, bcs_res = BUF[sub]["bcs"]
                for nt in range(NT):
                    bank = ctx.banks[2 * sub + (nt % 2)]
                    mm(ctx, bank.h[:, :], bank.res,
                       ctx.WP.h[0:32, one32_0 + 128 * n:
                                one32_0 + 128 * (n + 1)],
                       ctx.WP.res(one32_0, 2048),
                       DBC[:, PB * nt:PB * (nt + 1)],
                       DBC_res, start=True, stop=True, tag=tag + "_bc")
                    scopy(ctx, bcs_ap[:, PB * nt:PB * (nt + 1)], bcs_res,
                          bank.h[:, :], bank.res, tag=tag + "_bce")
            for sub in range(2):
                n = n2 + sub
                da_ap, da_res = BUF[sub]["da"]
                act(ctx, da_ap, da_res,
                    ctx.AF32.h[:, dl0:dl0 + 2048], ctx.AF32.res(dl0, 2048),
                    AF.Exp, scale=ctx.SROW.h[:, acol0 + 16 + n:
                                             acol0 + 17 + n],
                    extra_reads=ctx.SROW.res(acol0, 64), tag=tag + "_da")
            for sub in range(2):
                dbu_ap, dbu_res = BUF[sub]["dbu"]
                bcs_ap, bcs_res = BUF[sub]["bcs"]
                tt(ctx, "vector", dbu_ap, dbu_res,
                   ctx.AB16.h[:, du0:du0 + 2048], ctx.AB16.res(du0, 2048),
                   bcs_ap, bcs_res, ALU.mult, tag=tag + "_dbu")
            for sub in range(2):
                da_ap, da_res = BUF[sub]["da"]
                dbu_ap, dbu_res = BUF[sub]["dbu"]
                hh_ap, hh_res = BUF[sub]["hh"]
                ctx.P.op("vector",
                         lambda e, o=hh_ap, a=da_ap, b=dbu_ap:
                         e.tensor_tensor_scan(o, a, b, 0.0, ALU.mult,
                                              ALU.add),
                         reads=list(da_res) + list(dbu_res),
                         writes=list(hh_res), special=True,
                         tag=tag + "_scan")
            for sub in range(2):
                dbu_ap, dbu_res = BUF[sub]["dbu"]
                bcs_ap, bcs_res = BUF[sub]["bcs"]
                hh_ap, hh_res = BUF[sub]["hh"]
                tt(ctx, "gpsimd", dbu_ap, dbu_res, hh_ap, hh_res,
                   bcs_ap, bcs_res, ALU.mult, tag=tag + "_ym")
            for sub in range(2):
                dbu_ap, dbu_res = BUF[sub]["dbu"]
                yab, ya_res = YA[sub]
                tt(ctx, "vector", ctx.AF32.h[:, yab:yab + 2048], ya_res,
                   ctx.AF32.h[:, yab:yab + 2048], ya_res,
                   dbu_ap, dbu_res, ALU.add, tag=tag + "_ya")
        tt(ctx, "vector", ctx.AF32.h[:, ya0:ya0 + 2048],
           ctx.AF32.res(ya0, 2048), ctx.AF32.h[:, ya0:ya0 + 2048],
           ctx.AF32.res(ya0, 2048), ctx.AF32.h[:, sp0:sp0 + 2048],
           ctx.AF32.res(sp0, 2048), ALU.add, tag=tag + "_yam")
        if c == 0:
            dbg(ctx, "ya0", ctx.AF32.h[:, ya0:ya0 + 2048],
                ctx.AF32.res(ya0, 2048), [128, 2048], F32)
        stt(ctx, ctx.AF32.h[:, ya0:ya0 + 2048], ctx.AF32.res(ya0, 2048),
            ctx.AB16.h[:, xs0:xs0 + 2048], ctx.AB16.res(xs0, 2048),
            ctx.SROW.h[:, dcol0:dcol0 + 1], ctx.AF32.h[:, ya0:ya0 + 2048],
            ctx.AF32.res(ya0, 2048), ALU.mult, ALU.add,
            extra_reads=ctx.SROW.res(dcol0, 8), tag=tag + "_yd")
        dma(ctx, "sync", ctx.AB16.h[:, du0:du0 + 2048],
            ctx.AB16.res(du0, 2048), h3[2048 + 128 * c:2048 + 128 * (c + 1),
                                        :], ["H3:z"], tag=tag + "_zl")
        act(ctx, ctx.AB16.h[:, bcs0:bcs0 + 2048], ctx.AB16.res(bcs0, 2048),
            ctx.AB16.h[:, du0:du0 + 2048], ctx.AB16.res(du0, 2048),
            AF.Silu, tag=tag + "_zs")
        tt(ctx, "vector", ctx.AB16.h[:, yz0:yz0 + 2048],
           ctx.AB16.res(yz0, 2048), ctx.AF32.h[:, ya0:ya0 + 2048],
           ctx.AF32.res(ya0, 2048), ctx.AB16.h[:, bcs0:bcs0 + 2048],
           ctx.AB16.res(bcs0, 2048), ALU.mult, tag=tag + "_yzm")
        dma(ctx, "sync", ms[128 * c:128 * (c + 1), :], ["MS"],
            ctx.AB16.h[:, yz0:yz0 + 2048], ctx.AB16.res(yz0, 2048),
            tag=tag + "_yst")
    ctx.WP.reset()
    ctx.AB16.reset()
    wout = inp["L1_attn_W_out"].ap()
    wo0 = ctx.WP.alloc(16 * 1024)
    for ki in range(NCC):
        dma(ctx, "gpsimd", ctx.WP.h[:, wo0 + 1024 * ki:wo0 + 1024 * (ki + 1)],
            ctx.WP.res(wo0 + 1024 * ki, 1024),
            wout[128 * ki:128 * (ki + 1), :], ["HBM:L1wo"], tag=tag + "_wol")
    yb0 = ctx.AB16.alloc(16 * 512)
    for nt in range(NT):
        for ki in range(NCC):
            dma(ctx, "sync", ctx.AB16.h[:, yb0 + 512 * ki:yb0 + 512 * (ki + 1)],
                ctx.AB16.res(yb0 + 512 * ki, 512),
                ms[128 * ki:128 * (ki + 1), PB * nt:PB * (nt + 1)], ["MS"],
                tag=tag + "_ybl")
        for oc in range(NCH):
            bank = ctx.banks[2 + (oc % 4)]
            for ki in range(NCC):
                lhsT = ctx.WP.h[:, wo0 + 1024 * ki + 128 * oc:
                                wo0 + 1024 * ki + 128 * (oc + 1)]
                mm(ctx, bank.h[:, :], bank.res, lhsT,
                   ctx.WP.res(wo0 + 1024 * ki, 1024),
                   ctx.AB16.h[:, yb0 + 512 * ki:yb0 + 512 * (ki + 1)],
                   ctx.AB16.res(yb0 + 512 * ki, 512),
                   start=(ki == 0), stop=(ki == NCC - 1), tag=tag + "_o")
            x_ap = X.h[:, 2048 * oc + PB * nt: 2048 * oc + PB * (nt + 1)]
            tt(ctx, "vector", x_ap, [f"X:{oc}"], x_ap, [f"X:{oc}"],
               bank.h[:, :], bank.res, ALU.add, tag=tag + "_xa")
    ctx.reset_pools()




def _row_sincos(ctx, row_ap, row_res, n, tag, slot=0):
    sr = ctx.SROW
    t0 = 2048 + 512 * slot
    m_ap = sr.h[0:1, t0:t0 + n]
    m_res = sr.res(t0, 512)
    sin_ap = sr.h[0:1, t0 + 128:t0 + 128 + n]
    cos_ap = sr.h[0:1, t0 + 256:t0 + 256 + n]
    g_ap = sr.h[0:1, t0 + 384:t0 + 384 + n]
    ts(ctx, "vector", m_ap, m_res, row_ap, row_res, 1024 * np.pi, ALU.add,
       tag=tag + "_sh")
    for k in range(9, -1, -1):
        sub = 2.0 * np.pi * (2 ** k)
        ts(ctx, "vector", g_ap, m_res, m_ap, m_res, float(sub), ALU.is_ge,
           tag=tag + "_rm")
        stt(ctx, m_ap, m_res, g_ap, m_res, float(-sub), m_ap, m_res,
            ALU.mult, ALU.add, tag=tag + "_rs")
    ts(ctx, "vector", g_ap, m_res, m_ap, m_res, np.pi, ALU.is_gt,
       tag=tag + "_gt")
    stt(ctx, m_ap, m_res, g_ap, m_res, -2 * np.pi, m_ap, m_res,
        ALU.mult, ALU.add, tag=tag + "_rr")
    act(ctx, sin_ap, m_res, m_ap, m_res, AF.Sin, tag=tag + "_sin")
    act(ctx, g_ap, m_res, m_ap, m_res, AF.Sin, scale=0.5, tag=tag + "_s2")
    act(ctx, g_ap, m_res, g_ap, m_res, AF.Square, tag=tag + "_sq")
    ts(ctx, "vector", cos_ap, m_res, g_ap, m_res, -2.0, ALU.mult, 1.0,
       ALU.add, tag=tag + "_cos")
    return sin_ap, cos_ap, m_res


def l3_attn(ctx):
    tag = "L3a"
    ln_sublayer(ctx, tag + "ln")
    ctx.reset_pools()
    NB, X = ctx.NB, ctx.X
    H, hd = 8, 128
    inp = ctx.inp
    h3 = ctx.H3.ap()
    ms = ctx.MS.ap()
    wq = inp["L3_attn_W_qkv"].ap()

    ph0 = ctx.SROW.alloc(128)
    phrow = ctx.SROW.h[0:1, ph0:ph0 + 128]
    ph_res = ctx.SROW.res(ph0, 128)
    ctx.P.op("gpsimd",
             lambda g: g.dma_start(
                 out=ctx.SROW.h[0:1, ph0:ph0 + 128].rearrange(
                     "a (x y) -> a x y", x=8),
                 in_=inp["L3_attn_photonic_phases"].ap().rearrange(
                     "a b -> (a b)")[0:16 * 1024].rearrange(
                     "(b q a z) -> (a b z)", b=16, q=8, a=8, z=1)
                 if False else None),
             reads=[], writes=[], is_dma=True, tag="ph_skip") \
        if False else None
    ppflat = inp["L3_attn_photonic_phases"].ap().rearrange("a b -> (a b)")
    src = bass.AP(ppflat.tensor, ppflat.offset,
                  [[1, 1], [16, 8], [1025, 16]])
    ctx.P.op("gpsimd",
             lambda g, s_=src: g.dma_start(
                 out=ctx.SROW.h[0:1, ph0:ph0 + 128].rearrange(
                     "p (a b) -> p a b", a=8),
                 in_=s_, allow_slow_non_contiguous=True),
             reads=["HBM:pp"], writes=list(ph_res), is_dma=True,
             tag=tag + "_ph")
    sphi, cphi, rr1 = _row_sincos(ctx, phrow, ph_res, 128, tag + "_phi")
    c2r0 = ctx.SROW.alloc(128)
    c2row = ctx.SROW.h[0:1, c2r0:c2r0 + 128]
    c2_res = ctx.SROW.res(c2r0, 128)
    act(ctx, c2row, c2_res, sphi, rr1, AF.Square, tag=tag + "_c2s")
    ts(ctx, "vector", c2row, c2_res, c2row, c2_res, -2.0, ALU.mult, 1.0,
       ALU.add, tag=tag + "_c2")
    ccol0 = ctx.SROW.alloc(8)
    scol0 = ctx.SROW.alloc(8)
    c2col0 = ctx.SROW.alloc(8)
    for (rowap, rowres, dst) in ((cphi, rr1, ccol0), (sphi, rr1, scol0),
                                 (c2row, c2_res, c2col0)):
        bk = ctx.pbank(tag)
        mm(ctx, bk.h[:, 0:1], bk.res, rowap, rowres, ctx.ONESF[0:1, 0:1],
           ["CONST"], start=True, stop=True, tag=tag + "_pc")
        copy(ctx, "vector", ctx.SROW.h[:, dst:dst + 1],
             ctx.SROW.res(dst, 8), bk.h[:, 0:1], bk.res, tag=tag + "_pcc")
    sl0 = ctx.SROW.alloc(128)
    sl_ap = ctx.SROW.h[0:1, sl0:sl0 + 4]
    sl_res = ctx.SROW.res(sl0, 128)
    dma(ctx, "sync", sl_ap, sl_res,
        inp["L3_attn_superposition_logits"].ap()[:, :], ["HBM:sup"],
        tag=tag + "_sl")
    ex_ap = ctx.SROW.h[0:1, sl0 + 8:sl0 + 12]
    act(ctx, ex_ap, sl_res, sl_ap, sl_res, AF.Exp, tag=tag + "_se")
    sm_ap = ctx.SROW.h[0:1, sl0 + 16:sl0 + 17]
    reduce(ctx, "vector", sm_ap, sl_res, ex_ap, sl_res, ALU.add,
           tag=tag + "_ss")
    recip(ctx, sm_ap, sl_res, sm_ap, sl_res, tag=tag + "_sr")
    ts(ctx, "vector", ex_ap, sl_res, ex_ap, sl_res, sm_ap, ALU.mult,
       tag=tag + "_sn")
    qw0 = ctx.SROW.alloc(128)
    qw_ap = ctx.SROW.h[0:1, qw0:qw0 + 4]
    qw_res = ctx.SROW.res(qw0, 128)
    qwflat = inp["L3_attn_quantum_weights"].ap().rearrange("a b c -> (a b c)")
    src = bass.AP(qwflat.tensor, qwflat.offset, [[1, 1], [64, 4]])
    ctx.P.op("gpsimd",
             lambda g, s_=src: g.dma_start(out=ctx.SROW.h[0:1, qw0:qw0 + 4],
                                           in_=s_,
                                           allow_slow_non_contiguous=True),
             reads=["HBM:qw"], writes=list(qw_res), is_dma=True,
             tag=tag + "_qw")
    sqw, cqw, rr2 = _row_sincos(ctx, qw_ap, qw_res, 4, tag + "_qw")
    ra_ap = ctx.SROW.h[0:1, qw0 + 8:qw0 + 12]
    ts(ctx, "vector", ra_ap, qw_res, sqw, rr2, 0.3, ALU.mult, 0.5, ALU.add,
       tag=tag + "_ra")
    rb_ap = ctx.SROW.h[0:1, qw0 + 16:qw0 + 20]
    ts(ctx, "vector", rb_ap, qw_res, ra_ap, qw_res, -1.0, ALU.mult, 1.0,
       ALU.add, tag=tag + "_rb")
    al_ap = ctx.SROW.h[0:1, qw0 + 24:qw0 + 28]
    tt(ctx, "vector", al_ap, qw_res, ra_ap, qw_res, rb_ap, qw_res,
       ALU.mult, tag=tag + "_al")
    ts(ctx, "vector", al_ap, qw_res, al_ap, qw_res, float(hd ** -0.5),
       ALU.mult, tag=tag + "_als")
    supc0 = ctx.SROW.alloc(8)
    alc0 = ctx.SROW.alloc(8)
    for c in range(4):
        for (srcap, dst) in ((ex_ap[:, c:c + 1], supc0 + c),
                             (al_ap[:, c:c + 1], alc0 + c)):
            bk = ctx.pbank(tag)
            mm(ctx, bk.h[:, 0:1], bk.res, ctx.ONESF_ROW, ["CONST"],
               srcap, sl_res + qw_res, start=True, stop=True,
               tag=tag + "_cc")
            copy(ctx, "vector", ctx.SROW.h[:, dst:dst + 1],
                 ctx.SROW.res(dst, 8), bk.h[:, 0:1], bk.res,
                 tag=tag + "_ccc")
    gflat = inp["L3_attn_entanglement_gates"].ap().rearrange("a b c -> (a b c)")
    co0 = ctx.SROW.alloc(72)
    gr0 = ctx.SROW.alloc(128)
    for dep in range(3):
        th_ap = ctx.SROW.h[0:1, gr0:gr0 + 7]
        gr_res = ctx.SROW.res(gr0, 128)
        src = bass.AP(gflat.tensor, gflat.offset + 32 * dep, [[1, 1], [4, 7]])
        ctx.P.op("gpsimd",
                 lambda g, s=src, o=th_ap: g.dma_start(
                     out=o, in_=s, allow_slow_non_contiguous=True),
                 reads=["HBM:gates"], writes=list(gr_res), is_dma=True,
                 tag=tag + "_gth")
        ph_ap2 = ctx.SROW.h[0:1, gr0 + 8:gr0 + 15]
        src = bass.AP(gflat.tensor, gflat.offset + 32 * dep + 1,
                      [[1, 1], [4, 7]])
        ctx.P.op("gpsimd",
                 lambda g, s=src, o=ph_ap2: g.dma_start(
                     out=o, in_=s, allow_slow_non_contiguous=True),
                 reads=["HBM:gates"], writes=list(gr_res), is_dma=True,
                 tag=tag + "_gph")
        ts(ctx, "vector", th_ap, gr_res, th_ap, gr_res, 0.5, ALU.mult,
           tag=tag + "_th2")
        sT, cT, rrT = _row_sincos(ctx, th_ap, gr_res, 7, tag + f"_t{dep}",
                                  slot=0)
        sP, cP, rrP = _row_sincos(ctx, ph_ap2, gr_res, 7, tag + f"_p{dep}",
                                  slot=1)
        t1 = ctx.SROW.h[0:1, gr0 + 16:gr0 + 23]
        tt(ctx, "vector", t1, gr_res, sT, rrT, cP, rrP, ALU.mult,
           tag=tag + "_stcp")
        am1 = ctx.SROW.h[0:1, gr0 + 24:gr0 + 31]
        tt(ctx, "vector", am1, gr_res, cT, rrT, t1, gr_res, ALU.add,
           tag=tag + "_a")
        ts(ctx, "vector", am1, gr_res, am1, gr_res, -1.0, ALU.add,
           tag=tag + "_am1")
        dm1 = ctx.SROW.h[0:1, gr0 + 32:gr0 + 39]
        tt(ctx, "vector", dm1, gr_res, cT, rrT, t1, gr_res, ALU.subtract,
           tag=tag + "_d")
        ts(ctx, "vector", dm1, gr_res, dm1, gr_res, -1.0, ALU.add,
           tag=tag + "_dm1")
        bb_ = ctx.SROW.h[0:1, gr0 + 40:gr0 + 47]
        tt(ctx, "vector", bb_, gr_res, sT, rrT, sP, rrP, ALU.mult,
           tag=tag + "_b")
        for p in range(7):
            for k, rowap in enumerate((am1, bb_, dm1)):
                bk = ctx.pbank(tag)
                mm(ctx, bk.h[:, 0:1], bk.res, ctx.ONESF_ROW, ["CONST"],
                   rowap[:, p:p + 1], gr_res, start=True, stop=True,
                   tag=tag + "_cb")
                dst = co0 + (dep * 7 + p) * 3 + k
                copy(ctx, "vector", ctx.SROW.h[:, dst:dst + 1],
                     ctx.SROW.res(dst, 72), bk.h[:, 0:1], bk.res,
                     tag=tag + "_cbc")

    bq0 = ctx.SROW.alloc(24)
    load_vec_cols(ctx, "L3_attn_b_qkv", bq0, 24, tag=tag + "_bq")

    wqt0 = ctx.WP.alloc(2 * 1024)
    stg0 = ctx.AB16.alloc(2 * 1024)
    qraw0 = ctx.AF32.alloc(2048)
    sq0 = ctx.AF32.alloc(2048)
    nr0 = ctx.SROW.alloc(512)
    nrb0 = ctx.SROWB.alloc(512)
    qh0 = ctx.AB16.alloc(2048)
    pl0 = ctx.AB16.alloc(2048)
    ssum0 = ctx.AF32.alloc(2048)
    sg0 = ctx.AB16.alloc(2048)
    for oc in range(24):
        jj = oc % 8
        typ = oc // 8
        sl = stg0 + 1024 * (oc % 2)
        dma(ctx, "gpsimd", ctx.AB16.h[:, sl:sl + 1024],
            ctx.AB16.res(sl, 1024), wq[128 * oc:128 * (oc + 1), :],
            ["HBM:L3wq"], tag=tag + "_wl")
        wsl = wqt0 + 1024 * (oc % 2)
        for ki in range(NCH):
            bk16 = ctx.pbank16(tag)
            ctx.P.op("tensor",
                     lambda t, o=bk16.h[:, 0:128],
                     i=ctx.AB16.h[:, sl + 128 * ki:sl + 128 * (ki + 1)],
                     idn=ctx.IDB: t.transpose(o, i, idn),
                     reads=ctx.AB16.res(sl, 1024) + ["CONST"],
                     writes=bk16.res, tag=tag + "_wt")
            copy(ctx, "vector", ctx.WP.h[:, wsl + 128 * ki:
                                         wsl + 128 * (ki + 1)],
                 ctx.WP.res(wsl, 1024), bk16.h[:, 0:128], bk16.res,
                 tag=tag + "_wtc")
        for nt in range(NT):
            bank = ctx.banks[4 + (nt % 2)]
            for ki in range(NCH):
                lhsT = ctx.WP.h[:, wsl + 128 * ki:wsl + 128 * (ki + 1)]
                mm(ctx, bank.h[:, :], bank.res, lhsT, ctx.WP.res(wsl, 1024),
                   NB.h[:, 2048 * ki + PB * nt:2048 * ki + PB * (nt + 1)],
                   [f"NB:{ki}"], start=(ki == 0), stop=(ki == NCH - 1),
                   tag=tag + "_qkv")
            act(ctx, ctx.AF32.h[:, qraw0 + PB * nt:qraw0 + PB * (nt + 1)],
                ctx.AF32.res(qraw0, 2048), bank.h[:, :], bank.res,
                AF.Identity, bias=ctx.SROW.h[:, bq0 + oc:bq0 + oc + 1],
                extra_reads=ctx.SROW.res(bq0, 24), tag=tag + "_qe")
        act(ctx, ctx.AF32.h[:, sq0:sq0 + 2048], ctx.AF32.res(sq0, 2048),
            ctx.AF32.h[:, qraw0:qraw0 + 2048], ctx.AF32.res(qraw0, 2048),
            AF.Square, tag=tag + "_sq")
        for nt in range(NT):
            bank = ctx.banks[nt % 2]
            mm(ctx, bank.h[0:1, :], bank.res, ctx.ONESF, ["CONST"],
               ctx.AF32.h[:, sq0 + PB * nt:sq0 + PB * (nt + 1)],
               ctx.AF32.res(sq0, 2048), start=True, stop=True,
               tag=tag + "_nm")
            nr_ap = ctx.SROW.h[0:1, nr0 + PB * (nt % 1):nr0 + 512]
            nr_ap = ctx.SROW.h[0:1, nr0:nr0 + 512]
            nr_res = ctx.SROW.res(nr0, 512)
            act(ctx, nr_ap, nr_res, bank.h[0:1, :], bank.res, AF.Sqrt,
                tag=tag + "_nsq")
            ts(ctx, "vector", nr_ap, nr_res, nr_ap, nr_res, 1e-12,
               ALU.max, tag=tag + "_nmx")
            recip(ctx, nr_ap, nr_res, nr_ap, nr_res, tag=tag + "_nrc")
            nrb_ap = ctx.SROWB.h[0:1, nrb0:nrb0 + 512]
            nrb_res = ctx.SROWB.res(nrb0, 512)
            copy(ctx, "vector", nrb_ap, nrb_res, nr_ap, nr_res,
                 tag=tag + "_nrb")
            bk2 = ctx.banks[2 + (nt % 2)]
            mm(ctx, bk2.h[:, :], bk2.res, ctx.ONESB_ROW, ["CONST"],
               nrb_ap, nrb_res, start=True, stop=True, tag=tag + "_nbc")
            tt(ctx, "vector", ctx.AB16.h[:, qh0 + PB * nt:
                                         qh0 + PB * (nt + 1)],
               ctx.AB16.res(qh0, 2048),
               ctx.AF32.h[:, qraw0 + PB * nt:qraw0 + PB * (nt + 1)],
               ctx.AF32.res(qraw0, 2048), bk2.h[:, :], bk2.res, ALU.mult,
               tag=tag + "_nrm")
        if typ == 2:
            dma(ctx, "sync", ms[128 * jj:128 * (jj + 1), :], ["MS:v"],
                ctx.AB16.h[:, qh0:qh0 + 2048], ctx.AB16.res(qh0, 2048),
                tag=tag + "_vst")
            if jj == 0:
                copy(ctx, "vector", ctx.AF32.h[:, ssum0:ssum0 + 2048],
                     ctx.AF32.res(ssum0, 2048),
                     ctx.AB16.h[:, qh0:qh0 + 2048], ctx.AB16.res(qh0, 2048),
                     tag=tag + "_s0")
            else:
                tt(ctx, "vector", ctx.AF32.h[:, ssum0:ssum0 + 2048],
                   ctx.AF32.res(ssum0, 2048),
                   ctx.AF32.h[:, ssum0:ssum0 + 2048],
                   ctx.AF32.res(ssum0, 2048),
                   ctx.AB16.h[:, qh0:qh0 + 2048], ctx.AB16.res(qh0, 2048),
                   ALU.add, tag=tag + "_sa")
        else:
            base = 0 if typ == 0 else 2048
            ts(ctx, "vector", ctx.AB16.h[:, pl0:pl0 + 2048],
               ctx.AB16.res(pl0, 2048), ctx.AB16.h[:, qh0:qh0 + 2048],
               ctx.AB16.res(qh0, 2048), ctx.SROW.h[:, ccol0:ccol0 + 1],
               ALU.mult, extra_reads=ctx.SROW.res(ccol0, 8),
               tag=tag + "_enr")
            dma(ctx, "sync", h3[base + 128 * jj:base + 128 * (jj + 1), :],
                [f"H3:p{typ}r{jj}"], ctx.AB16.h[:, pl0:pl0 + 2048],
                ctx.AB16.res(pl0, 2048), tag=tag + "_str")
            ts(ctx, "vector", ctx.AB16.h[:, pl0:pl0 + 2048],
               ctx.AB16.res(pl0, 2048), ctx.AB16.h[:, qh0:qh0 + 2048],
               ctx.AB16.res(qh0, 2048), ctx.SROW.h[:, scol0:scol0 + 1],
               ALU.mult, extra_reads=ctx.SROW.res(scol0, 8),
               tag=tag + "_eni")
            dma(ctx, "sync", h3[base + 1024 + 128 * jj:
                               base + 1024 + 128 * (jj + 1), :],
                [f"H3:p{typ}i{jj}"], ctx.AB16.h[:, pl0:pl0 + 2048],
                ctx.AB16.res(pl0, 2048), tag=tag + "_sti")
    ts(ctx, "vector", ctx.AB16.h[:, sg0:sg0 + 2048], ctx.AB16.res(sg0, 2048),
       ctx.AF32.h[:, ssum0:ssum0 + 2048], ctx.AF32.res(ssum0, 2048), 0.0,
       ALU.is_ge, tag=tag + "_sge")
    ts(ctx, "vector", ctx.AB16.h[:, sg0:sg0 + 2048], ctx.AB16.res(sg0, 2048),
       ctx.AB16.h[:, sg0:sg0 + 2048], ctx.AB16.res(sg0, 2048), 2.0,
       ALU.mult, -1.0, ALU.add, tag=tag + "_sgn")
    sgt0 = ctx.SROWB.alloc(2048)
    for ti in range(NTT):
        bk16 = ctx.pbank16(tag)
        ctx.P.op("tensor",
                 lambda t, o=bk16.h[:, 0:128],
                 i=ctx.AB16.h[:, sg0 + 128 * ti:sg0 + 128 * (ti + 1)],
                 idn=ctx.IDB: t.transpose(o, i, idn),
                 reads=ctx.AB16.res(sg0, 2048) + ["CONST"],
                 writes=bk16.res, tag=tag + "_sgt")
        copy(ctx, "vector", ctx.SROWB.h[:, sgt0 + 128 * ti:
                                        sgt0 + 128 * (ti + 1)],
             ctx.SROWB.res(sgt0 + 128 * ti, 128), bk16.h[:, 0:128],
             bk16.res, tag=tag + "_sgtc")

    ctx.AB16.reset()
    ctx.AF32.reset()
    s1r0 = ctx.AB16.alloc(2048)
    s1i0 = ctx.AB16.alloc(2048)
    s2r0 = ctx.AB16.alloc(2048)
    s2i0 = ctx.AB16.alloc(2048)
    ct0 = ctx.AF32.alloc(2048)
    d1_0 = ctx.AF32.alloc(2048)
    d2_0 = ctx.AF32.alloc(2048)
    slotR = (s1r0, s2r0)
    slotI = (s1i0, s2i0)
    for typ in range(2):
        base = 2048 * typ
        for dep in range(3):
            for p in range(8):
                cu = p % 2
                pv = (p + 1) % 2
                cr, ci = slotR[cu], slotI[cu]
                pr_r, pr_i = slotR[pv], slotI[pv]
                dma(ctx, "sync", ctx.AB16.h[:, cr:cr + 2048],
                    ctx.AB16.res(cr, 2048),
                    h3[base + 128 * p:base + 128 * (p + 1), :],
                    [f"H3:p{typ}r{p}"], tag=tag + "_elr")
                dma(ctx, "sync", ctx.AB16.h[:, ci:ci + 2048],
                    ctx.AB16.res(ci, 2048),
                    h3[base + 1024 + 128 * p:base + 1024 + 128 * (p + 1), :],
                    [f"H3:p{typ}i{p}"], tag=tag + "_eli")
                if p == 0:
                    continue
                pr = p - 1
                cix = co0 + (dep * 7 + pr) * 3
                am1c = ctx.SROW.h[:, cix:cix + 1]
                bbc = ctx.SROW.h[:, cix + 1:cix + 2]
                dm1c = ctx.SROW.h[:, cix + 2:cix + 3]
                co_res = ctx.SROW.res(co0, 72)
                act(ctx, ctx.AF32.h[:, ct0:ct0 + 2048],
                    ctx.AF32.res(ct0, 2048),
                    ctx.AB16.h[:, pr_r:pr_r + 2048],
                    ctx.AB16.res(pr_r, 2048), AF.Square, tag=tag + "_ec1")
                act(ctx, ctx.AF32.h[:, d1_0:d1_0 + 2048],
                    ctx.AF32.res(d1_0, 2048),
                    ctx.AB16.h[:, pr_i:pr_i + 2048],
                    ctx.AB16.res(pr_i, 2048), AF.Square, tag=tag + "_ec2")
                tt(ctx, "vector", ctx.AF32.h[:, ct0:ct0 + 2048],
                   ctx.AF32.res(ct0, 2048), ctx.AF32.h[:, ct0:ct0 + 2048],
                   ctx.AF32.res(ct0, 2048), ctx.AF32.h[:, d1_0:d1_0 + 2048],
                   ctx.AF32.res(d1_0, 2048), ALU.add, tag=tag + "_ec3")
                act(ctx, ctx.AF32.h[:, d1_0:d1_0 + 2048],
                    ctx.AF32.res(d1_0, 2048), ctx.AB16.h[:, cr:cr + 2048],
                    ctx.AB16.res(cr, 2048), AF.Identity, scale=am1c,
                    extra_reads=co_res, tag=tag + "_dr1")
                stt(ctx, ctx.AF32.h[:, d1_0:d1_0 + 2048],
                    ctx.AF32.res(d1_0, 2048),
                    ctx.AB16.h[:, ci:ci + 2048],
                    ctx.AB16.res(ci, 2048), bbc,
                    ctx.AF32.h[:, d1_0:d1_0 + 2048],
                    ctx.AF32.res(d1_0, 2048), ALU.mult, ALU.add,
                    extra_reads=co_res, tag=tag + "_dr2")
                act(ctx, ctx.AF32.h[:, d2_0:d2_0 + 2048],
                    ctx.AF32.res(d2_0, 2048), ctx.AB16.h[:, ci:ci + 2048],
                    ctx.AB16.res(ci, 2048), AF.Identity, scale=dm1c,
                    extra_reads=co_res, tag=tag + "_di1")
                stt(ctx, ctx.AF32.h[:, d2_0:d2_0 + 2048],
                    ctx.AF32.res(d2_0, 2048),
                    ctx.AB16.h[:, cr:cr + 2048],
                    ctx.AB16.res(cr, 2048), bbc,
                    ctx.AF32.h[:, d2_0:d2_0 + 2048],
                    ctx.AF32.res(d2_0, 2048), ALU.mult, ALU.add,
                    extra_reads=co_res, tag=tag + "_di2")
                tt(ctx, "vector", ctx.AF32.h[:, d1_0:d1_0 + 2048],
                   ctx.AF32.res(d1_0, 2048), ctx.AF32.h[:, ct0:ct0 + 2048],
                   ctx.AF32.res(ct0, 2048), ctx.AF32.h[:, d1_0:d1_0 + 2048],
                   ctx.AF32.res(d1_0, 2048), ALU.mult, tag=tag + "_nr1")
                tt(ctx, "vector", ctx.AB16.h[:, pr_r:pr_r + 2048],
                   ctx.AB16.res(pr_r, 2048), ctx.AB16.h[:, cr:cr + 2048],
                   ctx.AB16.res(cr, 2048), ctx.AF32.h[:, d1_0:d1_0 + 2048],
                   ctx.AF32.res(d1_0, 2048), ALU.add, tag=tag + "_nr2")
                tt(ctx, "gpsimd", ctx.AF32.h[:, d2_0:d2_0 + 2048],
                   ctx.AF32.res(d2_0, 2048), ctx.AF32.h[:, ct0:ct0 + 2048],
                   ctx.AF32.res(ct0, 2048), ctx.AF32.h[:, d2_0:d2_0 + 2048],
                   ctx.AF32.res(d2_0, 2048), ALU.mult, tag=tag + "_ni1")
                tt(ctx, "gpsimd", ctx.AB16.h[:, pr_i:pr_i + 2048],
                   ctx.AB16.res(pr_i, 2048), ctx.AB16.h[:, ci:ci + 2048],
                   ctx.AB16.res(ci, 2048), ctx.AF32.h[:, d2_0:d2_0 + 2048],
                   ctx.AF32.res(d2_0, 2048), ALU.add, tag=tag + "_ni2")
                dma(ctx, "sync", h3[base + 128 * pr:base + 128 * (pr + 1), :],
                    [f"H3:p{typ}r{pr}"], ctx.AB16.h[:, pr_r:pr_r + 2048],
                    ctx.AB16.res(pr_r, 2048), tag=tag + "_esr")
                dma(ctx, "sync", h3[base + 1024 + 128 * pr:
                                   base + 1024 + 128 * (pr + 1), :],
                    [f"H3:p{typ}i{pr}"], ctx.AB16.h[:, pr_i:pr_i + 2048],
                    ctx.AB16.res(pr_i, 2048), tag=tag + "_esi")

    eyej0 = ctx.SROWB.alloc(64)
    memset(ctx, "vector", ctx.SROWB.h[:, eyej0:eyej0 + 64],
           ctx.SROWB.res(eyej0, 64), 1.0, tag=tag + "_eyi")
    for j in range(8):
        ctx.P.op("gpsimd", lambda g, j=j: g.affine_select(
            ctx.SROWB.h[:, eyej0 + 8 * j:eyej0 + 8 * (j + 1)],
            ctx.SROWB.h[:, eyej0 + 8 * j:eyej0 + 8 * (j + 1)],
            [[1, 8]], ALU.is_equal, 0.0, base=-j, channel_multiplier=0),
            reads=ctx.SROWB.res(eyej0, 64), writes=ctx.SROWB.res(eyej0, 64),
            tag=tag + "_eyj")
    ctx.WP.reset()
    bt0 = ctx.WP.alloc(2048)
    bt1 = ctx.WP.alloc(2048)
    qi_r0, qi_i0 = s1r0, s1i0
    ke_r0, ke_i0 = s2r0, s2i0
    pr_0 = ctx.AB16.alloc(1024)
    pi_0 = ctx.AB16.alloc(1024)
    t1_0 = ctx.AB16.alloc(1024)
    er0 = ctx.AB16.alloc(512)
    for i in range(8):
        dma(ctx, "sync", ctx.AB16.h[:, qi_r0:qi_r0 + 2048],
            ctx.AB16.res(qi_r0, 2048), h3[128 * i:128 * (i + 1), :],
            [f"H3:p0r{i}"], tag=tag + "_plq")
        dma(ctx, "sync", ctx.AB16.h[:, qi_i0:qi_i0 + 2048],
            ctx.AB16.res(qi_i0, 2048), h3[1024 + 128 * i:1024 + 128 * (i + 1),
                                          :], [f"H3:p0i{i}"],
            tag=tag + "_plqi")
        for half in range(2):
            hsl = slice(1024 * half, 1024 * (half + 1))
            bre = [ctx.banks[0], ctx.banks[1]]
            bim = [ctx.banks[2], ctx.banks[3]]
            for j in range(8):
                dma(ctx, "sync", ctx.AB16.h[:, ke_r0:ke_r0 + 1024],
                    ctx.AB16.res(ke_r0, 2048),
                    h3[2048 + 128 * j:2048 + 128 * (j + 1), hsl],
                    [f"H3:p1r{j}"], tag=tag + "_plk")
                dma(ctx, "sync", ctx.AB16.h[:, ke_i0:ke_i0 + 1024],
                    ctx.AB16.res(ke_i0, 2048),
                    h3[3072 + 128 * j:3072 + 128 * (j + 1), hsl],
                    [f"H3:p1i{j}"], tag=tag + "_plki")
                qr_ap = ctx.AB16.h[:, qi_r0 + 1024 * half:
                                   qi_r0 + 1024 * (half + 1)]
                qi_ap = ctx.AB16.h[:, qi_i0 + 1024 * half:
                                   qi_i0 + 1024 * (half + 1)]
                kr_ap = ctx.AB16.h[:, ke_r0:ke_r0 + 1024]
                ki_ap = ctx.AB16.h[:, ke_i0:ke_i0 + 1024]
                qres = ctx.AB16.res(qi_r0, 2048) + ctx.AB16.res(qi_i0, 2048)
                kres = ctx.AB16.res(ke_r0, 2048) + ctx.AB16.res(ke_i0, 2048)
                tt(ctx, "vector", ctx.AB16.h[:, pr_0:pr_0 + 1024],
                   ctx.AB16.res(pr_0, 1024), qr_ap, qres, kr_ap, kres,
                   ALU.mult, tag=tag + "_pm1")
                tt(ctx, "gpsimd", ctx.AB16.h[:, t1_0:t1_0 + 1024],
                   ctx.AB16.res(t1_0, 1024), qi_ap, qres, ki_ap, kres,
                   ALU.mult, tag=tag + "_pm2")
                tt(ctx, "vector", ctx.AB16.h[:, pr_0:pr_0 + 1024],
                   ctx.AB16.res(pr_0, 1024), ctx.AB16.h[:, pr_0:pr_0 + 1024],
                   ctx.AB16.res(pr_0, 1024), ctx.AB16.h[:, t1_0:t1_0 + 1024],
                   ctx.AB16.res(t1_0, 1024), ALU.add, tag=tag + "_pa1")
                tt(ctx, "gpsimd", ctx.AB16.h[:, pi_0:pi_0 + 1024],
                   ctx.AB16.res(pi_0, 1024), qr_ap, qres, ki_ap, kres,
                   ALU.mult, tag=tag + "_pm3")
                tt(ctx, "vector", ctx.AB16.h[:, t1_0:t1_0 + 1024],
                   ctx.AB16.res(t1_0, 1024), qi_ap, qres, kr_ap, kres,
                   ALU.mult, tag=tag + "_pm4")
                tt(ctx, "vector", ctx.AB16.h[:, pi_0:pi_0 + 1024],
                   ctx.AB16.res(pi_0, 1024), ctx.AB16.h[:, pi_0:pi_0 + 1024],
                   ctx.AB16.res(pi_0, 1024), ctx.AB16.h[:, t1_0:t1_0 + 1024],
                   ctx.AB16.res(t1_0, 1024), ALU.subtract, tag=tag + "_pa2")
                for sub in range(2):
                    ss = slice(PB * sub, PB * (sub + 1))
                    mm(ctx, bre[sub].h[0:8, :], bre[sub].res,
                       ctx.SROWB.h[:, eyej0 + 8 * j:eyej0 + 8 * (j + 1)],
                       ctx.SROWB.res(eyej0, 64),
                       ctx.AB16.h[:, pr_0 + PB * sub:pr_0 + PB * (sub + 1)],
                       ctx.AB16.res(pr_0, 1024), start=(j == 0),
                       stop=(j == 7), tag=tag + "_prm")
                    mm(ctx, bim[sub].h[0:8, :], bim[sub].res,
                       ctx.SROWB.h[:, eyej0 + 8 * j:eyej0 + 8 * (j + 1)],
                       ctx.SROWB.res(eyej0, 64),
                       ctx.AB16.h[:, pi_0 + PB * sub:pi_0 + PB * (sub + 1)],
                       ctx.AB16.res(pi_0, 1024), start=(j == 0),
                       stop=(j == 7), tag=tag + "_pim")
            for cmp_i, bks in ((0, bre), (1, bim)):
                dstb = bt0 if cmp_i == 0 else bt1
                for sub in range(2):
                    e_ap = ctx.AB16.h[0:8, er0:er0 + 512]
                    e_res = ctx.AB16.res(er0, 512)
                    scopy(ctx, e_ap, e_res, bks[sub].h[0:8, :],
                          bks[sub].res, tag=tag + "_pev")
                    for q4 in range(4):
                        ti = half * 8 + sub * 4 + q4
                        bk16 = ctx.pbank16(tag)
                        ctx.P.op("tensor",
                                 lambda t, o=bk16.h[:, 0:8],
                                 i=e_ap[:, 128 * q4:128 * (q4 + 1)],
                                 idn=ctx.IDB[0:8, 0:8]:
                                 t.transpose(o, i, idn),
                                 reads=list(e_res) + ["CONST"],
                                 writes=bk16.res, tag=tag + "_ptr")
                        copy(ctx, "vector",
                             ctx.WP.h[:, dstb + 128 * i + 8 * ti:
                                      dstb + 128 * i + 8 * (ti + 1)],
                             ctx.WP.res(dstb + 128 * i, 128),
                             bk16.h[:, 0:8], bk16.res, tag=tag + "_ptc")
    vt0 = ctx.WP.alloc(16384)
    ctx.AB16.reset()
    vpl0 = ctx.AB16.alloc(2048)
    for j in range(8):
        dma(ctx, "sync", ctx.AB16.h[:, vpl0:vpl0 + 2048],
            ctx.AB16.res(vpl0, 2048), ms[128 * j:128 * (j + 1), :],
            ["MS:v"], tag=tag + "_vl")
        for ti in range(NTT):
            bk16 = ctx.pbank16(tag)
            ctx.P.op("tensor",
                     lambda t, o=bk16.h[:, 0:128],
                     i=ctx.AB16.h[:, vpl0 + 128 * ti:vpl0 + 128 * (ti + 1)],
                     idn=ctx.IDB: t.transpose(o, i, idn),
                     reads=ctx.AB16.res(vpl0, 2048) + ["CONST"],
                     writes=bk16.res, tag=tag + "_vtr")
            copy(ctx, "vector",
                 ctx.WP.h[:, vt0 + 1024 * ti + 128 * j:
                          vt0 + 1024 * ti + 128 * (j + 1)],
                 ctx.WP.res(vt0 + 1024 * ti + 128 * j, 128),
                 bk16.h[:, 0:128], bk16.res, tag=tag + "_vtc")
    ctx.AB16.reset()
    ctx.AF32.reset()
    amp0 = ctx.AB16.alloc(4096)
    bsq0 = ctx.AF32.alloc(1024)
    ex0 = ctx.AF32.alloc(128)
    sm0 = ctx.SROW.alloc(128)
    for i in range(8):
        bt_re = ctx.WP.h[:, bt0 + 128 * i:bt0 + 128 * (i + 1)]
        bt_im = ctx.WP.h[:, bt1 + 128 * i:bt1 + 128 * (i + 1)]
        btres = ctx.WP.res(bt0 + 128 * i, 128) + \
            ctx.WP.res(bt1 + 128 * i, 128)
        act(ctx, ctx.AF32.h[:, bsq0:bsq0 + 128], ctx.AF32.res(bsq0, 1024),
            bt_re, btres, AF.Square, tag=tag + "_b2r")
        act(ctx, ctx.AF32.h[:, bsq0 + 128:bsq0 + 256],
            ctx.AF32.res(bsq0, 1024), bt_im, btres, AF.Square,
            tag=tag + "_b2i")
        tt(ctx, "vector", ctx.AF32.h[:, bsq0:bsq0 + 128],
           ctx.AF32.res(bsq0, 1024), ctx.AF32.h[:, bsq0:bsq0 + 128],
           ctx.AF32.res(bsq0, 1024), ctx.AF32.h[:, bsq0 + 128:bsq0 + 256],
           ctx.AF32.res(bsq0, 1024), ALU.add, tag=tag + "_b2a")
        for c in range(4):
            act(ctx, ctx.AF32.h[:, ex0:ex0 + 128], ctx.AF32.res(ex0, 128),
                ctx.AF32.h[:, bsq0:bsq0 + 128], ctx.AF32.res(bsq0, 1024),
                AF.Exp, scale=ctx.SROW.h[:, alc0 + c:alc0 + c + 1],
                extra_reads=ctx.SROW.res(alc0, 8), tag=tag + "_pex")
            ctx.P.op("vector",
                     lambda e, o=ctx.SROW.h[:, sm0:sm0 + 16],
                     i=ctx.AF32.h[:, ex0:ex0 + 128].rearrange(
                         "p (a b) -> p a b", b=8):
                     e.tensor_reduce(o.rearrange("p (a b) -> p a b", b=1),
                                     i, AXX, ALU.add),
                     reads=ctx.AF32.res(ex0, 128),
                     writes=ctx.SROW.res(sm0, 128), special=True,
                     tag=tag + "_psm")
            recip(ctx, ctx.SROW.h[:, sm0:sm0 + 16], ctx.SROW.res(sm0, 128),
                  ctx.SROW.h[:, sm0:sm0 + 16], ctx.SROW.res(sm0, 128),
                  tag=tag + "_prc")
            dsta = amp0 + 1024 * c + 128 * i
            for ti in range(NTT):
                ts(ctx, "vector",
                   ctx.AF32.h[:, ex0 + 8 * ti:ex0 + 8 * (ti + 1)],
                   ctx.AF32.res(ex0, 128),
                   ctx.AF32.h[:, ex0 + 8 * ti:ex0 + 8 * (ti + 1)],
                   ctx.AF32.res(ex0, 128),
                   ctx.SROW.h[:, sm0 + ti:sm0 + ti + 1], ALU.mult,
                   extra_reads=ctx.SROW.res(sm0, 128), tag=tag + "_pdv")
            ts(ctx, "vector", ctx.AF32.h[:, ex0:ex0 + 128],
               ctx.AF32.res(ex0, 128), ctx.AF32.h[:, ex0:ex0 + 128],
               ctx.AF32.res(ex0, 128), 1e-12, ALU.add, tag=tag + "_pae")
            act(ctx, ctx.AB16.h[:, dsta:dsta + 128],
                ctx.AB16.res(dsta, 128), ctx.AF32.h[:, ex0:ex0 + 128],
                ctx.AF32.res(ex0, 128), AF.Sqrt, tag=tag + "_pam")
    racc0 = ctx.AF32.alloc(1024)
    gacc0 = ctx.AF32.alloc(1024)
    gt0_ = ctx.AF32.alloc(1024)
    af_amp0 = ctx.AF32.alloc(512)
    mes0 = ctx.AB16.alloc(1024)
    mfm0 = ctx.AB16.alloc(1024)
    ctx.AF32.reset(0)
    raccs = [ctx.AF32.alloc(1024) for _ in range(4)]
    gacc0 = ctx.AF32.alloc(1024)
    gt0_ = ctx.AF32.alloc(1024)
    afs0 = ctx.SROW.alloc(1024)
    for ti in range(NTT):
        for c in range(4):
            slot = afs0 + 128 * (c + 4 * (ti % 2))
            ctx.P.op("scalar",
                     lambda e, o=ctx.SROW.h[:, slot:slot + 64].rearrange(
                         "p (i j) -> p i j", i=8),
                     i_=ctx.AB16.h[:, amp0 + 1024 * c:
                                   amp0 + 1024 * (c + 1)].rearrange(
                         "p (i t) -> p i t", i=8)[:, :, 8 * ti:8 * (ti + 1)]:
                     e.copy(o, i_),
                     reads=ctx.AB16.res(amp0 + 1024 * c, 1024),
                     writes=ctx.SROW.res(slot, 128), tag=tag + "_af")
        for j in range(8):
            for c in range(4):
                rb = raccs[c]
                slot = afs0 + 128 * (c + 4 * (ti % 2))
                a_res = ctx.SROW.res(slot, 128)
                vsl = ctx.WP.h[:, vt0 + 1024 * ti + 128 * j:
                               vt0 + 1024 * ti + 128 * (j + 1)]
                vres = ctx.WP.res(vt0 + 1024 * ti + 128 * j, 128)
                for i in range(8):
                    acol = ctx.SROW.h[:, slot + 8 * i + j:slot + 8 * i + j + 1]
                    r_ap = ctx.AF32.h[:, rb + 128 * i:rb + 128 * (i + 1)]
                    r_res = [f"RC{c}i{i}"]
                    if j == 0:
                        ts(ctx, "vector", r_ap, r_res, vsl, vres, acol,
                           ALU.mult, extra_reads=a_res, tag=tag + "_r0")
                    else:
                        stt(ctx, r_ap, r_res, vsl, vres, acol, r_ap, r_res,
                            ALU.mult, ALU.add, extra_reads=a_res,
                            tag=tag + "_rj")
        for c in range(4):
            rb = raccs[c]
            gb = (gt0_, raccs[0], gt0_, raccs[1])[c]
            r_res = [f"RC{c}i{k}" for k in range(8)]
            g_res = (ctx.AF32.res(gt0_, 1024) if gb is gt0_
                     else [f"RC{(0 if c == 1 else 1)}i{k}" for k in range(8)])
            act(ctx, ctx.AF32.h[:, gb:gb + 1024], g_res,
                ctx.AF32.h[:, rb:rb + 1024], r_res, AF.Abs, tag=tag + "_ga")
            ts(ctx, "gpsimd", ctx.AF32.h[:, gb:gb + 1024], g_res,
               ctx.AF32.h[:, gb:gb + 1024], g_res, 1.0, ALU.min, 1e-6,
               ALU.max, tag=tag + "_gc")
            ctx.P.op("scalar",
                     lambda e, o=ctx.AF32.h[:, rb:rb + 1024],
                     i=ctx.AF32.h[:, rb:rb + 1024]:
                     e.sign(o, i),
                     reads=list(r_res), writes=list(r_res), tag=tag + "_gs")
            tt(ctx, "gpsimd", ctx.AF32.h[:, gb:gb + 1024], g_res,
               ctx.AF32.h[:, gb:gb + 1024], g_res,
               ctx.AF32.h[:, rb:rb + 1024], r_res, ALU.mult,
               tag=tag + "_gm")
            if c == 0:
                ts(ctx, "vector", ctx.AF32.h[:, gacc0:gacc0 + 1024],
                   ctx.AF32.res(gacc0, 1024), ctx.AF32.h[:, gb:gb + 1024],
                   g_res,
                   ctx.SROW.h[:, supc0 + c:supc0 + c + 1], ALU.mult,
                   extra_reads=ctx.SROW.res(supc0, 8), tag=tag + "_gw0")
            else:
                stt(ctx, ctx.AF32.h[:, gacc0:gacc0 + 1024],
                    ctx.AF32.res(gacc0, 1024),
                    ctx.AF32.h[:, gb:gb + 1024], g_res,
                    ctx.SROW.h[:, supc0 + c:supc0 + c + 1],
                    ctx.AF32.h[:, gacc0:gacc0 + 1024],
                    ctx.AF32.res(gacc0, 1024), ALU.mult, ALU.add,
                    extra_reads=ctx.SROW.res(supc0, 8), tag=tag + "_gwc")
        for i in range(8):
            tt(ctx, "vector", ctx.AB16.h[:, mes0 + 128 * i:
                                         mes0 + 128 * (i + 1)],
               ctx.AB16.res(mes0, 1024),
               ctx.AF32.h[:, gacc0 + 128 * i:gacc0 + 128 * (i + 1)],
               ctx.AF32.res(gacc0, 1024),
               ctx.SROWB.h[:, sgt0 + 128 * ti:sgt0 + 128 * (ti + 1)],
               ctx.SROWB.res(sgt0 + 128 * ti, 128), ALU.mult,
               tag=tag + "_me")
        for fc in range(NCH):
            bk16 = ctx.pbank16(tag)
            ctx.P.op("tensor",
                     lambda t, o=bk16.h[:, 0:128],
                     i=ctx.AB16.h[:, mes0 + 128 * fc:mes0 + 128 * (fc + 1)],
                     idn=ctx.IDB: t.transpose(o, i, idn),
                     reads=ctx.AB16.res(mes0, 1024) + ["CONST"],
                     writes=bk16.res, tag=tag + "_mtr")
            copy(ctx, "vector", ctx.AB16.h[:, mfm0 + 128 * fc:
                                           mfm0 + 128 * (fc + 1)],
                 ctx.AB16.res(mfm0 + 128 * fc, 128), bk16.h[:, 0:128],
                 bk16.res, tag=tag + "_mtc")
            dma(ctx, "sync", ms[1024 + 128 * fc:1024 + 128 * (fc + 1),
                               128 * ti:128 * (ti + 1)], ["MS:m"],
                ctx.AB16.h[:, mfm0 + 128 * fc:mfm0 + 128 * (fc + 1)],
                ctx.AB16.res(mfm0 + 128 * fc, 128), tag=tag + "_mst")
    ctx.AB16.reset()
    wo_t0 = ctx.AB16.alloc(8192)
    wstg0 = ctx.AB16.alloc(1024)
    wout = inp["L3_attn_W_out"].ap()
    for fc in range(NCH):
        dma(ctx, "gpsimd", ctx.AB16.h[:, wstg0:wstg0 + 1024],
            ctx.AB16.res(wstg0, 1024), wout[128 * fc:128 * (fc + 1), :],
            ["HBM:L3wo"], tag=tag + "_wol")
        for oc in range(NCH):
            bk16 = ctx.pbank16(tag)
            ctx.P.op("tensor",
                     lambda t, o=bk16.h[:, 0:128],
                     i=ctx.AB16.h[:, wstg0 + 128 * oc:wstg0 + 128 * (oc + 1)],
                     idn=ctx.IDB: t.transpose(o, i, idn),
                     reads=ctx.AB16.res(wstg0, 1024) + ["CONST"],
                     writes=bk16.res, tag=tag + "_wot")
            ts(ctx, "vector",
               ctx.AB16.h[:, wo_t0 + 1024 * fc + 128 * oc:
                          wo_t0 + 1024 * fc + 128 * (oc + 1)],
               ctx.AB16.res(wo_t0 + 1024 * fc + 128 * oc, 128),
               bk16.h[:, 0:128], bk16.res,
               ctx.SROW.h[:, c2col0:c2col0 + 1], ALU.mult,
               extra_reads=ctx.SROW.res(c2col0, 8), tag=tag + "_wos")
    bo0 = ctx.SROW.alloc(8)
    ctx.AF32.reset(af_amp0)
    load_vec_cols(ctx, "L3_attn_b_out", bo0, 8, tag=tag + "_bo")
    mb0_ = wstg0
    ctx.AB16.alloc(12288 - ctx.AB16.pos)
    for nt in range(NT):
        for fc in range(NCH):
            dma(ctx, "sync", ctx.AB16.h[:, mb0_ + 512 * fc:
                                        mb0_ + 512 * (fc + 1)],
                ctx.AB16.res(mb0_ + 512 * fc, 512),
                ms[1024 + 128 * fc:1024 + 128 * (fc + 1),
                   PB * nt:PB * (nt + 1)], ["MS:m"], tag=tag + "_mld")
        for oc in range(NCH):
            bank = ctx.banks[4 + (oc % 2)]
            for fc in range(NCH):
                lhsT = ctx.AB16.h[:, wo_t0 + 1024 * oc + 128 * fc:
                                  wo_t0 + 1024 * oc + 128 * (fc + 1)]
                lres = ctx.AB16.res(wo_t0 + 1024 * oc + 128 * fc, 128)
                mm(ctx, bank.h[:, :], bank.res, lhsT, lres,
                   ctx.AB16.h[:, mb0_ + 512 * fc:mb0_ + 512 * (fc + 1)],
                   ctx.AB16.res(mb0_ + 512 * fc, 512),
                   start=(fc == 0), stop=(fc == NCH - 1), tag=tag + "_om")
            x_ap = X.h[:, 2048 * oc + PB * nt:2048 * oc + PB * (nt + 1)]
            stt(ctx, x_ap, [f"X:{oc}"], bank.h[:, :], bank.res,
                ctx.SROW.h[:, bo0 + oc:bo0 + oc + 1], x_ap, [f"X:{oc}"],
                ALU.add, ALU.add, extra_reads=ctx.SROW.res(bo0, 8),
                tag=tag + "_ox")
    ctx.reset_pools()





def l2_ffn_g(ctx, app):
    tag = f"L2g{app}"
    C2 = ctx.C2
    EC = 8 * C2
    WIN = 256
    NWIN = C2 // WIN
    ln_sublayer(ctx, tag + "ln")
    ctx.reset_pools()
    NB, X = ctx.NB, ctx.X
    E = 8
    CAP = 1280
    inp = ctx.inp
    out2 = ctx.MS.ap().rearrange("a (b c) -> (a b) c", c=1024)
    gt0 = ctx.SROWB.alloc(64)
    for ki in range(NCH):
        dst = ctx.SROWB.h[:, gt0 + 8 * ki:gt0 + 8 * (ki + 1)]
        dma(ctx, "gpsimd", dst, ctx.SROWB.res(gt0, 64),
            inp["L2_ffn_gate_w"].ap()[128 * ki:128 * (ki + 1), :],
            ["HBM:L2g"], tag=tag + "_g")
    m8_0 = ctx.SROWB.alloc(2048)
    M8 = ctx.SROWB.h[0:8, m8_0:m8_0 + 2048]
    M8_res = ctx.SROWB.res(m8_0, 2048)
    nm0 = ctx.SROW.alloc(128)
    nm_res = ctx.SROW.res(nm0, 128)
    t80 = ctx.SROW.alloc(16)
    t8_res = ctx.SROW.res(t80, 16)
    sv0 = ctx.AB16.alloc(32)
    ohall0 = ctx.AB16.alloc(128)
    pos0 = ctx.AF32.alloc(2048)
    pk8_0 = ctx.AF32.alloc(2048)
    dst0 = ctx.AF32.alloc(2048)
    OUT0 = pk8_0
    PK8 = ctx.AF32.h[0:8, pk8_0:pk8_0 + 2048]
    PK8_res = ctx.AF32.res(pk8_0, 2048)
    for tt_i in range(NTT):
        bank = ctx.pbank(tag)
        for ki in range(NCH):
            lhsT = NB.h[:, 2048 * ki + 128 * tt_i: 2048 * ki + 128 * (tt_i + 1)]
            mm(ctx, bank.h[:, 0:8], bank.res, lhsT, [f"NB:{ki}"],
               ctx.SROWB.h[:, gt0 + 8 * ki:gt0 + 8 * (ki + 1)],
               ctx.SROWB.res(gt0, 64),
               start=(ki == 0), stop=(ki == NCH - 1), tag=tag + "_rl")
        nm_ap = ctx.SROW.h[:, nm0:nm0 + 1]
        reduce(ctx, "vector", nm_ap, nm_res, bank.h[:, 0:8], bank.res,
               ALU.max, negate=True, tag=tag + "_nmax")
        ex_ap = ctx.SROW.h[:, t80:t80 + 8]
        ex_res = t8_res
        sm_ap = ctx.SROW.h[:, nm0 + 1:nm0 + 2]
        act(ctx, ex_ap, ex_res, bank.h[:, 0:8], bank.res, AF.Exp,
            bias=nm_ap, extra_reads=nm_res, accum_out=sm_ap,
            accum_res=nm_res, tag=tag + "_exp")
        rc_ap = ctx.SROW.h[:, nm0 + 2:nm0 + 3]
        recip(ctx, rc_ap, nm_res, sm_ap, nm_res, tag=tag + "_rc")
        sc_ap = ctx.SROW.h[:, t80 + 8:t80 + 16]
        ts(ctx, "vector", sc_ap, ex_res, ex_ap, ex_res, rc_ap, ALU.mult,
           extra_reads=nm_res, tag=tag + "_sc")
        mx_ap = ctx.SROW.h[:, nm0 + 3:nm0 + 4]
        reduce(ctx, "vector", mx_ap, nm_res, sc_ap, ex_res, ALU.max,
               tag=tag + "_mx")
        oh_ap = ctx.AB16.h[:, ohall0 + 8 * tt_i:ohall0 + 8 * (tt_i + 1)]
        oh_res = ctx.AB16.res(ohall0, 128)
        ts(ctx, "vector", oh_ap, oh_res, sc_ap, ex_res, mx_ap, ALU.is_ge,
           extra_reads=nm_res, tag=tag + "_oh")
        pk_ap = ctx.AB16.h[:, sv0:sv0 + 8]
        pk_res = ctx.AB16.res(sv0, 32)
        tt(ctx, "vector", pk_ap, pk_res, sc_ap, ex_res, oh_ap, oh_res,
           ALU.mult, tag=tag + "_pk")
        bank2 = ctx.pbank16(tag)
        ctx.P.op("tensor",
                 lambda t, o=bank2.h[0:8, 0:128], i=oh_ap,
                 idn=ctx.IDB: t.transpose(o, i, idn),
                 reads=list(oh_res) + ["CONST"], writes=bank2.res,
                 tag=tag + "_otr")
        copy(ctx, "vector", M8[:, 128 * tt_i:128 * (tt_i + 1)], M8_res,
             bank2.h[0:8, 0:128], bank2.res, tag=tag + "_m8")
        bank3 = ctx.pbank16(tag)
        ctx.P.op("tensor",
                 lambda t, o=bank3.h[0:8, 0:128], i=pk_ap,
                 idn=ctx.IDB: t.transpose(o, i, idn),
                 reads=list(pk_res) + ["CONST"], writes=bank3.res,
                 tag=tag + "_ptr")
        copy(ctx, "vector", PK8[:, 128 * tt_i:128 * (tt_i + 1)], PK8_res,
             bank3.h[0:8, 0:128], bank3.res, tag=tag + "_pk8")
    cnt_ap = ctx.SROW.h[0:8, nm0 + 4:nm0 + 5]
    reduce(ctx, "vector", cnt_ap, nm_res, M8, M8_res, ALU.add,
           tag=tag + "_cnt")
    bank = ctx.pbank(tag)
    transp(ctx, bank.h[0:1, 0:8], bank.res, cnt_ap, nm_res,
           ctx.IDF[0:8, 0:8], tag=tag + "_ctr")
    crow_ap = ctx.SROW.h[0:1, nm0 + 8:nm0 + 16]
    copy(ctx, "vector", crow_ap, nm_res, bank.h[0:1, 0:8], bank.res,
         tag=tag + "_crc")
    dma(ctx, "sync", ctx.CCIN.ap()[0:1, 0:8], ["CCIN"], crow_ap, nm_res,
        tag=tag + "_cst")
    ctx.P.op("gpsimd",
             lambda g: g.collective_compute(
                 "AllGather", ALU.bypass,
                 ins=[ctx.CCIN.ap().opt()], outs=[ctx.CCOUT.ap().opt()],
                 replica_groups=[list(range(NCORES))]),
             reads=["CCIN"], writes=["CCOUT"], is_coll=True,
             tag=tag + "_ag")
    cc0 = ctx.SROW.alloc(8)
    cc_ap = ctx.SROW.h[:, cc0:cc0 + 8]
    cc_res = ctx.SROW.res(cc0, 8)
    dma(ctx, "sync", cc_ap, cc_res, ctx.CCOUT.ap()[:, 0:8], ["CCOUT"],
        tag=tag + "_cld")
    bank = ctx.pbank(tag)
    mm(ctx, bank.h[0:1, 0:8], bank.res, ctx.RMASK[:, 0:1], ["RMASK"],
       cc_ap[:, 0:8], cc_res, start=True, stop=True, tag=tag + "_pfx")
    pfrow_ap = ctx.SROW.h[0:1, nm0 + 24:nm0 + 32]
    copy(ctx, "vector", pfrow_ap, nm_res, bank.h[0:1, 0:8], bank.res,
         tag=tag + "_pfr")
    bank2 = ctx.pbank(tag)
    transp(ctx, bank2.h[0:8, 0:1], bank2.res, pfrow_ap, nm_res,
           ctx.IDF[0:1, 0:1], tag=tag + "_pftr")
    pfx_ap = ctx.SROW.h[0:8, nm0 + 16:nm0 + 17]
    copy(ctx, "vector", pfx_ap, nm_res, bank2.h[0:8, 0:1], bank2.res,
         tag=tag + "_pfc")
    POS = ctx.AF32.h[0:8, pos0:pos0 + 2048]
    POS_res = ctx.AF32.res(pos0, 2048)
    ctx.P.op("vector",
             lambda e: e.tensor_tensor_scan(POS, M8, M8, 0.0, ALU.add,
                                            ALU.bypass),
             reads=list(M8_res), writes=list(POS_res), special=True,
             tag=tag + "_scan")
    DST = ctx.AF32.h[0:8, dst0:dst0 + 2048]
    DST_res = ctx.AF32.res(dst0, 2048)
    KW = PK8
    KW_res = PK8_res
    ts(ctx, "vector", DST, DST_res, POS, POS_res, pfx_ap, ALU.add,
       extra_reads=nm_res, tag=tag + "_gp")
    ts(ctx, "vector", DST, DST_res, DST, DST_res, CAP + 0.5, ALU.is_lt,
       tag=tag + "_klt")
    tt(ctx, "vector", KW, KW_res, KW, KW_res, DST, DST_res, ALU.mult,
       tag=tag + "_kp")
    ts(ctx, "vector", KW, KW_res, KW, KW_res, 0.5, ALU.mult,
       tag=tag + "_kp5")
    ecol_ap = ctx.SROW.h[0:8, nm0 + 32:nm0 + 33]
    ctx.P.op("gpsimd", lambda g: g.iota(
        ecol_ap, [[0, 1]], base=0, channel_multiplier=1,
        allow_small_or_imprecise_dtypes=True),
        reads=(), writes=nm_res, tag=tag + "_ec")
    ts(ctx, "vector", ecol_ap, nm_res, ecol_ap, nm_res, float(C2),
       ALU.mult, tag=tag + "_ecm")
    ts(ctx, "vector", DST, DST_res, POS, POS_res, -1.0, ALU.add,
       tag=tag + "_d1")
    OVF = POS
    OVF_res = POS_res
    ts(ctx, "vector", OVF, OVF_res, DST, DST_res, C2 - 0.5, ALU.is_gt,
       tag=tag + "_ov")
    stt(ctx, DST, DST_res, OVF, OVF_res, 1.0e7, DST, DST_res, ALU.mult,
        ALU.add, tag=tag + "_ovs")
    ts(ctx, "vector", DST, DST_res, DST, DST_res, ecol_ap, ALU.add,
       extra_reads=nm_res, tag=tag + "_de")
    stt(ctx, DST, DST_res, M8, M8_res, -1.0e7, DST, DST_res, ALU.mult,
        ALU.add, tag=tag + "_dm1")
    ts(ctx, "vector", DST, DST_res, DST, DST_res, 1.0e7, ALU.add,
       tag=tag + "_dm2")
    dcols0 = ctx.SROW.alloc(16)
    dsr0 = pos0
    DSR = ctx.AF32.h[0:1, dsr0:dsr0 + 2048]
    DSR_res = ctx.AF32.res(dsr0, 2048)
    pkcols0 = ctx.SROW.alloc(16)
    xtok0 = ctx.WP.alloc(16384)
    for tt_i in range(NTT):
        bank = ctx.pbank(tag)
        transp(ctx, bank.h[0:128, 0:8], bank.res,
               DST[:, 128 * tt_i:128 * (tt_i + 1)], DST_res,
               ctx.IDF[0:8, 0:8], tag=tag + "_dtr")
        dcol = ctx.SROW.h[:, dcols0 + tt_i:dcols0 + tt_i + 1]
        dcol_res = ctx.SROW.res(dcols0, 16)
        ctx.P.op("vector",
                 lambda e, o=dcol, i=bank.h[0:128, 0:8]:
                 e.tensor_reduce(o, i, AXX, ALU.min),
                 reads=bank.res, writes=dcol_res, special=True,
                 tag=tag + "_dmin")
        bank2 = ctx.pbank(tag)
        transp(ctx, bank2.h[0:1, 0:128], bank2.res, dcol, dcol_res,
               ctx.IDF, tag=tag + "_drow")
        copy(ctx, "vector", DSR[:, 128 * tt_i:128 * (tt_i + 1)], DSR_res,
             bank2.h[0:1, 0:128], bank2.res, tag=tag + "_drc")
        bank3 = ctx.pbank(tag)
        transp(ctx, bank3.h[0:128, 0:8], bank3.res,
               KW[:, 128 * tt_i:128 * (tt_i + 1)], KW_res,
               ctx.IDF[0:8, 0:8], tag=tag + "_ktr")
        pkcol = ctx.SROW.h[:, pkcols0 + tt_i:pkcols0 + tt_i + 1]
        pk_resc = ctx.SROW.res(pkcols0, 16)
        reduce(ctx, "vector", pkcol, pk_resc, bank3.h[0:128, 0:8],
               bank3.res, ALU.add, tag=tag + "_ks")
        for fc in range(NCH):
            bk16 = ctx.pbank16(tag)
            ctx.P.op("tensor",
                     lambda t, o=bk16.h[:, 0:128],
                     i=NB.h[:, 2048 * fc + 128 * tt_i:
                            2048 * fc + 128 * (tt_i + 1)],
                     idn=ctx.IDB: t.transpose(o, i, idn),
                     reads=[f"NB:{fc}"] + ["CONST"], writes=bk16.res,
                     tag=tag + "_xtr")
            copy(ctx, "vector",
                 ctx.WP.h[:, xtok0 + 1024 * tt_i + 128 * fc:
                          xtok0 + 1024 * tt_i + 128 * (fc + 1)],
                 ctx.WP.res(xtok0 + 1024 * tt_i + 128 * fc, 128),
                 bk16.h[:, 0:128], bk16.res, tag=tag + "_xtc")
    sio0 = ctx.SROW.alloc(256)
    ctx.P.op("gpsimd", lambda g: g.iota(
        ctx.SROW.h[:, sio0:sio0 + 256], [[1, 256]], base=0,
        channel_multiplier=0, allow_small_or_imprecise_dtypes=True),
        reads=(), writes=ctx.SROW.res(sio0, 256), tag=tag + "_sio")
    blk0 = ctx.AB16.alloc(4096)
    gst0 = ctx.AB16.alloc(2048)
    shc0 = ctx.SROW.alloc(16)
    gl = ctx.GL.ap()
    w1f = inp["L2_ffn_w1"].ap().rearrange("e k m -> (e k) m")
    w2f = inp["L2_ffn_w2"].ap().rearrange("e k m -> (e k) m")
    for e in range(E):
        for w in range(NWIN):
            sbase = C2 * e + WIN * w
            for tt_i in range(NTT):
                shc = ctx.SROW.h[:, shc0 + tt_i % 16:shc0 + tt_i % 16 + 1]
                shc_res = ctx.SROW.res(shc0, 16)
                ts(ctx, "vector", shc, shc_res,
                   ctx.SROW.h[:, dcols0 + tt_i:dcols0 + tt_i + 1],
                   ctx.SROW.res(dcols0, 16), float(-sbase), ALU.add,
                   tag=tag + "_sh")
                ts(ctx, "vector",
                   ctx.AB16.h[:, blk0 + 256 * tt_i:blk0 + 256 * (tt_i + 1)],
                   ctx.AB16.res(blk0 + 256 * tt_i, 256),
                   ctx.SROW.h[:, sio0:sio0 + 256],
                   ctx.SROW.res(sio0, 256), shc, ALU.is_equal,
                   ctx.SROW.h[:, pkcols0 + tt_i:pkcols0 + tt_i + 1],
                   ALU.mult,
                   extra_reads=shc_res + ctx.SROW.res(pkcols0, 16),
                   tag=tag + "_blk")
            for pas in range(2):
                for dc4 in range(4):
                    dc = 4 * pas + dc4
                    bank = ctx.banks[dc4]
                    for tt_i in range(NTT):
                        lhsT = ctx.WP.h[:, xtok0 + 1024 * tt_i + 128 * dc:
                                        xtok0 + 1024 * tt_i + 128 * (dc + 1)]
                        lres = ctx.WP.res(xtok0 + 1024 * tt_i + 128 * dc,
                                          128)
                        mm(ctx, bank.h[:, 0:256], bank.res, lhsT, lres,
                           ctx.AB16.h[:, blk0 + 256 * tt_i:
                                      blk0 + 256 * (tt_i + 1)],
                           ctx.AB16.res(blk0 + 256 * tt_i, 256),
                           start=(tt_i == 0), stop=(tt_i == NTT - 1),
                           tag=tag + "_gm")
                for dc4 in range(4):
                    dc = 4 * pas + dc4
                    gsl = gst0 + 256 * (4 * pas + dc4)
                    scopy(ctx, ctx.AB16.h[:, gsl:gsl + 256],
                          ctx.AB16.res(gsl, 256),
                          ctx.banks[dc4].h[:, 0:256], ctx.banks[dc4].res,
                          tag=tag + "_ge")
                    dma(ctx, "sync",
                        gl[1024 * e + 128 * dc:1024 * e + 128 * (dc + 1),
                           256 * w:256 * (w + 1)],
                        [f"GL:{e}"], ctx.AB16.h[:, gsl:gsl + 256],
                        ctx.AB16.res(gsl, 256), tag=tag + "_gs")
    ctx.WP.reset()
    ctx.AB16.reset()
    ctx.SROWB.reset()
    w1q0 = ctx.WP.alloc(2 * 8192)
    gw0 = ctx.AB16.alloc(2 * 4096)
    hs0 = ctx.AB16.alloc(4096)
    oev0 = ctx.SROWB.alloc(2 * 512)
    NBh, NBres = ctx.NB.h, ctx.NB.res
    for e in range(E):
        gsl = gw0 + 4096 * (e % 2)
        dma(ctx, "sync",
            ctx.AB16.h[:, gsl:gsl + 4096].rearrange("p (k s) -> p k s",
                                                    k=NCH),
            ctx.AB16.res(gsl, 4096),
            gl[1024 * e:1024 * (e + 1), :].rearrange("(k p) s -> p k s",
                                                     p=128),
            [f"GL:{e}"], tag=tag + "_gld")
        for qq in range(4):
            w1b = w1q0 + 8192 * (qq % 2)
            w2b = 8192 * (qq % 2)
            for ki in range(NCH):
                dma(ctx, "gpsimd",
                    ctx.WP.h[:, w1b + 1024 * ki:w1b + 1024 * (ki + 1)],
                    ctx.WP.res(w1b + 1024 * ki, 1024),
                    w1f[1024 * e + 128 * ki:1024 * e + 128 * (ki + 1),
                        1024 * qq:1024 * (qq + 1)],
                    ["HBM:L2w1"], tag=tag + "_w1")
            for mi in range(NCH):
                dma(ctx, "gpsimd",
                    NBh[:, w2b + 1024 * mi:w2b + 1024 * (mi + 1)],
                    NBres(w2b + 1024 * mi, 1024),
                    w2f[4096 * e + 1024 * qq + 128 * mi:
                        4096 * e + 1024 * qq + 128 * (mi + 1), :],
                    ["HBM:L2w2"], tag=tag + "_w2")
            for mi in range(NCH):
                b1 = ctx.banks[4 + (mi % 2)]
                for ki in range(NCH):
                    mm(ctx, b1.h[:, :], b1.res,
                       ctx.WP.h[:, w1b + 1024 * ki + 128 * mi:
                                w1b + 1024 * ki + 128 * (mi + 1)],
                       ctx.WP.res(w1b + 1024 * ki, 1024),
                       ctx.AB16.h[:, gsl + 512 * ki:gsl + 512 * (ki + 1)],
                       ctx.AB16.res(gsl, 4096),
                       start=(ki == 0), stop=(ki == NCH - 1),
                       tag=tag + "_h")
                act(ctx, ctx.AB16.h[:, hs0 + 512 * mi:hs0 + 512 * (mi + 1)],
                    ctx.AB16.res(hs0 + 512 * mi, 512), b1.h[:, :], b1.res,
                    AF.Relu, tag=tag + "_relu")
            for doh in range(2):
                for mi in range(NCH):
                    for st in range(4):
                        bank = ctx.banks[st]
                        mm(ctx, bank.h[:, :], bank.res,
                           ctx.AB16.h[:, hs0 + 512 * mi + 128 * st:
                                      hs0 + 512 * mi + 128 * (st + 1)],
                           ctx.AB16.res(hs0 + 512 * mi, 512),
                           NBh[:, w2b + 1024 * mi + 512 * doh:
                               w2b + 1024 * mi + 512 * (doh + 1)],
                           NBres(w2b + 1024 * mi, 1024),
                           start=(mi == 0), stop=(mi == NCH - 1),
                           tag=tag + "_o")
                for st in range(4):
                    oa = OUT0 + 1024 * st + 512 * doh
                    oa_ap = ctx.AF32.h[:, oa:oa + 512]
                    oa_res = ctx.AF32.res(oa, 512)
                    if qq == 0:
                        scopy(ctx, oa_ap, oa_res, ctx.banks[st].h[:, :],
                              ctx.banks[st].res, tag=tag + "_oi")
                    else:
                        tt(ctx, "vector", oa_ap, oa_res, oa_ap, oa_res,
                           ctx.banks[st].h[:, :], ctx.banks[st].res,
                           ALU.add, tag=tag + "_oa")
        for st in range(4):
            for doh in range(2):
                oa = OUT0 + 1024 * st + 512 * doh
                osl = oev0 + 512 * (doh % 2)
                scopy(ctx, ctx.SROWB.h[:, osl:osl + 512],
                      ctx.SROWB.res(osl, 512),
                      ctx.AF32.h[:, oa:oa + 512], ctx.AF32.res(oa, 512),
                      tag=tag + "_oe")
                dma(ctx, "sync",
                    out2[C2 * e + 128 * st:C2 * e + 128 * (st + 1),
                         512 * doh:512 * (doh + 1)],
                    [f"OUT2:{(4 * e + st) % 8}"],
                    ctx.SROWB.h[:, osl:osl + 512], ctx.SROWB.res(osl, 512),
                    tag=tag + "_ost")
    iotc0 = ctx.SROW.alloc(8)
    ctx.P.op("gpsimd", lambda g: g.iota(
        ctx.SROW.h[:, iotc0:iotc0 + 1], [[0, 1]], base=0,
        channel_multiplier=1, allow_small_or_imprecise_dtypes=True),
        reads=(), writes=ctx.SROW.res(iotc0, 8), tag=tag + "_ioc")
    ctx.AB16.reset()
    ctx.WP.reset()
    osl0 = ctx.AB16.alloc(2 * 4096)
    qb0 = ctx.WP.alloc(16384)
    bcd0 = pk8_0
    shs0 = ctx.SROW.alloc(8)
    for nt in range(NT):
        bankb = ctx.pbank(tag)
        mm(ctx, bankb.h[:, :], bankb.res, ctx.ONESF_ROW, ["CONST"],
           DSR[:, PB * nt:PB * (nt + 1)], DSR_res, start=True, stop=True,
           tag=tag + "_qbc")
        scopy(ctx, ctx.AF32.h[:, bcd0:bcd0 + 512],
              ctx.AF32.res(bcd0, 512), bankb.h[:, :], bankb.res,
              tag=tag + "_qbe")
        for sc in range(EC // 128):
            shs = ctx.SROW.h[:, shs0:shs0 + 1]
            shs_res = ctx.SROW.res(shs0, 8)
            ts(ctx, "vector", shs, shs_res,
               ctx.SROW.h[:, iotc0:iotc0 + 1], ctx.SROW.res(iotc0, 8),
               1.0, ALU.mult, float(128 * sc), ALU.add, tag=tag + "_qsh")
            ts(ctx, "vector",
               ctx.WP.h[:, qb0 + 512 * sc:qb0 + 512 * (sc + 1)],
               ctx.WP.res(qb0 + 512 * sc, 512),
               ctx.AF32.h[:, bcd0:bcd0 + 512], ctx.AF32.res(bcd0, 512),
               shs, ALU.is_equal, extra_reads=shs_res, tag=tag + "_qeq")
        for do in range(NCH):
            dma(ctx, "sync",
                ctx.AB16.h[:, osl0 + 4096 * (do % 2):
                           osl0 + 4096 * (do % 2) + 4096].rearrange(
                               "p (s c) -> p s c", s=EC // 128),
                ctx.AB16.res(osl0 + 4096 * (do % 2), 4096),
                out2[:, 128 * do:128 * (do + 1)].rearrange(
                    "(s p) c -> p s c", p=128),
                [f"OUT2:{g8}" for g8 in range(8)], tag=tag + "_osl")
            bank = ctx.banks[4 + (do % 2)]
            for sc in range(EC // 128):
                lhsT = ctx.AB16.h[:, osl0 + 4096 * (do % 2) + 128 * sc:
                                  osl0 + 4096 * (do % 2) + 128 * (sc + 1)]
                lres = ctx.AB16.res(osl0 + 4096 * (do % 2) + 128 * sc, 128)
                mm(ctx, bank.h[:, :], bank.res, lhsT, lres,
                   ctx.WP.h[:, qb0 + 512 * sc:qb0 + 512 * (sc + 1)],
                   ctx.WP.res(qb0 + 512 * sc, 512),
                   start=(sc == 0), stop=(sc == EC // 128 - 1),
                   tag=tag + "_rm")
            x_ap = X.h[:, 2048 * do + PB * nt:2048 * do + PB * (nt + 1)]
            tt(ctx, "vector", x_ap, [f"X:{do}"], x_ap, [f"X:{do}"],
               bank.h[:, :], bank.res, ALU.add, tag=tag + "_xa")
    ctx.reset_pools()




SUBLAYERS = {
    "L0a": l0_attn,
    "L1a": l1_attn,
    "L3a": l3_attn,
    "L2f1": lambda ctx: l2_ffn_g(ctx, 1),
    "L2f2": lambda ctx: l2_ffn_g(ctx, 2),
    "L1f1": lambda ctx: l1_ffn(ctx, 1),
    "L1f2": lambda ctx: l1_ffn(ctx, 2),
    "L0f1": lambda ctx: l0_ffn(ctx, 1),
    "L0f2": lambda ctx: l0_ffn(ctx, 2),
    "L3f1": lambda ctx: l3_ffn(ctx, 1),
    "L3f2": lambda ctx: l3_ffn(ctx, 2),
    "L2a": l2_attn,
}

ALL_SUBS = ["L0f1", "L0a", "L0f2", "L1f1", "L1a", "L1f2",
            "L2f1", "L2a", "L2f2", "L3f1", "L3a", "L3f2"]


def needed_inputs(subs):
    pref = {"L0f": "L0_ffn", "L0a": "L0_attn", "L1f": "L1_ffn",
            "L1a": "L1_attn", "L2f": "L2_ffn", "L2a": "L2_attn",
            "L3f": "L3_ffn", "L3a": "L3_attn"}
    need = set()
    for s in subs:
        p = pref[s[:3]]
        for name in INPUT_SPECS:
            if name.startswith(p):
                need.add(name)
    return sorted(need)


def build_program(subs):
    nc = bass.Bass("TRN2", target_bir_lowering=False, debug=False)
    ctx = Ctx()
    ctx.nc = nc
    ctx.P = Prog()
    ctx.inp = {}
    for name in needed_inputs(subs):
        shp = list(INPUT_SPECS[name])
        if len(shp) == 1:
            shp = [1] + shp
        ctx.inp[name] = nc.dram_tensor(name, shp, F32, kind="ExternalInput")
    xin = nc.dram_tensor("xin", [D, T], F32, kind="ExternalInput")
    rankmask = nc.dram_tensor("rankmask", [128, 1], F32, kind="ExternalInput")
    xout = nc.dram_tensor("xout", [D, T], F32, kind="ExternalOutput")
    ctx.H3 = nc.dram_tensor("h3scratch", [4096, T], BF16,
                            kind="ExternalOutput" if "h3" in DEBUG
                            else "Internal")
    ctx.GL = nc.dram_tensor("glspill", [8192, 512], BF16)
    ctx.C2 = 512
    ctx.MS = nc.dram_tensor("msscratch", [2048, T], BF16,
                            kind="ExternalOutput" if "ms" in DEBUG
                            else "Internal")
    ctx.CCIN = nc.dram_tensor("ccin", [16, 128], F32)
    ctx.CCOUT = nc.dram_tensor("ccout", [128, 128], F32)

    with ExitStack() as st:
        en = st.enter_context
        ctx.X = ColAlloc("X", en(nc.sbuf_tensor("Xr", [128, NCH * T], F32)),
                         NCH * T, T)
        ctx.NB = ColAlloc("NB", en(nc.sbuf_tensor("NBr", [128, NCH * T],
                                                  BF16)), NCH * T, T)
        ctx.WP = ColAlloc("WP", en(nc.sbuf_tensor("WPr", [128, 20480], BF16)),
                          20480, 1024)
        ctx.AB16 = ColAlloc("AB16", en(nc.sbuf_tensor("AB16r", [128, 12288],
                                                      BF16)), 12288, 1024)
        ctx.AF32 = ColAlloc("AF32", en(nc.sbuf_tensor("AF32r", [128, 6144],
                                                      F32)), 6144, 512)
        ctx.SROW = ColAlloc("SROW", en(nc.sbuf_tensor("SROWr", [128, 3072],
                                                      F32)), 3072, 128)
        ctx.SROWB = ColAlloc("SROWB", en(nc.sbuf_tensor("SROWBr", [128, 3072],
                                                        BF16)), 3072, 128)
        cstb = en(nc.sbuf_tensor("CONSTB", [128, 1280], BF16))
        cstf = en(nc.sbuf_tensor("CONSTF", [128, 264], F32))
        rmsk = en(nc.sbuf_tensor("RMASK", [128, 1], F32))
        ctx.IDX32 = en(nc.sbuf_tensor("IDX32", [128, 8], U32))
        ctx.banks = [Bank(en(nc.psum_tensor(f"pb{i}", [128, PB], F32)), i)
                     for i in range(6)]
        ctx.banks += [Bank(en(nc.psum_tensor(f"pb{i}", [128, 2 * PB], BF16)),
                           i) for i in (6, 7)]
        ctx.bank_rr_f32 = 0
        ctx.pbank_allowed = (0, 1, 2, 3, 4, 5)
        ctx.bank_rr_b16 = 6
        esem = {n: en(nc.semaphore(f"es_{n}")) for n in ENGINES}
        dsem = {n: [en(nc.semaphore(f"ds_{n}{k}")) for k in range(4)]
                for n in ENGINES}
        csem = en(nc.semaphore("cs"))
        block = en(nc.Block())

        ctx.ONESB_COL = cstb[:, 0:1]
        ctx.ONESB_ROW = cstb[0:1, 0:128]
        ctx.IDB = cstb[:, 128:256]
        ctx.ONEE = cstb[:, 256:1280]
        ctx.ONESF = cstf[:, 0:1]
        ctx.ONESF_ROW = cstf[0:1, 0:128]
        ctx.IDF = cstf[:, 136:264]
        ctx.RMASK = rmsk
        P = ctx.P
        memset(ctx, "vector", cstb[:, 0:1280], ["CONST"], 1.0, tag="c_onesb")
        memset(ctx, "vector", cstf[:, 0:264], ["CONST"], 1.0, tag="c_onesf")
        P.op("gpsimd", lambda g: g.affine_select(
            cstb[:, 128:256], cstb[:, 128:256], [[1, 128]], ALU.is_equal, 0.0,
            base=0, channel_multiplier=-1),
            reads=["CONST"], writes=["CONST"], tag="c_idb")
        P.op("gpsimd", lambda g: g.affine_select(
            cstf[:, 136:264], cstf[:, 136:264], [[1, 128]], ALU.is_equal,
            0.0, base=0, channel_multiplier=-1),
            reads=["CONST"], writes=["CONST"], tag="c_idf")
        for e8 in range(8):
            P.op("gpsimd", lambda g, e8=e8: g.affine_select(
                cstb[0:8, 256 + 128 * e8:256 + 128 * (e8 + 1)],
                cstb[0:8, 256 + 128 * e8:256 + 128 * (e8 + 1)],
                [[0, 128]], ALU.is_equal, 0.0,
                base=-e8, channel_multiplier=1),
                reads=["CONST"], writes=["CONST"], tag="c_onee")
        dma(ctx, "sync", rmsk[:, :], ["RMASK"], rankmask.ap()[:, :],
            ["HBM:rmask"], tag="c_rmask")

        for c in range(NCH):
            dma(ctx, "sync", ctx.X.h[:, T * c:T * (c + 1)], [f"X:{c}"],
                xin.ap()[128 * c:128 * (c + 1), :], ["HBM:xin"], tag="xload")

        for s in subs:
            SUBLAYERS[s](ctx)

        for c in range(NCH):
            dma(ctx, "sync", xout.ap()[128 * c:128 * (c + 1), :],
                [f"XOUT:{c}"], ctx.X.h[:, T * c:T * (c + 1)], [f"X:{c}"],
                tag="xstore")

        ctx.P.finalize(block, esem, dsem, csem)
    return nc, ctx


def kernel(**inputs):
    subs = [s for s in ALL_SUBS if s in SUBLAYERS]
    return _run(inputs, subs)


def _run(inputs, subs, x_override=None):
    nc, ctx = build_program(subs)
    need = needed_inputs(subs)
    x_full = np.asarray(inputs["L0_attn_x"], np.float32) \
        if x_override is None else np.asarray(x_override, np.float32)
    in_maps = []
    for c in range(NCORES):
        m = {}
        for name in need:
            a = np.asarray(inputs[name], np.float32)
            if a.ndim == 1:
                a = a.reshape(1, -1)
            m[name] = a
        m["xin"] = np.ascontiguousarray(x_full[c].T)
        rm = np.zeros((128, 1), np.float32)
        for r in range(c):
            rm[16 * r, 0] = 1.0
        m["rankmask"] = rm
        in_maps.append(m)
    res = run_bass_kernel_spmd(nc, in_maps, core_ids=list(range(NCORES)))
    out = np.stack([np.ascontiguousarray(np.asarray(r["xout"]).T)
                    for r in res.results])
    if DEBUG:
        return out, res.results
    return out


# version 07
# speedup vs baseline: 1.0830x; 1.0220x over previous
import numpy as np
import concourse.bass as bass
import concourse.mybir as mybir
from concourse.bass_utils import run_bass_kernel_spmd
from contextlib import ExitStack

F32 = mybir.dt.float32
BF16 = mybir.dt.bfloat16
U32 = mybir.dt.uint32
AF = mybir.ActivationFunctionType
ALU = mybir.AluOpType
AXX = mybir.AxisListType.X

T = 2048
D = 1024
NCH = 8
NCORES = 8
PB = 512
NT = T // PB
NTT = T // 128

ENGINES = ("tensor", "vector", "scalar", "gpsimd", "sync")

INPUT_SPECS = {
    "L0_attn_c_attn_w": (3072, 1024), "L0_attn_c_attn_b": (3072,),
    "L0_attn_q2l_w": (32, 64), "L0_attn_q2l_b": (32,),
    "L0_attn_kv2l_w": (32, 64), "L0_attn_kv2l_b": (32,),
    "L0_attn_l2o_w": (64, 32), "L0_attn_l2o_b": (64,),
    "L0_attn_wqk_w": (32, 32), "L0_attn_wqk_b": (32,),
    "L0_attn_cproj_w": (1024, 1024), "L0_attn_cproj_b": (1024,),
    "L0_ffn_gate_w": (8, 1024),
    "L0_ffn_w1": (8, 1024, 1024), "L0_ffn_w2": (8, 1024, 1024),
    "L0_ffn_w3": (8, 1024, 1024),
    "L0_ffn_w1s": (1, 1024, 1024), "L0_ffn_w2s": (1, 1024, 1024),
    "L0_ffn_w3s": (1, 1024, 1024),
    "L1_attn_W_in": (1024, 4096),
    "L1_attn_conv_w": (2048, 1, 4), "L1_attn_conv_b": (2048,),
    "L1_attn_W_x": (2048, 32),
    "L1_attn_W_dt": (16, 2048), "L1_attn_b_dt": (2048,),
    "L1_attn_A_log": (2048, 16), "L1_attn_D": (2048,),
    "L1_attn_W_out": (2048, 1024),
    "L1_ffn_Wg": (1024, 8),
    "L1_ffn_W1": (8, 1024, 2048), "L1_ffn_W2": (8, 1024, 2048),
    "L1_ffn_W3": (8, 2048, 1024),
    "L2_attn_Wq": (16, 1024, 64), "L2_attn_Wk": (16, 1024, 64),
    "L2_attn_Wv": (16, 1024, 64), "L2_attn_Wo": (16, 64, 1024),
    "L2_attn_bq": (16, 64), "L2_attn_bk": (16, 64), "L2_attn_bv": (16, 64),
    "L2_attn_bo": (1024,),
    "L2_ffn_gate_w": (1024, 8),
    "L2_ffn_w1": (8, 1024, 4096), "L2_ffn_w2": (8, 4096, 1024),
    "L3_attn_W_qkv": (3072, 1024), "L3_attn_b_qkv": (3072,),
    "L3_attn_W_out": (1024, 1024), "L3_attn_b_out": (1024,),
    "L3_attn_quantum_weights": (4, 8, 8),
    "L3_attn_photonic_phases": (16, 1024),
    "L3_attn_entanglement_gates": (3, 8, 4),
    "L3_attn_superposition_logits": (4,),
    "L3_ffn_W_router": (1024, 4),
    "L3_ffn_W1": (4, 1024, 4096), "L3_ffn_b1": (4, 4096),
    "L3_ffn_Wo": (4096, 1024), "L3_ffn_bo": (1024,),
    "L3_ffn_ln_gamma": (1024,), "L3_ffn_ln_beta": (1024,),
}


class _Op:
    __slots__ = ("eng", "fn", "reads", "writes", "special", "is_dma",
                 "is_coll", "needs_inc", "inc_no", "dma_no", "deps", "tag",
                 "qidx")

    def __init__(self, eng, fn, reads, writes, special, is_dma, is_coll, tag):
        self.eng, self.fn = eng, fn
        self.reads, self.writes = tuple(reads), tuple(writes)
        self.special, self.is_dma, self.is_coll = special, is_dma, is_coll
        self.needs_inc = False
        self.inc_no = self.dma_no = None
        self.deps = []
        self.tag = tag
        self.qidx = 0


class Prog:
    def __init__(self):
        self.ops = []
        self.byeng = {e: [] for e in ENGINES}
        self.last_writer = {}
        self.readers = {}
        self.coll_count = 0

    def op(self, eng, fn, reads=(), writes=(), special=False, is_dma=False,
           is_coll=False, tag=""):
        o = _Op(eng, fn, reads, writes, special, is_dma, is_coll, tag)
        deps = []
        for r in o.reads:
            w = self.last_writer.get(r)
            if w is not None:
                deps.append(w)
        for r in o.writes:
            w = self.last_writer.get(r)
            if w is not None:
                deps.append(w)
            for rd in self.readers.get(r, ()):
                deps.append(rd)
        o.deps = list({id(d): d for d in deps if d is not o}.values())
        for r in o.writes:
            self.last_writer[r] = o
            self.readers[r] = []
        for r in o.reads:
            self.readers.setdefault(r, []).append(o)
        o.qidx = len(self.byeng[eng])
        self.ops.append(o)
        self.byeng[eng].append(o)
        if is_coll:
            self.coll_count += 1
        return o

    SAFE_DIST = 16

    @staticmethod
    def _fifo_ok(p, o):
        if p.eng != o.eng or p.is_dma or p.is_coll or o.is_coll:
            return False
        if p.special or o.special:
            return False
        if p.eng == "tensor":
            return True
        return o.qidx - p.qidx >= Prog.SAFE_DIST

    def finalize(self, block, esem, dsem, csem, drain_engine="sync"):
        for o in self.ops:
            for p in o.deps:
                if p.is_dma or p.is_coll:
                    continue
                if self._fifo_ok(p, o):
                    continue
                p.needs_inc = True
        KD = 4
        self.KD = KD
        for eng in ENGINES:
            inc = dn = 0
            for o in self.byeng[eng]:
                if o.is_coll:
                    pass
                elif o.is_dma:
                    o.dma_no = dn
                    dn += 1
                if o.needs_inc:
                    inc += 1
                    o.inc_no = inc
        ccount = 0
        for o in self.ops:
            if o.is_coll:
                o.dma_no = ccount
                ccount += 1
        waited = {e: {} for e in ENGINES}
        all_waits = {}
        for o in self.ops:
            w = {}
            for p in o.deps:
                if p.is_coll:
                    key, val = ("c", "x", 0), p.dma_no + 1
                elif p.is_dma:
                    key = ("d", p.eng, p.dma_no % KD)
                    val = 16 * (p.dma_no // KD + 1)
                elif self._fifo_ok(p, o):
                    continue
                else:
                    key, val = ("e", p.eng), p.inc_no
                if val is None:
                    raise RuntimeError(f"dep without inc: {p.tag} <- {o.tag}")
                if waited[o.eng].get(key, 0) >= val:
                    continue
                if w.get(key, 0) < val:
                    w[key] = val
            for k, v in w.items():
                if waited[o.eng].get(k, 0) < v:
                    waited[o.eng][k] = v
            all_waits[id(o)] = w

        def emit_for(eng_name):
            def _go(eng):
                for o in self.byeng[eng_name]:
                    if o.is_coll:
                        if o.dma_no > 0:
                            eng.wait_ge(csem, o.dma_no)
                    elif o.is_dma:
                        if o.dma_no >= KD:
                            eng.wait_ge(dsem[eng_name][o.dma_no % KD],
                                        16 * (o.dma_no // KD))
                    for key, val in sorted(all_waits[id(o)].items()):
                        kind = key[0]
                        if kind == "d":
                            sem = dsem[key[1]][key[2]]
                        elif kind == "e":
                            sem = esem[key[1]]
                        else:
                            sem = csem
                        eng.wait_ge(sem, val)
                    ins = o.fn(eng)
                    if o.is_coll:
                        ins.then_inc(csem, 1)
                    elif o.is_dma:
                        ins.then_inc(dsem[eng_name][o.dma_no % KD], 16)
                    if o.needs_inc:
                        ins.then_inc(esem[eng_name], 1)
                if eng_name == drain_engine:
                    for e2 in ENGINES:
                        ndma = sum(1 for o in self.byeng[e2]
                                   if o.is_dma and not o.is_coll)
                        for k in range(min(KD, ndma)):
                            tot = (ndma - 1 - k) // KD + 1
                            eng.wait_ge(dsem[e2][(ndma - 1 - k) % KD],
                                        16 * tot)
                        ninc = sum(1 for o in self.byeng[e2] if o.needs_inc)
                        if ninc and e2 != eng_name:
                            eng.wait_ge(esem[e2], ninc)
                    if self.coll_count:
                        eng.wait_ge(csem, self.coll_count)
            return _go

        for name, starter in (("tensor", block.tensor),
                              ("vector", block.vector),
                              ("scalar", block.scalar),
                              ("gpsimd", block.gpsimd),
                              ("sync", block.sync)):
            starter(emit_for(name))

    def stats(self):
        return {e: dict(n=len(self.byeng[e]),
                        dma=sum(1 for o in self.byeng[e] if o.is_dma),
                        inc=sum(1 for o in self.byeng[e] if o.needs_inc))
                for e in ENGINES}


class ColAlloc:
    def __init__(self, name, handle, ncols, page):
        self.name, self.h, self.ncols, self.page = name, handle, ncols, page
        self.pos = 0

    def alloc(self, n):
        if self.pos + n > self.ncols:
            raise RuntimeError(
                f"pool {self.name} overflow: {self.pos}+{n}>{self.ncols}")
        c0 = self.pos
        self.pos += n
        return c0

    def reset(self, pos=0):
        self.pos = pos

    def res(self, c0, n):
        return [f"{self.name}:{p}"
                for p in range(c0 // self.page, (c0 + n - 1) // self.page + 1)]


class Bank:
    def __init__(self, h, i):
        self.h = h
        self.res = [f"PS:{i}"]


class Ctx:
    pbank_allowed = (0, 1, 2, 3, 4, 5)

    def pbank(self, tag=""):
        al = self.pbank_allowed
        i = self.bank_rr_f32 % len(al)
        self.bank_rr_f32 += 1
        return self.banks[al[i]]

    def pbank16(self, tag=""):
        i = self.bank_rr_b16
        self.bank_rr_b16 = 6 + (i + 1 - 6) % 2
        return self.banks[i]

    def reset_pools(self):
        self.WP.reset()
        self.AB16.reset()
        self.AF32.reset()
        self.SROW.reset()
        self.SROWB.reset()


def mm(ctx, psum_ap, psum_res, lhsT_ap, lhsT_res, rhs_ap, rhs_res,
       start, stop, tag="mm"):
    ctx.P.op("tensor",
             lambda t, a=psum_ap, b=lhsT_ap, c=rhs_ap, s1=start, s2=stop:
                 t.matmul(a, b, c, start=s1, stop=s2),
             reads=list(lhsT_res) + list(rhs_res), writes=list(psum_res),
             tag=tag)


def copy(ctx, eng, out_ap, out_res, in_ap, in_res, tag="copy"):
    ctx.P.op(eng, lambda e, o=out_ap, i=in_ap: e.tensor_copy(o, i),
             reads=list(in_res), writes=list(out_res), tag=tag)


def scopy(ctx, out_ap, out_res, in_ap, in_res, tag="scopy"):
    ctx.P.op("scalar", lambda e, o=out_ap, i=in_ap: e.copy(o, i),
             reads=list(in_res), writes=list(out_res), tag=tag)


def act(ctx, out_ap, out_res, in_ap, in_res, func, bias=0.0, scale=1.0,
        extra_reads=(), accum_out=None, accum_res=(), tag="act"):
    def f(e, o=out_ap, i=in_ap, fu=func, b=bias, s=scale, ao=accum_out):
        if ao is not None:
            return e.activation(o, i, fu, bias=b, scale=s, accum_out=ao)
        return e.activation(o, i, fu, bias=b, scale=s)
    ctx.P.op("scalar", f, reads=list(in_res) + list(extra_reads),
             writes=list(out_res) + list(accum_res), tag=tag)


def tt(ctx, eng, out_ap, out_res, a_ap, a_res, b_ap, b_res, op, tag="tt"):
    ctx.P.op(eng, lambda e, o=out_ap, x=a_ap, y=b_ap, al=op:
             e.tensor_tensor(o, x, y, al),
             reads=list(a_res) + list(b_res), writes=list(out_res), tag=tag)


def ts(ctx, eng, out_ap, out_res, in_ap, in_res, scalar1, op0,
       scalar2=None, op1=None, extra_reads=(), tag="ts"):
    def f(e, o=out_ap, i=in_ap, s1=scalar1, s2=scalar2, a0=op0, a1=op1):
        if a1 is None:
            return e.tensor_scalar(o, i, s1, None, a0)
        return e.tensor_scalar(o, i, s1, s2, a0, a1)
    ctx.P.op(eng, f, reads=list(in_res) + list(extra_reads),
             writes=list(out_res), tag=tag)


def stt(ctx, out_ap, out_res, in0_ap, in0_res, scalar, in1_ap, in1_res,
        op0, op1, extra_reads=(), tag="stt", eng="vector"):
    ctx.P.op(eng,
             lambda e, o=out_ap, i0=in0_ap, s=scalar, i1=in1_ap, a0=op0,
             a1=op1: e.scalar_tensor_tensor(o, i0, s, i1, a0, a1),
             reads=list(in0_res) + list(in1_res) + list(extra_reads),
             writes=list(out_res), tag=tag)


def memset(ctx, eng, ap, res, val, tag="memset"):
    ctx.P.op(eng, lambda e, a=ap, v=val: e.memset(a, v),
             reads=(), writes=list(res), tag=tag)


def reduce(ctx, eng, out_ap, out_res, in_ap, in_res, op, negate=False,
           tag="red"):
    ctx.P.op(eng, lambda e, o=out_ap, i=in_ap, a=op, n=negate:
             e.tensor_reduce(o, i, AXX, a, negate=n),
             reads=list(in_res), writes=list(out_res), special=True, tag=tag)


def recip(ctx, out_ap, out_res, in_ap, in_res, tag="recip"):
    ctx.P.op("vector", lambda e, o=out_ap, i=in_ap: e.reciprocal(o, i),
             reads=list(in_res), writes=list(out_res), tag=tag)


def dma(ctx, queue, out_ap, out_res, in_ap, in_res, tag="dma"):
    ctx.P.op(queue, lambda e, o=out_ap, i=in_ap: e.dma_start(out=o, in_=i),
             reads=list(in_res), writes=list(out_res), is_dma=True, tag=tag)


def transp(ctx, psum_ap, psum_res, in_ap, in_res, ident_ap, tag="tr"):
    ctx.P.op("tensor",
             lambda t, o=psum_ap, i=in_ap, idn=ident_ap: t.transpose(o, i, idn),
             reads=list(in_res) + ["CONST"], writes=list(psum_res), tag=tag)


DEBUG = set()


def dbg(ctx, name, ap, res, shape, dtype):
    if name not in DEBUG:
        return
    t = ctx.nc.dram_tensor("dbg_" + name, list(shape), dtype,
                           kind="ExternalOutput")
    dma(ctx, "sync", t.ap()[:, :], [f"DBG:{name}"], ap, res, tag="dbg")


def load_vec_cols(ctx, name, dstcol0, n_chunks, row0_elems=0, tag="vec"):
    hb = ctx.inp[name].ap().rearrange("a b -> (a b)")
    nelem = n_chunks * 128
    src = hb[row0_elems:row0_elems + nelem].rearrange("(c p) -> c p", p=128)
    st0 = ctx.AF32.alloc(128)
    stage = ctx.AF32.h[0:n_chunks, st0:st0 + 128]
    dma(ctx, "sync", stage, ctx.AF32.res(st0, 128), src, [f"HBM:{name}"],
        tag=tag + "_ld")
    bank = ctx.pbank(tag)
    transp(ctx, bank.h[:, 0:n_chunks], bank.res, stage,
           ctx.AF32.res(st0, 128), ctx.IDF[0:n_chunks, 0:n_chunks],
           tag=tag + "_tr")
    copy(ctx, "vector", ctx.SROW.h[:, dstcol0:dstcol0 + n_chunks],
         ctx.SROW.res(dstcol0, n_chunks), bank.h[:, 0:n_chunks], bank.res,
         tag=tag + "_cp")
    ctx.AF32.reset(st0)


def ln_sublayer(ctx, tag):
    X, NB = ctx.X, ctx.NB
    eps = 1e-5
    af_base, sr_base, srb_base = ctx.AF32.pos, ctx.SROW.pos, ctx.SROWB.pos
    sq0 = ctx.AF32.alloc(4096)
    m0 = ctx.SROW.alloc(512)
    v0 = ctx.SROW.alloc(512)
    q0 = ctx.SROW.alloc(512)
    mb0 = ctx.SROWB.alloc(512)
    rb0 = ctx.SROWB.alloc(512)
    t0 = ctx.AF32.alloc(512)
    m_ap = ctx.SROW.h[0:1, m0:m0 + 512]
    m_res = ctx.SROW.res(m0, 512)
    v_ap = ctx.SROW.h[0:1, v0:v0 + 512]
    v_res = ctx.SROW.res(v0, 512)
    q_ap = ctx.SROW.h[0:1, q0:q0 + 512]
    q_res = ctx.SROW.res(q0, 512)
    mb_ap = ctx.SROWB.h[0:1, mb0:mb0 + 512]
    mb_res = ctx.SROWB.res(mb0, 512)
    rb_ap = ctx.SROWB.h[0:1, rb0:rb0 + 512]
    rb_res = ctx.SROWB.res(rb0, 512)
    for half in range(2):
        nts = [2 * half, 2 * half + 1]
        for c in range(NCH):
            sqc = sq0 + 2048 * (c % 2)
            sq_ap = ctx.AF32.h[:, sqc:sqc + 2048]
            sq_res = ctx.AF32.res(sqc, 2048)
            act(ctx, sq_ap, sq_res, X.h[:, 2048 * c:2048 * (c + 1)],
                [f"X:{c}"], AF.Square, tag=f"{tag}_sq{c}")
            for j, nt in enumerate(nts):
                mm(ctx, ctx.banks[2 * j].h[0:1, :], ctx.banks[2 * j].res,
                   ctx.ONESF, ["CONST"],
                   X.h[:, 2048 * c + PB * nt: 2048 * c + PB * (nt + 1)],
                   [f"X:{c}"], start=(c == 0), stop=(c == NCH - 1),
                   tag=f"{tag}_sum")
                mm(ctx, ctx.banks[2 * j + 1].h[0:1, :],
                   ctx.banks[2 * j + 1].res, ctx.ONESF, ["CONST"],
                   ctx.AF32.h[:, sqc + PB * nt: sqc + PB * (nt + 1)],
                   sq_res, start=(c == 0), stop=(c == NCH - 1),
                   tag=f"{tag}_sumsq")
        for j, nt in enumerate(nts):
            sb = ctx.banks[2 * j]
            qb = ctx.banks[2 * j + 1]
            ts(ctx, "vector", m_ap, m_res, sb.h[0:1, :], sb.res,
               1.0 / D, ALU.mult, tag=f"{tag}_mean")
            act(ctx, q_ap, q_res, m_ap, m_res, AF.Square, tag=f"{tag}_msq")
            stt(ctx, v_ap, v_res, qb.h[0:1, :], qb.res, 1.0 / D,
                q_ap, q_res, ALU.mult, ALU.subtract, tag=f"{tag}_var")
            ts(ctx, "vector", v_ap, v_res, v_ap, v_res, eps, ALU.add,
               tag=f"{tag}_eps")
            recip(ctx, v_ap, v_res, v_ap, v_res, tag=f"{tag}_rc")
            act(ctx, v_ap, v_res, v_ap, v_res, AF.Sqrt, tag=f"{tag}_rstd")
            copy(ctx, "vector", mb_ap, mb_res, m_ap, m_res, tag=f"{tag}_mb")
            copy(ctx, "vector", rb_ap, rb_res, v_ap, v_res, tag=f"{tag}_rb")
            bm = ctx.banks[4]
            br = ctx.banks[5]
            mm(ctx, bm.h[:, :], bm.res, ctx.ONESB_ROW, ["CONST"], mb_ap,
               mb_res, start=True, stop=True, tag=f"{tag}_bcm")
            mm(ctx, br.h[:, :], br.res, ctx.ONESB_ROW, ["CONST"], rb_ap,
               rb_res, start=True, stop=True, tag=f"{tag}_bcr")
            t_ap = ctx.AF32.h[:, t0:t0 + 512]
            t_res = ctx.AF32.res(t0, 512)
            for c in range(NCH):
                xs = X.h[:, 2048 * c + PB * nt: 2048 * c + PB * (nt + 1)]
                nb_ap = NB.h[:, 2048 * c + PB * nt: 2048 * c + PB * (nt + 1)]
                tt(ctx, "vector", t_ap, t_res, xs, [f"X:{c}"], bm.h[:, :],
                   bm.res, ALU.subtract, tag=f"{tag}_sub")
                tt(ctx, "vector", nb_ap, [f"NB:{c}"], t_ap, t_res, br.h[:, :],
                   br.res, ALU.mult, tag=f"{tag}_mul")
    ctx.AF32.reset(af_base)
    ctx.SROW.reset(sr_base)
    ctx.SROWB.reset(srb_base)


def l3_ffn(ctx, app):
    tag = f"L3f{app}"
    ln_sublayer(ctx, tag + "ln")
    ctx.reset_pools()
    NB, X = ctx.NB, ctx.X
    dbg(ctx, "nb", NB.h[:, :], [f"NB:{c}" for c in range(NCH)],
        [128, NCH * T], BF16)
    E, I = 4, 4096
    NMI = I // 128
    inp = ctx.inp

    wr0 = ctx.WP.alloc(NCH * 8)
    for ki in range(NCH):
        dst = ctx.WP.h[:, wr0 + 8 * ki: wr0 + 8 * ki + 4]
        dma(ctx, "gpsimd", dst, ctx.WP.res(wr0 + 8 * ki, 8),
            inp["L3_ffn_W_router"].ap()[128 * ki:128 * (ki + 1), :],
            ["HBM:L3r"], tag=tag + "_wr")
    m4_0 = ctx.SROWB.alloc(2048)
    M4 = ctx.SROWB.h[0:4, m4_0:m4_0 + 2048]
    M4_res = ctx.SROWB.res(m4_0, 2048)
    mx0 = ctx.SROW.alloc(128)
    oh0 = ctx.AB16.alloc(8)
    for tt_i in range(NTT):
        bank = ctx.pbank(tag)
        for ki in range(NCH):
            lhsT = NB.h[:, 2048 * ki + 128 * tt_i: 2048 * ki + 128 * (tt_i + 1)]
            mm(ctx, bank.h[:, 0:4], bank.res, lhsT, [f"NB:{ki}"],
               ctx.WP.h[:, wr0 + 8 * ki: wr0 + 8 * ki + 4],
               ctx.WP.res(wr0 + 8 * ki, 8),
               start=(ki == 0), stop=(ki == NCH - 1), tag=tag + "_rl")
        mx_ap = ctx.SROW.h[:, mx0 + (tt_i % 2):mx0 + (tt_i % 2) + 1]
        mx_res = ctx.SROW.res(mx0, 128)
        reduce(ctx, "vector", mx_ap, mx_res, bank.h[:, 0:4], bank.res,
               ALU.max, tag=tag + "_rmax")
        ohc = oh0 + 4 * (tt_i % 2)
        oh_ap = ctx.AB16.h[:, ohc:ohc + 4]
        oh_res = ctx.AB16.res(ohc, 4)
        ts(ctx, "vector", oh_ap, oh_res, bank.h[:, 0:4], bank.res,
           mx_ap, ALU.is_ge, extra_reads=mx_res, tag=tag + "_oh")
        bank2 = ctx.pbank16(tag)
        ctx.P.op("tensor",
                 lambda t, o=bank2.h[0:4, 0:128], i=oh_ap, idn=ctx.IDB:
                     t.transpose(o, i, idn),
                 reads=list(oh_res) + ["CONST"], writes=bank2.res,
                 tag=tag + "_tr")
        copy(ctx, "vector", M4[:, 128 * tt_i:128 * (tt_i + 1)], M4_res,
             bank2.h[0:4, 0:128], bank2.res, tag=tag + "_m4")

    bc0 = ctx.AB16.alloc(E * NT * 512)
    BC = {}
    for e in range(E):
        for nt in range(NT):
            c0 = bc0 + (e * NT + nt) * 512
            bank = ctx.pbank(tag)
            mm(ctx, bank.h[:, :], bank.res,
               ctx.ONEE[0:4, 128 * e:128 * (e + 1)], ["CONST"],
               M4[:, PB * nt:PB * (nt + 1)], M4_res,
               start=True, stop=True, tag=tag + "_bc")
            ap_ = ctx.AB16.h[:, c0:c0 + 512]
            res_ = ctx.AB16.res(c0, 512)
            scopy(ctx, ap_, res_, bank.h[:, :], bank.res, tag=tag + "_bcc")
            BC[(e, nt)] = (ap_, res_)

    dbg(ctx, "m4", M4, M4_res, [4, 2048], BF16)
    b1c = {}
    for e in range(E):
        c0 = ctx.SROW.alloc(32)
        load_vec_cols(ctx, "L3_ffn_b1", c0, 32, row0_elems=4096 * e,
                      tag=tag + f"_b1{e}")
        b1c[e] = c0

    w1slot = ctx.WP.alloc(2 * E * NCH * 128)
    hacc0 = ctx.AF32.alloc(4096)
    he0 = ctx.AB16.alloc(4 * 512)
    accb0 = ctx.AB16.alloc(2 * 512)
    w1flat = inp["L3_ffn_W1"].ap().rearrange("e k m -> (e k) m")
    for mi in range(NMI):
        sl = w1slot + (mi % 2) * E * NCH * 128
        for e in range(E):
            dst = ctx.WP.h[:, sl + e * NCH * 128:
                           sl + (e + 1) * NCH * 128].rearrange(
                               "p (k m) -> p k m", k=NCH)
            src = w1flat[1024 * e: 1024 * (e + 1),
                         128 * mi:128 * (mi + 1)].rearrange(
                             "(k p) m -> p k m", p=128)
            dma(ctx, "gpsimd", dst, ctx.WP.res(sl + e * NCH * 128, NCH * 128),
                src, ["HBM:L3W1"], tag=tag + "_w1")
        for nt in range(NT):
            base = hacc0 + 2048 * (nt % 2)
            a_ap = ctx.AF32.h[:, base:base + 512]
            a_res = ctx.AF32.res(base, 512)
            t_ap = ctx.AF32.h[:, base + 512:base + 1024]
            t_res = ctx.AF32.res(base + 512, 512)
            b2_ap = ctx.AF32.h[:, base + 1024:base + 1536]
            b2_res = ctx.AF32.res(base + 1024, 512)
            t2_ap = ctx.AF32.h[:, base + 1536:base + 2048]
            t2_res = ctx.AF32.res(base + 1536, 512)
            for e in range(E):
                bank = ctx.pbank(tag)
                for ki in range(NCH):
                    lhsT = ctx.WP.h[:, sl + e * NCH * 128 + 128 * ki:
                                    sl + e * NCH * 128 + 128 * (ki + 1)]
                    lres = ctx.WP.res(sl + e * NCH * 128 + 128 * ki, 128)
                    rhs = NB.h[:, 2048 * ki + PB * nt:
                               2048 * ki + PB * (nt + 1)]
                    mm(ctx, bank.h[:, :], bank.res, lhsT, lres, rhs,
                       [f"NB:{ki}"], start=(ki == 0), stop=(ki == NCH - 1),
                       tag=tag + "_h")
                hec = he0 + 512 * e
                he_ap = ctx.AB16.h[:, hec:hec + 512]
                he_res = ctx.AB16.res(hec, 512)
                act(ctx, he_ap, he_res, bank.h[:, :], bank.res, AF.Gelu,
                    bias=ctx.SROW.h[:, b1c[e] + mi:b1c[e] + mi + 1],
                    extra_reads=ctx.SROW.res(b1c[e] + mi, 1),
                    tag=tag + "_gelu")
            he = [ctx.AB16.h[:, he0 + 512 * e:he0 + 512 * (e + 1)]
                  for e in range(E)]
            hr = [ctx.AB16.res(he0 + 512 * e, 512) for e in range(E)]
            tt(ctx, "vector", a_ap, a_res, he[0], hr[0], BC[(0, nt)][0],
               BC[(0, nt)][1], ALU.mult, tag=tag + "_acc0")
            tt(ctx, "gpsimd", b2_ap, b2_res, he[2], hr[2], BC[(2, nt)][0],
               BC[(2, nt)][1], ALU.mult, tag=tag + "_acc2")
            tt(ctx, "vector", t_ap, t_res, he[1], hr[1], BC[(1, nt)][0],
               BC[(1, nt)][1], ALU.mult, tag=tag + "_acc1")
            tt(ctx, "gpsimd", t2_ap, t2_res, he[3], hr[3], BC[(3, nt)][0],
               BC[(3, nt)][1], ALU.mult, tag=tag + "_acc3")
            tt(ctx, "vector", a_ap, a_res, a_ap, a_res, t_ap, t_res,
               ALU.add, tag=tag + "_acca")
            tt(ctx, "gpsimd", b2_ap, b2_res, b2_ap, b2_res, t2_ap, t2_res,
               ALU.add, tag=tag + "_accb2")
            tt(ctx, "vector", a_ap, a_res, a_ap, a_res, b2_ap, b2_res,
               ALU.add, tag=tag + "_accj")
            accc = accb0 + 512 * (nt % 2)
            acc_ap = ctx.AB16.h[:, accc:accc + 512]
            acc_res = ctx.AB16.res(accc, 512)
            scopy(ctx, acc_ap, acc_res, a_ap, a_res, tag=tag + "_accb")
            dma(ctx, "sync", ctx.H3.ap()[128 * mi:128 * (mi + 1),
                                         PB * nt:PB * (nt + 1)],
                [f"H3:{nt}"], acc_ap, acc_res, tag=tag + "_hst")

    ctx.WP.reset()
    ctx.AB16.reset()
    ctx.AF32.reset()
    boc = ctx.SROW.alloc(8)
    load_vec_cols(ctx, "L3_ffn_bo", boc, 8, tag=tag + "_bo")
    gc = ctx.SROW.alloc(8)
    load_vec_cols(ctx, "L3_ffn_ln_gamma", gc, 8, tag=tag + "_g")
    bcc = ctx.SROW.alloc(8)
    load_vec_cols(ctx, "L3_ffn_ln_beta", bcc, 8, tag=tag + "_b")
    ts(ctx, "vector", ctx.SROW.h[:, gc:gc + 8], ctx.SROW.res(gc, 8),
       ctx.SROW.h[:, gc:gc + 8], ctx.SROW.res(gc, 8), 0.5, ALU.mult,
       tag=tag + "_g5")
    ts(ctx, "vector", ctx.SROW.h[:, bcc:bcc + 8], ctx.SROW.res(bcc, 8),
       ctx.SROW.h[:, bcc:bcc + 8], ctx.SROW.res(bcc, 8), 0.5, ALU.mult,
       tag=tag + "_b5")
    sy0 = ctx.AF32.alloc(2048)
    sq0 = ctx.AF32.alloc(2048)
    sy_res = ctx.AF32.res(sy0, 2048)
    sq_res = ctx.AF32.res(sq0, 2048)
    memset(ctx, "vector", ctx.AF32.h[0:1, sy0:sy0 + 2048], sy_res, 0.0,
           tag=tag + "_z1")
    memset(ctx, "vector", ctx.AF32.h[0:1, sq0:sq0 + 2048], sq_res, 0.0,
           tag=tag + "_z2")
    woflat = inp["L3_ffn_Wo"].ap()
    wo0 = ctx.WP.alloc(32 * 512)
    hslot0 = ctx.AB16.alloc(8 * 512)
    sqt0 = ctx.AF32.alloc(2 * 512)
    for och in range(2):
        for ki in range(32):
            dst = ctx.WP.h[:, wo0 + 512 * ki: wo0 + 512 * (ki + 1)]
            dma(ctx, "gpsimd", dst, ctx.WP.res(wo0 + 512 * ki, 512),
                woflat[128 * ki:128 * (ki + 1), 512 * och:512 * (och + 1)],
                ["HBM:L3Wo"], tag=tag + "_wo")
        for nt in range(NT):
            obanks = [ctx.banks[i] for i in range(4)]
            for ki in range(32):
                hc = hslot0 + 512 * (ki % 8)
                h_ap = ctx.AB16.h[:, hc:hc + 512]
                h_res = ctx.AB16.res(hc, 512)
                dma(ctx, "sync", h_ap, h_res,
                    ctx.H3.ap()[128 * ki:128 * (ki + 1),
                                PB * nt:PB * (nt + 1)],
                    [f"H3:{nt}"], tag=tag + "_hld")
                for oc4 in range(4):
                    lhsT = ctx.WP.h[:, wo0 + 512 * ki + 128 * oc4:
                                    wo0 + 512 * ki + 128 * (oc4 + 1)]
                    lres = ctx.WP.res(wo0 + 512 * ki + 128 * oc4, 128)
                    mm(ctx, obanks[oc4].h[:, :], obanks[oc4].res, lhsT, lres,
                       h_ap, h_res, start=(ki == 0), stop=(ki == 31),
                       tag=tag + "_o")
            for oc4 in range(4):
                oc = och * 4 + oc4
                nb_ap = NB.h[:, 2048 * oc + PB * nt: 2048 * oc + PB * (nt + 1)]
                stt(ctx, nb_ap, [f"NB:{oc}"], obanks[oc4].h[:, :],
                    obanks[oc4].res,
                    ctx.SROW.h[:, boc + oc:boc + oc + 1], nb_ap, [f"NB:{oc}"],
                    ALU.add, ALU.add,
                    extra_reads=ctx.SROW.res(boc + oc, 1), tag=tag + "_y")
    for nt in range(NT):
        sb = ctx.banks[nt % 2]
        qb = ctx.banks[2 + nt % 2]
        for oc in range(NCH):
            nb_ap = NB.h[:, 2048 * oc + PB * nt: 2048 * oc + PB * (nt + 1)]
            mm(ctx, sb.h[0:1, :], sb.res, ctx.ONESB_COL, ["CONST"], nb_ap,
               [f"NB:{oc}"], start=(oc == 0), stop=(oc == NCH - 1),
               tag=tag + "_ys")
            sq_ap = ctx.AF32.h[:, sqt0 + 512 * (oc % 2):
                               sqt0 + 512 * (oc % 2) + 512]
            sqt_res = ctx.AF32.res(sqt0 + 512 * (oc % 2), 512)
            act(ctx, sq_ap, sqt_res, nb_ap, [f"NB:{oc}"], AF.Square,
                tag=tag + "_ysq")
            mm(ctx, qb.h[0:1, :], qb.res, ctx.ONESF, ["CONST"], sq_ap,
               sqt_res, start=(oc == 0), stop=(oc == NCH - 1),
               tag=tag + "_yq")
        copy(ctx, "vector",
             ctx.AF32.h[0:1, sy0 + PB * nt:sy0 + PB * (nt + 1)], sy_res,
             sb.h[0:1, :], sb.res, tag=tag + "_ysa")
        copy(ctx, "vector",
             ctx.AF32.h[0:1, sq0 + PB * nt:sq0 + PB * (nt + 1)], sq_res,
             qb.h[0:1, :], qb.res, tag=tag + "_yqa")
    dbg(ctx, "ystat", ctx.AF32.h[0:1, sy0:sy0 + 4096],
        sy_res + sq_res, [1, 4096], F32)
    dbg(ctx, "ynb", NB.h[:, :], [f"NB:{c}" for c in range(NCH)],
        [128, NCH * T], BF16)
    m0 = ctx.SROW.alloc(512)
    v0 = ctx.SROW.alloc(512)
    q0 = ctx.SROW.alloc(512)
    mb0 = ctx.SROWB.alloc(512)
    rb0 = ctx.SROWB.alloc(512)
    t0f = ctx.AF32.alloc(512)
    m_ap = ctx.SROW.h[0:1, m0:m0 + 512]
    m_res = ctx.SROW.res(m0, 512)
    v_ap = ctx.SROW.h[0:1, v0:v0 + 512]
    v_res = ctx.SROW.res(v0, 512)
    q_ap = ctx.SROW.h[0:1, q0:q0 + 512]
    q_res = ctx.SROW.res(q0, 512)
    mb_ap = ctx.SROWB.h[0:1, mb0:mb0 + 512]
    mb_res = ctx.SROWB.res(mb0, 512)
    rb_ap = ctx.SROWB.h[0:1, rb0:rb0 + 512]
    rb_res = ctx.SROWB.res(rb0, 512)
    for nt in range(NT):
        ts(ctx, "vector", m_ap, m_res,
           ctx.AF32.h[0:1, sy0 + PB * nt:sy0 + PB * (nt + 1)], sy_res,
           1.0 / D, ALU.mult, tag=tag + "_mu")
        act(ctx, q_ap, q_res, m_ap, m_res, AF.Square, tag=tag + "_mu2")
        stt(ctx, v_ap, v_res,
            ctx.AF32.h[0:1, sq0 + PB * nt:sq0 + PB * (nt + 1)], sq_res,
            1.0 / D, q_ap, q_res, ALU.mult, ALU.subtract, tag=tag + "_v")
        ts(ctx, "vector", v_ap, v_res, v_ap, v_res, 1e-12, ALU.add,
           tag=tag + "_ve")
        recip(ctx, v_ap, v_res, v_ap, v_res, tag=tag + "_vr")
        act(ctx, v_ap, v_res, v_ap, v_res, AF.Sqrt, tag=tag + "_vs")
        copy(ctx, "vector", mb_ap, mb_res, m_ap, m_res, tag=tag + "_cmb")
        copy(ctx, "vector", rb_ap, rb_res, v_ap, v_res, tag=tag + "_crb")
        bm = ctx.banks[4]
        br = ctx.banks[5]
        mm(ctx, bm.h[:, :], bm.res, ctx.ONESB_ROW, ["CONST"], mb_ap, mb_res,
           start=True, stop=True, tag=tag + "_bm")
        mm(ctx, br.h[:, :], br.res, ctx.ONESB_ROW, ["CONST"], rb_ap, rb_res,
           start=True, stop=True, tag=tag + "_br")
        for oc in range(NCH):
            nb_ap = NB.h[:, 2048 * oc + PB * nt: 2048 * oc + PB * (nt + 1)]
            x_ap = X.h[:, 2048 * oc + PB * nt: 2048 * oc + PB * (nt + 1)]
            t_ap = ctx.AF32.h[:, t0f:t0f + 512]
            t_res = ctx.AF32.res(t0f, 512)
            tt(ctx, "vector", t_ap, t_res, nb_ap, [f"NB:{oc}"], bm.h[:, :],
               bm.res, ALU.subtract, tag=tag + "_fs")
            tt(ctx, "vector", t_ap, t_res, t_ap, t_res, br.h[:, :], br.res,
               ALU.mult, tag=tag + "_fm")
            ts(ctx, "vector", t_ap, t_res, t_ap, t_res,
               ctx.SROW.h[:, gc + oc:gc + oc + 1], ALU.mult,
               ctx.SROW.h[:, bcc + oc:bcc + oc + 1], ALU.add,
               extra_reads=ctx.SROW.res(gc, 8) + ctx.SROW.res(bcc, 8),
               tag=tag + "_fg")
            tt(ctx, "vector", x_ap, [f"X:{oc}"], x_ap, [f"X:{oc}"], t_ap,
               t_res, ALU.add, tag=tag + "_fx")
    ctx.reset_pools()


def lvc64(ctx, name, dstcol, row0_elems=0, pbase=0, tag="v64"):
    hb = ctx.inp[name].ap().rearrange("a b -> (a b)")
    src = hb[row0_elems:row0_elems + 64].rearrange("(c p) -> c p", p=64)
    st0 = ctx.AF32.alloc(64)
    stage = ctx.AF32.h[0:1, st0:st0 + 64]
    dma(ctx, "sync", stage, ctx.AF32.res(st0, 64), src, [f"HBM:{name}"],
        tag=tag + "_ld")
    bank = ctx.pbank(tag)
    transp(ctx, bank.h[pbase:pbase + 64, 0:1], bank.res, stage,
           ctx.AF32.res(st0, 64), ctx.IDF[0:1, 0:1], tag=tag + "_tr")
    copy(ctx, "vector", ctx.SROW.h[pbase:pbase + 64, dstcol:dstcol + 1],
         ctx.SROW.res(dstcol, 1), bank.h[pbase:pbase + 64, 0:1], bank.res,
         tag=tag + "_cp")
    ctx.AF32.reset(st0)


def l2_attn(ctx):
    tag = "L2a"
    ln_sublayer(ctx, tag + "ln")
    ctx.reset_pools()
    NB, X = ctx.NB, ctx.X
    H, A = 16, 64
    inp = ctx.inp
    wq = inp["L2_attn_Wq"].ap().rearrange("h k a -> (h k) a")
    wk = inp["L2_attn_Wk"].ap().rearrange("h k a -> (h k) a")
    wv = inp["L2_attn_Wv"].ap().rearrange("h k a -> (h k) a")
    qk0 = ctx.AB16.alloc(2048)
    kk0 = ctx.AB16.alloc(2048)
    vv0 = ctx.AB16.alloc(2048)
    vt0 = ctx.AB16.alloc(1024)
    pt0 = ctx.AB16.alloc(2048)
    yh0 = ctx.WP.alloc(16384)
    wslot = ctx.WP.alloc(2 * 3 * 512)
    y00 = ctx.AF32.alloc(512)
    rr0 = ctx.SROW.alloc(128)
    rb0 = ctx.SROWB.alloc(128)
    bqc = ctx.SROW.alloc(16)
    bkc = ctx.SROW.alloc(16)
    bvc = ctx.SROW.alloc(16)
    for h in range(H):
        lvc64(ctx, "L2_attn_bq", bqc + h, row0_elems=64 * h, pbase=0,
              tag=tag + "_bq")
        lvc64(ctx, "L2_attn_bk", bkc + h, row0_elems=64 * h, pbase=0,
              tag=tag + "_bk")
        lvc64(ctx, "L2_attn_bv", bvc + h, row0_elems=64 * h, pbase=0,
              tag=tag + "_bv")
    QK = ctx.AB16.h
    for h in range(H):
        sl = wslot + (h % 2) * 3 * 512
        for j, w in enumerate((wq, wk, wv)):
            dst = ctx.WP.h[:, sl + 512 * j: sl + 512 * (j + 1)].rearrange(
                "p (k a) -> p k a", k=NCH)
            src = w[1024 * h:1024 * (h + 1), :].rearrange(
                "(k p) a -> p k a", p=128)
            dma(ctx, "gpsimd", dst, ctx.WP.res(sl + 512 * j, 512), src,
                ["HBM:L2w"], tag=tag + "_w")
        for nt in range(NT):
            for j, (b0, pb, bc) in enumerate(
                    ((qk0, 0, bqc), (kk0, 0, bkc), (vv0, 0, bvc))):
                bank = ctx.banks[j]
                for ki in range(NCH):
                    lhsT = ctx.WP.h[:, sl + 512 * j + 64 * ki:
                                    sl + 512 * j + 64 * (ki + 1)]
                    lres = ctx.WP.res(sl + 512 * j, 512)
                    rhs = NB.h[:, 2048 * ki + PB * nt:
                               2048 * ki + PB * (nt + 1)]
                    mm(ctx, bank.h[pb:pb + 64, :], bank.res, lhsT, lres, rhs,
                       [f"NB:{ki}"], start=(ki == 0), stop=(ki == NCH - 1),
                       tag=tag + "_qkv")
                dst = ctx.AB16.h[pb:pb + 64, b0 + PB * nt:b0 + PB * (nt + 1)]
                dres = ctx.AB16.res(b0 + PB * nt, PB)
                act(ctx, dst, dres, bank.h[pb:pb + 64, :], bank.res,
                    AF.Identity,
                    bias=ctx.SROW.h[pb:pb + 64, bc + h:bc + h + 1],
                    extra_reads=ctx.SROW.res(bc + h, 1), tag=tag + "_qkve")
        for kt in range(NTT):
            bk16 = ctx.pbank16(tag)
            ctx.P.op("tensor",
                     lambda t, o=bk16.h[:, 0:64],
                     i=ctx.AB16.h[0:64, vv0 + 128 * kt:vv0 + 128 * (kt + 1)],
                     idn=ctx.IDB[0:64, 0:64]: t.transpose(o, i, idn),
                     reads=ctx.AB16.res(vv0 + 128 * kt, 128) + ["CONST"],
                     writes=bk16.res, tag=tag + "_vt")
            copy(ctx, "vector", ctx.AB16.h[:, vt0 + 64 * kt:vt0 + 64 * (kt + 1)],
                 ctx.AB16.res(vt0 + 64 * kt, 64), bk16.h[:, 0:64], bk16.res,
                 tag=tag + "_vtc")
        for qnt in range(NT):
            nkt = 4 * qnt + 4
            ob = ctx.banks[4]
            db = ctx.banks[5]
            for kt in range(nkt + 3):
                if kt < nkt:
                    sb = ctx.banks[kt % 4]
                    mm(ctx, sb.h[:, :], sb.res,
                       ctx.AB16.h[0:64, kk0 + 128 * kt:kk0 + 128 * (kt + 1)],
                       ctx.AB16.res(kk0 + 128 * kt, 128),
                       ctx.AB16.h[0:64, qk0 + PB * qnt:qk0 + PB * (qnt + 1)],
                       ctx.AB16.res(qk0 + PB * qnt, PB),
                       start=True, stop=True, tag=tag + "_s")
                    pc = pt0 + 512 * (kt % 4)
                    p_ap = ctx.AB16.h[:, pc:pc + 512]
                    p_res = ctx.AB16.res(pc, 512)
                    act(ctx, p_ap, p_res, sb.h[:, :], sb.res, AF.Exp,
                        scale=0.125, tag=tag + "_exp")
                    if kt >= 4 * qnt:
                        base = PB * qnt - 128 * kt
                        ctx.P.op("gpsimd",
                                 lambda g, a=p_ap, b=base: g.affine_select(
                                     a, a, [[1, 512]], ALU.is_ge, 0.0,
                                     base=b, channel_multiplier=-1),
                                 reads=list(p_res), writes=list(p_res),
                                 tag=tag + "_mask")
                if kt >= 3:
                    k2 = kt - 3
                    pc2 = pt0 + 512 * (k2 % 4)
                    p2_ap = ctx.AB16.h[:, pc2:pc2 + 512]
                    p2_res = ctx.AB16.res(pc2, 512)
                    mm(ctx, ob.h[0:64, :], ob.res,
                       ctx.AB16.h[:, vt0 + 64 * k2:vt0 + 64 * (k2 + 1)],
                       ctx.AB16.res(vt0 + 64 * k2, 64), p2_ap, p2_res,
                       start=(k2 == 0), stop=(k2 == nkt - 1),
                       tag=tag + "_av")
                    mm(ctx, db.h[0:1, :], db.res, ctx.ONESB_COL, ["CONST"],
                       p2_ap, p2_res, start=(k2 == 0), stop=(k2 == nkt - 1),
                       tag=tag + "_den")
            r_ap = ctx.SROW.h[0:1, rr0:rr0 + 512]
            r_res = ctx.SROW.res(rr0, 512)
            recip(ctx, r_ap, r_res, db.h[0:1, :], db.res, tag=tag + "_rc")
            rb_ap = ctx.SROWB.h[0:1, rb0:rb0 + 512]
            rb_res = ctx.SROWB.res(rb0, 512)
            copy(ctx, "vector", rb_ap, rb_res, r_ap, r_res, tag=tag + "_rcb")
            y0_ap = ctx.AF32.h[0:64, y00:y00 + 512]
            y0_res = ctx.AF32.res(y00, 512)
            scopy(ctx, y0_ap, y0_res, ob.h[0:64, :], ob.res, tag=tag + "_ye")
            bb = ctx.banks[(nkt + 2) % 4]
            mm(ctx, bb.h[0:64, :], bb.res, ctx.ONESB_ROW[:, 0:64], ["CONST"],
               rb_ap, rb_res, start=True, stop=True, tag=tag + "_rbc")
            ydst = ctx.WP.h[64 * (h % 2):64 * (h % 2) + 64,
                            yh0 + 2048 * (h // 2) + PB * qnt:
                            yh0 + 2048 * (h // 2) + PB * (qnt + 1)]
            ydres = ctx.WP.res(yh0 + 2048 * (h // 2) + PB * qnt, PB)
            tt(ctx, "vector", ydst, ydres, y0_ap, y0_res, bb.h[0:64, :],
               bb.res, ALU.mult, tag=tag + "_yn")
    ctx.AB16.reset()
    wo0 = ctx.AB16.alloc(8192)
    woflat = inp["L2_attn_Wo"].ap().rearrange("h a d -> (h a) d")
    for ki in range(NCH):
        dst = ctx.AB16.h[:, wo0 + 1024 * ki: wo0 + 1024 * (ki + 1)]
        dma(ctx, "gpsimd", dst, ctx.AB16.res(wo0 + 1024 * ki, 1024),
            woflat[128 * ki:128 * (ki + 1), :], ["HBM:L2wo"], tag=tag + "_wo")
    boc = ctx.SROW.alloc(8)
    load_vec_cols(ctx, "L2_attn_bo", boc, 8, tag=tag + "_bo")
    for oc in range(NCH):
        for nt in range(NT):
            bank = ctx.pbank(tag)
            for ki in range(NCH):
                lhsT = ctx.AB16.h[:, wo0 + 1024 * ki + 128 * oc:
                                  wo0 + 1024 * ki + 128 * (oc + 1)]
                lres = ctx.AB16.res(wo0 + 1024 * ki + 128 * oc, 128)
                rhs = ctx.WP.h[:, yh0 + 2048 * ki + PB * nt:
                               yh0 + 2048 * ki + PB * (nt + 1)]
                rres = ctx.WP.res(yh0 + 2048 * ki + PB * nt, PB)
                mm(ctx, bank.h[:, :], bank.res, lhsT, lres, rhs, rres,
                   start=(ki == 0), stop=(ki == NCH - 1), tag=tag + "_out")
            x_ap = X.h[:, 2048 * oc + PB * nt: 2048 * oc + PB * (nt + 1)]
            stt(ctx, x_ap, [f"X:{oc}"], bank.h[:, :], bank.res,
                ctx.SROW.h[:, boc + oc:boc + oc + 1], x_ap, [f"X:{oc}"],
                ALU.add, ALU.add, extra_reads=ctx.SROW.res(boc + oc, 1),
                tag=tag + "_xa")
    ctx.reset_pools()




def l0_ffn(ctx, app):
    tag = f"L0f{app}"
    ln_sublayer(ctx, tag + "ln")
    ctx.reset_pools()
    NB, X = ctx.NB, ctx.X
    E = 8
    inp = ctx.inp
    gst = ctx.AF32.alloc(1024)
    stage = ctx.AF32.h[0:8, gst:gst + 1024]
    dma(ctx, "sync", stage, ctx.AF32.res(gst, 1024),
        inp["L0_ffn_gate_w"].ap()[:, :], ["HBM:L0g"], tag=tag + "_g")
    gt0 = ctx.SROWB.alloc(64)
    for ki in range(NCH):
        bank = ctx.pbank(tag)
        ctx.P.op("tensor",
                 lambda t, o=bank.h[:, 0:8],
                 i=ctx.AF32.h[0:8, gst + 128 * ki:gst + 128 * (ki + 1)],
                 idn=ctx.IDF[0:8, 0:8]: t.transpose(o, i, idn),
                 reads=ctx.AF32.res(gst, 1024) + ["CONST"], writes=bank.res,
                 tag=tag + "_gtr")
        copy(ctx, "vector", ctx.SROWB.h[:, gt0 + 8 * ki:gt0 + 8 * (ki + 1)],
             ctx.SROWB.res(gt0, 64), bank.h[:, 0:8], bank.res,
             tag=tag + "_gtc")
    s8_0 = ctx.SROWB.alloc(2048)
    S8 = ctx.SROWB.h[0:8, s8_0:s8_0 + 2048]
    S8_res = ctx.SROWB.res(s8_0, 2048)
    nm0 = ctx.SROW.alloc(128)
    t80 = ctx.AF32.alloc(512)
    sv0 = ctx.AB16.alloc(16)
    for tt_i in range(NTT):
        bank = ctx.pbank(tag)
        for ki in range(NCH):
            lhsT = NB.h[:, 2048 * ki + 128 * tt_i: 2048 * ki + 128 * (tt_i + 1)]
            mm(ctx, bank.h[:, 0:8], bank.res, lhsT, [f"NB:{ki}"],
               ctx.SROWB.h[:, gt0 + 8 * ki:gt0 + 8 * (ki + 1)],
               ctx.SROWB.res(gt0, 64),
               start=(ki == 0), stop=(ki == NCH - 1), tag=tag + "_rl")
        nm_ap = ctx.SROW.h[:, nm0:nm0 + 1]
        nm_res = ctx.SROW.res(nm0, 128)
        reduce(ctx, "vector", nm_ap, nm_res, bank.h[:, 0:8], bank.res,
               ALU.max, negate=True, tag=tag + "_nmax")
        ex_ap = ctx.AF32.h[:, t80:t80 + 8]
        ex_res = ctx.AF32.res(t80, 512)
        sm_ap = ctx.SROW.h[:, nm0 + 1:nm0 + 2]
        act(ctx, ex_ap, ex_res, bank.h[:, 0:8], bank.res, AF.Exp,
            bias=nm_ap, extra_reads=nm_res, accum_out=sm_ap,
            accum_res=nm_res, tag=tag + "_exp")
        rc_ap = ctx.SROW.h[:, nm0 + 2:nm0 + 3]
        recip(ctx, rc_ap, nm_res, sm_ap, nm_res, tag=tag + "_rc")
        sc_ap = ctx.AF32.h[:, t80 + 8:t80 + 16]
        ts(ctx, "vector", sc_ap, ex_res, ex_ap, ex_res, rc_ap, ALU.mult,
           extra_reads=nm_res, tag=tag + "_sc")
        t8_ap = ctx.AF32.h[:, t80 + 16:t80 + 24]
        ctx.P.op("vector", lambda e, o=t8_ap, i=sc_ap: e.max(o, i),
                 reads=list(ex_res), writes=list(ex_res), special=True,
                 tag=tag + "_top8")
        msk_ap = ctx.AF32.h[:, t80 + 24:t80 + 32]
        ts(ctx, "vector", msk_ap, ex_res, sc_ap, ex_res,
           t8_ap[:, 1:2], ALU.is_ge, extra_reads=ex_res, tag=tag + "_msk")
        svc = sv0 + 8 * (tt_i % 2)
        sv_ap = ctx.AB16.h[:, svc:svc + 8]
        sv_res = ctx.AB16.res(svc, 8)
        tt(ctx, "vector", sv_ap, sv_res, sc_ap, ex_res, msk_ap, ex_res,
           ALU.mult, tag=tag + "_sv")
        bank2 = ctx.pbank16(tag)
        ctx.P.op("tensor",
                 lambda t, o=bank2.h[0:8, 0:128], i=sv_ap,
                 idn=ctx.IDB: t.transpose(o, i, idn),
                 reads=list(sv_res) + ["CONST"], writes=bank2.res,
                 tag=tag + "_str")
        copy(ctx, "vector", S8[:, 128 * tt_i:128 * (tt_i + 1)], S8_res,
             bank2.h[0:8, 0:128], bank2.res, tag=tag + "_s8")
    w10 = ctx.WP.alloc(4096)
    w30 = ctx.WP.alloc(4096)
    w20 = ctx.WP.alloc(4096)
    bce0 = ctx.AB16.alloc(2048)
    hb0 = ctx.AB16.alloc(2048)
    tmp0 = ctx.AF32.alloc(1024)
    for e in range(E + 1):
        shared = (e == E)
        if shared:
            w1f = inp["L0_ffn_w1s"].ap().rearrange("e k m -> (e k) m")
            w3f = inp["L0_ffn_w3s"].ap().rearrange("e k m -> (e k) m")
            w2f = inp["L0_ffn_w2s"].ap().rearrange("e k m -> (e k) m")
            erow = 0
        else:
            w1f = inp["L0_ffn_w1"].ap().rearrange("e k m -> (e k) m")
            w3f = inp["L0_ffn_w3"].ap().rearrange("e k m -> (e k) m")
            w2f = inp["L0_ffn_w2"].ap().rearrange("e k m -> (e k) m")
            erow = 1024 * e
            for nt in range(NT):
                bank = ctx.pbank(tag)
                mm(ctx, bank.h[:, :], bank.res,
                   ctx.ONEE[0:8, 128 * e:128 * (e + 1)], ["CONST"],
                   S8[:, PB * nt:PB * (nt + 1)], S8_res,
                   start=True, stop=True, tag=tag + "_bce")
                scopy(ctx,
                      ctx.AB16.h[:, bce0 + 512 * nt:bce0 + 512 * (nt + 1)],
                      ctx.AB16.res(bce0 + 512 * nt, 512), bank.h[:, :],
                      bank.res, tag=tag + "_bcec")
        for hh in range(2):
            for wdst, wsrc in ((w10, w1f), (w30, w3f)):
                dst = ctx.WP.h[:, wdst:wdst + 4096].rearrange(
                    "p (k m) -> p k m", k=NCH)
                src = wsrc[erow:erow + 1024,
                           512 * hh:512 * (hh + 1)].rearrange(
                               "(k p) m -> p k m", p=128)
                dma(ctx, "gpsimd", dst, ctx.WP.res(wdst, 4096), src,
                    ["HBM:L0w"], tag=tag + "_w13")
            dst = ctx.WP.h[:, w20:w20 + 4096].rearrange(
                "p (k m) -> p k m", k=4)
            src = w2f[erow + 512 * hh:erow + 512 * (hh + 1),
                      :].rearrange("(k p) m -> p k m", p=128)
            dma(ctx, "gpsimd", dst, ctx.WP.res(w20, 4096), src,
                ["HBM:L0w"], tag=tag + "_w2")
            for nt in range(NT):
                bce_ap = ctx.AB16.h[:, bce0 + 512 * nt:bce0 + 512 * (nt + 1)]
                bce_res = ctx.AB16.res(bce0 + 512 * nt, 512)
                for mi in range(4):
                    b1 = ctx.banks[0]
                    b3 = ctx.banks[1]
                    for ki in range(NCH):
                        rhs = NB.h[:, 2048 * ki + PB * nt:
                                   2048 * ki + PB * (nt + 1)]
                        l1 = ctx.WP.h[:, w10 + 512 * ki + 128 * mi:
                                      w10 + 512 * ki + 128 * (mi + 1)]
                        mm(ctx, b1.h[:, :], b1.res, l1,
                           ctx.WP.res(w10 + 512 * ki, 512), rhs,
                           [f"NB:{ki}"], start=(ki == 0),
                           stop=(ki == NCH - 1), tag=tag + "_r1")
                        l3 = ctx.WP.h[:, w30 + 512 * ki + 128 * mi:
                                      w30 + 512 * ki + 128 * (mi + 1)]
                        mm(ctx, b3.h[:, :], b3.res, l3,
                           ctx.WP.res(w30 + 512 * ki, 512), rhs,
                           [f"NB:{ki}"], start=(ki == 0),
                           stop=(ki == NCH - 1), tag=tag + "_r3")
                    u_ap = ctx.AF32.h[:, tmp0:tmp0 + 512]
                    u_res = ctx.AF32.res(tmp0, 512)
                    ub_ap = ctx.AF32.h[:, tmp0 + 512:tmp0 + 1024]
                    ub_res = ctx.AF32.res(tmp0 + 512, 512)
                    h_ap = ctx.AB16.h[:, hb0 + 512 * mi:hb0 + 512 * (mi + 1)]
                    h_res = ctx.AB16.res(hb0 + 512 * mi, 512)
                    if shared:
                        act(ctx, ub_ap, ub_res, b1.h[:, :], b1.res, AF.Silu,
                            tag=tag + "_silu")
                        tt(ctx, "vector", h_ap, h_res, ub_ap, ub_res,
                           b3.h[:, :], b3.res, ALU.mult, tag=tag + "_h")
                    else:
                        tt(ctx, "vector", u_ap, u_res, b1.h[:, :], b1.res,
                           bce_ap, bce_res, ALU.mult, tag=tag + "_us")
                        act(ctx, ub_ap, ub_res, u_ap, u_res, AF.Silu,
                            tag=tag + "_silu")
                        tt(ctx, "vector", u_ap, u_res, b3.h[:, :], b3.res,
                           bce_ap, bce_res, ALU.mult, tag=tag + "_vs")
                        tt(ctx, "vector", h_ap, h_res, ub_ap, ub_res,
                           u_ap, u_res, ALU.mult, tag=tag + "_h")
                for oc in range(NCH):
                    bank = ctx.banks[2 + (oc % 4)]
                    for mi in range(4):
                        l2 = ctx.WP.h[:, w20 + 1024 * mi + 128 * oc:
                                      w20 + 1024 * mi + 128 * (oc + 1)]
                        mm(ctx, bank.h[:, :], bank.res, l2,
                           ctx.WP.res(w20 + 1024 * mi, 1024),
                           ctx.AB16.h[:, hb0 + 512 * mi:hb0 + 512 * (mi + 1)],
                           ctx.AB16.res(hb0 + 512 * mi, 512),
                           start=(mi == 0), stop=(mi == 3), tag=tag + "_o")
                    x_ap = X.h[:, 2048 * oc + PB * nt:
                               2048 * oc + PB * (nt + 1)]
                    stt(ctx, x_ap, [f"X:{oc}"], bank.h[:, :], bank.res, 0.5,
                        x_ap, [f"X:{oc}"], ALU.mult, ALU.add,
                        tag=tag + "_xa")
    ctx.reset_pools()




def l1_ffn(ctx, app):
    tag = f"L1f{app}"
    ln_sublayer(ctx, tag + "ln")
    ctx.reset_pools()
    NB, X = ctx.NB, ctx.X
    E = 8
    HD = 2048
    inp = ctx.inp
    gt0 = ctx.SROWB.alloc(64)
    for ki in range(NCH):
        dst = ctx.SROWB.h[:, gt0 + 8 * ki:gt0 + 8 * (ki + 1)]
        dma(ctx, "gpsimd", dst, ctx.SROWB.res(gt0, 64),
            inp["L1_ffn_Wg"].ap()[128 * ki:128 * (ki + 1), :],
            ["HBM:L1g"], tag=tag + "_g")
    w8_0 = ctx.SROWB.alloc(2048)
    W8 = ctx.SROWB.h[0:8, w8_0:w8_0 + 2048]
    W8_res = ctx.SROWB.res(w8_0, 2048)
    nm0 = ctx.SROW.alloc(128)
    t80 = ctx.AF32.alloc(512)
    sv0 = ctx.AB16.alloc(16)
    for tt_i in range(NTT):
        bank = ctx.pbank(tag)
        for ki in range(NCH):
            lhsT = NB.h[:, 2048 * ki + 128 * tt_i: 2048 * ki + 128 * (tt_i + 1)]
            mm(ctx, bank.h[:, 0:8], bank.res, lhsT, [f"NB:{ki}"],
               ctx.SROWB.h[:, gt0 + 8 * ki:gt0 + 8 * (ki + 1)],
               ctx.SROWB.res(gt0, 64),
               start=(ki == 0), stop=(ki == NCH - 1), tag=tag + "_rl")
        lg_ap = ctx.AF32.h[:, t80:t80 + 8]
        lg_res = ctx.AF32.res(t80, 512)
        copy(ctx, "vector", lg_ap, lg_res, bank.h[:, 0:8], bank.res,
             tag=tag + "_lg")
        t8_ap = ctx.AF32.h[:, t80 + 8:t80 + 16]
        ctx.P.op("vector", lambda e, o=t8_ap, i=lg_ap: e.max(o, i),
                 reads=list(lg_res), writes=list(lg_res), special=True,
                 tag=tag + "_top8")
        d_ap = ctx.SROW.h[:, nm0:nm0 + 1]
        d_res = ctx.SROW.res(nm0, 128)
        tt(ctx, "vector", d_ap, d_res, t8_ap[:, 0:1], lg_res,
           t8_ap[:, 1:2], lg_res, ALU.subtract, tag=tag + "_d")
        p1_ap = ctx.SROW.h[:, nm0 + 1:nm0 + 2]
        act(ctx, p1_ap, d_res, d_ap, d_res, AF.Sigmoid, scale=1.0,
            tag=tag + "_p1")
        ts(ctx, "vector", p1_ap, d_res, p1_ap, d_res, 0.5, ALU.mult,
           tag=tag + "_p1h")
        p2_ap = ctx.SROW.h[:, nm0 + 2:nm0 + 3]
        ts(ctx, "vector", p2_ap, d_res, p1_ap, d_res, -1.0, ALU.mult, 0.5,
           ALU.add, tag=tag + "_p2")
        m1_ap = ctx.AF32.h[:, t80 + 16:t80 + 24]
        ts(ctx, "vector", m1_ap, lg_res, lg_ap, lg_res, t8_ap[:, 0:1],
           ALU.is_ge, tag=tag + "_m1")
        m2_ap = ctx.AF32.h[:, t80 + 24:t80 + 32]
        ts(ctx, "vector", m2_ap, lg_res, lg_ap, lg_res, t8_ap[:, 1:2],
           ALU.is_ge, tag=tag + "_m2g")
        tt(ctx, "vector", m2_ap, lg_res, m2_ap, lg_res, m1_ap, lg_res,
           ALU.subtract, tag=tag + "_m2")
        wv_ap = ctx.AF32.h[:, t80 + 32:t80 + 40]
        ts(ctx, "vector", wv_ap, lg_res, m1_ap, lg_res, p1_ap, ALU.mult,
           extra_reads=d_res, tag=tag + "_w1m")
        stt(ctx, wv_ap, lg_res, m2_ap, lg_res, p2_ap, wv_ap, lg_res,
            ALU.mult, ALU.add, extra_reads=d_res, tag=tag + "_w2m")
        svc = sv0 + 8 * (tt_i % 2)
        sv_ap = ctx.AB16.h[:, svc:svc + 8]
        sv_res = ctx.AB16.res(svc, 8)
        copy(ctx, "vector", sv_ap, sv_res, wv_ap, lg_res, tag=tag + "_svb")
        bank2 = ctx.pbank16(tag)
        ctx.P.op("tensor",
                 lambda t, o=bank2.h[0:8, 0:128], i=sv_ap,
                 idn=ctx.IDB: t.transpose(o, i, idn),
                 reads=list(sv_res) + ["CONST"], writes=bank2.res,
                 tag=tag + "_str")
        copy(ctx, "vector", W8[:, 128 * tt_i:128 * (tt_i + 1)], W8_res,
             bank2.h[0:8, 0:128], bank2.res, tag=tag + "_w8")
    w10 = ctx.WP.alloc(4096)
    w20 = ctx.WP.alloc(4096)
    w30 = ctx.WP.alloc(4096)
    bce0 = ctx.AB16.alloc(2048)
    hb0 = ctx.AB16.alloc(2048)
    tmp0 = ctx.AF32.alloc(1024)
    w1f = inp["L1_ffn_W1"].ap().rearrange("e k m -> (e k) m")
    w2f = inp["L1_ffn_W2"].ap().rearrange("e k m -> (e k) m")
    w3f = inp["L1_ffn_W3"].ap().rearrange("e k m -> (e k) m")
    for e in range(E):
        for nt in range(NT):
            bank = ctx.pbank(tag)
            mm(ctx, bank.h[:, :], bank.res,
               ctx.ONEE[0:8, 128 * e:128 * (e + 1)], ["CONST"],
               W8[:, PB * nt:PB * (nt + 1)], W8_res,
               start=True, stop=True, tag=tag + "_bce")
            scopy(ctx, ctx.AB16.h[:, bce0 + 512 * nt:bce0 + 512 * (nt + 1)],
                  ctx.AB16.res(bce0 + 512 * nt, 512), bank.h[:, :],
                  bank.res, tag=tag + "_bcec")
        for qq in range(4):
            for wdst, wsrc in ((w10, w1f), (w20, w2f)):
                dst = ctx.WP.h[:, wdst:wdst + 4096].rearrange(
                    "p (k m) -> p k m", k=NCH)
                src = wsrc[1024 * e:1024 * (e + 1),
                           512 * qq:512 * (qq + 1)].rearrange(
                               "(k p) m -> p k m", p=128)
                dma(ctx, "gpsimd", dst, ctx.WP.res(wdst, 4096), src,
                    ["HBM:L1w"], tag=tag + "_w12")
            dst = ctx.WP.h[:, w30:w30 + 4096].rearrange(
                "p (k m) -> p k m", k=4)
            src = w3f[2048 * e + 512 * qq:2048 * e + 512 * (qq + 1),
                      :].rearrange("(k p) m -> p k m", p=128)
            dma(ctx, "gpsimd", dst, ctx.WP.res(w30, 4096), src,
                ["HBM:L1w"], tag=tag + "_w3")
            for nt in range(NT):
                bce_ap = ctx.AB16.h[:, bce0 + 512 * nt:bce0 + 512 * (nt + 1)]
                bce_res = ctx.AB16.res(bce0 + 512 * nt, 512)
                for mi in range(4):
                    b1 = ctx.banks[0]
                    b2 = ctx.banks[1]
                    for ki in range(NCH):
                        rhs = NB.h[:, 2048 * ki + PB * nt:
                                   2048 * ki + PB * (nt + 1)]
                        l1 = ctx.WP.h[:, w10 + 512 * ki + 128 * mi:
                                      w10 + 512 * ki + 128 * (mi + 1)]
                        mm(ctx, b1.h[:, :], b1.res, l1,
                           ctx.WP.res(w10 + 512 * ki, 512), rhs,
                           [f"NB:{ki}"], start=(ki == 0),
                           stop=(ki == NCH - 1), tag=tag + "_u")
                        l2 = ctx.WP.h[:, w20 + 512 * ki + 128 * mi:
                                      w20 + 512 * ki + 128 * (mi + 1)]
                        mm(ctx, b2.h[:, :], b2.res, l2,
                           ctx.WP.res(w20 + 512 * ki, 512), rhs,
                           [f"NB:{ki}"], start=(ki == 0),
                           stop=(ki == NCH - 1), tag=tag + "_v")
                    ub_ap = ctx.AF32.h[:, tmp0:tmp0 + 512]
                    ub_res = ctx.AF32.res(tmp0, 512)
                    h_ap = ctx.AB16.h[:, hb0 + 512 * mi:hb0 + 512 * (mi + 1)]
                    h_res = ctx.AB16.res(hb0 + 512 * mi, 512)
                    act(ctx, ub_ap, ub_res, b1.h[:, :], b1.res, AF.Silu,
                        tag=tag + "_silu")
                    tt(ctx, "vector", h_ap, h_res, ub_ap, ub_res,
                       b2.h[:, :], b2.res, ALU.mult, tag=tag + "_h")
                for oc in range(NCH):
                    bank = ctx.banks[2 + (oc % 4)]
                    for mi in range(4):
                        l3 = ctx.WP.h[:, w30 + 1024 * mi + 128 * oc:
                                      w30 + 1024 * mi + 128 * (oc + 1)]
                        mm(ctx, bank.h[:, :], bank.res, l3,
                           ctx.WP.res(w30 + 1024 * mi, 1024),
                           ctx.AB16.h[:, hb0 + 512 * mi:hb0 + 512 * (mi + 1)],
                           ctx.AB16.res(hb0 + 512 * mi, 512),
                           start=(mi == 0), stop=(mi == 3), tag=tag + "_o")
                    t_ap = ctx.AF32.h[:, tmp0 + 512:tmp0 + 1024]
                    t_res = ctx.AF32.res(tmp0 + 512, 512)
                    tt(ctx, "vector", t_ap, t_res, bank.h[:, :], bank.res,
                       bce_ap, bce_res, ALU.mult, tag=tag + "_ow")
                    x_ap = X.h[:, 2048 * oc + PB * nt:
                               2048 * oc + PB * (nt + 1)]
                    tt(ctx, "vector", x_ap, [f"X:{oc}"], x_ap, [f"X:{oc}"],
                       t_ap, t_res, ALU.add, tag=tag + "_xa")
    ctx.reset_pools()




def l2_ffn(ctx, app):
    tag = f"L2f{app}"
    ln_sublayer(ctx, tag + "ln")
    ctx.reset_pools()
    NB, X = ctx.NB, ctx.X
    E = 8
    CAP = 1280
    inp = ctx.inp
    gt0 = ctx.SROWB.alloc(64)
    for ki in range(NCH):
        dst = ctx.SROWB.h[:, gt0 + 8 * ki:gt0 + 8 * (ki + 1)]
        dma(ctx, "gpsimd", dst, ctx.SROWB.res(gt0, 64),
            inp["L2_ffn_gate_w"].ap()[128 * ki:128 * (ki + 1), :],
            ["HBM:L2g"], tag=tag + "_g")
    m8_0 = ctx.SROWB.alloc(2048)
    M8 = ctx.SROWB.h[0:8, m8_0:m8_0 + 2048]
    M8_res = ctx.SROWB.res(m8_0, 2048)
    nm0 = ctx.SROW.alloc(128)
    t80 = ctx.AF32.alloc(512)
    sv0 = ctx.AB16.alloc(32)
    pk8_0 = ctx.SROW.alloc(2048)
    PK8 = ctx.SROW.h[0:8, pk8_0:pk8_0 + 2048]
    PK8_res = ctx.SROW.res(pk8_0, 2048)
    for tt_i in range(NTT):
        bank = ctx.pbank(tag)
        for ki in range(NCH):
            lhsT = NB.h[:, 2048 * ki + 128 * tt_i: 2048 * ki + 128 * (tt_i + 1)]
            mm(ctx, bank.h[:, 0:8], bank.res, lhsT, [f"NB:{ki}"],
               ctx.SROWB.h[:, gt0 + 8 * ki:gt0 + 8 * (ki + 1)],
               ctx.SROWB.res(gt0, 64),
               start=(ki == 0), stop=(ki == NCH - 1), tag=tag + "_rl")
        nm_ap = ctx.SROW.h[:, nm0:nm0 + 1]
        nm_res = ctx.SROW.res(nm0, 128)
        reduce(ctx, "vector", nm_ap, nm_res, bank.h[:, 0:8], bank.res,
               ALU.max, negate=True, tag=tag + "_nmax")
        ex_ap = ctx.AF32.h[:, t80:t80 + 8]
        ex_res = ctx.AF32.res(t80, 512)
        sm_ap = ctx.SROW.h[:, nm0 + 1:nm0 + 2]
        act(ctx, ex_ap, ex_res, bank.h[:, 0:8], bank.res, AF.Exp,
            bias=nm_ap, extra_reads=nm_res, accum_out=sm_ap,
            accum_res=nm_res, tag=tag + "_exp")
        rc_ap = ctx.SROW.h[:, nm0 + 2:nm0 + 3]
        recip(ctx, rc_ap, nm_res, sm_ap, nm_res, tag=tag + "_rc")
        sc_ap = ctx.AF32.h[:, t80 + 8:t80 + 16]
        ts(ctx, "vector", sc_ap, ex_res, ex_ap, ex_res, rc_ap, ALU.mult,
           extra_reads=nm_res, tag=tag + "_sc")
        mx_ap = ctx.SROW.h[:, nm0 + 3:nm0 + 4]
        reduce(ctx, "vector", mx_ap, nm_res, sc_ap, ex_res, ALU.max,
               tag=tag + "_mx")
        oh_ap = ctx.AB16.h[:, sv0:sv0 + 8]
        oh_res = ctx.AB16.res(sv0, 32)
        ts(ctx, "vector", oh_ap, oh_res, sc_ap, ex_res, mx_ap, ALU.is_ge,
           extra_reads=nm_res, tag=tag + "_oh")
        pk_ap = ctx.AB16.h[:, sv0 + 8:sv0 + 16]
        tt(ctx, "vector", pk_ap, oh_res, sc_ap, ex_res, oh_ap, oh_res,
           ALU.mult, tag=tag + "_pk")
        bank2 = ctx.pbank16(tag)
        ctx.P.op("tensor",
                 lambda t, o=bank2.h[0:8, 0:128], i=oh_ap,
                 idn=ctx.IDB: t.transpose(o, i, idn),
                 reads=list(oh_res) + ["CONST"], writes=bank2.res,
                 tag=tag + "_otr")
        copy(ctx, "vector", M8[:, 128 * tt_i:128 * (tt_i + 1)], M8_res,
             bank2.h[0:8, 0:128], bank2.res, tag=tag + "_m8")
        bank3 = ctx.pbank16(tag)
        ctx.P.op("tensor",
                 lambda t, o=bank3.h[0:8, 0:128], i=pk_ap,
                 idn=ctx.IDB: t.transpose(o, i, idn),
                 reads=list(oh_res) + ["CONST"], writes=bank3.res,
                 tag=tag + "_ptr")
        copy(ctx, "vector", PK8[:, 128 * tt_i:128 * (tt_i + 1)], PK8_res,
             bank3.h[0:8, 0:128], bank3.res, tag=tag + "_pk8")
    cnt_ap = ctx.SROW.h[0:8, nm0 + 4:nm0 + 5]
    reduce(ctx, "vector", cnt_ap, nm_res, M8, M8_res, ALU.add,
           tag=tag + "_cnt")
    bank = ctx.pbank(tag)
    transp(ctx, bank.h[0:1, 0:8], bank.res, cnt_ap, nm_res,
           ctx.IDF[0:8, 0:8], tag=tag + "_ctr")
    crow_ap = ctx.SROW.h[0:1, nm0 + 8:nm0 + 16]
    copy(ctx, "vector", crow_ap, nm_res, bank.h[0:1, 0:8], bank.res,
         tag=tag + "_crc")
    dma(ctx, "sync", ctx.CCIN.ap()[0:1, 0:8], ["CCIN"], crow_ap, nm_res,
        tag=tag + "_cst")
    ctx.P.op("gpsimd",
             lambda g: g.collective_compute(
                 "AllGather", ALU.bypass,
                 ins=[ctx.CCIN.ap().opt()], outs=[ctx.CCOUT.ap().opt()],
                 replica_groups=[list(range(NCORES))]),
             reads=["CCIN"], writes=["CCOUT"], is_coll=True,
             tag=tag + "_ag")
    cc0 = ctx.AF32.alloc(8)
    cc_ap = ctx.AF32.h[:, cc0:cc0 + 8]
    cc_res = ctx.AF32.res(cc0, 8)
    dma(ctx, "sync", cc_ap, cc_res, ctx.CCOUT.ap()[:, 0:8], ["CCOUT"],
        tag=tag + "_cld")
    bank = ctx.pbank(tag)
    mm(ctx, bank.h[0:1, 0:8], bank.res, ctx.RMASK[:, 0:1], ["RMASK"],
       cc_ap[:, 0:8], cc_res, start=True, stop=True, tag=tag + "_pfx")
    pfrow_ap = ctx.SROW.h[0:1, nm0 + 24:nm0 + 32]
    copy(ctx, "vector", pfrow_ap, nm_res, bank.h[0:1, 0:8], bank.res,
         tag=tag + "_pfr")
    bank2 = ctx.pbank(tag)
    transp(ctx, bank2.h[0:8, 0:1], bank2.res, pfrow_ap, nm_res,
           ctx.IDF[0:1, 0:1], tag=tag + "_pftr")
    pfx_ap = ctx.SROW.h[0:8, nm0 + 16:nm0 + 17]
    copy(ctx, "vector", pfx_ap, nm_res, bank2.h[0:8, 0:1], bank2.res,
         tag=tag + "_pfc")
    POS = ctx.AF32.h[0:8, pos0:pos0 + 2048]
    POS_res = ctx.AF32.res(pos0, 2048)
    ctx.P.op("vector",
             lambda e: e.tensor_tensor_scan(POS, M8, M8, 0.0, ALU.add,
                                            ALU.bypass),
             reads=list(M8_res), writes=list(POS_res), special=True,
             tag=tag + "_scan")
    ts(ctx, "vector", POS, POS_res, POS, POS_res, pfx_ap, ALU.add,
       extra_reads=nm_res, tag=tag + "_padd")
    ts(ctx, "vector", POS, POS_res, POS, POS_res, CAP + 0.5, ALU.is_lt,
       tag=tag + "_klt")
    pkf0 = m8_0
    PKF = ctx.SROWB.h[0:8, pkf0:pkf0 + 2048]
    PKF_res = ctx.SROWB.res(pkf0, 2048)
    tt(ctx, "vector", POS, POS_res, POS, POS_res, PK8, PK8_res, ALU.mult,
       tag=tag + "_kw")
    ts(ctx, "vector", PKF, PKF_res, POS, POS_res, 0.5, ALU.mult,
       tag=tag + "_kw5")
    w10 = ctx.WP.alloc(8192)
    w20 = ctx.WP.alloc(8192)
    bce0 = ctx.AB16.alloc(2048)
    hb0 = ctx.AB16.alloc(4096)
    tmp0 = ctx.AF32.alloc(512)
    w1f = inp["L2_ffn_w1"].ap().rearrange("e k m -> (e k) m")
    w2f = inp["L2_ffn_w2"].ap().rearrange("e k m -> (e k) m")
    for e in range(E):
        for nt in range(NT):
            bank = ctx.pbank(tag)
            mm(ctx, bank.h[:, :], bank.res,
               ctx.ONEE[0:8, 128 * e:128 * (e + 1)], ["CONST"],
               PKF[:, PB * nt:PB * (nt + 1)], PKF_res,
               start=True, stop=True, tag=tag + "_bce")
            scopy(ctx, ctx.AB16.h[:, bce0 + 512 * nt:bce0 + 512 * (nt + 1)],
                  ctx.AB16.res(bce0 + 512 * nt, 512), bank.h[:, :],
                  bank.res, tag=tag + "_bcec")
        for qq in range(4):
            dst = ctx.WP.h[:, w10:w10 + 8192].rearrange(
                "p (k m) -> p k m", k=NCH)
            src = w1f[1024 * e:1024 * (e + 1),
                      1024 * qq:1024 * (qq + 1)].rearrange(
                          "(k p) m -> p k m", p=128)
            dma(ctx, "gpsimd", dst, ctx.WP.res(w10, 8192), src,
                ["HBM:L2w1"], tag=tag + "_w1")
            dst = ctx.WP.h[:, w20:w20 + 8192].rearrange(
                "p (k m) -> p k m", k=NCH)
            src = w2f[4096 * e + 1024 * qq:4096 * e + 1024 * (qq + 1),
                      :].rearrange("(k p) m -> p k m", p=128)
            dma(ctx, "gpsimd", dst, ctx.WP.res(w20, 8192), src,
                ["HBM:L2w2"], tag=tag + "_w2")
            for nt in range(NT):
                bce_ap = ctx.AB16.h[:, bce0 + 512 * nt:bce0 + 512 * (nt + 1)]
                bce_res = ctx.AB16.res(bce0 + 512 * nt, 512)
                for mi in range(NCH):
                    b1 = ctx.banks[0]
                    for ki in range(NCH):
                        rhs = NB.h[:, 2048 * ki + PB * nt:
                                   2048 * ki + PB * (nt + 1)]
                        l1 = ctx.WP.h[:, w10 + 1024 * ki + 128 * mi:
                                      w10 + 1024 * ki + 128 * (mi + 1)]
                        mm(ctx, b1.h[:, :], b1.res, l1,
                           ctx.WP.res(w10 + 1024 * ki, 1024), rhs,
                           [f"NB:{ki}"], start=(ki == 0),
                           stop=(ki == NCH - 1), tag=tag + "_h")
                    h_ap = ctx.AB16.h[:, hb0 + 512 * mi:hb0 + 512 * (mi + 1)]
                    h_res = ctx.AB16.res(hb0 + 512 * mi, 512)
                    act(ctx, h_ap, h_res, b1.h[:, :], b1.res, AF.Relu,
                        tag=tag + "_relu")
                for oc in range(NCH):
                    bank = ctx.banks[2 + (oc % 4)]
                    for mi in range(NCH):
                        l2 = ctx.WP.h[:, w20 + 1024 * mi + 128 * oc:
                                      w20 + 1024 * mi + 128 * (oc + 1)]
                        mm(ctx, bank.h[:, :], bank.res, l2,
                           ctx.WP.res(w20 + 1024 * mi, 1024),
                           ctx.AB16.h[:, hb0 + 512 * mi:hb0 + 512 * (mi + 1)],
                           ctx.AB16.res(hb0 + 512 * mi, 512),
                           start=(mi == 0), stop=(mi == NCH - 1),
                           tag=tag + "_o")
                    t_ap = ctx.AF32.h[:, tmp0:tmp0 + 512]
                    t_res = ctx.AF32.res(tmp0, 512)
                    tt(ctx, "vector", t_ap, t_res, bank.h[:, :], bank.res,
                       bce_ap, bce_res, ALU.mult, tag=tag + "_ow")
                    x_ap = X.h[:, 2048 * oc + PB * nt:
                               2048 * oc + PB * (nt + 1)]
                    tt(ctx, "vector", x_ap, [f"X:{oc}"], x_ap, [f"X:{oc}"],
                       t_ap, t_res, ALU.add, tag=tag + "_xa")
    ctx.reset_pools()




def lvcN(ctx, name, n, dstcol, row0_elems=0, pbase=0, dstb=None, tag="vN"):
    hb = ctx.inp[name].ap().rearrange("a b -> (a b)")
    src = hb[row0_elems:row0_elems + n].rearrange("(c p) -> c p", p=n)
    st0 = ctx.SROW.alloc(128)
    stage = ctx.SROW.h[0:1, st0:st0 + n]
    dma(ctx, "sync", stage, ctx.SROW.res(st0, 128), src, [f"HBM:{name}"],
        tag=tag + "_ld")
    bank = ctx.pbank(tag)
    mm(ctx, bank.h[pbase:pbase + n, 0:1], bank.res, stage,
       ctx.SROW.res(st0, 128), ctx.ONESF[0:1, 0:1], ["CONST"],
       start=True, stop=True, tag=tag + "_tr")
    pool = ctx.SROWB if dstb else ctx.SROW
    copy(ctx, "vector", pool.h[pbase:pbase + n, dstcol:dstcol + 1],
         pool.res(dstcol, 1), bank.h[pbase:pbase + n, 0:1], bank.res,
         tag=tag + "_cp")
    ctx.SROW.reset(st0)


def small_transpose(ctx, name, rows, cols, dst_pool, dstcol, tag):
    st0 = ctx.AF32.alloc(cols)
    stage = ctx.AF32.h[0:rows, st0:st0 + cols]
    dma(ctx, "sync", stage, ctx.AF32.res(st0, cols),
        ctx.inp[name].ap()[:, :], [f"HBM:{name}"], tag=tag + "_ld")
    bank = ctx.pbank(tag)
    transp(ctx, bank.h[0:cols, 0:rows], bank.res, stage,
           ctx.AF32.res(st0, cols), ctx.IDF[0:rows, 0:rows], tag=tag + "_tr")
    copy(ctx, "vector", dst_pool.h[0:cols, dstcol:dstcol + rows],
         dst_pool.res(dstcol, rows), bank.h[0:cols, 0:rows], bank.res,
         tag=tag + "_cp")
    ctx.AF32.reset(st0)


def l0_attn(ctx):
    tag = "L0a"
    ln_sublayer(ctx, tag + "ln")
    ctx.reset_pools()
    NB, X = ctx.NB, ctx.X
    H, hd, L = 16, 64, 32
    inp = ctx.inp
    q2lT0 = ctx.SROWB.alloc(32)
    small_transpose(ctx, "L0_attn_q2l_w", 32, 64, ctx.SROWB, q2lT0,
                    tag + "_q2l")
    kv2lT0 = ctx.SROWB.alloc(32)
    small_transpose(ctx, "L0_attn_kv2l_w", 32, 64, ctx.SROWB, kv2lT0,
                    tag + "_kv2l")
    wqkT0 = ctx.SROWB.alloc(32)
    small_transpose(ctx, "L0_attn_wqk_w", 32, 32, ctx.SROWB, wqkT0,
                    tag + "_wqk")
    l2oT0 = ctx.SROWB.alloc(64)
    small_transpose(ctx, "L0_attn_l2o_w", 64, 32, ctx.SROWB, l2oT0,
                    tag + "_l2o")
    q2lb0 = ctx.SROW.alloc(8)
    lvcN(ctx, "L0_attn_q2l_b", 32, q2lb0, tag=tag + "_q2lb")
    kv2lb0 = q2lb0 + 1
    lvcN(ctx, "L0_attn_kv2l_b", 32, kv2lb0, tag=tag + "_kv2lb")
    wqkb0 = q2lb0 + 2
    lvcN(ctx, "L0_attn_wqk_b", 32, wqkb0, tag=tag + "_wqkb")
    l2ob0 = q2lb0 + 3
    lvcN(ctx, "L0_attn_l2o_b", 64, l2ob0, pbase=0, tag=tag + "_l2ob")
    lvcN(ctx, "L0_attn_l2o_b", 64, l2ob0 + 1, pbase=64, tag=tag + "_l2ob2")
    cab0 = ctx.SROWB.alloc(8)
    cosT0 = ctx.AB16.alloc(2048)
    sinmT0 = ctx.AB16.alloc(2048)
    rt0 = ctx.AF32.alloc(2048)
    rc0 = ctx.SROW.alloc(8)
    trow = ctx.AF32.h[0:32, rt0:rt0 + 2048]
    trow_res = ctx.AF32.res(rt0, 2048)
    ctx.P.op("gpsimd", lambda g: g.iota(
        trow, [[1, 2048]], base=0, channel_multiplier=0,
        allow_small_or_imprecise_dtypes=True),
        reads=(), writes=list(trow_res), tag=tag + "_iot")
    lcol = ctx.SROW.h[0:32, rc0:rc0 + 1]
    lcol_res = ctx.SROW.res(rc0, 8)
    ctx.P.op("gpsimd", lambda g: g.iota(
        lcol, [[0, 1]], base=0, channel_multiplier=1,
        allow_small_or_imprecise_dtypes=True),
        reads=(), writes=list(lcol_res), tag=tag + "_iol")
    m2 = ctx.SROW.h[0:32, rc0 + 1:rc0 + 2]
    copy(ctx, "vector", m2, lcol_res, lcol, lcol_res, tag=tag + "_m2c")
    msk1 = ctx.SROW.h[0:32, rc0 + 6:rc0 + 7]
    for k in (16, 8, 4, 2):
        ts(ctx, "vector", msk1, lcol_res, m2, lcol_res, float(k),
           ALU.is_ge, tag=tag + "_m2m")
        stt(ctx, m2, lcol_res, msk1, lcol_res, float(-k), m2, lcol_res,
            ALU.mult, ALU.add, tag=tag + "_m2s")
    fi = ctx.SROW.h[0:32, rc0 + 2:rc0 + 3]
    tt(ctx, "vector", fi, lcol_res, lcol, lcol_res, m2, lcol_res,
       ALU.subtract, tag=tag + "_fi")
    inv = ctx.SROW.h[0:32, rc0 + 3:rc0 + 4]
    act(ctx, inv, lcol_res, fi, lcol_res, AF.Exp,
        scale=-np.log(10000.0) / 32.0, tag=tag + "_inv")
    sgn = ctx.SROW.h[0:32, rc0 + 4:rc0 + 5]
    ts(ctx, "vector", sgn, lcol_res, m2, lcol_res, 2.0, ALU.mult, -1.0,
       ALU.add, tag=tag + "_sgn")
    ts(ctx, "vector", trow, trow_res, trow, trow_res, inv, ALU.mult,
       extra_reads=lcol_res, tag=tag + "_fr")
    g0 = ctx.AF32.alloc(2048)
    gtile = ctx.AF32.h[0:32, g0:g0 + 2048]
    g_res = ctx.AF32.res(g0, 2048)
    for k in range(9, -1, -1):
        sub = 2.0 * np.pi * (2 ** k)
        ts(ctx, "vector", gtile, g_res, trow, trow_res, float(sub),
           ALU.is_ge, tag=tag + "_rrm")
        stt(ctx, trow, trow_res, gtile, g_res, float(-sub), trow, trow_res,
            ALU.mult, ALU.add, tag=tag + "_rrs")
    ts(ctx, "vector", gtile, g_res, trow, trow_res, np.pi, ALU.is_gt,
       tag=tag + "_gt")
    stt(ctx, trow, trow_res, gtile, g_res, -2 * np.pi, trow, trow_res,
        ALU.mult, ALU.add, tag=tag + "_rr")
    act(ctx, gtile, g_res, trow, trow_res, AF.Sin, tag=tag + "_sin")
    ts(ctx, "vector", ctx.AB16.h[0:32, sinmT0:sinmT0 + 2048],
       ctx.AB16.res(sinmT0, 2048), gtile, g_res, sgn, ALU.mult,
       extra_reads=lcol_res, tag=tag + "_sinm")
    act(ctx, gtile, g_res, trow, trow_res, AF.Sin, scale=0.5,
        tag=tag + "_sh")
    act(ctx, gtile, g_res, gtile, g_res, AF.Square, tag=tag + "_sh2")
    ts(ctx, "vector", ctx.AB16.h[0:32, cosT0:cosT0 + 2048],
       ctx.AB16.res(cosT0, 2048), gtile, g_res, -2.0, ALU.mult, 1.0,
       ALU.add, tag=tag + "_cos")
    dbg(ctx, "ropecos", ctx.AB16.h[0:32, cosT0:cosT0 + 2048],
        ctx.AB16.res(cosT0, 2048), [32, 2048], BF16)
    dbg(ctx, "ropesin", ctx.AB16.h[0:32, sinmT0:sinmT0 + 2048],
        ctx.AB16.res(sinmT0, 2048), [32, 2048], BF16)
    yf0 = ctx.WP.alloc(16384)
    memset(ctx, "vector", ctx.WP.h[:, yf0:yf0 + 16384],
           ctx.WP.res(yf0, 16384), 0.0, tag=tag + "_yfz")
    mq0 = ctx.WP.alloc(2 * 3 * 256)
    caw = inp["L0_attn_c_attn_w"].ap()
    ql0 = ctx.AF32.alloc(2048)
    kl0 = ctx.AF32.alloc(0) if False else None
    kl0 = rt0
    sh0 = g0
    qr0 = ctx.AB16.alloc(2048)
    kr0 = ctx.AB16.alloc(2048)
    qf0 = qr0
    vt0 = ctx.AB16.alloc(512)
    pt0 = ctx.AB16.alloc(2048)
    pairswap = [1, 0, 3, 2, 5, 4, 7, 6, 9, 8, 11, 10, 13, 12, 15, 14,
                17, 16, 19, 18, 21, 20, 23, 22, 25, 24, 27, 26, 29, 28,
                31, 30]
    for h in range(H):
        sl = mq0 + (h % 2) * 768
        for j, (lt0, boff, brow) in enumerate(
                ((q2lT0, q2lb0, 0), (kv2lT0, kv2lb0, 1024),
                 (kv2lT0, kv2lb0, 2048))):
            stg0 = ctx.AB16.alloc(1024)
            dma(ctx, "gpsimd", ctx.AB16.h[0:64, stg0:stg0 + 1024],
                ctx.AB16.res(stg0, 1024),
                caw[brow + 64 * h: brow + 64 * (h + 1), :], ["HBM:L0caw"],
                tag=tag + "_cw")
            for ki in range(NCH):
                bank = ctx.pbank(tag)
                mm(ctx, bank.h[0:128, 0:32], bank.res,
                   ctx.AB16.h[0:64, stg0 + 128 * ki:stg0 + 128 * (ki + 1)],
                   ctx.AB16.res(stg0, 1024),
                   ctx.SROWB.h[0:64, lt0:lt0 + 32], ctx.SROWB.res(lt0, 32),
                   start=True, stop=True, tag=tag + "_mq")
                scopy(ctx, ctx.WP.h[:, sl + 256 * j + 32 * ki:
                                    sl + 256 * j + 32 * (ki + 1)],
                      ctx.WP.res(sl + 256 * j, 256), bank.h[0:128, 0:32],
                      bank.res, tag=tag + "_mqc")
            bst0 = ctx.AB16.alloc(16)
            lvcN_b16(ctx, "L0_attn_c_attn_b", 64, bst0, brow + 64 * h,
                     tag + "_cb")
            bank = ctx.pbank(tag)
            mm(ctx, bank.h[0:32, 0:1], bank.res,
               ctx.SROWB.h[0:64, lt0:lt0 + 32], ctx.SROWB.res(lt0, 32),
               ctx.AB16.h[0:64, bst0:bst0 + 1], ctx.AB16.res(bst0, 16),
               start=True, stop=True, tag=tag + "_cbm")
            bcmb = ctx.SROW.h[0:32, rc0 + 5 + j:rc0 + 6 + j]
            ctx.P.op("vector",
                     lambda e, o=bcmb, i=bank.h[0:32, 0:1],
                     b=ctx.SROW.h[0:32, boff:boff + 1]:
                     e.tensor_tensor(o, i, b, ALU.add),
                     reads=bank.res + ctx.SROW.res(boff, 1) + lcol_res,
                     writes=lcol_res, tag=tag + "_cba")
            ctx.AB16.reset(stg0)
        for j, (dst0, dres, isv) in enumerate(
                ((ql0, None, False), (kl0, None, False), (None, None, True))):
            bank = ctx.banks[j]
            for nt in range(NT):
                for ki in range(NCH):
                    lhsT = ctx.WP.h[:, sl + 256 * j + 32 * ki:
                                    sl + 256 * j + 32 * (ki + 1)]
                    mm(ctx, bank.h[0:32, :], bank.res, lhsT,
                       ctx.WP.res(sl + 256 * j, 256),
                       NB.h[:, 2048 * ki + PB * nt:2048 * ki + PB * (nt + 1)],
                       [f"NB:{ki}"], start=(ki == 0), stop=(ki == NCH - 1),
                       tag=tag + "_lat")
                bcmb = ctx.SROW.h[0:32, rc0 + 5 + j:rc0 + 6 + j]
                if isv:
                    vb0 = ctx.AB16.alloc(512)
                    act(ctx, ctx.AB16.h[0:32, vb0:vb0 + 512],
                        ctx.AB16.res(vb0, 512), bank.h[0:32, :], bank.res,
                        AF.Identity, bias=bcmb, extra_reads=lcol_res,
                        tag=tag + "_ve")
                    for q4 in range(4):
                        bk16 = ctx.pbank16(tag)
                        ctx.P.op("tensor",
                                 lambda t, o=bk16.h[:, 0:32],
                                 i=ctx.AB16.h[0:32, vb0 + 128 * q4:
                                              vb0 + 128 * (q4 + 1)],
                                 idn=ctx.IDB[0:32, 0:32]:
                                 t.transpose(o, i, idn),
                                 reads=ctx.AB16.res(vb0, 512) + ["CONST"],
                                 writes=bk16.res, tag=tag + "_vtr")
                        kt = 4 * nt + q4
                        copy(ctx, "vector",
                             ctx.AB16.h[:, vt0 + 32 * kt:vt0 + 32 * (kt + 1)],
                             ctx.AB16.res(vt0 + 32 * kt, 32),
                             bk16.h[:, 0:32], bk16.res, tag=tag + "_vtc")
                    ctx.AB16.reset(vb0)
                else:
                    act(ctx, ctx.AF32.h[0:32, dst0 + PB * nt:
                                        dst0 + PB * (nt + 1)],
                        ctx.AF32.res(dst0 + PB * nt, PB), bank.h[0:32, :],
                        bank.res, AF.Identity, bias=bcmb,
                        extra_reads=lcol_res, tag=tag + "_le")
        if h == 0:
            dbg(ctx, "ql0", ctx.AF32.h[0:32, ql0:ql0 + 2048],
                ctx.AF32.res(ql0, 2048), [32, 2048], F32)
            dbg(ctx, "kl0", ctx.AF32.h[0:32, kl0:kl0 + 2048],
                ctx.AF32.res(kl0, 2048), [32, 2048], F32)
            dbg(ctx, "vt0", ctx.AB16.h[:, vt0:vt0 + 512],
                ctx.AB16.res(vt0, 512), [128, 512], BF16)
        for src0, dst0 in ((ql0, qr0), (kl0, kr0)):
            s_ap = ctx.AF32.h[0:32, src0:src0 + 2048]
            s_res = ctx.AF32.res(src0, 2048)
            sh_ap = ctx.AF32.h[0:32, sh0:sh0 + 2048]
            sh_res = ctx.AF32.res(sh0, 2048)
            ctx.P.op("vector",
                     lambda e, o=sh_ap, i=s_ap, m=list(pairswap):
                     e.stream_shuffle(o, i, m),
                     reads=list(s_res), writes=list(sh_res), special=True,
                     tag=tag + "_shf")
            tt(ctx, "vector", s_ap, s_res, s_ap, s_res,
               ctx.AB16.h[0:32, cosT0:cosT0 + 2048],
               ctx.AB16.res(cosT0, 2048), ALU.mult, tag=tag + "_rc")
            tt(ctx, "vector", sh_ap, sh_res, sh_ap, sh_res,
               ctx.AB16.h[0:32, sinmT0:sinmT0 + 2048],
               ctx.AB16.res(sinmT0, 2048), ALU.mult, tag=tag + "_rs")
            tt(ctx, "vector", ctx.AB16.h[0:32, dst0:dst0 + 2048],
               ctx.AB16.res(dst0, 2048), s_ap, s_res, sh_ap, sh_res,
               ALU.add, tag=tag + "_ra")
        if h == 0:
            dbg(ctx, "kr0", ctx.AB16.h[0:32, kr0:kr0 + 2048],
                ctx.AB16.res(kr0, 2048), [32, 2048], BF16)
        for nt in range(NT):
            bank = ctx.banks[3]
            mm(ctx, bank.h[0:32, :], bank.res,
               ctx.SROWB.h[0:32, wqkT0:wqkT0 + 32],
               ctx.SROWB.res(wqkT0, 32),
               ctx.AB16.h[0:32, qr0 + PB * nt:qr0 + PB * (nt + 1)],
               ctx.AB16.res(qr0, 2048), start=True, stop=True,
               tag=tag + "_qf")
            act(ctx, ctx.AB16.h[0:32, qf0 + PB * nt:qf0 + PB * (nt + 1)],
                ctx.AB16.res(qf0, 2048), bank.h[0:32, :], bank.res,
                AF.Identity, bias=ctx.SROW.h[0:32, wqkb0:wqkb0 + 1],
                extra_reads=ctx.SROW.res(wqkb0, 1), tag=tag + "_qfe")
        for qnt in range(NT):
            nkt = 4 * qnt + 4
            ob = ctx.banks[4]
            db = ctx.banks[5]
            pbase = 64 * (h % 2)
            for kt in range(nkt + 3):
                if kt < nkt:
                    sb = ctx.banks[kt % 4]
                    mm(ctx, sb.h[:, :], sb.res,
                       ctx.AB16.h[0:32, kr0 + 128 * kt:kr0 + 128 * (kt + 1)],
                       ctx.AB16.res(kr0, 2048),
                       ctx.AB16.h[0:32, qf0 + PB * qnt:qf0 + PB * (qnt + 1)],
                       ctx.AB16.res(qf0, 2048),
                       start=True, stop=True, tag=tag + "_s")
                    pc = pt0 + 512 * (kt % 4)
                    p_ap = ctx.AB16.h[:, pc:pc + 512]
                    p_res = ctx.AB16.res(pc, 512)
                    act(ctx, p_ap, p_res, sb.h[:, :], sb.res, AF.Exp,
                        scale=float(32 ** -0.5), tag=tag + "_exp")
                    if kt >= 4 * qnt:
                        base = PB * qnt - 128 * kt
                        ctx.P.op("gpsimd",
                                 lambda g, a=p_ap, b=base: g.affine_select(
                                     a, a, [[1, 512]], ALU.is_ge, 0.0,
                                     base=b, channel_multiplier=-1),
                                 reads=list(p_res), writes=list(p_res),
                                 tag=tag + "_msk")
                if kt >= 3:
                    k2 = kt - 3
                    pc2 = pt0 + 512 * (k2 % 4)
                    p2_ap = ctx.AB16.h[:, pc2:pc2 + 512]
                    p2_res = ctx.AB16.res(pc2, 512)
                    mm(ctx, ob.h[0:32, :], ob.res,
                       ctx.AB16.h[:, vt0 + 32 * k2:vt0 + 32 * (k2 + 1)],
                       ctx.AB16.res(vt0 + 32 * k2, 32), p2_ap, p2_res,
                       start=(k2 == 0), stop=(k2 == nkt - 1),
                       tag=tag + "_av")
                    mm(ctx, db.h[0:1, :], db.res, ctx.ONESB_COL, ["CONST"],
                       p2_ap, p2_res, start=(k2 == 0), stop=(k2 == nkt - 1),
                       tag=tag + "_den")
            r_ap = ctx.SROW.h[0:1, rc0 + 8 - 8:rc0 + 1] if False else None
            rr_ap = ctx.SROW.h[0:1, q2lb0 + 4:q2lb0 + 4 + 512] \
                if False else None
            rr0a = ctx.SROW.alloc(0)
            rrow = ctx.SROW.h[0:1, 1024:1536]
            rrow_res = ctx.SROW.res(1024, 512)
            recip(ctx, rrow, rrow_res, db.h[0:1, :], db.res,
                  tag=tag + "_rcp")
            rbrow = ctx.SROWB.h[0:1, 1024:1536]
            rbrow_res = ctx.SROWB.res(1024, 512)
            copy(ctx, "vector", rbrow, rbrow_res, rrow, rrow_res,
                 tag=tag + "_rcb")
            bb = ctx.banks[(nkt) % 4]
            mm(ctx, bb.h[0:32, :], bb.res,
               ctx.ONESB_ROW[:, 0:32], ["CONST"], rbrow, rbrow_res,
               start=True, stop=True, tag=tag + "_rbc")
            y0 = ctx.AF32.h[0:32, sh0:sh0 + 512]
            y0_res = ctx.AF32.res(sh0, 512)
            scopy(ctx, y0, y0_res, ob.h[0:32, :], ob.res,
                  tag=tag + "_ye")
            ylat = ctx.AB16.h[0:32, pt0:pt0 + 512]
            ylat_res = ctx.AB16.res(pt0, 512)
            tt(ctx, "vector", ylat, ylat_res, y0, y0_res,
               bb.h[0:32, :], bb.res, ALU.mult, tag=tag + "_yn")

            ofb = ctx.banks[(nkt + 1) % 4]
            mm(ctx, ofb.h[pbase:pbase + 64, :], ofb.res,
               ctx.SROWB.h[0:32, l2oT0:l2oT0 + 64],
               ctx.SROWB.res(l2oT0, 64), ylat, ylat_res,
               start=True, stop=True, tag=tag + "_l2o")
            yfdst = ctx.WP.h[pbase:pbase + 64,
                             yf0 + 2048 * (h // 2) + PB * qnt:
                             yf0 + 2048 * (h // 2) + PB * (qnt + 1)]
            yfres = ctx.WP.res(yf0 + 2048 * (h // 2) + PB * qnt, PB)
            act(ctx, yfdst, yfres, ofb.h[pbase:pbase + 64, :], ofb.res,
                AF.Identity,
                bias=ctx.SROW.h[pbase:pbase + 64,
                                l2ob0 + (h % 2):l2ob0 + (h % 2) + 1],
                extra_reads=ctx.SROW.res(l2ob0, 2), tag=tag + "_yf")
    dbg(ctx, "yf", ctx.WP.h[:, yf0:yf0 + 16384], ctx.WP.res(yf0, 16384),
        [128, 16384], BF16)
    ctx.AB16.reset()
    ctx.AF32.reset()
    cpt0 = ctx.AB16.alloc(8192)
    stg0 = ctx.AB16.alloc(2048)
    cpw = inp["L0_attn_cproj_w"].ap()
    for fc in range(NCH):
        sc = stg0 + 1024 * (fc % 2)
        dma(ctx, "gpsimd", ctx.AB16.h[:, sc:sc + 1024],
            ctx.AB16.res(sc, 1024), cpw[128 * fc:128 * (fc + 1), :],
            ["HBM:L0cp"], tag=tag + "_cpl")
        for oc in range(NCH):
            bk16 = ctx.pbank16(tag)
            ctx.P.op("tensor",
                     lambda t, o=bk16.h[:, 0:128],
                     i=ctx.AB16.h[:, sc + 128 * oc:sc + 128 * (oc + 1)],
                     idn=ctx.IDB: t.transpose(o, i, idn),
                     reads=ctx.AB16.res(sc, 1024) + ["CONST"],
                     writes=bk16.res, tag=tag + "_cpt")
            copy(ctx, "vector",
                 ctx.AB16.h[:, cpt0 + 1024 * oc + 128 * fc:
                            cpt0 + 1024 * oc + 128 * (fc + 1)],
                 ctx.AB16.res(cpt0 + 1024 * oc + 128 * fc, 128),
                 bk16.h[:, 0:128], bk16.res, tag=tag + "_cptc")
    cpb0 = ctx.SROW.alloc(8)
    load_vec_cols(ctx, "L0_attn_cproj_b", cpb0, 8, tag=tag + "_cpb")
    for oc in range(NCH):
        for nt in range(NT):
            bank = ctx.pbank(tag)
            for fc in range(NCH):
                lhsT = ctx.AB16.h[:, cpt0 + 1024 * fc + 128 * oc:
                                  cpt0 + 1024 * fc + 128 * (oc + 1)]
                lres = ctx.AB16.res(cpt0 + 1024 * fc + 128 * oc, 128)
                rhs = ctx.WP.h[:, yf0 + 2048 * fc + PB * nt:
                               yf0 + 2048 * fc + PB * (nt + 1)]
                rres = ctx.WP.res(yf0 + 2048 * fc + PB * nt, PB)
                mm(ctx, bank.h[:, :], bank.res, lhsT, lres, rhs, rres,
                   start=(fc == 0), stop=(fc == NCH - 1), tag=tag + "_o")
            x_ap = X.h[:, 2048 * oc + PB * nt: 2048 * oc + PB * (nt + 1)]
            stt(ctx, x_ap, [f"X:{oc}"], bank.h[:, :], bank.res,
                ctx.SROW.h[:, cpb0 + oc:cpb0 + oc + 1], x_ap, [f"X:{oc}"],
                ALU.add, ALU.add, extra_reads=ctx.SROW.res(cpb0, 8),
                tag=tag + "_xa")
    ctx.reset_pools()


def lvcN_b16(ctx, name, n, dstcol_ab16, row0_elems, tag):
    hb = ctx.inp[name].ap().rearrange("a b -> (a b)")
    src = hb[row0_elems:row0_elems + n].rearrange("(c p) -> c p", p=n)
    st0 = ctx.SROW.alloc(128)
    stage = ctx.SROW.h[0:1, st0:st0 + n]
    dma(ctx, "sync", stage, ctx.SROW.res(st0, 128), src, [f"HBM:{name}"],
        tag=tag + "_ld")
    bank = ctx.pbank(tag)
    mm(ctx, bank.h[0:n, 0:1], bank.res, stage, ctx.SROW.res(st0, 128),
       ctx.ONESF[0:1, 0:1], ["CONST"], start=True, stop=True,
       tag=tag + "_tr")
    copy(ctx, "vector", ctx.AB16.h[0:n, dstcol_ab16:dstcol_ab16 + 1],
         ctx.AB16.res(dstcol_ab16, 1), bank.h[0:n, 0:1], bank.res,
         tag=tag + "_cp")
    ctx.SROW.reset(st0)




def l1_attn(ctx):
    tag = "L1a"
    ln_sublayer(ctx, tag + "ln")
    ctx.reset_pools()
    NB, X = ctx.NB, ctx.X
    DI, NS = 2048, 16
    NCC = DI // 128
    inp = ctx.inp
    win = inp["L1_attn_W_in"].ap()
    wx = inp["L1_attn_W_x"].ap()
    wdt = inp["L1_attn_W_dt"].ap()
    alog = inp["L1_attn_A_log"].ap()
    convw = inp["L1_attn_conv_w"].ap().rearrange("c a k -> c (a k)")
    h3 = ctx.H3.ap()
    ms = ctx.MS.ap()
    one32_0 = ctx.WP.alloc(2048)
    memset(ctx, "vector", ctx.WP.h[0:32, one32_0:one32_0 + 2048],
           ctx.WP.res(one32_0, 2048), 1.0, tag=tag + "_o32i")
    for n in range(NS):
        ctx.P.op("gpsimd", lambda g, n=n: g.affine_select(
            ctx.WP.h[0:32, one32_0 + 128 * n:one32_0 + 128 * (n + 1)],
            ctx.WP.h[0:32, one32_0 + 128 * n:one32_0 + 128 * (n + 1)],
            [[0, 128]], ALU.is_equal, 0.0, base=-(16 + n),
            channel_multiplier=1),
            reads=ctx.WP.res(one32_0, 2048), writes=ctx.WP.res(one32_0, 2048),
            tag=tag + "_o32")
    wslot = ctx.WP.alloc(2 * 1024)
    wxs0 = ctx.WP.alloc(2 * 32)
    dbc0 = ctx.SROWB.alloc(2048)
    zt0 = ctx.AB16.alloc(2048)
    xs0 = ctx.AB16.alloc(2048)
    xcp0 = ctx.AF32.alloc(2052)
    cv0 = ctx.AF32.alloc(2048)
    cw0 = ctx.SROW.alloc(8)
    cb0 = ctx.SROW.alloc(8)
    DBC = ctx.SROWB.h[0:32, dbc0:dbc0 + 2048]
    DBC_res = ctx.SROWB.res(dbc0, 2048)
    ctx.pbank_allowed = (4, 5)
    for c in range(2 * NCC):
        isz = c >= NCC
        sl = wslot + 1024 * (c % 2)
        dst = ctx.WP.h[:, sl:sl + 1024].rearrange("p (k m) -> p k m", k=NCH)
        src = win[:, 128 * c:128 * (c + 1)].rearrange(
            "(k p) m -> p k m", p=128)
        dma(ctx, "gpsimd", dst, ctx.WP.res(sl, 1024), src, ["HBM:L1win"],
            tag=tag + "_wl")
        if not isz:
            dma(ctx, "sync", ctx.SROW.h[:, cw0:cw0 + 4],
                ctx.SROW.res(cw0, 8), convw[128 * c:128 * (c + 1), :],
                ["HBM:L1cw"], tag=tag + "_cwl")
            lvcN(ctx, "L1_attn_conv_b", 128, cb0, row0_elems=128 * c,
                 tag=tag + "_cb")
            wxsl = wxs0 + 32 * (c % 2)
            dma(ctx, "gpsimd", ctx.WP.h[:, wxsl:wxsl + 32],
                ctx.WP.res(wxsl, 32), wx[128 * c:128 * (c + 1), :],
                ["HBM:L1wx"], tag=tag + "_wxl")
        for nt in range(NT):
            bank = ctx.banks[4 + (nt % 2)]
            for ki in range(NCH):
                lhsT = ctx.WP.h[:, sl + 128 * ki:sl + 128 * (ki + 1)]
                mm(ctx, bank.h[:, :], bank.res, lhsT, ctx.WP.res(sl, 1024),
                   NB.h[:, 2048 * ki + PB * nt:2048 * ki + PB * (nt + 1)],
                   [f"NB:{ki}"], start=(ki == 0), stop=(ki == NCH - 1),
                   tag=tag + "_xz")
            if isz:
                scopy(ctx, ctx.AB16.h[:, zt0 + PB * nt:zt0 + PB * (nt + 1)],
                      ctx.AB16.res(zt0 + PB * nt, PB), bank.h[:, :],
                      bank.res, tag=tag + "_ze")
            else:
                scopy(ctx, ctx.AF32.h[:, xcp0 + 3 + PB * nt:
                                      xcp0 + 3 + PB * (nt + 1)],
                      ctx.AF32.res(xcp0 + 3 + PB * nt, PB + 3),
                      bank.h[:, :], bank.res, tag=tag + "_xce")
        if isz:
            dma(ctx, "sync", h3[2048 + 128 * (c - NCC):
                               2048 + 128 * (c - NCC + 1), :], [f"H3:z"],
                ctx.AB16.h[:, zt0:zt0 + 2048], ctx.AB16.res(zt0, 2048),
                tag=tag + "_zst")
        else:
            memset(ctx, "vector", ctx.AF32.h[:, xcp0:xcp0 + 3],
                   ctx.AF32.res(xcp0, 3), 0.0, tag=tag + "_pz")
            ts(ctx, "vector", ctx.AF32.h[:, cv0:cv0 + 2048],
               ctx.AF32.res(cv0, 2048), ctx.AF32.h[:, xcp0:xcp0 + 2048],
               ctx.AF32.res(xcp0, 2052), ctx.SROW.h[:, cw0:cw0 + 1],
               ALU.mult, extra_reads=ctx.SROW.res(cw0, 8), tag=tag + "_cv0")
            for k in range(1, 4):
                stt(ctx, ctx.AF32.h[:, cv0:cv0 + 2048],
                    ctx.AF32.res(cv0, 2048),
                    ctx.AF32.h[:, xcp0 + k:xcp0 + k + 2048],
                    ctx.AF32.res(xcp0, 2052),
                    ctx.SROW.h[:, cw0 + k:cw0 + k + 1],
                    ctx.AF32.h[:, cv0:cv0 + 2048], ctx.AF32.res(cv0, 2048),
                    ALU.mult, ALU.add, extra_reads=ctx.SROW.res(cw0, 8),
                    tag=tag + "_cvk")
            act(ctx, ctx.AB16.h[:, xs0:xs0 + 2048], ctx.AB16.res(xs0, 2048),
                ctx.AF32.h[:, cv0:cv0 + 2048], ctx.AF32.res(cv0, 2048),
                AF.Silu, bias=ctx.SROW.h[:, cb0:cb0 + 1],
                extra_reads=ctx.SROW.res(cb0, 8), tag=tag + "_xs")
            dma(ctx, "sync", h3[128 * c:128 * (c + 1), :], [f"H3:xs"],
                ctx.AB16.h[:, xs0:xs0 + 2048], ctx.AB16.res(xs0, 2048),
                tag=tag + "_xst")
            wxsl = wxs0 + 32 * (c % 2)
            for nt in range(NT):
                mm(ctx, ctx.banks[nt].h[0:32, :], ctx.banks[nt].res,
                   ctx.WP.h[:, wxsl:wxsl + 32], ctx.WP.res(wxsl, 32),
                   ctx.AB16.h[:, xs0 + PB * nt:xs0 + PB * (nt + 1)],
                   ctx.AB16.res(xs0, 2048), start=(c == 0),
                   stop=(c == NCC - 1), tag=tag + "_dbc")
    for nt in range(NT):
        scopy(ctx, DBC[:, PB * nt:PB * (nt + 1)], DBC_res,
              ctx.banks[nt].h[0:32, :], ctx.banks[nt].res, tag=tag + "_dbce")
    ctx.pbank_allowed = (0, 1, 2, 3, 4, 5)
    dbg(ctx, "dbc", DBC, DBC_res, [32, 2048], BF16)
    ctx.AF32.reset()
    ctx.AB16.reset()
    xs0 = ctx.AB16.alloc(2048)
    du0 = ctx.AB16.alloc(2048)
    bcs0 = ctx.AB16.alloc(2048)
    da0 = ctx.AB16.alloc(2048)
    dbu0 = ctx.AB16.alloc(2048)
    hh0 = ctx.AB16.alloc(2048)
    dl0 = ctx.AF32.alloc(2048)
    ya0 = ctx.AF32.alloc(2048)
    sp0 = ctx.AF32.alloc(2048)
    ctx.WP.reset(2048)
    wdt0 = ctx.WP.alloc(2 * 128)
    da1 = ctx.WP.alloc(2048)
    dbu1 = ctx.WP.alloc(2048)
    bcs1 = ctx.WP.alloc(2048)
    hh1 = ctx.WP.alloc(2048)
    acol0 = ctx.SROW.alloc(64)
    bdt0 = ctx.SROW.alloc(8)
    dcol0 = ctx.SROW.alloc(8)
    yz0 = hh0
    for c in range(NCC):
        dma(ctx, "sync", ctx.AB16.h[:, xs0:xs0 + 2048],
            ctx.AB16.res(xs0, 2048), h3[128 * c:128 * (c + 1), :],
            ["H3:xs"], tag=tag + "_xsl")
        wsl = wdt0 + 128 * (c % 2)
        dma(ctx, "gpsimd", ctx.WP.h[0:16, wsl:wsl + 128],
            ctx.WP.res(wsl, 128), wdt[:, 128 * c:128 * (c + 1)],
            ["HBM:L1wdt"], tag=tag + "_wdtl")
        dma(ctx, "sync", ctx.SROW.h[:, acol0:acol0 + 16],
            ctx.SROW.res(acol0, 64), alog[128 * c:128 * (c + 1), :],
            ["HBM:L1al"], tag=tag + "_all")
        act(ctx, ctx.SROW.h[:, acol0 + 16:acol0 + 32],
            ctx.SROW.res(acol0, 64), ctx.SROW.h[:, acol0:acol0 + 16],
            ctx.SROW.res(acol0, 64), AF.Exp, tag=tag + "_ae")
        ts(ctx, "vector", ctx.SROW.h[:, acol0 + 16:acol0 + 32],
           ctx.SROW.res(acol0, 64), ctx.SROW.h[:, acol0 + 16:acol0 + 32],
           ctx.SROW.res(acol0, 64), -1.0, ALU.mult, tag=tag + "_an")
        lvcN(ctx, "L1_attn_b_dt", 128, bdt0, row0_elems=128 * c,
             tag=tag + "_bdt")
        lvcN(ctx, "L1_attn_D", 128, dcol0, row0_elems=128 * c,
             tag=tag + "_dc")
        if c == 0:
            dbg(ctx, "wdt0", ctx.WP.h[0:16, wsl:wsl + 128],
                ctx.WP.res(wsl, 128), [16, 128], BF16)
        for nt in range(NT):
            bank = ctx.banks[4 + (nt % 2)]
            mm(ctx, bank.h[:, :], bank.res, ctx.WP.h[0:16, wsl:wsl + 128],
               ctx.WP.res(wsl, 128), DBC[0:16, PB * nt:PB * (nt + 1)],
               DBC_res, start=True, stop=True, tag=tag + "_dp")
            act(ctx, ctx.AF32.h[:, dl0 + PB * nt:dl0 + PB * (nt + 1)],
                ctx.AF32.res(dl0, 2048), bank.h[:, :], bank.res,
                AF.Identity, bias=ctx.SROW.h[:, bdt0:bdt0 + 1],
                extra_reads=ctx.SROW.res(bdt0, 8), tag=tag + "_dpe")
        if c == 0:
            dbg(ctx, "dpre0", ctx.AF32.h[:, dl0:dl0 + 2048],
                ctx.AF32.res(dl0, 2048), [128, 2048], F32)
        act(ctx, ctx.AF32.h[:, sp0:sp0 + 2048], ctx.AF32.res(sp0, 2048),
            ctx.AF32.h[:, dl0:dl0 + 2048], ctx.AF32.res(dl0, 2048),
            AF.Abs, tag=tag + "_sa")
        act(ctx, ctx.AF32.h[:, sp0:sp0 + 2048], ctx.AF32.res(sp0, 2048),
            ctx.AF32.h[:, sp0:sp0 + 2048], ctx.AF32.res(sp0, 2048),
            AF.Exp, scale=-1.0, tag=tag + "_se")
        act(ctx, ctx.AF32.h[:, sp0:sp0 + 2048], ctx.AF32.res(sp0, 2048),
            ctx.AF32.h[:, sp0:sp0 + 2048], ctx.AF32.res(sp0, 2048),
            AF.Ln, bias=1.0, tag=tag + "_sl")
        ts(ctx, "vector", ctx.AF32.h[:, dl0:dl0 + 2048],
           ctx.AF32.res(dl0, 2048), ctx.AF32.h[:, dl0:dl0 + 2048],
           ctx.AF32.res(dl0, 2048), 0.0, ALU.max, tag=tag + "_sm")
        tt(ctx, "vector", ctx.AF32.h[:, dl0:dl0 + 2048],
           ctx.AF32.res(dl0, 2048), ctx.AF32.h[:, dl0:dl0 + 2048],
           ctx.AF32.res(dl0, 2048), ctx.AF32.h[:, sp0:sp0 + 2048],
           ctx.AF32.res(sp0, 2048), ALU.add, tag=tag + "_sadd")
        if c == 0:
            dbg(ctx, "delta0", ctx.AF32.h[:, dl0:dl0 + 2048],
                ctx.AF32.res(dl0, 2048), [128, 2048], F32)
        tt(ctx, "gpsimd", ctx.AB16.h[:, du0:du0 + 2048],
           ctx.AB16.res(du0, 2048), ctx.AB16.h[:, xs0:xs0 + 2048],
           ctx.AB16.res(xs0, 2048), ctx.AF32.h[:, dl0:dl0 + 2048],
           ctx.AF32.res(dl0, 2048), ALU.mult, tag=tag + "_du")
        memset(ctx, "vector", ctx.AF32.h[:, ya0:ya0 + 2048],
               ctx.AF32.res(ya0, 2048), 0.0, tag=tag + "_yz")
        memset(ctx, "vector", ctx.AF32.h[:, sp0:sp0 + 2048],
               ctx.AF32.res(sp0, 2048), 0.0, tag=tag + "_yz1")
        def nbufs(sub):
            if sub == 0:
                H, R = ctx.AB16.h, ctx.AB16.res
                da_, dbu_, bcs_, hh_ = da0, dbu0, bcs0, hh0
            else:
                H, R = ctx.WP.h, ctx.WP.res
                da_, dbu_, bcs_, hh_ = da1, dbu1, bcs1, hh1
            return {k: (H[:, v:v + 2048], R(v, 2048)) for k, v in
                    (("da", da_), ("dbu", dbu_), ("bcs", bcs_), ("hh", hh_))}
        BUF = (nbufs(0), nbufs(1))
        YA = ((ya0, ctx.AF32.res(ya0, 2048)), (sp0, ctx.AF32.res(sp0, 2048)))
        for n2 in range(0, NS, 2):
            for sub in range(2):
                n = n2 + sub
                bcs_ap, bcs_res = BUF[sub]["bcs"]
                for nt in range(NT):
                    bank = ctx.banks[2 * sub + (nt % 2)]
                    mm(ctx, bank.h[:, :], bank.res,
                       ctx.WP.h[0:32, one32_0 + 128 * n:
                                one32_0 + 128 * (n + 1)],
                       ctx.WP.res(one32_0, 2048),
                       DBC[:, PB * nt:PB * (nt + 1)],
                       DBC_res, start=True, stop=True, tag=tag + "_bc")
                    scopy(ctx, bcs_ap[:, PB * nt:PB * (nt + 1)], bcs_res,
                          bank.h[:, :], bank.res, tag=tag + "_bce")
            for sub in range(2):
                n = n2 + sub
                da_ap, da_res = BUF[sub]["da"]
                act(ctx, da_ap, da_res,
                    ctx.AF32.h[:, dl0:dl0 + 2048], ctx.AF32.res(dl0, 2048),
                    AF.Exp, scale=ctx.SROW.h[:, acol0 + 16 + n:
                                             acol0 + 17 + n],
                    extra_reads=ctx.SROW.res(acol0, 64), tag=tag + "_da")
            for sub in range(2):
                dbu_ap, dbu_res = BUF[sub]["dbu"]
                bcs_ap, bcs_res = BUF[sub]["bcs"]
                tt(ctx, "vector", dbu_ap, dbu_res,
                   ctx.AB16.h[:, du0:du0 + 2048], ctx.AB16.res(du0, 2048),
                   bcs_ap, bcs_res, ALU.mult, tag=tag + "_dbu")
            for sub in range(2):
                da_ap, da_res = BUF[sub]["da"]
                dbu_ap, dbu_res = BUF[sub]["dbu"]
                hh_ap, hh_res = BUF[sub]["hh"]
                ctx.P.op("vector",
                         lambda e, o=hh_ap, a=da_ap, b=dbu_ap:
                         e.tensor_tensor_scan(o, a, b, 0.0, ALU.mult,
                                              ALU.add),
                         reads=list(da_res) + list(dbu_res),
                         writes=list(hh_res), special=True,
                         tag=tag + "_scan")
            for sub in range(2):
                dbu_ap, dbu_res = BUF[sub]["dbu"]
                bcs_ap, bcs_res = BUF[sub]["bcs"]
                hh_ap, hh_res = BUF[sub]["hh"]
                tt(ctx, "gpsimd", dbu_ap, dbu_res, hh_ap, hh_res,
                   bcs_ap, bcs_res, ALU.mult, tag=tag + "_ym")
            for sub in range(2):
                dbu_ap, dbu_res = BUF[sub]["dbu"]
                yab, ya_res = YA[sub]
                tt(ctx, "vector", ctx.AF32.h[:, yab:yab + 2048], ya_res,
                   ctx.AF32.h[:, yab:yab + 2048], ya_res,
                   dbu_ap, dbu_res, ALU.add, tag=tag + "_ya")
        tt(ctx, "vector", ctx.AF32.h[:, ya0:ya0 + 2048],
           ctx.AF32.res(ya0, 2048), ctx.AF32.h[:, ya0:ya0 + 2048],
           ctx.AF32.res(ya0, 2048), ctx.AF32.h[:, sp0:sp0 + 2048],
           ctx.AF32.res(sp0, 2048), ALU.add, tag=tag + "_yam")
        if c == 0:
            dbg(ctx, "ya0", ctx.AF32.h[:, ya0:ya0 + 2048],
                ctx.AF32.res(ya0, 2048), [128, 2048], F32)
        stt(ctx, ctx.AF32.h[:, ya0:ya0 + 2048], ctx.AF32.res(ya0, 2048),
            ctx.AB16.h[:, xs0:xs0 + 2048], ctx.AB16.res(xs0, 2048),
            ctx.SROW.h[:, dcol0:dcol0 + 1], ctx.AF32.h[:, ya0:ya0 + 2048],
            ctx.AF32.res(ya0, 2048), ALU.mult, ALU.add,
            extra_reads=ctx.SROW.res(dcol0, 8), tag=tag + "_yd")
        dma(ctx, "sync", ctx.AB16.h[:, du0:du0 + 2048],
            ctx.AB16.res(du0, 2048), h3[2048 + 128 * c:2048 + 128 * (c + 1),
                                        :], ["H3:z"], tag=tag + "_zl")
        act(ctx, ctx.AB16.h[:, bcs0:bcs0 + 2048], ctx.AB16.res(bcs0, 2048),
            ctx.AB16.h[:, du0:du0 + 2048], ctx.AB16.res(du0, 2048),
            AF.Silu, tag=tag + "_zs")
        tt(ctx, "vector", ctx.AB16.h[:, yz0:yz0 + 2048],
           ctx.AB16.res(yz0, 2048), ctx.AF32.h[:, ya0:ya0 + 2048],
           ctx.AF32.res(ya0, 2048), ctx.AB16.h[:, bcs0:bcs0 + 2048],
           ctx.AB16.res(bcs0, 2048), ALU.mult, tag=tag + "_yzm")
        dma(ctx, "sync", ms[128 * c:128 * (c + 1), :], ["MS"],
            ctx.AB16.h[:, yz0:yz0 + 2048], ctx.AB16.res(yz0, 2048),
            tag=tag + "_yst")
    ctx.WP.reset()
    ctx.AB16.reset()
    wout = inp["L1_attn_W_out"].ap()
    wo0 = ctx.WP.alloc(16 * 1024)
    for ki in range(NCC):
        dma(ctx, "gpsimd", ctx.WP.h[:, wo0 + 1024 * ki:wo0 + 1024 * (ki + 1)],
            ctx.WP.res(wo0 + 1024 * ki, 1024),
            wout[128 * ki:128 * (ki + 1), :], ["HBM:L1wo"], tag=tag + "_wol")
    yb0 = ctx.AB16.alloc(16 * 512)
    for nt in range(NT):
        for ki in range(NCC):
            dma(ctx, "sync", ctx.AB16.h[:, yb0 + 512 * ki:yb0 + 512 * (ki + 1)],
                ctx.AB16.res(yb0 + 512 * ki, 512),
                ms[128 * ki:128 * (ki + 1), PB * nt:PB * (nt + 1)], ["MS"],
                tag=tag + "_ybl")
        for oc in range(NCH):
            bank = ctx.banks[2 + (oc % 4)]
            for ki in range(NCC):
                lhsT = ctx.WP.h[:, wo0 + 1024 * ki + 128 * oc:
                                wo0 + 1024 * ki + 128 * (oc + 1)]
                mm(ctx, bank.h[:, :], bank.res, lhsT,
                   ctx.WP.res(wo0 + 1024 * ki, 1024),
                   ctx.AB16.h[:, yb0 + 512 * ki:yb0 + 512 * (ki + 1)],
                   ctx.AB16.res(yb0 + 512 * ki, 512),
                   start=(ki == 0), stop=(ki == NCC - 1), tag=tag + "_o")
            x_ap = X.h[:, 2048 * oc + PB * nt: 2048 * oc + PB * (nt + 1)]
            tt(ctx, "vector", x_ap, [f"X:{oc}"], x_ap, [f"X:{oc}"],
               bank.h[:, :], bank.res, ALU.add, tag=tag + "_xa")
    ctx.reset_pools()




def _row_sincos(ctx, row_ap, row_res, n, tag, slot=0):
    sr = ctx.SROW
    t0 = 2048 + 512 * slot
    m_ap = sr.h[0:1, t0:t0 + n]
    m_res = sr.res(t0, 512)
    sin_ap = sr.h[0:1, t0 + 128:t0 + 128 + n]
    cos_ap = sr.h[0:1, t0 + 256:t0 + 256 + n]
    g_ap = sr.h[0:1, t0 + 384:t0 + 384 + n]
    ts(ctx, "vector", m_ap, m_res, row_ap, row_res, 1024 * np.pi, ALU.add,
       tag=tag + "_sh")
    for k in range(9, -1, -1):
        sub = 2.0 * np.pi * (2 ** k)
        ts(ctx, "vector", g_ap, m_res, m_ap, m_res, float(sub), ALU.is_ge,
           tag=tag + "_rm")
        stt(ctx, m_ap, m_res, g_ap, m_res, float(-sub), m_ap, m_res,
            ALU.mult, ALU.add, tag=tag + "_rs")
    ts(ctx, "vector", g_ap, m_res, m_ap, m_res, np.pi, ALU.is_gt,
       tag=tag + "_gt")
    stt(ctx, m_ap, m_res, g_ap, m_res, -2 * np.pi, m_ap, m_res,
        ALU.mult, ALU.add, tag=tag + "_rr")
    act(ctx, sin_ap, m_res, m_ap, m_res, AF.Sin, tag=tag + "_sin")
    act(ctx, g_ap, m_res, m_ap, m_res, AF.Sin, scale=0.5, tag=tag + "_s2")
    act(ctx, g_ap, m_res, g_ap, m_res, AF.Square, tag=tag + "_sq")
    ts(ctx, "vector", cos_ap, m_res, g_ap, m_res, -2.0, ALU.mult, 1.0,
       ALU.add, tag=tag + "_cos")
    return sin_ap, cos_ap, m_res


def l3_attn(ctx):
    tag = "L3a"
    ln_sublayer(ctx, tag + "ln")
    ctx.reset_pools()
    NB, X = ctx.NB, ctx.X
    H, hd = 8, 128
    inp = ctx.inp
    h3 = ctx.H3.ap()
    ms = ctx.MS.ap()
    wq = inp["L3_attn_W_qkv"].ap()

    ph0 = ctx.SROW.alloc(128)
    phrow = ctx.SROW.h[0:1, ph0:ph0 + 128]
    ph_res = ctx.SROW.res(ph0, 128)
    ctx.P.op("gpsimd",
             lambda g: g.dma_start(
                 out=ctx.SROW.h[0:1, ph0:ph0 + 128].rearrange(
                     "a (x y) -> a x y", x=8),
                 in_=inp["L3_attn_photonic_phases"].ap().rearrange(
                     "a b -> (a b)")[0:16 * 1024].rearrange(
                     "(b q a z) -> (a b z)", b=16, q=8, a=8, z=1)
                 if False else None),
             reads=[], writes=[], is_dma=True, tag="ph_skip") \
        if False else None
    ppflat = inp["L3_attn_photonic_phases"].ap().rearrange("a b -> (a b)")
    src = bass.AP(ppflat.tensor, ppflat.offset,
                  [[1, 1], [16, 8], [1025, 16]])
    ctx.P.op("gpsimd",
             lambda g, s_=src: g.dma_start(
                 out=ctx.SROW.h[0:1, ph0:ph0 + 128].rearrange(
                     "p (a b) -> p a b", a=8),
                 in_=s_, allow_slow_non_contiguous=True),
             reads=["HBM:pp"], writes=list(ph_res), is_dma=True,
             tag=tag + "_ph")
    sphi, cphi, rr1 = _row_sincos(ctx, phrow, ph_res, 128, tag + "_phi")
    c2r0 = ctx.SROW.alloc(128)
    c2row = ctx.SROW.h[0:1, c2r0:c2r0 + 128]
    c2_res = ctx.SROW.res(c2r0, 128)
    act(ctx, c2row, c2_res, sphi, rr1, AF.Square, tag=tag + "_c2s")
    ts(ctx, "vector", c2row, c2_res, c2row, c2_res, -2.0, ALU.mult, 1.0,
       ALU.add, tag=tag + "_c2")
    ccol0 = ctx.SROW.alloc(8)
    scol0 = ctx.SROW.alloc(8)
    c2col0 = ctx.SROW.alloc(8)
    for (rowap, rowres, dst) in ((cphi, rr1, ccol0), (sphi, rr1, scol0),
                                 (c2row, c2_res, c2col0)):
        bk = ctx.pbank(tag)
        mm(ctx, bk.h[:, 0:1], bk.res, rowap, rowres, ctx.ONESF[0:1, 0:1],
           ["CONST"], start=True, stop=True, tag=tag + "_pc")
        copy(ctx, "vector", ctx.SROW.h[:, dst:dst + 1],
             ctx.SROW.res(dst, 8), bk.h[:, 0:1], bk.res, tag=tag + "_pcc")
    sl0 = ctx.SROW.alloc(128)
    sl_ap = ctx.SROW.h[0:1, sl0:sl0 + 4]
    sl_res = ctx.SROW.res(sl0, 128)
    dma(ctx, "sync", sl_ap, sl_res,
        inp["L3_attn_superposition_logits"].ap()[:, :], ["HBM:sup"],
        tag=tag + "_sl")
    ex_ap = ctx.SROW.h[0:1, sl0 + 8:sl0 + 12]
    act(ctx, ex_ap, sl_res, sl_ap, sl_res, AF.Exp, tag=tag + "_se")
    sm_ap = ctx.SROW.h[0:1, sl0 + 16:sl0 + 17]
    reduce(ctx, "vector", sm_ap, sl_res, ex_ap, sl_res, ALU.add,
           tag=tag + "_ss")
    recip(ctx, sm_ap, sl_res, sm_ap, sl_res, tag=tag + "_sr")
    ts(ctx, "vector", ex_ap, sl_res, ex_ap, sl_res, sm_ap, ALU.mult,
       tag=tag + "_sn")
    qw0 = ctx.SROW.alloc(128)
    qw_ap = ctx.SROW.h[0:1, qw0:qw0 + 4]
    qw_res = ctx.SROW.res(qw0, 128)
    qwflat = inp["L3_attn_quantum_weights"].ap().rearrange("a b c -> (a b c)")
    src = bass.AP(qwflat.tensor, qwflat.offset, [[1, 1], [64, 4]])
    ctx.P.op("gpsimd",
             lambda g, s_=src: g.dma_start(out=ctx.SROW.h[0:1, qw0:qw0 + 4],
                                           in_=s_,
                                           allow_slow_non_contiguous=True),
             reads=["HBM:qw"], writes=list(qw_res), is_dma=True,
             tag=tag + "_qw")
    sqw, cqw, rr2 = _row_sincos(ctx, qw_ap, qw_res, 4, tag + "_qw")
    ra_ap = ctx.SROW.h[0:1, qw0 + 8:qw0 + 12]
    ts(ctx, "vector", ra_ap, qw_res, sqw, rr2, 0.3, ALU.mult, 0.5, ALU.add,
       tag=tag + "_ra")
    rb_ap = ctx.SROW.h[0:1, qw0 + 16:qw0 + 20]
    ts(ctx, "vector", rb_ap, qw_res, ra_ap, qw_res, -1.0, ALU.mult, 1.0,
       ALU.add, tag=tag + "_rb")
    al_ap = ctx.SROW.h[0:1, qw0 + 24:qw0 + 28]
    tt(ctx, "vector", al_ap, qw_res, ra_ap, qw_res, rb_ap, qw_res,
       ALU.mult, tag=tag + "_al")
    ts(ctx, "vector", al_ap, qw_res, al_ap, qw_res, float(hd ** -0.5),
       ALU.mult, tag=tag + "_als")
    supc0 = ctx.SROW.alloc(8)
    alc0 = ctx.SROW.alloc(8)
    for c in range(4):
        for (srcap, dst) in ((ex_ap[:, c:c + 1], supc0 + c),
                             (al_ap[:, c:c + 1], alc0 + c)):
            bk = ctx.pbank(tag)
            mm(ctx, bk.h[:, 0:1], bk.res, ctx.ONESF_ROW, ["CONST"],
               srcap, sl_res + qw_res, start=True, stop=True,
               tag=tag + "_cc")
            copy(ctx, "vector", ctx.SROW.h[:, dst:dst + 1],
                 ctx.SROW.res(dst, 8), bk.h[:, 0:1], bk.res,
                 tag=tag + "_ccc")
    gflat = inp["L3_attn_entanglement_gates"].ap().rearrange("a b c -> (a b c)")
    co0 = ctx.SROW.alloc(72)
    gr0 = ctx.SROW.alloc(128)
    for dep in range(3):
        th_ap = ctx.SROW.h[0:1, gr0:gr0 + 7]
        gr_res = ctx.SROW.res(gr0, 128)
        src = bass.AP(gflat.tensor, gflat.offset + 32 * dep, [[1, 1], [4, 7]])
        ctx.P.op("gpsimd",
                 lambda g, s=src, o=th_ap: g.dma_start(
                     out=o, in_=s, allow_slow_non_contiguous=True),
                 reads=["HBM:gates"], writes=list(gr_res), is_dma=True,
                 tag=tag + "_gth")
        ph_ap2 = ctx.SROW.h[0:1, gr0 + 8:gr0 + 15]
        src = bass.AP(gflat.tensor, gflat.offset + 32 * dep + 1,
                      [[1, 1], [4, 7]])
        ctx.P.op("gpsimd",
                 lambda g, s=src, o=ph_ap2: g.dma_start(
                     out=o, in_=s, allow_slow_non_contiguous=True),
                 reads=["HBM:gates"], writes=list(gr_res), is_dma=True,
                 tag=tag + "_gph")
        ts(ctx, "vector", th_ap, gr_res, th_ap, gr_res, 0.5, ALU.mult,
           tag=tag + "_th2")
        sT, cT, rrT = _row_sincos(ctx, th_ap, gr_res, 7, tag + f"_t{dep}",
                                  slot=0)
        sP, cP, rrP = _row_sincos(ctx, ph_ap2, gr_res, 7, tag + f"_p{dep}",
                                  slot=1)
        t1 = ctx.SROW.h[0:1, gr0 + 16:gr0 + 23]
        tt(ctx, "vector", t1, gr_res, sT, rrT, cP, rrP, ALU.mult,
           tag=tag + "_stcp")
        am1 = ctx.SROW.h[0:1, gr0 + 24:gr0 + 31]
        tt(ctx, "vector", am1, gr_res, cT, rrT, t1, gr_res, ALU.add,
           tag=tag + "_a")
        ts(ctx, "vector", am1, gr_res, am1, gr_res, -1.0, ALU.add,
           tag=tag + "_am1")
        dm1 = ctx.SROW.h[0:1, gr0 + 32:gr0 + 39]
        tt(ctx, "vector", dm1, gr_res, cT, rrT, t1, gr_res, ALU.subtract,
           tag=tag + "_d")
        ts(ctx, "vector", dm1, gr_res, dm1, gr_res, -1.0, ALU.add,
           tag=tag + "_dm1")
        bb_ = ctx.SROW.h[0:1, gr0 + 40:gr0 + 47]
        tt(ctx, "vector", bb_, gr_res, sT, rrT, sP, rrP, ALU.mult,
           tag=tag + "_b")
        for p in range(7):
            for k, rowap in enumerate((am1, bb_, dm1)):
                bk = ctx.pbank(tag)
                mm(ctx, bk.h[:, 0:1], bk.res, ctx.ONESF_ROW, ["CONST"],
                   rowap[:, p:p + 1], gr_res, start=True, stop=True,
                   tag=tag + "_cb")
                dst = co0 + (dep * 7 + p) * 3 + k
                copy(ctx, "vector", ctx.SROW.h[:, dst:dst + 1],
                     ctx.SROW.res(dst, 72), bk.h[:, 0:1], bk.res,
                     tag=tag + "_cbc")

    bq0 = ctx.SROW.alloc(24)
    load_vec_cols(ctx, "L3_attn_b_qkv", bq0, 24, tag=tag + "_bq")

    wqt0 = ctx.WP.alloc(2 * 1024)
    stg0 = ctx.AB16.alloc(2 * 1024)
    qraw0 = ctx.AF32.alloc(2048)
    sq0 = ctx.AF32.alloc(2048)
    nr0 = ctx.SROW.alloc(512)
    nrb0 = ctx.SROWB.alloc(512)
    qh0 = ctx.AB16.alloc(2048)
    pl0 = ctx.AB16.alloc(2048)
    ssum0 = ctx.AF32.alloc(2048)
    sg0 = ctx.AB16.alloc(2048)
    for oc in range(24):
        jj = oc % 8
        typ = oc // 8
        sl = stg0 + 1024 * (oc % 2)
        dma(ctx, "gpsimd", ctx.AB16.h[:, sl:sl + 1024],
            ctx.AB16.res(sl, 1024), wq[128 * oc:128 * (oc + 1), :],
            ["HBM:L3wq"], tag=tag + "_wl")
        wsl = wqt0 + 1024 * (oc % 2)
        for ki in range(NCH):
            bk16 = ctx.pbank16(tag)
            ctx.P.op("tensor",
                     lambda t, o=bk16.h[:, 0:128],
                     i=ctx.AB16.h[:, sl + 128 * ki:sl + 128 * (ki + 1)],
                     idn=ctx.IDB: t.transpose(o, i, idn),
                     reads=ctx.AB16.res(sl, 1024) + ["CONST"],
                     writes=bk16.res, tag=tag + "_wt")
            copy(ctx, "vector", ctx.WP.h[:, wsl + 128 * ki:
                                         wsl + 128 * (ki + 1)],
                 ctx.WP.res(wsl, 1024), bk16.h[:, 0:128], bk16.res,
                 tag=tag + "_wtc")
        for nt in range(NT):
            bank = ctx.banks[4 + (nt % 2)]
            for ki in range(NCH):
                lhsT = ctx.WP.h[:, wsl + 128 * ki:wsl + 128 * (ki + 1)]
                mm(ctx, bank.h[:, :], bank.res, lhsT, ctx.WP.res(wsl, 1024),
                   NB.h[:, 2048 * ki + PB * nt:2048 * ki + PB * (nt + 1)],
                   [f"NB:{ki}"], start=(ki == 0), stop=(ki == NCH - 1),
                   tag=tag + "_qkv")
            act(ctx, ctx.AF32.h[:, qraw0 + PB * nt:qraw0 + PB * (nt + 1)],
                ctx.AF32.res(qraw0, 2048), bank.h[:, :], bank.res,
                AF.Identity, bias=ctx.SROW.h[:, bq0 + oc:bq0 + oc + 1],
                extra_reads=ctx.SROW.res(bq0, 24), tag=tag + "_qe")
        act(ctx, ctx.AF32.h[:, sq0:sq0 + 2048], ctx.AF32.res(sq0, 2048),
            ctx.AF32.h[:, qraw0:qraw0 + 2048], ctx.AF32.res(qraw0, 2048),
            AF.Square, tag=tag + "_sq")
        for nt in range(NT):
            bank = ctx.banks[nt % 2]
            mm(ctx, bank.h[0:1, :], bank.res, ctx.ONESF, ["CONST"],
               ctx.AF32.h[:, sq0 + PB * nt:sq0 + PB * (nt + 1)],
               ctx.AF32.res(sq0, 2048), start=True, stop=True,
               tag=tag + "_nm")
            nr_ap = ctx.SROW.h[0:1, nr0 + PB * (nt % 1):nr0 + 512]
            nr_ap = ctx.SROW.h[0:1, nr0:nr0 + 512]
            nr_res = ctx.SROW.res(nr0, 512)
            act(ctx, nr_ap, nr_res, bank.h[0:1, :], bank.res, AF.Sqrt,
                tag=tag + "_nsq")
            ts(ctx, "vector", nr_ap, nr_res, nr_ap, nr_res, 1e-12,
               ALU.max, tag=tag + "_nmx")
            recip(ctx, nr_ap, nr_res, nr_ap, nr_res, tag=tag + "_nrc")
            nrb_ap = ctx.SROWB.h[0:1, nrb0:nrb0 + 512]
            nrb_res = ctx.SROWB.res(nrb0, 512)
            copy(ctx, "vector", nrb_ap, nrb_res, nr_ap, nr_res,
                 tag=tag + "_nrb")
            bk2 = ctx.banks[2 + (nt % 2)]
            mm(ctx, bk2.h[:, :], bk2.res, ctx.ONESB_ROW, ["CONST"],
               nrb_ap, nrb_res, start=True, stop=True, tag=tag + "_nbc")
            tt(ctx, "vector", ctx.AB16.h[:, qh0 + PB * nt:
                                         qh0 + PB * (nt + 1)],
               ctx.AB16.res(qh0, 2048),
               ctx.AF32.h[:, qraw0 + PB * nt:qraw0 + PB * (nt + 1)],
               ctx.AF32.res(qraw0, 2048), bk2.h[:, :], bk2.res, ALU.mult,
               tag=tag + "_nrm")
        if typ == 2:
            dma(ctx, "sync", ms[128 * jj:128 * (jj + 1), :], ["MS:v"],
                ctx.AB16.h[:, qh0:qh0 + 2048], ctx.AB16.res(qh0, 2048),
                tag=tag + "_vst")
            if jj == 0:
                copy(ctx, "vector", ctx.AF32.h[:, ssum0:ssum0 + 2048],
                     ctx.AF32.res(ssum0, 2048),
                     ctx.AB16.h[:, qh0:qh0 + 2048], ctx.AB16.res(qh0, 2048),
                     tag=tag + "_s0")
            else:
                tt(ctx, "vector", ctx.AF32.h[:, ssum0:ssum0 + 2048],
                   ctx.AF32.res(ssum0, 2048),
                   ctx.AF32.h[:, ssum0:ssum0 + 2048],
                   ctx.AF32.res(ssum0, 2048),
                   ctx.AB16.h[:, qh0:qh0 + 2048], ctx.AB16.res(qh0, 2048),
                   ALU.add, tag=tag + "_sa")
        else:
            base = 0 if typ == 0 else 2048
            ts(ctx, "vector", ctx.AB16.h[:, pl0:pl0 + 2048],
               ctx.AB16.res(pl0, 2048), ctx.AB16.h[:, qh0:qh0 + 2048],
               ctx.AB16.res(qh0, 2048), ctx.SROW.h[:, ccol0:ccol0 + 1],
               ALU.mult, extra_reads=ctx.SROW.res(ccol0, 8),
               tag=tag + "_enr")
            dma(ctx, "sync", h3[base + 128 * jj:base + 128 * (jj + 1), :],
                [f"H3:p{typ}r{jj}"], ctx.AB16.h[:, pl0:pl0 + 2048],
                ctx.AB16.res(pl0, 2048), tag=tag + "_str")
            ts(ctx, "vector", ctx.AB16.h[:, pl0:pl0 + 2048],
               ctx.AB16.res(pl0, 2048), ctx.AB16.h[:, qh0:qh0 + 2048],
               ctx.AB16.res(qh0, 2048), ctx.SROW.h[:, scol0:scol0 + 1],
               ALU.mult, extra_reads=ctx.SROW.res(scol0, 8),
               tag=tag + "_eni")
            dma(ctx, "sync", h3[base + 1024 + 128 * jj:
                               base + 1024 + 128 * (jj + 1), :],
                [f"H3:p{typ}i{jj}"], ctx.AB16.h[:, pl0:pl0 + 2048],
                ctx.AB16.res(pl0, 2048), tag=tag + "_sti")
    ts(ctx, "vector", ctx.AB16.h[:, sg0:sg0 + 2048], ctx.AB16.res(sg0, 2048),
       ctx.AF32.h[:, ssum0:ssum0 + 2048], ctx.AF32.res(ssum0, 2048), 0.0,
       ALU.is_ge, tag=tag + "_sge")
    ts(ctx, "vector", ctx.AB16.h[:, sg0:sg0 + 2048], ctx.AB16.res(sg0, 2048),
       ctx.AB16.h[:, sg0:sg0 + 2048], ctx.AB16.res(sg0, 2048), 2.0,
       ALU.mult, -1.0, ALU.add, tag=tag + "_sgn")
    sgt0 = ctx.SROWB.alloc(2048)
    for ti in range(NTT):
        bk16 = ctx.pbank16(tag)
        ctx.P.op("tensor",
                 lambda t, o=bk16.h[:, 0:128],
                 i=ctx.AB16.h[:, sg0 + 128 * ti:sg0 + 128 * (ti + 1)],
                 idn=ctx.IDB: t.transpose(o, i, idn),
                 reads=ctx.AB16.res(sg0, 2048) + ["CONST"],
                 writes=bk16.res, tag=tag + "_sgt")
        copy(ctx, "vector", ctx.SROWB.h[:, sgt0 + 128 * ti:
                                        sgt0 + 128 * (ti + 1)],
             ctx.SROWB.res(sgt0 + 128 * ti, 128), bk16.h[:, 0:128],
             bk16.res, tag=tag + "_sgtc")

    ctx.AB16.reset()
    ctx.AF32.reset()
    s1r0 = ctx.AB16.alloc(2048)
    s1i0 = ctx.AB16.alloc(2048)
    s2r0 = ctx.AB16.alloc(2048)
    s2i0 = ctx.AB16.alloc(2048)
    ct0 = ctx.AF32.alloc(2048)
    d1_0 = ctx.AF32.alloc(2048)
    d2_0 = ctx.AF32.alloc(2048)
    slotR = (s1r0, s2r0)
    slotI = (s1i0, s2i0)
    for typ in range(2):
        base = 2048 * typ
        for dep in range(3):
            for p in range(8):
                cu = p % 2
                pv = (p + 1) % 2
                cr, ci = slotR[cu], slotI[cu]
                pr_r, pr_i = slotR[pv], slotI[pv]
                dma(ctx, "sync", ctx.AB16.h[:, cr:cr + 2048],
                    ctx.AB16.res(cr, 2048),
                    h3[base + 128 * p:base + 128 * (p + 1), :],
                    [f"H3:p{typ}r{p}"], tag=tag + "_elr")
                dma(ctx, "sync", ctx.AB16.h[:, ci:ci + 2048],
                    ctx.AB16.res(ci, 2048),
                    h3[base + 1024 + 128 * p:base + 1024 + 128 * (p + 1), :],
                    [f"H3:p{typ}i{p}"], tag=tag + "_eli")
                if p == 0:
                    continue
                pr = p - 1
                cix = co0 + (dep * 7 + pr) * 3
                am1c = ctx.SROW.h[:, cix:cix + 1]
                bbc = ctx.SROW.h[:, cix + 1:cix + 2]
                dm1c = ctx.SROW.h[:, cix + 2:cix + 3]
                co_res = ctx.SROW.res(co0, 72)
                act(ctx, ctx.AF32.h[:, ct0:ct0 + 2048],
                    ctx.AF32.res(ct0, 2048),
                    ctx.AB16.h[:, pr_r:pr_r + 2048],
                    ctx.AB16.res(pr_r, 2048), AF.Square, tag=tag + "_ec1")
                act(ctx, ctx.AF32.h[:, d1_0:d1_0 + 2048],
                    ctx.AF32.res(d1_0, 2048),
                    ctx.AB16.h[:, pr_i:pr_i + 2048],
                    ctx.AB16.res(pr_i, 2048), AF.Square, tag=tag + "_ec2")
                tt(ctx, "vector", ctx.AF32.h[:, ct0:ct0 + 2048],
                   ctx.AF32.res(ct0, 2048), ctx.AF32.h[:, ct0:ct0 + 2048],
                   ctx.AF32.res(ct0, 2048), ctx.AF32.h[:, d1_0:d1_0 + 2048],
                   ctx.AF32.res(d1_0, 2048), ALU.add, tag=tag + "_ec3")
                act(ctx, ctx.AF32.h[:, d1_0:d1_0 + 2048],
                    ctx.AF32.res(d1_0, 2048), ctx.AB16.h[:, cr:cr + 2048],
                    ctx.AB16.res(cr, 2048), AF.Identity, scale=am1c,
                    extra_reads=co_res, tag=tag + "_dr1")
                stt(ctx, ctx.AF32.h[:, d1_0:d1_0 + 2048],
                    ctx.AF32.res(d1_0, 2048),
                    ctx.AB16.h[:, ci:ci + 2048],
                    ctx.AB16.res(ci, 2048), bbc,
                    ctx.AF32.h[:, d1_0:d1_0 + 2048],
                    ctx.AF32.res(d1_0, 2048), ALU.mult, ALU.add,
                    extra_reads=co_res, tag=tag + "_dr2")
                act(ctx, ctx.AF32.h[:, d2_0:d2_0 + 2048],
                    ctx.AF32.res(d2_0, 2048), ctx.AB16.h[:, ci:ci + 2048],
                    ctx.AB16.res(ci, 2048), AF.Identity, scale=dm1c,
                    extra_reads=co_res, tag=tag + "_di1")
                stt(ctx, ctx.AF32.h[:, d2_0:d2_0 + 2048],
                    ctx.AF32.res(d2_0, 2048),
                    ctx.AB16.h[:, cr:cr + 2048],
                    ctx.AB16.res(cr, 2048), bbc,
                    ctx.AF32.h[:, d2_0:d2_0 + 2048],
                    ctx.AF32.res(d2_0, 2048), ALU.mult, ALU.add,
                    extra_reads=co_res, tag=tag + "_di2")
                tt(ctx, "vector", ctx.AF32.h[:, d1_0:d1_0 + 2048],
                   ctx.AF32.res(d1_0, 2048), ctx.AF32.h[:, ct0:ct0 + 2048],
                   ctx.AF32.res(ct0, 2048), ctx.AF32.h[:, d1_0:d1_0 + 2048],
                   ctx.AF32.res(d1_0, 2048), ALU.mult, tag=tag + "_nr1")
                tt(ctx, "vector", ctx.AB16.h[:, pr_r:pr_r + 2048],
                   ctx.AB16.res(pr_r, 2048), ctx.AB16.h[:, cr:cr + 2048],
                   ctx.AB16.res(cr, 2048), ctx.AF32.h[:, d1_0:d1_0 + 2048],
                   ctx.AF32.res(d1_0, 2048), ALU.add, tag=tag + "_nr2")
                tt(ctx, "gpsimd", ctx.AF32.h[:, d2_0:d2_0 + 2048],
                   ctx.AF32.res(d2_0, 2048), ctx.AF32.h[:, ct0:ct0 + 2048],
                   ctx.AF32.res(ct0, 2048), ctx.AF32.h[:, d2_0:d2_0 + 2048],
                   ctx.AF32.res(d2_0, 2048), ALU.mult, tag=tag + "_ni1")
                tt(ctx, "gpsimd", ctx.AB16.h[:, pr_i:pr_i + 2048],
                   ctx.AB16.res(pr_i, 2048), ctx.AB16.h[:, ci:ci + 2048],
                   ctx.AB16.res(ci, 2048), ctx.AF32.h[:, d2_0:d2_0 + 2048],
                   ctx.AF32.res(d2_0, 2048), ALU.add, tag=tag + "_ni2")
                dma(ctx, "sync", h3[base + 128 * pr:base + 128 * (pr + 1), :],
                    [f"H3:p{typ}r{pr}"], ctx.AB16.h[:, pr_r:pr_r + 2048],
                    ctx.AB16.res(pr_r, 2048), tag=tag + "_esr")
                dma(ctx, "sync", h3[base + 1024 + 128 * pr:
                                   base + 1024 + 128 * (pr + 1), :],
                    [f"H3:p{typ}i{pr}"], ctx.AB16.h[:, pr_i:pr_i + 2048],
                    ctx.AB16.res(pr_i, 2048), tag=tag + "_esi")

    eyej0 = ctx.SROWB.alloc(64)
    memset(ctx, "vector", ctx.SROWB.h[:, eyej0:eyej0 + 64],
           ctx.SROWB.res(eyej0, 64), 1.0, tag=tag + "_eyi")
    for j in range(8):
        ctx.P.op("gpsimd", lambda g, j=j: g.affine_select(
            ctx.SROWB.h[:, eyej0 + 8 * j:eyej0 + 8 * (j + 1)],
            ctx.SROWB.h[:, eyej0 + 8 * j:eyej0 + 8 * (j + 1)],
            [[1, 8]], ALU.is_equal, 0.0, base=-j, channel_multiplier=0),
            reads=ctx.SROWB.res(eyej0, 64), writes=ctx.SROWB.res(eyej0, 64),
            tag=tag + "_eyj")
    ctx.WP.reset()
    bt0 = ctx.WP.alloc(2048)
    bt1 = ctx.WP.alloc(2048)
    qi_r0, qi_i0 = s1r0, s1i0
    ke_r0, ke_i0 = s2r0, s2i0
    pr_0 = ctx.AB16.alloc(1024)
    pi_0 = ctx.AB16.alloc(1024)
    t1_0 = ctx.AB16.alloc(1024)
    er0 = ctx.AB16.alloc(512)
    for i in range(8):
        dma(ctx, "sync", ctx.AB16.h[:, qi_r0:qi_r0 + 2048],
            ctx.AB16.res(qi_r0, 2048), h3[128 * i:128 * (i + 1), :],
            [f"H3:p0r{i}"], tag=tag + "_plq")
        dma(ctx, "sync", ctx.AB16.h[:, qi_i0:qi_i0 + 2048],
            ctx.AB16.res(qi_i0, 2048), h3[1024 + 128 * i:1024 + 128 * (i + 1),
                                          :], [f"H3:p0i{i}"],
            tag=tag + "_plqi")
        for half in range(2):
            hsl = slice(1024 * half, 1024 * (half + 1))
            bre = [ctx.banks[0], ctx.banks[1]]
            bim = [ctx.banks[2], ctx.banks[3]]
            for j in range(8):
                dma(ctx, "sync", ctx.AB16.h[:, ke_r0:ke_r0 + 1024],
                    ctx.AB16.res(ke_r0, 2048),
                    h3[2048 + 128 * j:2048 + 128 * (j + 1), hsl],
                    [f"H3:p1r{j}"], tag=tag + "_plk")
                dma(ctx, "sync", ctx.AB16.h[:, ke_i0:ke_i0 + 1024],
                    ctx.AB16.res(ke_i0, 2048),
                    h3[3072 + 128 * j:3072 + 128 * (j + 1), hsl],
                    [f"H3:p1i{j}"], tag=tag + "_plki")
                qr_ap = ctx.AB16.h[:, qi_r0 + 1024 * half:
                                   qi_r0 + 1024 * (half + 1)]
                qi_ap = ctx.AB16.h[:, qi_i0 + 1024 * half:
                                   qi_i0 + 1024 * (half + 1)]
                kr_ap = ctx.AB16.h[:, ke_r0:ke_r0 + 1024]
                ki_ap = ctx.AB16.h[:, ke_i0:ke_i0 + 1024]
                qres = ctx.AB16.res(qi_r0, 2048) + ctx.AB16.res(qi_i0, 2048)
                kres = ctx.AB16.res(ke_r0, 2048) + ctx.AB16.res(ke_i0, 2048)
                tt(ctx, "vector", ctx.AB16.h[:, pr_0:pr_0 + 1024],
                   ctx.AB16.res(pr_0, 1024), qr_ap, qres, kr_ap, kres,
                   ALU.mult, tag=tag + "_pm1")
                tt(ctx, "gpsimd", ctx.AB16.h[:, t1_0:t1_0 + 1024],
                   ctx.AB16.res(t1_0, 1024), qi_ap, qres, ki_ap, kres,
                   ALU.mult, tag=tag + "_pm2")
                tt(ctx, "vector", ctx.AB16.h[:, pr_0:pr_0 + 1024],
                   ctx.AB16.res(pr_0, 1024), ctx.AB16.h[:, pr_0:pr_0 + 1024],
                   ctx.AB16.res(pr_0, 1024), ctx.AB16.h[:, t1_0:t1_0 + 1024],
                   ctx.AB16.res(t1_0, 1024), ALU.add, tag=tag + "_pa1")
                tt(ctx, "gpsimd", ctx.AB16.h[:, pi_0:pi_0 + 1024],
                   ctx.AB16.res(pi_0, 1024), qr_ap, qres, ki_ap, kres,
                   ALU.mult, tag=tag + "_pm3")
                tt(ctx, "vector", ctx.AB16.h[:, t1_0:t1_0 + 1024],
                   ctx.AB16.res(t1_0, 1024), qi_ap, qres, kr_ap, kres,
                   ALU.mult, tag=tag + "_pm4")
                tt(ctx, "vector", ctx.AB16.h[:, pi_0:pi_0 + 1024],
                   ctx.AB16.res(pi_0, 1024), ctx.AB16.h[:, pi_0:pi_0 + 1024],
                   ctx.AB16.res(pi_0, 1024), ctx.AB16.h[:, t1_0:t1_0 + 1024],
                   ctx.AB16.res(t1_0, 1024), ALU.subtract, tag=tag + "_pa2")
                for sub in range(2):
                    ss = slice(PB * sub, PB * (sub + 1))
                    mm(ctx, bre[sub].h[0:8, :], bre[sub].res,
                       ctx.SROWB.h[:, eyej0 + 8 * j:eyej0 + 8 * (j + 1)],
                       ctx.SROWB.res(eyej0, 64),
                       ctx.AB16.h[:, pr_0 + PB * sub:pr_0 + PB * (sub + 1)],
                       ctx.AB16.res(pr_0, 1024), start=(j == 0),
                       stop=(j == 7), tag=tag + "_prm")
                    mm(ctx, bim[sub].h[0:8, :], bim[sub].res,
                       ctx.SROWB.h[:, eyej0 + 8 * j:eyej0 + 8 * (j + 1)],
                       ctx.SROWB.res(eyej0, 64),
                       ctx.AB16.h[:, pi_0 + PB * sub:pi_0 + PB * (sub + 1)],
                       ctx.AB16.res(pi_0, 1024), start=(j == 0),
                       stop=(j == 7), tag=tag + "_pim")
            for cmp_i, bks in ((0, bre), (1, bim)):
                dstb = bt0 if cmp_i == 0 else bt1
                for sub in range(2):
                    e_ap = ctx.AB16.h[0:8, er0:er0 + 512]
                    e_res = ctx.AB16.res(er0, 512)
                    scopy(ctx, e_ap, e_res, bks[sub].h[0:8, :],
                          bks[sub].res, tag=tag + "_pev")
                    for q4 in range(4):
                        ti = half * 8 + sub * 4 + q4
                        bk16 = ctx.pbank16(tag)
                        ctx.P.op("tensor",
                                 lambda t, o=bk16.h[:, 0:8],
                                 i=e_ap[:, 128 * q4:128 * (q4 + 1)],
                                 idn=ctx.IDB[0:8, 0:8]:
                                 t.transpose(o, i, idn),
                                 reads=list(e_res) + ["CONST"],
                                 writes=bk16.res, tag=tag + "_ptr")
                        copy(ctx, "vector",
                             ctx.WP.h[:, dstb + 128 * i + 8 * ti:
                                      dstb + 128 * i + 8 * (ti + 1)],
                             ctx.WP.res(dstb + 128 * i, 128),
                             bk16.h[:, 0:8], bk16.res, tag=tag + "_ptc")
    vt0 = ctx.WP.alloc(16384)
    ctx.AB16.reset()
    vpl0 = ctx.AB16.alloc(2048)
    for j in range(8):
        dma(ctx, "sync", ctx.AB16.h[:, vpl0:vpl0 + 2048],
            ctx.AB16.res(vpl0, 2048), ms[128 * j:128 * (j + 1), :],
            ["MS:v"], tag=tag + "_vl")
        for ti in range(NTT):
            bk16 = ctx.pbank16(tag)
            ctx.P.op("tensor",
                     lambda t, o=bk16.h[:, 0:128],
                     i=ctx.AB16.h[:, vpl0 + 128 * ti:vpl0 + 128 * (ti + 1)],
                     idn=ctx.IDB: t.transpose(o, i, idn),
                     reads=ctx.AB16.res(vpl0, 2048) + ["CONST"],
                     writes=bk16.res, tag=tag + "_vtr")
            copy(ctx, "vector",
                 ctx.WP.h[:, vt0 + 1024 * ti + 128 * j:
                          vt0 + 1024 * ti + 128 * (j + 1)],
                 ctx.WP.res(vt0 + 1024 * ti + 128 * j, 128),
                 bk16.h[:, 0:128], bk16.res, tag=tag + "_vtc")
    ctx.AB16.reset()
    ctx.AF32.reset()
    amp0 = ctx.AB16.alloc(4096)
    bsq0 = ctx.AF32.alloc(1024)
    ex0 = ctx.AF32.alloc(128)
    sm0 = ctx.SROW.alloc(128)
    for i in range(8):
        bt_re = ctx.WP.h[:, bt0 + 128 * i:bt0 + 128 * (i + 1)]
        bt_im = ctx.WP.h[:, bt1 + 128 * i:bt1 + 128 * (i + 1)]
        btres = ctx.WP.res(bt0 + 128 * i, 128) + \
            ctx.WP.res(bt1 + 128 * i, 128)
        act(ctx, ctx.AF32.h[:, bsq0:bsq0 + 128], ctx.AF32.res(bsq0, 1024),
            bt_re, btres, AF.Square, tag=tag + "_b2r")
        act(ctx, ctx.AF32.h[:, bsq0 + 128:bsq0 + 256],
            ctx.AF32.res(bsq0, 1024), bt_im, btres, AF.Square,
            tag=tag + "_b2i")
        tt(ctx, "vector", ctx.AF32.h[:, bsq0:bsq0 + 128],
           ctx.AF32.res(bsq0, 1024), ctx.AF32.h[:, bsq0:bsq0 + 128],
           ctx.AF32.res(bsq0, 1024), ctx.AF32.h[:, bsq0 + 128:bsq0 + 256],
           ctx.AF32.res(bsq0, 1024), ALU.add, tag=tag + "_b2a")
        for c in range(4):
            act(ctx, ctx.AF32.h[:, ex0:ex0 + 128], ctx.AF32.res(ex0, 128),
                ctx.AF32.h[:, bsq0:bsq0 + 128], ctx.AF32.res(bsq0, 1024),
                AF.Exp, scale=ctx.SROW.h[:, alc0 + c:alc0 + c + 1],
                extra_reads=ctx.SROW.res(alc0, 8), tag=tag + "_pex")
            ctx.P.op("vector",
                     lambda e, o=ctx.SROW.h[:, sm0:sm0 + 16],
                     i=ctx.AF32.h[:, ex0:ex0 + 128].rearrange(
                         "p (a b) -> p a b", b=8):
                     e.tensor_reduce(o.rearrange("p (a b) -> p a b", b=1),
                                     i, AXX, ALU.add),
                     reads=ctx.AF32.res(ex0, 128),
                     writes=ctx.SROW.res(sm0, 128), special=True,
                     tag=tag + "_psm")
            recip(ctx, ctx.SROW.h[:, sm0:sm0 + 16], ctx.SROW.res(sm0, 128),
                  ctx.SROW.h[:, sm0:sm0 + 16], ctx.SROW.res(sm0, 128),
                  tag=tag + "_prc")
            dsta = amp0 + 1024 * c + 128 * i
            for ti in range(NTT):
                ts(ctx, "vector",
                   ctx.AF32.h[:, ex0 + 8 * ti:ex0 + 8 * (ti + 1)],
                   ctx.AF32.res(ex0, 128),
                   ctx.AF32.h[:, ex0 + 8 * ti:ex0 + 8 * (ti + 1)],
                   ctx.AF32.res(ex0, 128),
                   ctx.SROW.h[:, sm0 + ti:sm0 + ti + 1], ALU.mult,
                   extra_reads=ctx.SROW.res(sm0, 128), tag=tag + "_pdv")
            ts(ctx, "vector", ctx.AF32.h[:, ex0:ex0 + 128],
               ctx.AF32.res(ex0, 128), ctx.AF32.h[:, ex0:ex0 + 128],
               ctx.AF32.res(ex0, 128), 1e-12, ALU.add, tag=tag + "_pae")
            act(ctx, ctx.AB16.h[:, dsta:dsta + 128],
                ctx.AB16.res(dsta, 128), ctx.AF32.h[:, ex0:ex0 + 128],
                ctx.AF32.res(ex0, 128), AF.Sqrt, tag=tag + "_pam")
    racc0 = ctx.AF32.alloc(1024)
    gacc0 = ctx.AF32.alloc(1024)
    gt0_ = ctx.AF32.alloc(1024)
    af_amp0 = ctx.AF32.alloc(512)
    mes0 = ctx.AB16.alloc(1024)
    mfm0 = ctx.AB16.alloc(1024)
    ctx.AF32.reset(0)
    raccs = [ctx.AF32.alloc(1024) for _ in range(4)]
    gacc0 = ctx.AF32.alloc(1024)
    gt0_ = ctx.AF32.alloc(1024)
    afs0 = ctx.SROW.alloc(1024)
    for ti in range(NTT):
        for c in range(4):
            slot = afs0 + 128 * (c + 4 * (ti % 2))
            ctx.P.op("scalar",
                     lambda e, o=ctx.SROW.h[:, slot:slot + 64].rearrange(
                         "p (i j) -> p i j", i=8),
                     i_=ctx.AB16.h[:, amp0 + 1024 * c:
                                   amp0 + 1024 * (c + 1)].rearrange(
                         "p (i t) -> p i t", i=8)[:, :, 8 * ti:8 * (ti + 1)]:
                     e.copy(o, i_),
                     reads=ctx.AB16.res(amp0 + 1024 * c, 1024),
                     writes=ctx.SROW.res(slot, 128), tag=tag + "_af")
        for j in range(8):
            for c in range(4):
                rb = raccs[c]
                slot = afs0 + 128 * (c + 4 * (ti % 2))
                a_res = ctx.SROW.res(slot, 128)
                vsl = ctx.WP.h[:, vt0 + 1024 * ti + 128 * j:
                               vt0 + 1024 * ti + 128 * (j + 1)]
                vres = ctx.WP.res(vt0 + 1024 * ti + 128 * j, 128)
                for i in range(8):
                    acol = ctx.SROW.h[:, slot + 8 * i + j:slot + 8 * i + j + 1]
                    r_ap = ctx.AF32.h[:, rb + 128 * i:rb + 128 * (i + 1)]
                    r_res = [f"RC{c}i{i}"]
                    if j == 0:
                        ts(ctx, "vector", r_ap, r_res, vsl, vres, acol,
                           ALU.mult, extra_reads=a_res, tag=tag + "_r0")
                    else:
                        stt(ctx, r_ap, r_res, vsl, vres, acol, r_ap, r_res,
                            ALU.mult, ALU.add, extra_reads=a_res,
                            tag=tag + "_rj")
        for c in range(4):
            rb = raccs[c]
            gb = (gt0_, raccs[0], gt0_, raccs[1])[c]
            r_res = [f"RC{c}i{k}" for k in range(8)]
            g_res = (ctx.AF32.res(gt0_, 1024) if gb is gt0_
                     else [f"RC{(0 if c == 1 else 1)}i{k}" for k in range(8)])
            act(ctx, ctx.AF32.h[:, gb:gb + 1024], g_res,
                ctx.AF32.h[:, rb:rb + 1024], r_res, AF.Abs, tag=tag + "_ga")
            ts(ctx, "gpsimd", ctx.AF32.h[:, gb:gb + 1024], g_res,
               ctx.AF32.h[:, gb:gb + 1024], g_res, 1.0, ALU.min, 1e-6,
               ALU.max, tag=tag + "_gc")
            ctx.P.op("scalar",
                     lambda e, o=ctx.AF32.h[:, rb:rb + 1024],
                     i=ctx.AF32.h[:, rb:rb + 1024]:
                     e.sign(o, i),
                     reads=list(r_res), writes=list(r_res), tag=tag + "_gs")
            tt(ctx, "gpsimd", ctx.AF32.h[:, gb:gb + 1024], g_res,
               ctx.AF32.h[:, gb:gb + 1024], g_res,
               ctx.AF32.h[:, rb:rb + 1024], r_res, ALU.mult,
               tag=tag + "_gm")
            if c == 0:
                ts(ctx, "vector", ctx.AF32.h[:, gacc0:gacc0 + 1024],
                   ctx.AF32.res(gacc0, 1024), ctx.AF32.h[:, gb:gb + 1024],
                   g_res,
                   ctx.SROW.h[:, supc0 + c:supc0 + c + 1], ALU.mult,
                   extra_reads=ctx.SROW.res(supc0, 8), tag=tag + "_gw0")
            else:
                stt(ctx, ctx.AF32.h[:, gacc0:gacc0 + 1024],
                    ctx.AF32.res(gacc0, 1024),
                    ctx.AF32.h[:, gb:gb + 1024], g_res,
                    ctx.SROW.h[:, supc0 + c:supc0 + c + 1],
                    ctx.AF32.h[:, gacc0:gacc0 + 1024],
                    ctx.AF32.res(gacc0, 1024), ALU.mult, ALU.add,
                    extra_reads=ctx.SROW.res(supc0, 8), tag=tag + "_gwc")
        for i in range(8):
            tt(ctx, "vector", ctx.AB16.h[:, mes0 + 128 * i:
                                         mes0 + 128 * (i + 1)],
               ctx.AB16.res(mes0, 1024),
               ctx.AF32.h[:, gacc0 + 128 * i:gacc0 + 128 * (i + 1)],
               ctx.AF32.res(gacc0, 1024),
               ctx.SROWB.h[:, sgt0 + 128 * ti:sgt0 + 128 * (ti + 1)],
               ctx.SROWB.res(sgt0 + 128 * ti, 128), ALU.mult,
               tag=tag + "_me")
        for fc in range(NCH):
            bk16 = ctx.pbank16(tag)
            ctx.P.op("tensor",
                     lambda t, o=bk16.h[:, 0:128],
                     i=ctx.AB16.h[:, mes0 + 128 * fc:mes0 + 128 * (fc + 1)],
                     idn=ctx.IDB: t.transpose(o, i, idn),
                     reads=ctx.AB16.res(mes0, 1024) + ["CONST"],
                     writes=bk16.res, tag=tag + "_mtr")
            copy(ctx, "vector", ctx.AB16.h[:, mfm0 + 128 * fc:
                                           mfm0 + 128 * (fc + 1)],
                 ctx.AB16.res(mfm0 + 128 * fc, 128), bk16.h[:, 0:128],
                 bk16.res, tag=tag + "_mtc")
            dma(ctx, "sync", ms[1024 + 128 * fc:1024 + 128 * (fc + 1),
                               128 * ti:128 * (ti + 1)], ["MS:m"],
                ctx.AB16.h[:, mfm0 + 128 * fc:mfm0 + 128 * (fc + 1)],
                ctx.AB16.res(mfm0 + 128 * fc, 128), tag=tag + "_mst")
    ctx.AB16.reset()
    wo_t0 = ctx.AB16.alloc(8192)
    wstg0 = ctx.AB16.alloc(1024)
    wout = inp["L3_attn_W_out"].ap()
    for fc in range(NCH):
        dma(ctx, "gpsimd", ctx.AB16.h[:, wstg0:wstg0 + 1024],
            ctx.AB16.res(wstg0, 1024), wout[128 * fc:128 * (fc + 1), :],
            ["HBM:L3wo"], tag=tag + "_wol")
        for oc in range(NCH):
            bk16 = ctx.pbank16(tag)
            ctx.P.op("tensor",
                     lambda t, o=bk16.h[:, 0:128],
                     i=ctx.AB16.h[:, wstg0 + 128 * oc:wstg0 + 128 * (oc + 1)],
                     idn=ctx.IDB: t.transpose(o, i, idn),
                     reads=ctx.AB16.res(wstg0, 1024) + ["CONST"],
                     writes=bk16.res, tag=tag + "_wot")
            ts(ctx, "vector",
               ctx.AB16.h[:, wo_t0 + 1024 * fc + 128 * oc:
                          wo_t0 + 1024 * fc + 128 * (oc + 1)],
               ctx.AB16.res(wo_t0 + 1024 * fc + 128 * oc, 128),
               bk16.h[:, 0:128], bk16.res,
               ctx.SROW.h[:, c2col0:c2col0 + 1], ALU.mult,
               extra_reads=ctx.SROW.res(c2col0, 8), tag=tag + "_wos")
    bo0 = ctx.SROW.alloc(8)
    ctx.AF32.reset(af_amp0)
    load_vec_cols(ctx, "L3_attn_b_out", bo0, 8, tag=tag + "_bo")
    mb0_ = wstg0
    ctx.AB16.alloc(12288 - ctx.AB16.pos)
    for nt in range(NT):
        for fc in range(NCH):
            dma(ctx, "sync", ctx.AB16.h[:, mb0_ + 512 * fc:
                                        mb0_ + 512 * (fc + 1)],
                ctx.AB16.res(mb0_ + 512 * fc, 512),
                ms[1024 + 128 * fc:1024 + 128 * (fc + 1),
                   PB * nt:PB * (nt + 1)], ["MS:m"], tag=tag + "_mld")
        for oc in range(NCH):
            bank = ctx.banks[4 + (oc % 2)]
            for fc in range(NCH):
                lhsT = ctx.AB16.h[:, wo_t0 + 1024 * oc + 128 * fc:
                                  wo_t0 + 1024 * oc + 128 * (fc + 1)]
                lres = ctx.AB16.res(wo_t0 + 1024 * oc + 128 * fc, 128)
                mm(ctx, bank.h[:, :], bank.res, lhsT, lres,
                   ctx.AB16.h[:, mb0_ + 512 * fc:mb0_ + 512 * (fc + 1)],
                   ctx.AB16.res(mb0_ + 512 * fc, 512),
                   start=(fc == 0), stop=(fc == NCH - 1), tag=tag + "_om")
            x_ap = X.h[:, 2048 * oc + PB * nt:2048 * oc + PB * (nt + 1)]
            stt(ctx, x_ap, [f"X:{oc}"], bank.h[:, :], bank.res,
                ctx.SROW.h[:, bo0 + oc:bo0 + oc + 1], x_ap, [f"X:{oc}"],
                ALU.add, ALU.add, extra_reads=ctx.SROW.res(bo0, 8),
                tag=tag + "_ox")
    ctx.reset_pools()





def l2_ffn_g(ctx, app):
    tag = f"L2g{app}"
    C2 = ctx.C2
    EC = 8 * C2
    WIN = 256
    NWIN = C2 // WIN
    ln_sublayer(ctx, tag + "ln")
    ctx.reset_pools()
    NB, X = ctx.NB, ctx.X
    E = 8
    CAP = 1280
    inp = ctx.inp
    out2 = ctx.MS.ap().rearrange("a (b c) -> (a b) c", c=1024)
    gt0 = ctx.SROWB.alloc(64)
    for ki in range(NCH):
        dst = ctx.SROWB.h[:, gt0 + 8 * ki:gt0 + 8 * (ki + 1)]
        dma(ctx, "gpsimd", dst, ctx.SROWB.res(gt0, 64),
            inp["L2_ffn_gate_w"].ap()[128 * ki:128 * (ki + 1), :],
            ["HBM:L2g"], tag=tag + "_g")
    m8_0 = ctx.SROWB.alloc(2048)
    M8 = ctx.SROWB.h[0:8, m8_0:m8_0 + 2048]
    M8_res = ctx.SROWB.res(m8_0, 2048)
    nm0 = ctx.SROW.alloc(128)
    nm_res = ctx.SROW.res(nm0, 128)
    t80 = ctx.SROW.alloc(16)
    t8_res = ctx.SROW.res(t80, 16)
    sv0 = ctx.AB16.alloc(32)
    ohall0 = ctx.AB16.alloc(128)
    pos0 = ctx.AF32.alloc(2048)
    pk8_0 = ctx.AF32.alloc(2048)
    dst0 = ctx.AF32.alloc(2048)
    OUT0 = pk8_0
    PK8 = ctx.AF32.h[0:8, pk8_0:pk8_0 + 2048]
    PK8_res = ctx.AF32.res(pk8_0, 2048)
    for tt_i in range(NTT):
        bank = ctx.pbank(tag)
        for ki in range(NCH):
            lhsT = NB.h[:, 2048 * ki + 128 * tt_i: 2048 * ki + 128 * (tt_i + 1)]
            mm(ctx, bank.h[:, 0:8], bank.res, lhsT, [f"NB:{ki}"],
               ctx.SROWB.h[:, gt0 + 8 * ki:gt0 + 8 * (ki + 1)],
               ctx.SROWB.res(gt0, 64),
               start=(ki == 0), stop=(ki == NCH - 1), tag=tag + "_rl")
        nm_ap = ctx.SROW.h[:, nm0:nm0 + 1]
        reduce(ctx, "vector", nm_ap, nm_res, bank.h[:, 0:8], bank.res,
               ALU.max, negate=True, tag=tag + "_nmax")
        ex_ap = ctx.SROW.h[:, t80:t80 + 8]
        ex_res = t8_res
        sm_ap = ctx.SROW.h[:, nm0 + 1:nm0 + 2]
        act(ctx, ex_ap, ex_res, bank.h[:, 0:8], bank.res, AF.Exp,
            bias=nm_ap, extra_reads=nm_res, accum_out=sm_ap,
            accum_res=nm_res, tag=tag + "_exp")
        rc_ap = ctx.SROW.h[:, nm0 + 2:nm0 + 3]
        recip(ctx, rc_ap, nm_res, sm_ap, nm_res, tag=tag + "_rc")
        sc_ap = ctx.SROW.h[:, t80 + 8:t80 + 16]
        ts(ctx, "vector", sc_ap, ex_res, ex_ap, ex_res, rc_ap, ALU.mult,
           extra_reads=nm_res, tag=tag + "_sc")
        mx_ap = ctx.SROW.h[:, nm0 + 3:nm0 + 4]
        reduce(ctx, "vector", mx_ap, nm_res, sc_ap, ex_res, ALU.max,
               tag=tag + "_mx")
        oh_ap = ctx.AB16.h[:, ohall0 + 8 * tt_i:ohall0 + 8 * (tt_i + 1)]
        oh_res = ctx.AB16.res(ohall0, 128)
        ts(ctx, "vector", oh_ap, oh_res, sc_ap, ex_res, mx_ap, ALU.is_ge,
           extra_reads=nm_res, tag=tag + "_oh")
        pk_ap = ctx.AB16.h[:, sv0:sv0 + 8]
        pk_res = ctx.AB16.res(sv0, 32)
        tt(ctx, "vector", pk_ap, pk_res, sc_ap, ex_res, oh_ap, oh_res,
           ALU.mult, tag=tag + "_pk")
        bank2 = ctx.pbank16(tag)
        ctx.P.op("tensor",
                 lambda t, o=bank2.h[0:8, 0:128], i=oh_ap,
                 idn=ctx.IDB: t.transpose(o, i, idn),
                 reads=list(oh_res) + ["CONST"], writes=bank2.res,
                 tag=tag + "_otr")
        copy(ctx, "vector", M8[:, 128 * tt_i:128 * (tt_i + 1)], M8_res,
             bank2.h[0:8, 0:128], bank2.res, tag=tag + "_m8")
        bank3 = ctx.pbank16(tag)
        ctx.P.op("tensor",
                 lambda t, o=bank3.h[0:8, 0:128], i=pk_ap,
                 idn=ctx.IDB: t.transpose(o, i, idn),
                 reads=list(pk_res) + ["CONST"], writes=bank3.res,
                 tag=tag + "_ptr")
        copy(ctx, "vector", PK8[:, 128 * tt_i:128 * (tt_i + 1)], PK8_res,
             bank3.h[0:8, 0:128], bank3.res, tag=tag + "_pk8")
    cnt_ap = ctx.SROW.h[0:8, nm0 + 4:nm0 + 5]
    reduce(ctx, "vector", cnt_ap, nm_res, M8, M8_res, ALU.add,
           tag=tag + "_cnt")
    bank = ctx.pbank(tag)
    transp(ctx, bank.h[0:1, 0:8], bank.res, cnt_ap, nm_res,
           ctx.IDF[0:8, 0:8], tag=tag + "_ctr")
    crow_ap = ctx.SROW.h[0:1, nm0 + 8:nm0 + 16]
    copy(ctx, "vector", crow_ap, nm_res, bank.h[0:1, 0:8], bank.res,
         tag=tag + "_crc")
    dma(ctx, "sync", ctx.CCIN.ap()[0:1, 0:8], ["CCIN"], crow_ap, nm_res,
        tag=tag + "_cst")
    ctx.P.op("gpsimd",
             lambda g: g.collective_compute(
                 "AllGather", ALU.bypass,
                 ins=[ctx.CCIN.ap().opt()], outs=[ctx.CCOUT.ap().opt()],
                 replica_groups=[list(range(NCORES))]),
             reads=["CCIN"], writes=["CCOUT"], is_coll=True,
             tag=tag + "_ag")
    cc0 = ctx.SROW.alloc(8)
    cc_ap = ctx.SROW.h[:, cc0:cc0 + 8]
    cc_res = ctx.SROW.res(cc0, 8)
    dma(ctx, "sync", cc_ap, cc_res, ctx.CCOUT.ap()[:, 0:8], ["CCOUT"],
        tag=tag + "_cld")
    bank = ctx.pbank(tag)
    mm(ctx, bank.h[0:1, 0:8], bank.res, ctx.RMASK[:, 0:1], ["RMASK"],
       cc_ap[:, 0:8], cc_res, start=True, stop=True, tag=tag + "_pfx")
    pfrow_ap = ctx.SROW.h[0:1, nm0 + 24:nm0 + 32]
    copy(ctx, "vector", pfrow_ap, nm_res, bank.h[0:1, 0:8], bank.res,
         tag=tag + "_pfr")
    bank2 = ctx.pbank(tag)
    transp(ctx, bank2.h[0:8, 0:1], bank2.res, pfrow_ap, nm_res,
           ctx.IDF[0:1, 0:1], tag=tag + "_pftr")
    pfx_ap = ctx.SROW.h[0:8, nm0 + 16:nm0 + 17]
    copy(ctx, "vector", pfx_ap, nm_res, bank2.h[0:8, 0:1], bank2.res,
         tag=tag + "_pfc")
    POS = ctx.AF32.h[0:8, pos0:pos0 + 2048]
    POS_res = ctx.AF32.res(pos0, 2048)
    ctx.P.op("vector",
             lambda e: e.tensor_tensor_scan(POS, M8, M8, 0.0, ALU.add,
                                            ALU.bypass),
             reads=list(M8_res), writes=list(POS_res), special=True,
             tag=tag + "_scan")
    DST = ctx.AF32.h[0:8, dst0:dst0 + 2048]
    DST_res = ctx.AF32.res(dst0, 2048)
    KW = PK8
    KW_res = PK8_res
    ts(ctx, "vector", DST, DST_res, POS, POS_res, pfx_ap, ALU.add,
       extra_reads=nm_res, tag=tag + "_gp")
    ts(ctx, "vector", DST, DST_res, DST, DST_res, CAP + 0.5, ALU.is_lt,
       tag=tag + "_klt")
    tt(ctx, "vector", KW, KW_res, KW, KW_res, DST, DST_res, ALU.mult,
       tag=tag + "_kp")
    ts(ctx, "vector", KW, KW_res, KW, KW_res, 0.5, ALU.mult,
       tag=tag + "_kp5")
    ecol_ap = ctx.SROW.h[0:8, nm0 + 32:nm0 + 33]
    ctx.P.op("gpsimd", lambda g: g.iota(
        ecol_ap, [[0, 1]], base=0, channel_multiplier=1,
        allow_small_or_imprecise_dtypes=True),
        reads=(), writes=nm_res, tag=tag + "_ec")
    ts(ctx, "vector", ecol_ap, nm_res, ecol_ap, nm_res, float(C2),
       ALU.mult, tag=tag + "_ecm")
    ts(ctx, "vector", DST, DST_res, POS, POS_res, -1.0, ALU.add,
       tag=tag + "_d1")
    OVF = POS
    OVF_res = POS_res
    ts(ctx, "vector", OVF, OVF_res, DST, DST_res, C2 - 0.5, ALU.is_gt,
       tag=tag + "_ov")
    stt(ctx, DST, DST_res, OVF, OVF_res, 1.0e7, DST, DST_res, ALU.mult,
        ALU.add, tag=tag + "_ovs")
    ts(ctx, "vector", DST, DST_res, DST, DST_res, ecol_ap, ALU.add,
       extra_reads=nm_res, tag=tag + "_de")
    stt(ctx, DST, DST_res, M8, M8_res, -1.0e7, DST, DST_res, ALU.mult,
        ALU.add, tag=tag + "_dm1")
    ts(ctx, "vector", DST, DST_res, DST, DST_res, 1.0e7, ALU.add,
       tag=tag + "_dm2")
    dcols0 = ctx.SROW.alloc(16)
    dsr0 = pos0
    DSR = ctx.AF32.h[0:1, dsr0:dsr0 + 2048]
    DSR_res = ctx.AF32.res(dsr0, 2048)
    pkcols0 = ctx.SROW.alloc(16)
    xtok0 = ctx.WP.alloc(16384)
    for tt_i in range(NTT):
        bank = ctx.pbank(tag)
        transp(ctx, bank.h[0:128, 0:8], bank.res,
               DST[:, 128 * tt_i:128 * (tt_i + 1)], DST_res,
               ctx.IDF[0:8, 0:8], tag=tag + "_dtr")
        dcol = ctx.SROW.h[:, dcols0 + tt_i:dcols0 + tt_i + 1]
        dcol_res = ctx.SROW.res(dcols0, 16)
        ctx.P.op("vector",
                 lambda e, o=dcol, i=bank.h[0:128, 0:8]:
                 e.tensor_reduce(o, i, AXX, ALU.min),
                 reads=bank.res, writes=dcol_res, special=True,
                 tag=tag + "_dmin")
        bank2 = ctx.pbank(tag)
        transp(ctx, bank2.h[0:1, 0:128], bank2.res, dcol, dcol_res,
               ctx.IDF, tag=tag + "_drow")
        copy(ctx, "vector", DSR[:, 128 * tt_i:128 * (tt_i + 1)], DSR_res,
             bank2.h[0:1, 0:128], bank2.res, tag=tag + "_drc")
        bank3 = ctx.pbank(tag)
        transp(ctx, bank3.h[0:128, 0:8], bank3.res,
               KW[:, 128 * tt_i:128 * (tt_i + 1)], KW_res,
               ctx.IDF[0:8, 0:8], tag=tag + "_ktr")
        pkcol = ctx.SROW.h[:, pkcols0 + tt_i:pkcols0 + tt_i + 1]
        pk_resc = ctx.SROW.res(pkcols0, 16)
        reduce(ctx, "vector", pkcol, pk_resc, bank3.h[0:128, 0:8],
               bank3.res, ALU.add, tag=tag + "_ks")
        for fc in range(NCH):
            bk16 = ctx.pbank16(tag)
            ctx.P.op("tensor",
                     lambda t, o=bk16.h[:, 0:128],
                     i=NB.h[:, 2048 * fc + 128 * tt_i:
                            2048 * fc + 128 * (tt_i + 1)],
                     idn=ctx.IDB: t.transpose(o, i, idn),
                     reads=[f"NB:{fc}"] + ["CONST"], writes=bk16.res,
                     tag=tag + "_xtr")
            copy(ctx, "vector",
                 ctx.WP.h[:, xtok0 + 1024 * tt_i + 128 * fc:
                          xtok0 + 1024 * tt_i + 128 * (fc + 1)],
                 ctx.WP.res(xtok0 + 1024 * tt_i + 128 * fc, 128),
                 bk16.h[:, 0:128], bk16.res, tag=tag + "_xtc")
    sio0 = ctx.SROW.alloc(256)
    ctx.P.op("gpsimd", lambda g: g.iota(
        ctx.SROW.h[:, sio0:sio0 + 256], [[1, 256]], base=0,
        channel_multiplier=0, allow_small_or_imprecise_dtypes=True),
        reads=(), writes=ctx.SROW.res(sio0, 256), tag=tag + "_sio")
    blk0 = ctx.AB16.alloc(4096)
    gst0 = ctx.AB16.alloc(2048)
    shc0 = ctx.SROW.alloc(16)
    gl = ctx.GL.ap()
    w1f = inp["L2_ffn_w1"].ap().rearrange("e k m -> (e k) m")
    w2f = inp["L2_ffn_w2"].ap().rearrange("e k m -> (e k) m")
    for e in range(E):
        for w in range(NWIN):
            sbase = C2 * e + WIN * w
            for tt_i in range(NTT):
                shc = ctx.SROW.h[:, shc0 + tt_i % 16:shc0 + tt_i % 16 + 1]
                shc_res = ctx.SROW.res(shc0, 16)
                ts(ctx, "vector", shc, shc_res,
                   ctx.SROW.h[:, dcols0 + tt_i:dcols0 + tt_i + 1],
                   ctx.SROW.res(dcols0, 16), float(-sbase), ALU.add,
                   tag=tag + "_sh")
                ts(ctx, "vector",
                   ctx.AB16.h[:, blk0 + 256 * tt_i:blk0 + 256 * (tt_i + 1)],
                   ctx.AB16.res(blk0 + 256 * tt_i, 256),
                   ctx.SROW.h[:, sio0:sio0 + 256],
                   ctx.SROW.res(sio0, 256), shc, ALU.is_equal,
                   ctx.SROW.h[:, pkcols0 + tt_i:pkcols0 + tt_i + 1],
                   ALU.mult,
                   extra_reads=shc_res + ctx.SROW.res(pkcols0, 16),
                   tag=tag + "_blk")
            for pas in range(2):
                for dc4 in range(4):
                    dc = 4 * pas + dc4
                    bank = ctx.banks[dc4]
                    for tt_i in range(NTT):
                        lhsT = ctx.WP.h[:, xtok0 + 1024 * tt_i + 128 * dc:
                                        xtok0 + 1024 * tt_i + 128 * (dc + 1)]
                        lres = ctx.WP.res(xtok0 + 1024 * tt_i + 128 * dc,
                                          128)
                        mm(ctx, bank.h[:, 0:256], bank.res, lhsT, lres,
                           ctx.AB16.h[:, blk0 + 256 * tt_i:
                                      blk0 + 256 * (tt_i + 1)],
                           ctx.AB16.res(blk0 + 256 * tt_i, 256),
                           start=(tt_i == 0), stop=(tt_i == NTT - 1),
                           tag=tag + "_gm")
                for dc4 in range(4):
                    dc = 4 * pas + dc4
                    gsl = gst0 + 256 * (4 * pas + dc4)
                    scopy(ctx, ctx.AB16.h[:, gsl:gsl + 256],
                          ctx.AB16.res(gsl, 256),
                          ctx.banks[dc4].h[:, 0:256], ctx.banks[dc4].res,
                          tag=tag + "_ge")
                    dma(ctx, "sync",
                        gl[1024 * e + 128 * dc:1024 * e + 128 * (dc + 1),
                           256 * w:256 * (w + 1)],
                        [f"GL:{e}"], ctx.AB16.h[:, gsl:gsl + 256],
                        ctx.AB16.res(gsl, 256), tag=tag + "_gs")
    ctx.WP.reset()
    ctx.AB16.reset()
    ctx.SROWB.reset()
    w1q0 = ctx.WP.alloc(2 * 8192)
    gw0 = ctx.AB16.alloc(2 * 4096)
    hs0 = ctx.AB16.alloc(4096)
    oev0 = ctx.SROWB.alloc(2 * 512)
    NBh, NBres = ctx.NB.h, ctx.NB.res
    for e in range(E):
        gsl = gw0 + 4096 * (e % 2)
        dma(ctx, "sync",
            ctx.AB16.h[:, gsl:gsl + 4096].rearrange("p (k s) -> p k s",
                                                    k=NCH),
            ctx.AB16.res(gsl, 4096),
            gl[1024 * e:1024 * (e + 1), :].rearrange("(k p) s -> p k s",
                                                     p=128),
            [f"GL:{e}"], tag=tag + "_gld")
        for qq in range(4):
            w1b = w1q0 + 8192 * (qq % 2)
            w2b = 8192 * (qq % 2)
            for ki in range(NCH):
                dma(ctx, "gpsimd",
                    ctx.WP.h[:, w1b + 1024 * ki:w1b + 1024 * (ki + 1)],
                    ctx.WP.res(w1b + 1024 * ki, 1024),
                    w1f[1024 * e + 128 * ki:1024 * e + 128 * (ki + 1),
                        1024 * qq:1024 * (qq + 1)],
                    ["HBM:L2w1"], tag=tag + "_w1")
            for mi in range(NCH):
                dma(ctx, "gpsimd",
                    NBh[:, w2b + 1024 * mi:w2b + 1024 * (mi + 1)],
                    NBres(w2b + 1024 * mi, 1024),
                    w2f[4096 * e + 1024 * qq + 128 * mi:
                        4096 * e + 1024 * qq + 128 * (mi + 1), :],
                    ["HBM:L2w2"], tag=tag + "_w2")
            for mi in range(NCH):
                b1 = ctx.banks[4 + (mi % 2)]
                for ki in range(NCH):
                    mm(ctx, b1.h[:, :], b1.res,
                       ctx.WP.h[:, w1b + 1024 * ki + 128 * mi:
                                w1b + 1024 * ki + 128 * (mi + 1)],
                       ctx.WP.res(w1b + 1024 * ki, 1024),
                       ctx.AB16.h[:, gsl + 512 * ki:gsl + 512 * (ki + 1)],
                       ctx.AB16.res(gsl, 4096),
                       start=(ki == 0), stop=(ki == NCH - 1),
                       tag=tag + "_h")
                act(ctx, ctx.AB16.h[:, hs0 + 512 * mi:hs0 + 512 * (mi + 1)],
                    ctx.AB16.res(hs0 + 512 * mi, 512), b1.h[:, :], b1.res,
                    AF.Relu, tag=tag + "_relu")
            for doh in range(2):
                for mi in range(NCH):
                    for st in range(4):
                        bank = ctx.banks[st]
                        mm(ctx, bank.h[:, :], bank.res,
                           ctx.AB16.h[:, hs0 + 512 * mi + 128 * st:
                                      hs0 + 512 * mi + 128 * (st + 1)],
                           ctx.AB16.res(hs0 + 512 * mi, 512),
                           NBh[:, w2b + 1024 * mi + 512 * doh:
                               w2b + 1024 * mi + 512 * (doh + 1)],
                           NBres(w2b + 1024 * mi, 1024),
                           start=(mi == 0), stop=(mi == NCH - 1),
                           tag=tag + "_o")
                for st in range(4):
                    oa = OUT0 + 1024 * st + 512 * doh
                    oa_ap = ctx.AF32.h[:, oa:oa + 512]
                    oa_res = ctx.AF32.res(oa, 512)
                    if qq == 0:
                        scopy(ctx, oa_ap, oa_res, ctx.banks[st].h[:, :],
                              ctx.banks[st].res, tag=tag + "_oi")
                    else:
                        tt(ctx, "vector", oa_ap, oa_res, oa_ap, oa_res,
                           ctx.banks[st].h[:, :], ctx.banks[st].res,
                           ALU.add, tag=tag + "_oa")
        for st in range(4):
            for doh in range(2):
                oa = OUT0 + 1024 * st + 512 * doh
                osl = oev0 + 512 * (doh % 2)
                scopy(ctx, ctx.SROWB.h[:, osl:osl + 512],
                      ctx.SROWB.res(osl, 512),
                      ctx.AF32.h[:, oa:oa + 512], ctx.AF32.res(oa, 512),
                      tag=tag + "_oe")
                dma(ctx, "sync",
                    out2[C2 * e + 128 * st:C2 * e + 128 * (st + 1),
                         512 * doh:512 * (doh + 1)],
                    [f"OUT2:{(4 * e + st) % 8}"],
                    ctx.SROWB.h[:, osl:osl + 512], ctx.SROWB.res(osl, 512),
                    tag=tag + "_ost")
    iotc0 = ctx.SROW.alloc(8)
    ctx.P.op("gpsimd", lambda g: g.iota(
        ctx.SROW.h[:, iotc0:iotc0 + 1], [[0, 1]], base=0,
        channel_multiplier=1, allow_small_or_imprecise_dtypes=True),
        reads=(), writes=ctx.SROW.res(iotc0, 8), tag=tag + "_ioc")
    ctx.AB16.reset()
    ctx.WP.reset()
    osl0 = ctx.AB16.alloc(2 * 4096)
    qb0 = ctx.WP.alloc(16384)
    bcd0 = pk8_0
    shs0 = ctx.SROW.alloc(8)
    for nt in range(NT):
        bankb = ctx.pbank(tag)
        mm(ctx, bankb.h[:, :], bankb.res, ctx.ONESF_ROW, ["CONST"],
           DSR[:, PB * nt:PB * (nt + 1)], DSR_res, start=True, stop=True,
           tag=tag + "_qbc")
        scopy(ctx, ctx.AF32.h[:, bcd0:bcd0 + 512],
              ctx.AF32.res(bcd0, 512), bankb.h[:, :], bankb.res,
              tag=tag + "_qbe")
        for sc in range(EC // 128):
            shs = ctx.SROW.h[:, shs0:shs0 + 1]
            shs_res = ctx.SROW.res(shs0, 8)
            ts(ctx, "vector", shs, shs_res,
               ctx.SROW.h[:, iotc0:iotc0 + 1], ctx.SROW.res(iotc0, 8),
               1.0, ALU.mult, float(128 * sc), ALU.add, tag=tag + "_qsh")
            ts(ctx, "vector",
               ctx.WP.h[:, qb0 + 512 * sc:qb0 + 512 * (sc + 1)],
               ctx.WP.res(qb0 + 512 * sc, 512),
               ctx.AF32.h[:, bcd0:bcd0 + 512], ctx.AF32.res(bcd0, 512),
               shs, ALU.is_equal, extra_reads=shs_res, tag=tag + "_qeq")
        for do in range(NCH):
            dma(ctx, "sync",
                ctx.AB16.h[:, osl0 + 4096 * (do % 2):
                           osl0 + 4096 * (do % 2) + 4096].rearrange(
                               "p (s c) -> p s c", s=EC // 128),
                ctx.AB16.res(osl0 + 4096 * (do % 2), 4096),
                out2[:, 128 * do:128 * (do + 1)].rearrange(
                    "(s p) c -> p s c", p=128),
                [f"OUT2:{g8}" for g8 in range(8)], tag=tag + "_osl")
            bank = ctx.banks[4 + (do % 2)]
            for sc in range(EC // 128):
                lhsT = ctx.AB16.h[:, osl0 + 4096 * (do % 2) + 128 * sc:
                                  osl0 + 4096 * (do % 2) + 128 * (sc + 1)]
                lres = ctx.AB16.res(osl0 + 4096 * (do % 2) + 128 * sc, 128)
                mm(ctx, bank.h[:, :], bank.res, lhsT, lres,
                   ctx.WP.h[:, qb0 + 512 * sc:qb0 + 512 * (sc + 1)],
                   ctx.WP.res(qb0 + 512 * sc, 512),
                   start=(sc == 0), stop=(sc == EC // 128 - 1),
                   tag=tag + "_rm")
            x_ap = X.h[:, 2048 * do + PB * nt:2048 * do + PB * (nt + 1)]
            tt(ctx, "vector", x_ap, [f"X:{do}"], x_ap, [f"X:{do}"],
               bank.h[:, :], bank.res, ALU.add, tag=tag + "_xa")
    ctx.reset_pools()




SUBLAYERS = {
    "L0a": l0_attn,
    "L1a": l1_attn,
    "L3a": l3_attn,
    "L2f1": lambda ctx: l2_ffn_g(ctx, 1),
    "L2f2": lambda ctx: l2_ffn_g(ctx, 2),
    "L1f1": lambda ctx: l1_ffn(ctx, 1),
    "L1f2": lambda ctx: l1_ffn(ctx, 2),
    "L0f1": lambda ctx: l0_ffn(ctx, 1),
    "L0f2": lambda ctx: l0_ffn(ctx, 2),
    "L3f1": lambda ctx: l3_ffn(ctx, 1),
    "L3f2": lambda ctx: l3_ffn(ctx, 2),
    "L2a": l2_attn,
}

ALL_SUBS = ["L0f1", "L0a", "L0f2", "L1f1", "L1a", "L1f2",
            "L2f1", "L2a", "L2f2", "L3f1", "L3a", "L3f2"]


def needed_inputs(subs):
    pref = {"L0f": "L0_ffn", "L0a": "L0_attn", "L1f": "L1_ffn",
            "L1a": "L1_attn", "L2f": "L2_ffn", "L2a": "L2_attn",
            "L3f": "L3_ffn", "L3a": "L3_attn"}
    need = set()
    for s in subs:
        p = pref[s[:3]]
        for name in INPUT_SPECS:
            if name.startswith(p):
                need.add(name)
    return sorted(need)


def build_program(subs):
    nc = bass.Bass("TRN2", target_bir_lowering=False, debug=False)
    ctx = Ctx()
    ctx.nc = nc
    ctx.P = Prog()
    ctx.inp = {}
    for name in needed_inputs(subs):
        shp = list(INPUT_SPECS[name])
        if len(shp) == 1:
            shp = [1] + shp
        ctx.inp[name] = nc.dram_tensor(name, shp, F32, kind="ExternalInput")
    xin = nc.dram_tensor("xin", [D, T], F32, kind="ExternalInput")
    rankmask = nc.dram_tensor("rankmask", [128, 1], F32, kind="ExternalInput")
    xout = nc.dram_tensor("xout", [D, T], F32, kind="ExternalOutput")
    ctx.H3 = nc.dram_tensor("h3scratch", [4096, T], BF16,
                            kind="ExternalOutput" if "h3" in DEBUG
                            else "Internal")
    ctx.GL = nc.dram_tensor("glspill", [8192, 512], BF16)
    ctx.C2 = 512
    ctx.MS = nc.dram_tensor("msscratch", [2048, T], BF16,
                            kind="ExternalOutput" if "ms" in DEBUG
                            else "Internal")
    ctx.CCIN = nc.dram_tensor("ccin", [16, 128], F32)
    ctx.CCOUT = nc.dram_tensor("ccout", [128, 128], F32)

    with ExitStack() as st:
        en = st.enter_context
        ctx.X = ColAlloc("X", en(nc.sbuf_tensor("Xr", [128, NCH * T], F32)),
                         NCH * T, T)
        ctx.NB = ColAlloc("NB", en(nc.sbuf_tensor("NBr", [128, NCH * T],
                                                  BF16)), NCH * T, T)
        ctx.WP = ColAlloc("WP", en(nc.sbuf_tensor("WPr", [128, 20480], BF16)),
                          20480, 1024)
        ctx.AB16 = ColAlloc("AB16", en(nc.sbuf_tensor("AB16r", [128, 12288],
                                                      BF16)), 12288, 1024)
        ctx.AF32 = ColAlloc("AF32", en(nc.sbuf_tensor("AF32r", [128, 6144],
                                                      F32)), 6144, 512)
        ctx.SROW = ColAlloc("SROW", en(nc.sbuf_tensor("SROWr", [128, 3072],
                                                      F32)), 3072, 128)
        ctx.SROWB = ColAlloc("SROWB", en(nc.sbuf_tensor("SROWBr", [128, 3072],
                                                        BF16)), 3072, 128)
        cstb = en(nc.sbuf_tensor("CONSTB", [128, 1280], BF16))
        cstf = en(nc.sbuf_tensor("CONSTF", [128, 264], F32))
        rmsk = en(nc.sbuf_tensor("RMASK", [128, 1], F32))
        ctx.IDX32 = en(nc.sbuf_tensor("IDX32", [128, 8], U32))
        ctx.banks = [Bank(en(nc.psum_tensor(f"pb{i}", [128, PB], F32)), i)
                     for i in range(6)]
        ctx.banks += [Bank(en(nc.psum_tensor(f"pb{i}", [128, 2 * PB], BF16)),
                           i) for i in (6, 7)]
        ctx.bank_rr_f32 = 0
        ctx.pbank_allowed = (0, 1, 2, 3, 4, 5)
        ctx.bank_rr_b16 = 6
        esem = {n: en(nc.semaphore(f"es_{n}")) for n in ENGINES}
        dsem = {n: [en(nc.semaphore(f"ds_{n}{k}")) for k in range(4)]
                for n in ENGINES}
        csem = en(nc.semaphore("cs"))
        block = en(nc.Block())

        ctx.ONESB_COL = cstb[:, 0:1]
        ctx.ONESB_ROW = cstb[0:1, 0:128]
        ctx.IDB = cstb[:, 128:256]
        ctx.ONEE = cstb[:, 256:1280]
        ctx.ONESF = cstf[:, 0:1]
        ctx.ONESF_ROW = cstf[0:1, 0:128]
        ctx.IDF = cstf[:, 136:264]
        ctx.RMASK = rmsk
        P = ctx.P
        memset(ctx, "vector", cstb[:, 0:1280], ["CONST"], 1.0, tag="c_onesb")
        memset(ctx, "vector", cstf[:, 0:264], ["CONST"], 1.0, tag="c_onesf")
        P.op("gpsimd", lambda g: g.affine_select(
            cstb[:, 128:256], cstb[:, 128:256], [[1, 128]], ALU.is_equal, 0.0,
            base=0, channel_multiplier=-1),
            reads=["CONST"], writes=["CONST"], tag="c_idb")
        P.op("gpsimd", lambda g: g.affine_select(
            cstf[:, 136:264], cstf[:, 136:264], [[1, 128]], ALU.is_equal,
            0.0, base=0, channel_multiplier=-1),
            reads=["CONST"], writes=["CONST"], tag="c_idf")
        for e8 in range(8):
            P.op("gpsimd", lambda g, e8=e8: g.affine_select(
                cstb[0:8, 256 + 128 * e8:256 + 128 * (e8 + 1)],
                cstb[0:8, 256 + 128 * e8:256 + 128 * (e8 + 1)],
                [[0, 128]], ALU.is_equal, 0.0,
                base=-e8, channel_multiplier=1),
                reads=["CONST"], writes=["CONST"], tag="c_onee")
        dma(ctx, "sync", rmsk[:, :], ["RMASK"], rankmask.ap()[:, :],
            ["HBM:rmask"], tag="c_rmask")

        for c in range(NCH):
            dma(ctx, "sync", ctx.X.h[:, T * c:T * (c + 1)], [f"X:{c}"],
                xin.ap()[128 * c:128 * (c + 1), :], ["HBM:xin"], tag="xload")

        for s in subs:
            SUBLAYERS[s](ctx)

        for c in range(NCH):
            dma(ctx, "sync", xout.ap()[128 * c:128 * (c + 1), :],
                [f"XOUT:{c}"], ctx.X.h[:, T * c:T * (c + 1)], [f"X:{c}"],
                tag="xstore")

        ctx.P.finalize(block, esem, dsem, csem)
    return nc, ctx


def kernel(**inputs):
    subs = [s for s in ALL_SUBS if s in SUBLAYERS]
    return _run(inputs, subs)


def _run(inputs, subs, x_override=None):
    nc, ctx = build_program(subs)
    need = needed_inputs(subs)
    x_full = np.asarray(inputs["L0_attn_x"], np.float32) \
        if x_override is None else np.asarray(x_override, np.float32)
    in_maps = []
    for c in range(NCORES):
        m = {}
        for name in need:
            a = np.asarray(inputs[name], np.float32)
            if a.ndim == 1:
                a = a.reshape(1, -1)
            m[name] = a
        m["xin"] = np.ascontiguousarray(x_full[c].T)
        rm = np.zeros((128, 1), np.float32)
        for r in range(c):
            rm[16 * r, 0] = 1.0
        m["rankmask"] = rm
        in_maps.append(m)
    res = run_bass_kernel_spmd(nc, in_maps, core_ids=list(range(NCORES)))
    out = np.stack([np.ascontiguousarray(np.asarray(r["xout"]).T)
                    for r in res.results])
    if DEBUG:
        return out, res.results
    return out
